# version 01
# baseline (speedup 1.0000x reference)
import math
import numpy as np
import ml_dtypes
import concourse.bass as bass
import concourse.mybir as mybir
from concourse.bass_utils import run_bass_kernel_spmd

F32 = mybir.dt.float32
BF16 = mybir.dt.bfloat16
AX = mybir.AxisListType.X
ALU = mybir.AluOpType
AF = mybir.ActivationFunctionType

T, D, NC, DC = 2048, 1024, 16, 8
NH, HD, DR, DKV = 16, 64, 32, 128
E, DFF = 8, 4096
SC_MLA = 1.0 / math.sqrt(HD + DR)
SC_FFN = 1.0 / math.sqrt(HD)
EPS = 1e-5

TAPS = ()
STAGES = 99


class KB:
    def __init__(self, nc):
        self.nc = nc
        self.streams = {n: [] for n in ("t", "v", "s", "g", "y")}
        self.cnt = {}
        self.handles = {}
        self.lastw = {}
        self.free = {}

    def sem(self, name):
        self.handles[name] = self.nc.semaphore(name).__enter__()
        self.cnt[name] = 0

    def wait(self, st, tok):
        if tok is None:
            return
        sem, val = tok
        assert val <= self.cnt[sem], f"wait {sem}>={val} > emitted {self.cnt[sem]}"
        if self.lastw.get((st, sem), 0) < val:
            self.streams[st].append(("w", sem, val))
            self.lastw[(st, sem)] = val

    AUTO_SERIAL = {"v": "pv", "s": "ps"}

    def op(self, st, fn, inc=None, n=1):
        auto = self.AUTO_SERIAL.get(st)
        if auto is not None:
            self.wait(st, (auto, self.cnt[auto]))
            if inc is None:
                inc, n = auto, 1
        self.streams[st].append(("c", fn, inc, n))
        if inc is not None:
            self.cnt[inc] += n
            return (inc, self.cnt[inc])
        return None

    def tok(self, sem):
        return (sem, self.cnt[sem])

    def need(self, st, *keys):
        for k in keys:
            self.wait(st, self.free.get(k))

    def done(self, key, tok):
        self.free[key] = tok

    def barrier(self):
        for st in ("t", "v", "s", "g", "y"):
            for sem in ("pt", "pv", "ps", "dy", "dg"):
                self.wait(st, (sem, self.cnt[sem]))

    def replay(self, st, eng):
        for item in self.streams[st]:
            if item[0] == "w":
                eng.wait_ge(self.handles[item[1]], item[2])
            else:
                _, fn, inc, n = item
                ins = fn(eng)
                if inc is not None:
                    ins.then_inc(self.handles[inc], n)


def build_program(taps=(), stages=99):
    nc = bass.Bass("TRN2", target_bir_lowering=False)
    tap_req = set(taps)

    dr = {}
    def din(name, shape, dt=F32):
        dr[name] = nc.dram_tensor(name, list(shape), dt, kind="ExternalInput").ap()
    din("x", (T, D))
    din("w_kv_down", (D, DKV)); din("w_k_up", (DKV, D)); din("w_v_up", (DKV, D))
    din("w_q", (D, D)); din("w_q_rope", (D, DR)); din("w_k_rope", (DKV, DR))
    din("w_o1", (D, D))
    din("ln1_g", (128, 8)); din("ln1_b", (128, 8))
    din("w_qkv", (D, 3 * D))
    din("b_qkv_q", (64, 16)); din("b_qkv_k", (64, 16)); din("b_qkv_v", (1, D))
    din("w_o2", (D, D)); din("b_o", (1, D))
    din("ln2_g", (128, 8)); din("ln2_b", (128, 8))
    din("gate_w", (128, 64)); din("gate_b", (1, 8))
    din("W1", (E, D, DFF)); din("b1", (E, DFF))
    din("W2", (E, DFF, D)); din("b2", (E, D))
    din("c_idn", (128, 128))
    din("c_cos", (T, 16)); din("c_sin", (T, 16))
    din("c_maskT", (128, 4 * 512), BF16)
    din("c_ones", (1, T), BF16)
    din("c_onescol", (128, 1), BF16)
    out_d = nc.dram_tensor("out", [T, D], F32, kind="ExternalOutput").ap()
    tap_d = {}
    for tp in tap_req - {"min0", "noW", "serial"}:
        shp = {"xn": (D, T), "h": (D, T), "ckv": (DKV, T), "q": (D, T),
               "s": (T, D), "o": (D, T), "x2": (T, D), "h2": (D, T),
               "ew": (T, 8), "gpre": (T, D), "xnt": (T, D),
               "stm": (128, 16), "str": (128, 16)}[tp]
        tap_d[tp] = nc.dram_tensor("tap_" + tp, list(shp), F32, kind="ExternalOutput").ap()
    s_dram = nc.dram_tensor("s_scr", [T, D], F32, kind="Internal").ap()
    x2_dram = nc.dram_tensor("x2_scr", [T, D], F32, kind="Internal").ap()

    from contextlib import ExitStack
    ctx = ExitStack()
    def sb(name, shape, dt):
        return ctx.enter_context(nc.sbuf_tensor(name, list(shape), dt)).ap()

    AR1 = sb("AR1", (128, 16384), BF16)
    AR2 = sb("AR2", (128, 16384), BF16)
    AR3 = sb("AR3", (128, 16384), BF16)
    MOE = AR3.bitcast(F32)
    WAR = sb("WAR", (128, 20480), BF16)
    CKV = sb("CKV", (128, T), BF16)
    ROPE = sb("ROPE", (128, T), BF16)
    STG = sb("STG", (128, 6144), F32)
    q_st = sb("q_st", (128, T), BF16)
    k_st = sb("k_st", (128, T), BF16)
    v_st = sb("v_st", (128, NC * 65), BF16)
    sst = sb("sst", (128, 2 * 512), F32)
    ptb = sb("ptb", (128, 2 * 512), BF16)
    m_all = sb("m_all", (128, 16), F32)
    mT = sb("mT", (16, 128), F32)
    stats_m = sb("stats_m", (128, 16), F32)
    stats_r = sb("stats_r", (128, 16), F32)
    stats_mr = sb("stats_mr", (128, 16), F32)
    sm = sb("sm", (128, 128), F32)
    eps_c = sb("eps_c", (128, 1), F32)
    idn = sb("idn", (128, 128), F32)
    cost = sb("cost", (128, NC * 16), F32)
    sint = sb("sint", (128, NC * 16), F32)
    maskT = sb("maskT", (128, 4 * 512), BF16)
    onesr = sb("onesr", (1, T), BF16)
    onesc = sb("onesc", (128, 1), BF16)
    ln1g = sb("ln1g", (128, 8), F32); ln1b = sb("ln1b", (128, 8), F32)
    ln2g = sb("ln2g", (128, 8), F32); ln2b = sb("ln2b", (128, 8), F32)
    bqq = sb("bqq", (64, 16), F32); bqk = sb("bqk", (64, 16), F32)
    bqv = sb("bqv", (1, D), BF16)
    bo_row = sb("bo_row", (1, D), BF16)
    gw32 = sb("gw32", (128, 64), F32)
    gwhi = sb("gwhi", (128, 64), BF16); gwlo = sb("gwlo", (128, 64), BF16)
    grhi = sb("grhi", (128, 64), BF16); grlo = sb("grlo", (128, 64), BF16)
    bchi = sb("bchi", (128, 8), BF16); bclo = sb("bclo", (128, 8), BF16)
    bpr = sb("bpr", (1, 16), F32)
    bprh = sb("bprh", (1, 8), BF16); bprl = sb("bprl", (1, 8), BF16)
    bprbc = sb("bprbc", (128, 8), F32)
    gateb = sb("gateb", (1, 8), F32)
    ew_all = sb("ew_all", (128, 128), F32)
    ewCb = ROPE[0:8, 0:2048]
    ew_rbf = CKV[0:1, 1024:2048]
    ew_bc = CKV[:, 0:1024]
    whb = AR3[:, 0:4096]
    w1buf = WAR[:, 0:8192]
    w2buf = WAR[:, 8192:12288]
    h2s = WAR[:, 12288:20480]
    sstb = None
    b2bf = q_st[0:8, 0:1024]
    invhb = sb("invhb", (1, 512), BF16)
    invlb = sb("invlb", (1, 512), BF16)
    invu2 = sb("invu2", (1, 512), F32)
    nstg = sb("nstg", (64, 512), BF16)

    PS_A = ctx.enter_context(nc.psum_tensor("PS_A", [128, 2048], F32)).ap()
    PS_B = ctx.enter_context(nc.psum_tensor("PS_B", [128, 1024], F32)).ap()
    PS_C = ctx.enter_context(nc.psum_tensor("PS_C", [128, 1024], F32)).ap()

    kb = KB(nc)
    for s_ in ("pt", "pv", "ps", "dy", "dg", "od"):
        kb.sem(s_)
    W, need, done = kb.wait, kb.need, kb.done

    def t_op(fn, inc=None):
        return kb.op("t", fn, inc)
    def v_op(fn, inc=None):
        return kb.op("v", fn, inc)
    def s_op(fn, inc=None):
        return kb.op("s", fn, inc)
    def y_dma(dst, src, inc="dy"):
        return kb.op("y", lambda e, d=dst, s2=src: e.dma_start(out=d, in_=s2), inc, 16)
    def g_dma(dst, src, inc="dg"):
        return kb.op("g", lambda e, d=dst, s2=src: e.dma_start(out=d, in_=s2), inc, 16)

    SA, SB_, SC_, SD, SE, SF = 0, 1024, 2048, 3072, 4096, 5120
    inv_h = STG[0:1, SD + 512:SD + 1024]
    invbc = STG[0:64, SD:SD + 512]
    lghi = sst.bitcast(BF16)[:, 0:1024]
    lglo = sst.bitcast(BF16)[:, 1024:2048]
    b1row = sst.bitcast(BF16)[0:1, 0:2048]

    y_dma(idn[:, :], dr["c_idn"][:, :])
    y_dma(cost.rearrange("p (c f) -> p c f", c=16), dr["c_cos"].rearrange("(c p) f -> p c f", p=128))
    y_dma(sint.rearrange("p (c f) -> p c f", c=16), dr["c_sin"].rearrange("(c p) f -> p c f", p=128))
    y_dma(maskT[:, :], dr["c_maskT"][:, :])
    y_dma(onesr[:, :], dr["c_ones"][:, :])
    y_dma(onesc[:, :], dr["c_onescol"][:, :])
    y_dma(ln1g[:, :], dr["ln1_g"][:, :]); y_dma(ln1b[:, :], dr["ln1_b"][:, :])
    y_dma(ln2g[:, :], dr["ln2_g"][:, :]); y_dma(ln2b[:, :], dr["ln2_b"][:, :])
    y_dma(bqq[:, :], dr["b_qkv_q"][:, :]); y_dma(bqk[:, :], dr["b_qkv_k"][:, :])
    y_dma(gw32[:, :], dr["gate_w"][:, :])
    y_dma(gateb[:, :], dr["gate_b"][:, :])
    consts_tok = kb.tok("dy")

    O_WO2 = 0
    O_WKV, O_WQ, O_WQR, O_WKR = 0, 1024, 9216, 9472
    O_WKUP, O_WVUP, O_WO1 = 9504, 10528, 11552
    if "noW" not in tap_req:
        g_dma(WAR[:, O_WKV:O_WKV + 1024].rearrange("p (c m) -> p c m", c=8), dr["w_kv_down"].rearrange("(c p) m -> p c m", p=128))
        g_dma(WAR[:, O_WQ:O_WQ + 8192].rearrange("p (c m) -> p c m", c=8), dr["w_q"].rearrange("(c p) m -> p c m", p=128))
        g_dma(WAR[:, O_WQR:O_WQR + 256].rearrange("p (c m) -> p c m", c=8), dr["w_q_rope"].rearrange("(c p) m -> p c m", p=128))
        g_dma(WAR[:, O_WKR:O_WKR + 32], dr["w_k_rope"][:, :])
        g_dma(WAR[:, O_WKUP:O_WKUP + 1024], dr["w_k_up"][:, :])
        g_dma(WAR[:, O_WVUP:O_WVUP + 1024], dr["w_v_up"][:, :])
        g_dma(WAR[:, O_WO1:O_WO1 + 8192].rearrange("p (c m) -> p c m", c=8), dr["w_o1"].rearrange("(c p) m -> p c m", p=128))
        g_dma(bqv[:, :], dr["b_qkv_v"][:, :])
        g_dma(bo_row[:, :], dr["b_o"][:, :])
    w1w_tok = kb.tok("dg")

    W("v", consts_tok)
    v_op(lambda e: e.memset(eps_c[:, :], EPS))
    for kc in range(NC):
        vt = v_op(lambda e, kc_=kc: e.tensor_copy(
            v_st[:, kc_ * 65 + 64:kc_ * 65 + 65], onesc[:, 0:1]),
            "pv" if kc == NC - 1 else None)
    vst_ones_tok = vt

    W("t", consts_tok)
    for i in range(NC):
        c8 = i * 8
        need("y", ("stg", SA))
        xtok = y_dma(STG[:, SA:SA + 1024], dr["x"][i * 128:(i + 1) * 128, :])
        W("v", xtok)
        rt = v_op(lambda e, c_=c8: e.reduce_sum(sm[:, c_:c_ + 1], STG[:, SA:SA + 1024],
                                                axis=AX), "pv")
        W("v", rt)
        vt = v_op(lambda e, i_=i, c_=c8: e.tensor_scalar_mul(
            stats_m[:, i_:i_ + 1], sm[:, c_:c_ + 1], -1.0 / D), "pv")
        need("v", ("stg", SB_))
        st_ = v_op(lambda e, i_=i: e.tensor_scalar(
            STG[:, SB_:SB_ + 1024], STG[:, SA:SA + 1024],
            stats_m[:, i_:i_ + 1], None, op0=ALU.add), "pv")
        done(("stg", SA), st_)
        need("v", ("stg", SD))
        v_op(lambda e: e.tensor_mul(STG[:, SD:SD + 1024], STG[:, SB_:SB_ + 1024],
                                    STG[:, SB_:SB_ + 1024]))
        vt = v_op(lambda e, c_=c8: e.reduce_sum(sm[:, c_ + 1:c_ + 2],
                                                STG[:, SD:SD + 1024], axis=AX), "pv")
        W("s", vt)
        s_op(lambda e, c_=c8: e.activation(sm[:, c_ + 2:c_ + 3], sm[:, c_ + 1:c_ + 2],
                                           AF.Ln, bias=eps_c[:, 0:1], scale=1.0 / D))
        st_ = s_op(lambda e, i_=i, c_=c8: e.activation(
            stats_r[:, i_:i_ + 1], sm[:, c_ + 2:c_ + 3], AF.Exp, bias=0.0, scale=-0.5), "ps")
        W("v", st_)
        v_op(lambda e, i_=i: e.tensor_mul(stats_mr[:, i_:i_ + 1],
                                          stats_m[:, i_:i_ + 1], stats_r[:, i_:i_ + 1]))
        need("v", ("stg", SC_))
        st_ = v_op(lambda e, i_=i: e.tensor_scalar(
            STG[:, SC_:SC_ + 1024], STG[:, SB_:SB_ + 1024],
            stats_r[:, i_:i_ + 1], None, op0=ALU.mult), "pv")
        if "xnt" in tap_req:
            W("y", st_)
            xnt_t = y_dma(tap_d["xnt"][i * 128:(i + 1) * 128, :], STG[:, SC_:SC_ + 1024])
        rt = v_op(lambda e, c_=c8: e.reduce_sum(sm[:, c_ + 3:c_ + 4],
                                                STG[:, SC_:SC_ + 1024], axis=AX), "pv")
        W("v", rt)
        vt = v_op(lambda e, c_=c8: e.tensor_scalar_mul(
            sm[:, c_ + 4:c_ + 5], sm[:, c_ + 3:c_ + 4], -1.0 / D), "pv")
        st_ = v_op(lambda e, c_=c8: e.tensor_scalar(
            STG[:, SB_:SB_ + 1024], STG[:, SC_:SC_ + 1024],
            sm[:, c_ + 4:c_ + 5], None, op0=ALU.add), "pv")
        v_op(lambda e: e.tensor_mul(STG[:, SD:SD + 1024], STG[:, SB_:SB_ + 1024],
                                    STG[:, SB_:SB_ + 1024]))
        vt = v_op(lambda e, c_=c8: e.reduce_sum(sm[:, c_ + 5:c_ + 6],
                                                STG[:, SD:SD + 1024], axis=AX), "pv")
        done(("stg", SD), vt)
        W("s", vt)
        s_op(lambda e, c_=c8: e.activation(sm[:, c_ + 6:c_ + 7], sm[:, c_ + 5:c_ + 6],
                                           AF.Ln, bias=eps_c[:, 0:1], scale=1.0 / D))
        st_ = s_op(lambda e, c_=c8: e.activation(sm[:, c_ + 7:c_ + 8], sm[:, c_ + 6:c_ + 7],
                                           AF.Exp, bias=0.0, scale=-0.5), "ps")
        W("v", st_)
        st_ = v_op(lambda e, c_=c8: e.tensor_scalar(
            STG[:, SB_:SB_ + 1024], STG[:, SB_:SB_ + 1024],
            sm[:, c_ + 7:c_ + 8], None, op0=ALU.mult), "pv")
        W("t", st_)
        need("t", "psB0", "psB1", "psA0")
        for c in range(8):
            tk1 = t_op(lambda e, c_=c: e.transpose(
                PS_B[:, c_ * 128:(c_ + 1) * 128],
                STG[:, SC_ + c_ * 128:SC_ + (c_ + 1) * 128], idn[:, :]),
                "pt" if c == 7 else None)
        for c in range(8):
            tk2 = t_op(lambda e, c_=c: e.transpose(
                PS_A[:, c_ * 128:(c_ + 1) * 128],
                STG[:, SB_ + c_ * 128:SB_ + (c_ + 1) * 128], idn[:, :]),
                "pt" if c == 7 else None)
        W("v", tk1)
        vt = v_op(lambda e, i_=i: e.tensor_copy(
            AR1.rearrange("p (c t) -> p c t", c=8)[:, :, i_ * 128:(i_ + 1) * 128],
            PS_B.rearrange("p (c t) -> p c t", c=8)[:, :, :]), "pv")
        done("psB0", vt); done("psB1", vt)
        done(("stg", SC_), vt)
        if "min0" in tap_req:
            done("psA0", tk2)
            done(("stg", SB_), tk2)
            continue
        W("v", tk2)
        for c in range(8):
            vt = v_op(lambda e, i_=i, c_=c: e.tensor_scalar(
                AR2[:, c_ * 2048 + i_ * 128: c_ * 2048 + (i_ + 1) * 128],
                PS_A[:, c_ * 128:(c_ + 1) * 128],
                ln1g[:, c_:c_ + 1], ln1b[:, c_:c_ + 1], op0=ALU.mult, op1=ALU.add),
                "pv" if c == 7 else None)
        done("psA0", vt)
        done(("stg", SB_), vt)
        if "serial" in tap_req:
            kb.barrier()

    def tap_fm(name, src, rows=128):
        if name not in tap_req:
            return
        nchunks = {1024: 1}.get(src.shape[1] // 16, DC)
        total = src.shape[1]
        ncol = total // 1024
        for cj in range(ncol):
            need("v", ("stg", SA))
            vt = v_op(lambda e, c_=cj: e.tensor_copy(
                STG[0:rows, SA:SA + 1024], src[0:rows, c_ * 1024:(c_ + 1) * 1024]), "pv")
            W("y", vt)
            r0 = (cj // 2) * 128
            col = (cj % 2) * 1024
            if src.shape[1] == T:
                dt_ = y_dma(tap_d[name][0:rows, cj * 1024:(cj + 1) * 1024],
                            STG[0:rows, SA:SA + 1024])
            else:
                dt_ = y_dma(tap_d[name][r0:r0 + 128, col:col + 1024],
                            STG[0:rows, SA:SA + 1024])
            done(("stg", SA), dt_)

    if "stm" in tap_req:
        W("y", kb.tok("pv"))
        y_dma(tap_d["stm"][:, :], stats_m[:, :])
        y_dma(tap_d["str"][:, :], stats_r[:, :])
    tap_fm("xn", AR1)
    tap_fm("h", AR2)
    if stages <= 0:
        return _finish(nc, kb, ctx)

    W("t", w1w_tok)
    W("t", kb.tok("pv"))
    for nb in range(4):
        slot = "psC%d" % (nb % 2)
        need("t", slot)
        for c in range(8):
            tk = t_op(lambda e, c_=c, nb_=nb: e.matmul(
                PS_C[:, (nb_ % 2) * 512:(nb_ % 2) * 512 + 512],
                WAR[:, O_WKV + c_ * 128:O_WKV + (c_ + 1) * 128],
                AR1[:, c_ * 2048 + nb_ * 512: c_ * 2048 + nb_ * 512 + 512],
                start=(c_ == 0), stop=(c_ == 7)), "pt" if c == 7 else None)
        W("v", tk)
        vt = v_op(lambda e, nb_=nb: e.tensor_copy(
            CKV[:, nb_ * 512:nb_ * 512 + 512],
            PS_C[:, (nb_ % 2) * 512:(nb_ % 2) * 512 + 512]), "pv")
        done(slot, vt)
    tap_fm("ckv", CKV)

    for m in range(8):
        for nb in range(4):
            slot = "psC%d" % (nb % 2)
            need("t", slot)
            for c in range(8):
                tk = t_op(lambda e, c_=c, m_=m, nb_=nb: e.matmul(
                    PS_C[:, (nb_ % 2) * 512:(nb_ % 2) * 512 + 512],
                    WAR[:, O_WQ + c_ * 1024 + m_ * 128: O_WQ + c_ * 1024 + (m_ + 1) * 128],
                    AR1[:, c_ * 2048 + nb_ * 512: c_ * 2048 + nb_ * 512 + 512],
                    start=(c_ == 0), stop=(c_ == 7)), "pt" if c == 7 else None)
            W("v", tk)
            vt = v_op(lambda e, m_=m, nb_=nb: e.tensor_single_scalar(
                AR3[:, m_ * 2048 + nb_ * 512: m_ * 2048 + nb_ * 512 + 512],
                PS_C[:, (nb_ % 2) * 512:(nb_ % 2) * 512 + 512], SC_MLA, op=ALU.mult), "pv")
            done(slot, vt)
    tap_fm("q", AR3)

    QR, ROT, TA, TB = SE, SE + 32, SE + 64, SE + 80
    for i in range(NC):
        for which in range(2):
            need("t", "psC0")
            if which == 0:
                for c in range(8):
                    tk = t_op(lambda e, c_=c, i_=i: e.matmul(
                        PS_C[:, 0:32],
                        AR1[:, c_ * 2048 + i_ * 128: c_ * 2048 + (i_ + 1) * 128],
                        WAR[:, O_WQR + c_ * 32:O_WQR + (c_ + 1) * 32],
                        start=(c_ == 0), stop=(c_ == 7)), "pt" if c == 7 else None)
            else:
                tk = t_op(lambda e, i_=i: e.matmul(
                    PS_C[:, 0:32],
                    CKV[:, i_ * 128:(i_ + 1) * 128],
                    WAR[:, O_WKR:O_WKR + 32], start=True, stop=True), "pt")
            W("v", tk)
            need("v", ("stg", SE))
            vt = v_op(lambda e: e.tensor_copy(STG[:, QR:QR + 32], PS_C[:, 0:32]), "pv")
            done("psC0", vt)
            v_op(lambda e, i_=i: e.tensor_mul(STG[:, TA:TA + 16], STG[:, QR:QR + 16],
                                              cost[:, i_ * 16:(i_ + 1) * 16]))
            v_op(lambda e, i_=i: e.tensor_mul(STG[:, TB:TB + 16], STG[:, QR + 16:QR + 32],
                                              sint[:, i_ * 16:(i_ + 1) * 16]))
            v_op(lambda e: e.tensor_sub(STG[:, ROT:ROT + 16], STG[:, TA:TA + 16],
                                        STG[:, TB:TB + 16]))
            v_op(lambda e, i_=i: e.tensor_mul(STG[:, TA:TA + 16], STG[:, QR + 16:QR + 32],
                                              cost[:, i_ * 16:(i_ + 1) * 16]))
            v_op(lambda e, i_=i: e.tensor_mul(STG[:, TB:TB + 16], STG[:, QR:QR + 16],
                                              sint[:, i_ * 16:(i_ + 1) * 16]))
            vt = v_op(lambda e: e.tensor_add(STG[:, ROT + 16:ROT + 32], STG[:, TA:TA + 16],
                                             STG[:, TB:TB + 16]), "pv")
            W("t", vt)
            need("t", "psC1")
            tk = t_op(lambda e: e.transpose(PS_C[0:32, 512:640], STG[:, ROT:ROT + 32],
                                            idn[:, :]), "pt")
            W("v", tk)
            base = 0 if which == 0 else 64
            scale = SC_MLA if which == 0 else 1.0
            vt = v_op(lambda e, i_=i, b_=base, s2=scale: e.tensor_single_scalar(
                ROPE[b_:b_ + 32, i_ * 128:(i_ + 1) * 128],
                PS_C[0:32, 512:640], s2, op=ALU.mult), "pv")
            done("psC1", vt)
            done(("stg", SE), vt)
    rope_tok = kb.tok("pv")

    if stages <= 1:
        return _finish(nc, kb, ctx)

    def attention(kind):
        mla = kind == "mla"
        KQ = 96 if mla else 64
        KBr = 97 if mla else 65
        brow = 96 if mla else 64

        if mla:
            W("v", rope_tok)
            vt = v_op(lambda e: e.tensor_copy(k_st[64:96, :], ROPE[64:96, :]), "pv")
            W("g", rope_tok)
            g_dma(q_st[64:96, :], ROPE[0:32, :])
            W("y", vt)
            y_dma(k_st[96:97, :], onesr[:, :])
        else:
            W("y", kb.tok("pt"))
            y_dma(k_st[64:65, :], onesr[:, :])
            W("g", kb.tok("pt"))
        aug_tok = kb.tok("dy")

        for h in range(NH):
            pe_prev = kb.tok("pt")
            if mla:
                W("g", pe_prev)
                vt = g_dma(q_st[0:64, :], AR3[(h % 2) * 64:(h % 2) * 64 + 64,
                                              (h // 2) * 2048:(h // 2) * 2048 + 2048])
                W("t", vt)
                W("v", pe_prev)
                for nb in range(4):
                    slot = "psC%d" % (nb % 2)
                    need("t", slot)
                    tk = t_op(lambda e, h_=h, nb_=nb: e.matmul(
                        PS_C[0:64, (nb_ % 2) * 512:(nb_ % 2) * 512 + 512],
                        WAR[:, O_WKUP + h_ * 64:O_WKUP + (h_ + 1) * 64],
                        CKV[:, nb_ * 512:nb_ * 512 + 512], start=True, stop=True), "pt")
                    W("v", tk)
                    vk = v_op(lambda e, nb_=nb: e.tensor_copy(
                        k_st[0:64, nb_ * 512:nb_ * 512 + 512],
                        PS_C[0:64, (nb_ % 2) * 512:(nb_ % 2) * 512 + 512]), "pv")
                    done(slot, vk)
                for kc in range(NC):
                    slot = "psC%d" % (kc % 2)
                    need("t", slot)
                    tk = t_op(lambda e, h_=h, kc_=kc: e.matmul(
                        PS_C[:, (kc_ % 2) * 512:(kc_ % 2) * 512 + 64],
                        CKV[:, kc_ * 128:(kc_ + 1) * 128],
                        WAR[:, O_WVUP + h_ * 64:O_WVUP + (h_ + 1) * 64],
                        start=True, stop=True), "pt")
                    W("v", tk)
                    W("v", vst_ones_tok)
                    vk = v_op(lambda e, kc_=kc: e.tensor_copy(
                        v_st[:, kc_ * 65:kc_ * 65 + 64],
                        PS_C[:, (kc_ % 2) * 512:(kc_ % 2) * 512 + 64]), "pv")
                    done(slot, vk)
            else:
                sl = h % 2
                off = sl * 1536
                need("g", ("whb", sl))
                g_dma(whb[:, off:off + 512].rearrange("p (c f) -> p c f", c=8),
                      dr["w_qkv"][:, h * 64:(h + 1) * 64].rearrange("(c p) f -> p c f", p=128))
                g_dma(whb[:, off + 512:off + 1024].rearrange("p (c f) -> p c f", c=8),
                      dr["w_qkv"][:, D + h * 64:D + (h + 1) * 64].rearrange("(c p) f -> p c f", p=128))
                gt = g_dma(whb[:, off + 1024:off + 1536].rearrange("p (c f) -> p c f", c=8),
                           dr["w_qkv"][:, 2 * D + h * 64:2 * D + (h + 1) * 64].rearrange("(c p) f -> p c f", p=128))
                W("t", gt)
                W("t", pe_prev)
                for nb in range(4):
                    slot = "psC%d" % (nb % 2)
                    need("t", slot)
                    for c in range(8):
                        tk = t_op(lambda e, nb_=nb, c_=c, o_=off: e.matmul(
                            PS_C[0:64, (nb_ % 2) * 512:(nb_ % 2) * 512 + 512],
                            whb[:, o_ + c_ * 64:o_ + (c_ + 1) * 64],
                            AR2[:, c_ * 2048 + nb_ * 512: c_ * 2048 + nb_ * 512 + 512],
                            start=(c_ == 0), stop=(c_ == 7)), "pt" if c == 7 else None)
                    W("v", tk)
                    W("v", pe_prev)
                    vk = v_op(lambda e, h_=h, nb_=nb: e.tensor_scalar(
                        q_st[0:64, nb_ * 512:nb_ * 512 + 512],
                        PS_C[0:64, (nb_ % 2) * 512:(nb_ % 2) * 512 + 512],
                        bqq[:, h_:h_ + 1], SC_FFN, op0=ALU.add, op1=ALU.mult), "pv")
                    done(slot, vk)
                for nb in range(4):
                    slot = "psC%d" % (nb % 2)
                    need("t", slot)
                    for c in range(8):
                        tk = t_op(lambda e, nb_=nb, c_=c, o_=off: e.matmul(
                            PS_C[0:64, (nb_ % 2) * 512:(nb_ % 2) * 512 + 512],
                            whb[:, o_ + 512 + c_ * 64:o_ + 512 + (c_ + 1) * 64],
                            AR2[:, c_ * 2048 + nb_ * 512: c_ * 2048 + nb_ * 512 + 512],
                            start=(c_ == 0), stop=(c_ == 7)), "pt" if c == 7 else None)
                    W("v", tk)
                    vk = v_op(lambda e, h_=h, nb_=nb: e.tensor_scalar(
                        k_st[0:64, nb_ * 512:nb_ * 512 + 512],
                        PS_C[0:64, (nb_ % 2) * 512:(nb_ % 2) * 512 + 512],
                        bqk[:, h_:h_ + 1], None, op0=ALU.add), "pv")
                    done(slot, vk)
                for kc in range(NC):
                    slot = "psC%d" % (kc % 2)
                    need("t", slot)
                    for c in range(8):
                        t_op(lambda e, kc_=kc, c_=c, o_=off: e.matmul(
                            PS_C[:, (kc_ % 2) * 512:(kc_ % 2) * 512 + 64],
                            AR2[:, c_ * 2048 + kc_ * 128: c_ * 2048 + (kc_ + 1) * 128],
                            whb[:, o_ + 1024 + c_ * 64:o_ + 1024 + (c_ + 1) * 64],
                            start=(c_ == 0), stop=False))
                    tk = t_op(lambda e, h_=h, kc_=kc: e.matmul(
                        PS_C[:, (kc_ % 2) * 512:(kc_ % 2) * 512 + 64],
                        onesr[0:1, kc_ * 128:(kc_ + 1) * 128],
                        bqv[0:1, h_ * 64:(h_ + 1) * 64], start=False, stop=True), "pt")
                    W("v", tk)
                    W("v", vst_ones_tok)
                    vk = v_op(lambda e, kc_=kc: e.tensor_copy(
                        v_st[:, kc_ * 65:kc_ * 65 + 64],
                        PS_C[:, (kc_ % 2) * 512:(kc_ % 2) * 512 + 64]), "pv")
                    done(slot, vk)
                done(("whb", sl), kb.tok("pt"))
            qkv_tok = kb.tok("pv")

            W("t", qkv_tok)
            for qc in range(NC):
                nb = 4 if not mla else min(4, qc // 4 + 1)
                need("t", "psA0", "psA1")
                for b in range(nb):
                    tk = t_op(lambda e, qc_=qc, b_=b, K_=KQ: e.matmul(
                        PS_A[:, b_ * 512:(b_ + 1) * 512],
                        q_st[0:K_, qc_ * 128:(qc_ + 1) * 128],
                        k_st[0:K_, b_ * 512:(b_ + 1) * 512],
                        start=True, stop=True), "pt" if b == nb - 1 else None)
                W("v", tk)
                vt = v_op(lambda e, qc_=qc, n_=nb: e.reduce_max(
                    m_all[:, qc_:qc_ + 1], PS_A[:, 0:n_ * 512], axis=AX, negate=True), "pv")
                done("psA0", vt); done("psA1", vt)
            W("t", kb.tok("pv"))
            need("t", "psC0")
            tk = t_op(lambda e: e.transpose(PS_C[0:16, 0:128], m_all[:, :], idn[:, :]), "pt")
            W("v", tk)
            vt = v_op(lambda e: e.tensor_copy(mT[:, :], PS_C[0:16, 0:128]), "pv")
            done("psC0", vt)
            W("g", vt)
            mrow_tok = g_dma(q_st[brow:brow + 1, :].rearrange("o (c q) -> o c q", c=16),
                             mT[:, :])

            for qb in range(4):
                kcs = list(range((qb + 1) * 4)) if mla else list(range(16))
                pvk = "psA%d" % (qb % 2)
                need("t", pvk)
                for j, kc in enumerate(kcs):
                    ssl = j % 2
                    need("t", ("psB2", ssl))
                    if j == 0:
                        W("t", mrow_tok)
                        W("t", aug_tok)
                    tk = t_op(lambda e, kc_=kc, qb_=qb, ssl_=ssl, K_=KBr: e.matmul(
                        PS_B[:, ssl_ * 512:ssl_ * 512 + 512],
                        k_st[0:K_, kc_ * 128:(kc_ + 1) * 128],
                        q_st[0:K_, qb_ * 512:qb_ * 512 + 512],
                        start=True, stop=True), "pt")
                    W("v", tk)
                    need("v", ("sst", ssl))
                    vt = v_op(lambda e, ssl_=ssl: e.tensor_copy(
                        sst[:, ssl_ * 512:ssl_ * 512 + 512],
                        PS_B[:, ssl_ * 512:ssl_ * 512 + 512]), "pv")
                    done(("psB2", ssl), vt)
                    W("s", vt)
                    need("s", ("ptb", ssl))
                    st_ = s_op(lambda e, ssl_=ssl: e.activation(
                        ptb[:, ssl_ * 512:ssl_ * 512 + 512],
                        sst[:, ssl_ * 512:ssl_ * 512 + 512], AF.Exp,
                        bias=0.0, scale=1.0), "ps")
                    done(("sst", ssl), st_)
                    gtk = st_
                    if mla and kc >= qb * 4:
                        a = kc - qb * 4
                        W("v", st_)
                        vt = v_op(lambda e, ssl_=ssl, a_=a: e.tensor_mul(
                            ptb[:, ssl_ * 512:ssl_ * 512 + 512],
                            ptb[:, ssl_ * 512:ssl_ * 512 + 512],
                            maskT[:, a_ * 512:(a_ + 1) * 512]), "pv")
                        gtk = vt
                    W("t", gtk)
                    tk = t_op(lambda e, kc_=kc, q_=qb, ssl_=ssl, f_=(j == 0),
                              l_=(j == len(kcs) - 1): e.matmul(
                        PS_A[0:65, (q_ % 2) * 512:(q_ % 2) * 512 + 512],
                        v_st[:, kc_ * 65:(kc_ + 1) * 65],
                        ptb[:, ssl_ * 512:ssl_ * 512 + 512],
                        start=f_, stop=l_), "pt")
                    done(("ptb", ssl), tk)
                pv_done = tk
                W("v", pv_done)
                need("v", "invb", ("stg", SD))
                rt = v_op(lambda e, q_=qb: e.reciprocal(
                    inv_h[:, :], PS_A[64:65, (q_ % 2) * 512:(q_ % 2) * 512 + 512]), "pv")
                W("v", rt)
                v_op(lambda e: e.tensor_copy(invhb[0:1, :], inv_h[:, :]))
                v_op(lambda e: e.tensor_copy(invu2[:, :], invhb[0:1, :]))
                vt = v_op(lambda e: e.tensor_sub(invlb[0:1, :], inv_h[:, :],
                                                 invu2[:, :]), "pv")
                W("t", vt)
                need("t", "psC1")
                t_op(lambda e: e.matmul(PS_C[0:64, 512:1024], onesr[0:1, 0:64],
                                        invhb[0:1, :], start=True, stop=False))
                tk = t_op(lambda e: e.matmul(PS_C[0:64, 512:1024], onesr[0:1, 0:64],
                                             invlb[0:1, :], start=False, stop=True), "pt")
                W("v", tk)
                vt = v_op(lambda e: e.tensor_copy(invbc[:, :], PS_C[0:64, 512:1024]), "pv")
                done("psC1", vt)
                if h % 2 == 0:
                    vt2 = v_op(lambda e, h_=h, q_=qb: e.tensor_mul(
                        AR1[0:64,
                            (h_ // 2) * 2048 + q_ * 512:(h_ // 2) * 2048 + q_ * 512 + 512],
                        PS_A[0:64, (q_ % 2) * 512:(q_ % 2) * 512 + 512], invbc[:, :]), "pv")
                    done(pvk, vt2)
                else:
                    need("v", "nstg")
                    vt2 = v_op(lambda e, q_=qb: e.tensor_mul(
                        nstg[:, :],
                        PS_A[0:64, (q_ % 2) * 512:(q_ % 2) * 512 + 512], invbc[:, :]), "pv")
                    done(pvk, vt2)
                    W("g", vt2)
                    gt2 = g_dma(AR1[64:128,
                                    (h // 2) * 2048 + qb * 512:(h // 2) * 2048 + qb * 512 + 512],
                                nstg[:, :])
                    done("nstg", gt2)
                done("invb", vt2)
                done(("stg", SD), vt2)

    attention("mla")
    if stages <= 2:
        return _finish(nc, kb, ctx)

    mla_tok = kb.tok("pv")
    W("t", kb.tok("dg"))
    for i in range(NC):
        need("t", "psB0", "psB1")
        W("t", mla_tok)
        for nb in range(2):
            for c in range(8):
                tk = t_op(lambda e, i_=i, nb_=nb, c_=c: e.matmul(
                    PS_B[:, nb_ * 512:nb_ * 512 + 512],
                    AR1[:, c_ * 2048 + i_ * 128: c_ * 2048 + (i_ + 1) * 128],
                    WAR[:, O_WO1 + c_ * 1024 + nb_ * 512: O_WO1 + c_ * 1024 + nb_ * 512 + 512],
                    start=(c_ == 0), stop=(c_ == 7)), "pt" if c == 7 else None)
        W("v", tk)
        need("v", ("stg", SF))
        vt = v_op(lambda e: e.tensor_copy(STG[:, SF:SF + 1024], PS_B[:, :]), "pv")
        done("psB0", vt); done("psB1", vt)
        W("y", vt)
        dt_ = y_dma(s_dram[i * 128:(i + 1) * 128, :], STG[:, SF:SF + 1024])
        if "s" in tap_req:
            dt_ = y_dma(tap_d["s"][i * 128:(i + 1) * 128, :], STG[:, SF:SF + 1024])
        done(("stg", SF), dt_)
    s_mm_tok = kb.tok("pt")

    if stages <= 3:
        return _finish(nc, kb, ctx)

    W("g", s_mm_tok)
    wo2_tok = g_dma(WAR[:, O_WO2:O_WO2 + 8192].rearrange("p (c m) -> p c m", c=8),
                    dr["w_o2"].rearrange("(c p) m -> p c m", p=128))

    W("v", s_mm_tok)
    attention("ffn")
    ffn_tok = kb.tok("pv")
    tap_fm("o", AR1)
    if stages <= 4:
        return _finish(nc, kb, ctx)

    W("v", consts_tok)
    need("v", ("stg", SE))
    for c in range(8):
        v_op(lambda e, c_=c: e.tensor_scalar(
            STG[:, SE + c_ * 8:SE + (c_ + 1) * 8], gw32[:, c_ * 8:(c_ + 1) * 8],
            ln2g[:, c_:c_ + 1], None, op0=ALU.mult))
    v_op(lambda e: e.tensor_copy(gwhi[:, :], STG[:, SE:SE + 64]))
    v_op(lambda e: e.tensor_copy(STG[:, SE + 64:SE + 128], gwhi[:, :]))
    v_op(lambda e: e.tensor_sub(gwlo[:, :], STG[:, SE:SE + 64], STG[:, SE + 64:SE + 128]))
    v_op(lambda e: e.tensor_copy(grhi[:, :], gw32[:, :]))
    v_op(lambda e: e.tensor_copy(STG[:, SE + 64:SE + 128], grhi[:, :]))
    v_op(lambda e: e.tensor_sub(grlo[:, :], gw32[:, :], STG[:, SE + 64:SE + 128]))
    v_op(lambda e: e.tensor_copy(bchi[:, :], ln2b[:, :]))
    v_op(lambda e: e.tensor_copy(STG[:, SE + 64:SE + 72], bchi[:, :]))
    vt = v_op(lambda e: e.tensor_sub(bclo[:, :], ln2b[:, :], STG[:, SE + 64:SE + 72]), "pv")
    done(("stg", SE), vt)
    W("t", vt)
    need("t", "psC0")
    np_ = 0
    for c in range(8):
        for bb, gg in ((bchi, grhi), (bchi, grlo), (bclo, grhi)):
            np_ += 1
            tk = t_op(lambda e, c_=c, b_=bb, g_=gg, f_=(np_ == 1), l_=(np_ == 24): e.matmul(
                PS_C[0:1, 0:8], b_[:, c_:c_ + 1], g_[:, c_ * 8:(c_ + 1) * 8],
                start=f_, stop=l_), "pt" if np_ == 24 else None)
    W("v", tk)
    v_op(lambda e: e.tensor_copy(bpr[0:1, 0:8], PS_C[0:1, 0:8]))
    v_op(lambda e: e.tensor_add(bpr[0:1, 0:8], bpr[0:1, 0:8], gateb[0:1, :]))
    v_op(lambda e: e.tensor_copy(bprh[0:1, :], bpr[0:1, 0:8]))
    v_op(lambda e: e.tensor_copy(bpr[0:1, 8:16], bprh[0:1, :]))
    vt = v_op(lambda e: e.tensor_sub(bprl[0:1, :], bpr[0:1, 0:8], bpr[0:1, 8:16]), "pv")
    done("psC0", vt)
    W("t", vt)
    need("t", "psC0")
    t_op(lambda e: e.matmul(PS_C[:, 0:8], onesr[0:1, 0:128], bprh[0:1, :],
                            start=True, stop=False))
    tk = t_op(lambda e: e.matmul(PS_C[:, 0:8], onesr[0:1, 0:128], bprl[0:1, :],
                                 start=False, stop=True), "pt")
    W("v", tk)
    gp_tok = v_op(lambda e: e.tensor_copy(bprbc[:, :], PS_C[:, 0:8]), "pv")
    done("psC0", gp_tok)

    W("t", kb.tok("dg"))
    for i in range(NC):
        c8 = i * 8
        need("y", ("stg", SA))
        xtok = y_dma(STG[:, SA:SA + 1024], dr["x"][i * 128:(i + 1) * 128, :])
        need("t", "psB0", "psB1")
        W("t", ffn_tok)
        W("t", wo2_tok)
        for nb in range(2):
            for c in range(8):
                t_op(lambda e, i_=i, nb_=nb, c_=c: e.matmul(
                    PS_B[:, nb_ * 512:nb_ * 512 + 512],
                    AR1[:, c_ * 2048 + i_ * 128: c_ * 2048 + (i_ + 1) * 128],
                    WAR[:, O_WO2 + c_ * 1024 + nb_ * 512: O_WO2 + c_ * 1024 + nb_ * 512 + 512],
                    start=(c_ == 0), stop=False))
            tk = t_op(lambda e, i_=i, nb_=nb: e.matmul(
                PS_B[:, nb_ * 512:nb_ * 512 + 512],
                onesr[0:1, i_ * 128:(i_ + 1) * 128],
                bo_row[0:1, nb_ * 512:nb_ * 512 + 512], start=False, stop=True), "pt")
        W("v", xtok)
        need("v", ("stg", SD))
        st_ = v_op(lambda e, i_=i: e.tensor_scalar(
            STG[:, SD:SD + 1024], STG[:, SA:SA + 1024],
            stats_r[:, i_:i_ + 1], stats_mr[:, i_:i_ + 1],
            op0=ALU.mult, op1=ALU.add), "pv")
        done(("stg", SA), st_)
        W("v", tk)
        need("v", ("stg", SB_))
        vt = v_op(lambda e: e.tensor_add(STG[:, SB_:SB_ + 1024], PS_B[:, :],
                                         STG[:, SD:SD + 1024]), "pv")
        done("psB0", vt); done("psB1", vt)
        done(("stg", SD), vt)
        W("y", vt)
        dt_ = y_dma(x2_dram[i * 128:(i + 1) * 128, :], STG[:, SB_:SB_ + 1024])
        if "x2" in tap_req:
            dt_ = y_dma(tap_d["x2"][i * 128:(i + 1) * 128, :], STG[:, SB_:SB_ + 1024])
        rt = v_op(lambda e, c_=c8: e.reduce_sum(sm[:, c_:c_ + 1], STG[:, SB_:SB_ + 1024],
                                                axis=AX), "pv")
        W("v", rt)
        vt = v_op(lambda e, c_=c8: e.tensor_scalar_mul(sm[:, c_ + 1:c_ + 2],
                                                       sm[:, c_:c_ + 1], -1.0 / D), "pv")
        need("v", ("stg", SC_))
        st_ = v_op(lambda e, c_=c8: e.tensor_scalar(
            STG[:, SC_:SC_ + 1024], STG[:, SB_:SB_ + 1024],
            sm[:, c_ + 1:c_ + 2], None, op0=ALU.add), "pv")
        done(("stg", SB_), (("dy", dt_[1])))
        need("v", ("stg", SD))
        v_op(lambda e: e.tensor_mul(STG[:, SD:SD + 1024], STG[:, SC_:SC_ + 1024],
                                    STG[:, SC_:SC_ + 1024]))
        vt = v_op(lambda e, c_=c8: e.reduce_sum(sm[:, c_ + 2:c_ + 3],
                                                STG[:, SD:SD + 1024], axis=AX), "pv")
        done(("stg", SD), vt)
        W("s", vt)
        s_op(lambda e, c_=c8: e.activation(sm[:, c_ + 3:c_ + 4], sm[:, c_ + 2:c_ + 3],
                                           AF.Ln, bias=eps_c[:, 0:1], scale=1.0 / D))
        st2 = s_op(lambda e, c_=c8: e.activation(sm[:, c_ + 4:c_ + 5], sm[:, c_ + 3:c_ + 4],
                                           AF.Exp, bias=0.0, scale=-0.5), "ps")
        W("v", st2)
        st_ = v_op(lambda e, c_=c8: e.tensor_scalar(
            STG[:, SC_:SC_ + 1024], STG[:, SC_:SC_ + 1024],
            sm[:, c_ + 4:c_ + 5], None, op0=ALU.mult), "pv")
        W("t", st_)
        need("t", "psA0")
        for c in range(8):
            tk = t_op(lambda e, c_=c: e.transpose(
                PS_A[:, c_ * 128:(c_ + 1) * 128],
                STG[:, SC_ + c_ * 128:SC_ + (c_ + 1) * 128], idn[:, :]),
                "pt" if c == 7 else None)
        done(("stg", SC_), tk)
        W("v", tk)
        for c in range(8):
            v_op(lambda e, i_=i, c_=c: e.tensor_scalar(
                AR2[:, c_ * 2048 + i_ * 128: c_ * 2048 + (i_ + 1) * 128],
                PS_A[:, c_ * 128:(c_ + 1) * 128],
                ln2g[:, c_:c_ + 1], ln2b[:, c_:c_ + 1], op0=ALU.mult, op1=ALU.add))
        v_op(lambda e: e.tensor_copy(lghi[:, :], PS_A[:, 0:1024]))
        need("v", ("stg", SD))
        v_op(lambda e: e.tensor_copy(STG[:, SD:SD + 1024], lghi[:, :]))
        vt = v_op(lambda e: e.tensor_sub(lglo[:, :], PS_A[:, 0:1024],
                                         STG[:, SD:SD + 1024]), "pv")
        done("psA0", vt)
        done(("stg", SD), vt)
        W("t", vt); W("t", gp_tok)
        need("t", "psC0")
        np2 = 0
        for c in range(8):
            for lhs, rhs in ((lghi, gwhi), (lghi, gwlo), (lglo, gwhi)):
                np2 += 1
                tk = t_op(lambda e, c_=c, l_=lhs, r_=rhs, f_=(np2 == 1),
                          z_=(np2 == 24): e.matmul(
                    PS_C[:, 0:8], l_[:, c_ * 128:(c_ + 1) * 128],
                    r_[:, c_ * 8:(c_ + 1) * 8], start=f_, stop=z_),
                    "pt" if np2 == 24 else None)
        W("v", tk)
        need("v", ("stg", SE))
        v_op(lambda e: e.tensor_add(STG[:, SE:SE + 8], PS_C[:, 0:8], bprbc[:, :]))
        vt = v_op(lambda e, c_=c8: e.reduce_max(sm[:, c_ + 5:c_ + 6],
                                                STG[:, SE:SE + 8], axis=AX, negate=True), "pv")
        done("psC0", vt)
        W("v", vt)
        v_op(lambda e, c_=c8: e.tensor_scalar(sm[:, c_ + 6:c_ + 7], sm[:, c_ + 5:c_ + 6],
                                              -1.0, None, op0=ALU.mult))
        v_op(lambda e, c_=c8: e.tensor_scalar(
            STG[:, SE + 8:SE + 16], STG[:, SE:SE + 8],
            sm[:, c_ + 6:c_ + 7], None, op0=ALU.is_equal))
        v_op(lambda e, c_=c8: e.scalar_tensor_tensor(
            STG[:, SE + 16:SE + 24], STG[:, SE + 8:SE + 16], -1e30,
            STG[:, SE:SE + 8], op0=ALU.mult, op1=ALU.add))
        vt = v_op(lambda e, c_=c8: e.reduce_max(sm[:, c_ + 7:c_ + 8],
                                                STG[:, SE + 16:SE + 24], axis=AX), "pv")
        W("v", vt)
        v_op(lambda e, c_=c8: e.tensor_scalar(
            STG[:, SE + 24:SE + 32], STG[:, SE + 16:SE + 24],
            sm[:, c_ + 7:c_ + 8], None, op0=ALU.is_equal))
        W("s", kb.tok("pv"))
        st_ = s_op(lambda e, c_=c8: e.activation(
            sm[:, c_ + 2:c_ + 3], sm[:, c_ + 7:c_ + 8], AF.Exp,
            bias=sm[:, c_ + 5:c_ + 6], scale=1.0), "ps")
        W("v", st_)
        v_op(lambda e, c_=c8: e.tensor_scalar(sm[:, c_ + 3:c_ + 4], sm[:, c_ + 2:c_ + 3],
                                              1.0, None, op0=ALU.add))
        rt = v_op(lambda e, c_=c8: e.reciprocal(sm[:, c_ + 0:c_ + 1],
                                                sm[:, c_ + 3:c_ + 4]), "pv")
        W("v", rt)
        v_op(lambda e, c_=c8: e.tensor_mul(sm[:, c_ + 1:c_ + 2], sm[:, c_ + 2:c_ + 3],
                                           sm[:, c_ + 0:c_ + 1]))
        v_op(lambda e, c_=c8: e.tensor_scalar(
            STG[:, SE + 24:SE + 32], STG[:, SE + 24:SE + 32],
            sm[:, c_ + 1:c_ + 2], None, op0=ALU.mult))
        vt = v_op(lambda e, i_=i, c_=c8: e.scalar_tensor_tensor(
            ew_all[:, i_ * 8:(i_ + 1) * 8], STG[:, SE + 8:SE + 16],
            sm[:, c_ + 0:c_ + 1], STG[:, SE + 24:SE + 32],
            op0=ALU.mult, op1=ALU.add), "pv")
        done(("stg", SE), vt)
    h2_tok = kb.tok("pv")
    tap_fm("h2", AR2)
    if "ew" in tap_req:
        for i in range(NC):
            W("y", h2_tok)
            y_dma(tap_d["ew"][i * 128:(i + 1) * 128, :], ew_all[:, i * 8:(i + 1) * 8])
    if stages <= 5:
        return _finish(nc, kb, ctx)

    for i in range(NC):
        need("t", "psC0")
        W("t", h2_tok)
        tk = t_op(lambda e, i_=i: e.transpose(
            PS_C[0:8, 0:128], ew_all[:, i_ * 8:(i_ + 1) * 8], idn[:, :]), "pt")
        W("v", tk)
        vt = v_op(lambda e, i_=i: e.tensor_copy(
            ewCb[:, i_ * 128:(i_ + 1) * 128], PS_C[0:8, 0:128]), "pv")
        done("psC0", vt)
    ew_tok = kb.tok("pv")

    W("g", kb.tok("pt"))
    b2t = g_dma(b2bf, dr["b2"][:, :])
    for tile in range(2):
        first_acc = True
        for e_ in range(E):
            W("g", ew_tok)
            need("g", "ewrbf")
            gt = g_dma(ew_rbf[0:1, :], ewCb[e_:e_ + 1, tile * 1024:(tile + 1) * 1024])
            W("t", gt)
            need("t", "psC0", "psC1")
            t_op(lambda e: e.matmul(PS_C[:, 0:512], onesr[0:1, 0:128],
                                    ew_rbf[0:1, 0:512], start=True, stop=True))
            tk = t_op(lambda e: e.matmul(PS_C[:, 512:1024], onesr[0:1, 0:128],
                                         ew_rbf[0:1, 512:1024], start=True, stop=True), "pt")
            W("v", tk)
            need("v", "ewbc")
            vt = v_op(lambda e: e.tensor_copy(ew_bc[:, :], PS_C[:, 0:1024]), "pv")
            done("psC0", vt); done("psC1", vt)
            need("v", "h2s")
            for c in range(8):
                vt = v_op(lambda e, c_=c, t_=tile: e.tensor_mul(
                    h2s[:, c_ * 1024:(c_ + 1) * 1024],
                    AR2[:, c_ * 2048 + t_ * 1024: c_ * 2048 + t_ * 1024 + 1024],
                    ew_bc[:, :]), "pv" if c == 7 else None)
            h2s_tok = vt
            done("ewbc", vt)

            for grp in range(2):
                need("g", "b1row")
                b1t = g_dma(b1row[0:1, :],
                            dr["b1"][e_:e_ + 1, grp * 2048:(grp + 1) * 2048])
                for fi in range(16):
                    slab = fi // 4
                    if fi % 4 == 0:
                        sl1 = slab % 2
                        fbase = grp * 2048 + slab * 512
                        need("g", ("w1s", sl1))
                        w1t = g_dma(
                            w1buf[:, sl1 * 4096:sl1 * 4096 + 4096].rearrange(
                                "p (c f) -> p c f", c=8),
                            dr["W1"][e_, :, fbase:fbase + 512].rearrange(
                                "(c p) f -> p c f", p=128))
                    psk = "psB0" if fi % 2 == 0 else "psA1"
                    ps = PS_B if fi % 2 == 0 else PS_A
                    pso = 0 if fi % 2 == 0 else 1024
                    need("t", psk)
                    if fi % 2 == 0:
                        need("t", "psB1")
                    W("t", w1t)
                    W("t", h2s_tok)
                    W("t", b1t)
                    o1 = (slab % 2) * 4096
                    fr = fi % 4
                    for nb2 in range(2):
                        for c in range(8):
                            t_op(lambda e, c_=c, o_=o1, fr_=fr, nb_=nb2, p_=ps, po_=pso:
                                 e.matmul(
                                p_[:, po_ + nb_ * 512:po_ + nb_ * 512 + 512],
                                w1buf[:, o_ + c_ * 512 + fr_ * 128:
                                      o_ + c_ * 512 + fr_ * 128 + 128],
                                h2s[:, c_ * 1024 + nb_ * 512:c_ * 1024 + nb_ * 512 + 512],
                                start=(c_ == 0), stop=False))
                        tk = t_op(lambda e, fi_=fi, nb_=nb2, p_=ps, po_=pso:
                                  e.matmul(
                            p_[:, po_ + nb_ * 512:po_ + nb_ * 512 + 512],
                            b1row[0:1, fi_ * 128:(fi_ + 1) * 128],
                            ew_rbf[0:1, nb_ * 512:nb_ * 512 + 512],
                            start=False, stop=True), "pt")
                    if fi % 4 == 3:
                        done(("w1s", slab % 2), tk)
                    W("v", tk)
                    need("v", "hid")
                    vt = v_op(lambda e, fi_=fi, p_=ps, po_=pso: e.tensor_single_scalar(
                        AR1[:, fi_ * 1024:(fi_ + 1) * 1024],
                        p_[:, po_:po_ + 1024], 0.0, op=ALU.max), "pv")
                    done(psk, vt)
                    if fi % 2 == 0:
                        done("psB1", vt)
                hid_tok = kb.tok("pv")

                for dk in range(8):
                    sl2 = dk % 2
                    need("g", ("w2s", sl2))
                    fb = grp * 2048
                    w2t1 = g_dma(
                        w2buf[:, sl2 * 2048:sl2 * 2048 + 1024].rearrange(
                            "p (c d) -> p c d", c=8),
                        dr["W2"][e_, fb:fb + 1024, dk * 128:(dk + 1) * 128].rearrange(
                            "(c p) d -> p c d", p=128))
                    w2t = g_dma(
                        w2buf[:, sl2 * 2048 + 1024:sl2 * 2048 + 2048].rearrange(
                            "p (c d) -> p c d", c=8),
                        dr["W2"][e_, fb + 1024:fb + 2048, dk * 128:(dk + 1) * 128].rearrange(
                            "(c p) d -> p c d", p=128))
                    psk = "psA0" if dk % 2 == 0 else "psC0"
                    ps = PS_A if dk % 2 == 0 else PS_C
                    need("t", psk)
                    if dk % 2 == 1:
                        need("t", "psC1")
                    W("t", w2t)
                    W("t", hid_tok)
                    for fi in range(16):
                        for nb2 in range(2):
                            tk = t_op(lambda e, fi_=fi, nb_=nb2, o_=sl2 * 2048, p_=ps:
                                      e.matmul(
                                p_[:, nb_ * 512:nb_ * 512 + 512],
                                w2buf[:, o_ + fi_ * 128:o_ + (fi_ + 1) * 128],
                                AR1[:, fi_ * 1024 + nb_ * 512:fi_ * 1024 + nb_ * 512 + 512],
                                start=(fi_ == 0), stop=(fi_ == 15 and nb_ == 1)),
                                "pt" if (fi == 15 and nb2 == 1) else None)
                    done(("w2s", sl2), tk)
                    W("v", tk)
                    if first_acc:
                        need("v", "moeT")
                        vt = v_op(lambda e, dk_=dk, p_=ps: e.tensor_copy(
                            MOE[:, dk_ * 1024:(dk_ + 1) * 1024], p_[:, 0:1024]), "pv")
                    else:
                        vt = v_op(lambda e, dk_=dk, p_=ps: e.tensor_add(
                            MOE[:, dk_ * 1024:(dk_ + 1) * 1024], p_[:, 0:1024],
                            MOE[:, dk_ * 1024:(dk_ + 1) * 1024]), "pv")
                    done(psk, vt)
                    if dk % 2 == 1:
                        done("psC1", vt)
                done("hid", kb.tok("pt"))
                done("b1row", kb.tok("pt"))
                first_acc = False
            done("h2s", kb.tok("pt"))
            done("ewrbf", kb.tok("pt"))
        moe_tok = kb.tok("pv")

        for j in range(8):
            i = tile * 8 + j
            need("y", ("stg", SA), ("stg", SB_), ("stg", SC_))
            x2t = y_dma(STG[:, SA:SA + 1024], x2_dram[i * 128:(i + 1) * 128, :])
            stt_ = y_dma(STG[:, SB_:SB_ + 1024], s_dram[i * 128:(i + 1) * 128, :])
            xt = y_dma(STG[:, SC_:SC_ + 1024], dr["x"][i * 128:(i + 1) * 128, :])
            need("t", "psC0", "psC1")
            W("t", ew_tok)
            W("t", b2t)
            tk0 = t_op(lambda e, i_=i: e.matmul(
                PS_C[:, 0:512], ewCb[0:8, i_ * 128:(i_ + 1) * 128],
                b2bf[:, 0:512], start=True, stop=True))
            tk0 = t_op(lambda e, i_=i: e.matmul(
                PS_C[:, 512:1024], ewCb[0:8, i_ * 128:(i_ + 1) * 128],
                b2bf[:, 512:1024], start=True, stop=True), "pt")
            need("t", "psB0", "psB1")
            W("t", moe_tok)
            for dk in range(8):
                tk1 = t_op(lambda e, dk_=dk, j_=j: e.transpose(
                    PS_B[:, dk_ * 128:(dk_ + 1) * 128],
                    MOE[:, dk_ * 1024 + j_ * 128:dk_ * 1024 + (j_ + 1) * 128],
                    idn[:, :]), "pt" if dk == 7 else None)
            W("v", tk0); W("v", x2t)
            need("v", ("stg", SD))
            v_op(lambda e: e.tensor_add(STG[:, SD:SD + 1024], PS_C[:, 0:1024],
                                        STG[:, SA:SA + 1024]))
            W("v", tk1)
            vt = v_op(lambda e: e.tensor_add(STG[:, SD:SD + 1024], PS_B[:, 0:1024],
                                             STG[:, SD:SD + 1024]), "pv")
            done("moeT", kb.tok("pt"))
            done("psB0", vt); done("psB1", vt); done("psC0", vt); done("psC1", vt)
            done(("stg", SA), vt)
            if "gpre" in tap_req:
                W("y", vt)
                y_dma(tap_d["gpre"][i * 128:(i + 1) * 128, :], STG[:, SD:SD + 1024])
            W("s", vt)
            need("s", ("stg", SF))
            st2 = s_op(lambda e: e.activation(STG[:, SF:SF + 1024], STG[:, SD:SD + 1024],
                                              AF.Silu, bias=0.0, scale=1.0), "ps")
            done(("stg", SD), st2)
            W("v", st2); W("v", stt_); W("v", xt)
            v_op(lambda e: e.tensor_mul(STG[:, SF:SF + 1024], STG[:, SF:SF + 1024],
                                        STG[:, SB_:SB_ + 1024]))
            vt = v_op(lambda e: e.tensor_add(STG[:, SF:SF + 1024], STG[:, SF:SF + 1024],
                                             STG[:, SC_:SC_ + 1024]), "pv")
            done(("stg", SB_), vt)
            done(("stg", SC_), vt)
            W("y", vt)
            ot = y_dma(out_d[i * 128:(i + 1) * 128, :], STG[:, SF:SF + 1024], inc="od")
            done(("stg", SF), ot)

    return _finish(nc, kb, ctx)


def _finish(nc, kb, ctx):
    kb.wait("y", kb.tok("od"))
    kb.wait("y", kb.tok("dy"))
    kb.wait("g", kb.tok("dg"))
    with nc.Block() as block:
        @block.tensor
        def _(e):
            kb.replay("t", e)
        @block.vector
        def _(e):
            kb.replay("v", e)
        @block.scalar
        def _(e):
            kb.replay("s", e)
        @block.gpsimd
        def _(e):
            kb.replay("g", e)
        @block.sync
        def _(e):
            kb.replay("y", e)
    ctx.close()
    return nc


def _consts():
    inv = 1.0 / 10000.0 ** (np.arange(0, DR, 2, dtype=np.float32) / DR)
    tt = np.arange(T, dtype=np.float32)
    fr = np.outer(tt, inv)
    cos, sin = np.cos(fr).astype(np.float32), np.sin(fr).astype(np.float32)
    idn = np.eye(128, dtype=np.float32)
    masks = np.zeros((128, 4 * 512), dtype=ml_dtypes.bfloat16)
    for a, r in enumerate((0, 128, 256, 384)):
        ii = np.arange(128)[:, None] + r
        jj = np.arange(512)[None, :]
        masks[:, a * 512:(a + 1) * 512] = (ii <= jj).astype(ml_dtypes.bfloat16)
    return {"c_cos": cos, "c_sin": sin, "c_idn": idn, "c_maskT": masks,
            "c_ones": np.ones((1, T), dtype=ml_dtypes.bfloat16),
            "c_onescol": np.ones((128, 1), dtype=ml_dtypes.bfloat16)}


def _prep_shared(inputs):
    f = np.ascontiguousarray
    sh = {
        "w_kv_down": f(inputs["L0_attn_W_kv_down"]),
        "w_k_up": f(inputs["L0_attn_W_k_up"]),
        "w_v_up": f(inputs["L0_attn_W_v_up"]),
        "w_q": f(inputs["L0_attn_W_q"]),
        "w_q_rope": f(inputs["L0_attn_W_q_rope"]),
        "w_k_rope": f(inputs["L0_attn_W_k_rope"]),
        "w_o1": f(inputs["L0_attn_W_o"]),
        "ln1_g": f(inputs["L0_ffn_ln1_g"].reshape(8, 128).T),
        "ln1_b": f(inputs["L0_ffn_ln1_b"].reshape(8, 128).T),
        "w_qkv": f(inputs["L0_ffn_w_qkv"]),
        "b_qkv_q": f(inputs["L0_ffn_b_qkv"][0:D].reshape(16, 64).T),
        "b_qkv_k": f(inputs["L0_ffn_b_qkv"][D:2 * D].reshape(16, 64).T),
        "b_qkv_v": f(inputs["L0_ffn_b_qkv"][2 * D:3 * D].reshape(1, D)),
        "w_o2": f(inputs["L0_ffn_w_o"]),
        "b_o": f(inputs["L0_ffn_b_o"].reshape(1, D)),
        "ln2_g": f(inputs["L0_ffn_ln2_g"].reshape(8, 128).T),
        "ln2_b": f(inputs["L0_ffn_ln2_b"].reshape(8, 128).T),
        "gate_w": f(inputs["L0_ffn_gate_w"].reshape(8, 128, 8).transpose(1, 0, 2).reshape(128, 64)),
        "gate_b": f(inputs["L0_ffn_gate_b"].reshape(1, 8)),
        "W1": f(inputs["L0_ffn_W1"]),
        "b1": f(inputs["L0_ffn_b1"]),
        "W2": f(inputs["L0_ffn_W2"]),
        "b2": f(inputs["L0_ffn_b2"]),
    }
    sh.update(_consts())
    return sh


def kernel(**inputs):
    x = np.ascontiguousarray(np.asarray(inputs["L0_attn_x"], dtype=np.float32))
    shared = _prep_shared({k: np.asarray(v) for k, v in inputs.items()})
    nc = build_program(taps=TAPS, stages=STAGES)
    in_maps = []
    for c in range(8):
        m = dict(shared)
        m["x"] = x[c]
        in_maps.append(m)
    res = run_bass_kernel_spmd(nc, in_maps, core_ids=list(range(8)))
    out = np.stack([np.asarray(r["out"], dtype=np.float32) for r in res.results], axis=0)
    if TAPS:
        kernel.tap_results = res.results
    return out


# version 02
# speedup vs baseline: 1.1452x; 1.1452x over previous
import math
import numpy as np
import ml_dtypes
import concourse.bass as bass
import concourse.mybir as mybir
from concourse.bass_utils import run_bass_kernel_spmd

F32 = mybir.dt.float32
BF16 = mybir.dt.bfloat16
AX = mybir.AxisListType.X
ALU = mybir.AluOpType
AF = mybir.ActivationFunctionType

T, D, NC, DC = 2048, 1024, 16, 8
NH, HD, DR, DKV = 16, 64, 32, 128
E, DFF = 8, 4096
SC_MLA = 1.0 / math.sqrt(HD + DR)
SC_FFN = 1.0 / math.sqrt(HD)
EPS = 1e-5

TAPS = ()
STAGES = 99


class KB:
    def __init__(self, nc):
        self.nc = nc
        self.streams = {n: [] for n in ("t", "v", "s", "g", "y")}
        self.cnt = {}
        self.handles = {}
        self.lastw = {}
        self.free = {}

    def sem(self, name):
        self.handles[name] = self.nc.semaphore(name).__enter__()
        self.cnt[name] = 0

    def wait(self, st, tok):
        if tok is None:
            return
        sem, val = tok
        assert val <= self.cnt[sem], f"wait {sem}>={val} > emitted {self.cnt[sem]}"
        if self.lastw.get((st, sem), 0) < val:
            self.streams[st].append(("w", sem, val))
            self.lastw[(st, sem)] = val

    AUTO_SERIAL = {"v": "pv", "s": "ps"}

    def op(self, st, fn, inc=None, n=1):
        auto = self.AUTO_SERIAL.get(st)
        if auto is not None:
            self.wait(st, (auto, self.cnt[auto]))
            if inc is None:
                inc, n = auto, 1
        self.streams[st].append(("c", fn, inc, n))
        if inc is not None:
            self.cnt[inc] += n
            return (inc, self.cnt[inc])
        return None

    def tok(self, sem):
        return (sem, self.cnt[sem])

    def need(self, st, *keys):
        for k in keys:
            self.wait(st, self.free.get(k))

    def done(self, key, tok):
        self.free[key] = tok

    def barrier(self):
        for st in ("t", "v", "s", "g", "y"):
            for sem in ("pt", "pv", "ps", "dy", "dg"):
                self.wait(st, (sem, self.cnt[sem]))

    def replay(self, st, eng):
        for item in self.streams[st]:
            if item[0] == "w":
                eng.wait_ge(self.handles[item[1]], item[2])
            else:
                _, fn, inc, n = item
                ins = fn(eng)
                if inc is not None:
                    ins.then_inc(self.handles[inc], n)


def build_program(taps=(), stages=99):
    nc = bass.Bass("TRN2", target_bir_lowering=False)
    tap_req = set(taps)

    dr = {}
    def din(name, shape, dt=F32):
        dr[name] = nc.dram_tensor(name, list(shape), dt, kind="ExternalInput").ap()
    din("x", (T, D))
    din("w_kv_down", (D, DKV)); din("w_k_up", (DKV, D)); din("w_v_up", (DKV, D))
    din("w_q", (D, D)); din("w_q_rope", (D, DR)); din("w_k_rope", (DKV, DR))
    din("w_o1", (D, D))
    din("ln1_g", (128, 8)); din("ln1_b", (128, 8))
    din("w_qkv", (D, 3 * D))
    din("b_qkv_q", (64, 16)); din("b_qkv_k", (64, 16)); din("b_qkv_v", (1, D))
    din("w_o2", (D, D)); din("b_o", (1, D))
    din("ln2_g", (128, 8)); din("ln2_b", (128, 8))
    din("gate_w", (128, 64)); din("gate_b", (1, 8))
    din("W1", (E, D, DFF)); din("b1", (E, DFF))
    din("W2", (E, DFF, D)); din("b2", (E, D))
    din("c_idn", (128, 128))
    din("c_cos", (T, 16)); din("c_sin", (T, 16))
    din("c_maskT", (128, 4 * 512), BF16)
    din("c_ones", (1, T), BF16)
    din("c_onescol", (128, 1), BF16)
    out_d = nc.dram_tensor("out", [T, D], F32, kind="ExternalOutput").ap()
    tap_d = {}
    for tp in tap_req - {"min0", "noW", "serial"}:
        shp = {"xn": (D, T), "h": (D, T), "ckv": (DKV, T), "q": (D, T),
               "s": (T, D), "o": (D, T), "x2": (T, D), "h2": (D, T),
               "ew": (T, 8), "gpre": (T, D), "xnt": (T, D),
               "stm": (128, 16), "str": (128, 16)}[tp]
        tap_d[tp] = nc.dram_tensor("tap_" + tp, list(shp), F32, kind="ExternalOutput").ap()
    s_dram = nc.dram_tensor("s_scr", [T, D], F32, kind="Internal").ap()
    x2_dram = nc.dram_tensor("x2_scr", [T, D], F32, kind="Internal").ap()

    from contextlib import ExitStack
    ctx = ExitStack()
    def sb(name, shape, dt):
        return ctx.enter_context(nc.sbuf_tensor(name, list(shape), dt)).ap()

    AR1 = sb("AR1", (128, 16384), BF16)
    AR2 = sb("AR2", (128, 16384), BF16)
    AR3 = sb("AR3", (128, 16384), BF16)
    MOE = AR3.bitcast(F32)
    WAR = sb("WAR", (128, 20480), BF16)
    CKV = sb("CKV", (128, T), BF16)
    ROPE = sb("ROPE", (128, T), BF16)
    STG = sb("STG", (128, 6144), F32)
    q_st = sb("q_st", (128, T), BF16)
    k_st = sb("k_st", (128, T), BF16)
    v_st = sb("v_st", (128, NC * 65), BF16)
    sst = sb("sst", (128, 2 * 1024), BF16)
    ptb = sb("ptb", (128, 2 * 1024), BF16)
    m_all = sb("m_all", (128, 16), F32)
    mT = sb("mT", (16, 128), F32)
    stats_m = sb("stats_m", (128, 16), F32)
    stats_r = sb("stats_r", (128, 16), F32)
    stats_mr = sb("stats_mr", (128, 16), F32)
    sm = sb("sm", (128, 128), F32)
    eps_c = sb("eps_c", (128, 1), F32)
    idn = sb("idn", (128, 128), F32)
    cost = sb("cost", (128, NC * 16), F32)
    sint = sb("sint", (128, NC * 16), F32)
    maskT = sb("maskT", (128, 4 * 512), BF16)
    onesr = sb("onesr", (1, T), BF16)
    onesc = sb("onesc", (128, 1), BF16)
    ln1g = sb("ln1g", (128, 8), F32); ln1b = sb("ln1b", (128, 8), F32)
    ln2g = sb("ln2g", (128, 8), F32); ln2b = sb("ln2b", (128, 8), F32)
    bqq = sb("bqq", (64, 16), F32); bqk = sb("bqk", (64, 16), F32)
    bqv = sb("bqv", (1, D), BF16)
    bo_row = sb("bo_row", (1, D), BF16)
    gw32 = sb("gw32", (128, 64), F32)
    gwhi = sb("gwhi", (128, 64), BF16); gwlo = sb("gwlo", (128, 64), BF16)
    grhi = sb("grhi", (128, 64), BF16); grlo = sb("grlo", (128, 64), BF16)
    bchi = sb("bchi", (128, 8), BF16); bclo = sb("bclo", (128, 8), BF16)
    bpr = sb("bpr", (1, 16), F32)
    bprh = sb("bprh", (1, 8), BF16); bprl = sb("bprl", (1, 8), BF16)
    bprbc = sb("bprbc", (128, 8), F32)
    gateb = sb("gateb", (1, 8), F32)
    ew_all = sb("ew_all", (128, 128), F32)
    ewCb = ROPE[0:8, 0:2048]
    nstg = ROPE[0:64, 0:512]
    ew_rbf = CKV[0:1, 1024:2048]
    ew_bc = CKV[:, 0:1024]
    whb = AR3[:, 0:4096]
    w1buf = WAR[:, 0:8192]
    w2buf = WAR[:, 8192:12288]
    h2s = WAR[:, 12288:20480]
    sstb = None
    b2bf = q_st[0:8, 0:1024]
    invhb = sb("invhb", (1, 512), BF16)
    invlb = sb("invlb", (1, 512), BF16)
    invu2 = sb("invu2", (1, 512), F32)

    PS_A = ctx.enter_context(nc.psum_tensor("PS_A", [128, 2048], F32)).ap()
    PS_B = ctx.enter_context(nc.psum_tensor("PS_B", [128, 1024], F32)).ap()
    PS_C = ctx.enter_context(nc.psum_tensor("PS_C", [128, 1024], F32)).ap()

    kb = KB(nc)
    for s_ in ("pt", "pv", "ps", "dy", "dg", "od"):
        kb.sem(s_)
    W, need, done = kb.wait, kb.need, kb.done

    def t_op(fn, inc=None):
        return kb.op("t", fn, inc)
    def v_op(fn, inc=None):
        return kb.op("v", fn, inc)
    def s_op(fn, inc=None):
        return kb.op("s", fn, inc)
    def y_dma(dst, src, inc="dy"):
        return kb.op("y", lambda e, d=dst, s2=src: e.dma_start(out=d, in_=s2), inc, 16)
    def g_dma(dst, src, inc="dg"):
        return kb.op("g", lambda e, d=dst, s2=src: e.dma_start(out=d, in_=s2), inc, 16)

    SA, SB_, SC_, SD, SE, SF = 0, 1024, 2048, 3072, 4096, 5120
    inv_h = STG[0:1, SD + 512:SD + 1024]
    invbc = STG[0:64, SD:SD + 512]
    lghi = sst.bitcast(BF16)[:, 0:1024]
    lglo = sst.bitcast(BF16)[:, 1024:2048]
    b1row = sst.bitcast(BF16)[0:1, 0:2048]

    y_dma(idn[:, :], dr["c_idn"][:, :])
    y_dma(cost.rearrange("p (c f) -> p c f", c=16), dr["c_cos"].rearrange("(c p) f -> p c f", p=128))
    y_dma(sint.rearrange("p (c f) -> p c f", c=16), dr["c_sin"].rearrange("(c p) f -> p c f", p=128))
    y_dma(maskT[:, :], dr["c_maskT"][:, :])
    y_dma(onesr[:, :], dr["c_ones"][:, :])
    y_dma(onesc[:, :], dr["c_onescol"][:, :])
    y_dma(ln1g[:, :], dr["ln1_g"][:, :]); y_dma(ln1b[:, :], dr["ln1_b"][:, :])
    y_dma(ln2g[:, :], dr["ln2_g"][:, :]); y_dma(ln2b[:, :], dr["ln2_b"][:, :])
    y_dma(bqq[:, :], dr["b_qkv_q"][:, :]); y_dma(bqk[:, :], dr["b_qkv_k"][:, :])
    y_dma(gw32[:, :], dr["gate_w"][:, :])
    y_dma(gateb[:, :], dr["gate_b"][:, :])
    consts_tok = kb.tok("dy")

    O_WO2 = 0
    O_WKV, O_WQ, O_WQR, O_WKR = 0, 1024, 9216, 9472
    O_WKUP, O_WVUP, O_WO1 = 9504, 10528, 11552
    if "noW" not in tap_req:
        g_dma(WAR[:, O_WKV:O_WKV + 1024].rearrange("p (c m) -> p c m", c=8), dr["w_kv_down"].rearrange("(c p) m -> p c m", p=128))
        g_dma(WAR[:, O_WQ:O_WQ + 8192].rearrange("p (c m) -> p c m", c=8), dr["w_q"].rearrange("(c p) m -> p c m", p=128))
        g_dma(WAR[:, O_WQR:O_WQR + 256].rearrange("p (c m) -> p c m", c=8), dr["w_q_rope"].rearrange("(c p) m -> p c m", p=128))
        g_dma(WAR[:, O_WKR:O_WKR + 32], dr["w_k_rope"][:, :])
        g_dma(WAR[:, O_WKUP:O_WKUP + 1024], dr["w_k_up"][:, :])
        g_dma(WAR[:, O_WVUP:O_WVUP + 1024], dr["w_v_up"][:, :])
        g_dma(WAR[:, O_WO1:O_WO1 + 8192].rearrange("p (c m) -> p c m", c=8), dr["w_o1"].rearrange("(c p) m -> p c m", p=128))
        g_dma(bqv[:, :], dr["b_qkv_v"][:, :])
        g_dma(bo_row[:, :], dr["b_o"][:, :])
    w1w_tok = kb.tok("dg")

    W("v", consts_tok)
    v_op(lambda e: e.memset(eps_c[:, :], EPS))
    for kc in range(NC):
        vt = v_op(lambda e, kc_=kc: e.tensor_copy(
            v_st[:, kc_ * 65 + 64:kc_ * 65 + 65], onesc[:, 0:1]),
            "pv" if kc == NC - 1 else None)
    vst_ones_tok = vt

    W("t", consts_tok)
    for i in range(NC):
        c8 = i * 8
        need("y", ("stg", SA))
        xtok = y_dma(STG[:, SA:SA + 1024], dr["x"][i * 128:(i + 1) * 128, :])
        W("v", xtok)
        rt = v_op(lambda e, c_=c8: e.reduce_sum(sm[:, c_:c_ + 1], STG[:, SA:SA + 1024],
                                                axis=AX), "pv")
        W("v", rt)
        vt = v_op(lambda e, i_=i, c_=c8: e.tensor_scalar_mul(
            stats_m[:, i_:i_ + 1], sm[:, c_:c_ + 1], -1.0 / D), "pv")
        need("v", ("stg", SB_))
        st_ = v_op(lambda e, i_=i: e.tensor_scalar(
            STG[:, SB_:SB_ + 1024], STG[:, SA:SA + 1024],
            stats_m[:, i_:i_ + 1], None, op0=ALU.add), "pv")
        done(("stg", SA), st_)
        need("v", ("stg", SD))
        v_op(lambda e: e.tensor_mul(STG[:, SD:SD + 1024], STG[:, SB_:SB_ + 1024],
                                    STG[:, SB_:SB_ + 1024]))
        vt = v_op(lambda e, c_=c8: e.reduce_sum(sm[:, c_ + 1:c_ + 2],
                                                STG[:, SD:SD + 1024], axis=AX), "pv")
        W("s", vt)
        s_op(lambda e, c_=c8: e.activation(sm[:, c_ + 2:c_ + 3], sm[:, c_ + 1:c_ + 2],
                                           AF.Ln, bias=eps_c[:, 0:1], scale=1.0 / D))
        st_ = s_op(lambda e, i_=i, c_=c8: e.activation(
            stats_r[:, i_:i_ + 1], sm[:, c_ + 2:c_ + 3], AF.Exp, bias=0.0, scale=-0.5), "ps")
        W("v", st_)
        v_op(lambda e, i_=i: e.tensor_mul(stats_mr[:, i_:i_ + 1],
                                          stats_m[:, i_:i_ + 1], stats_r[:, i_:i_ + 1]))
        need("v", ("stg", SC_))
        st_ = v_op(lambda e, i_=i: e.tensor_scalar(
            STG[:, SC_:SC_ + 1024], STG[:, SB_:SB_ + 1024],
            stats_r[:, i_:i_ + 1], None, op0=ALU.mult), "pv")
        if "xnt" in tap_req:
            W("y", st_)
            xnt_t = y_dma(tap_d["xnt"][i * 128:(i + 1) * 128, :], STG[:, SC_:SC_ + 1024])
        rt = v_op(lambda e, c_=c8: e.reduce_sum(sm[:, c_ + 3:c_ + 4],
                                                STG[:, SC_:SC_ + 1024], axis=AX), "pv")
        W("v", rt)
        vt = v_op(lambda e, c_=c8: e.tensor_scalar_mul(
            sm[:, c_ + 4:c_ + 5], sm[:, c_ + 3:c_ + 4], -1.0 / D), "pv")
        st_ = v_op(lambda e, c_=c8: e.tensor_scalar(
            STG[:, SB_:SB_ + 1024], STG[:, SC_:SC_ + 1024],
            sm[:, c_ + 4:c_ + 5], None, op0=ALU.add), "pv")
        v_op(lambda e: e.tensor_mul(STG[:, SD:SD + 1024], STG[:, SB_:SB_ + 1024],
                                    STG[:, SB_:SB_ + 1024]))
        vt = v_op(lambda e, c_=c8: e.reduce_sum(sm[:, c_ + 5:c_ + 6],
                                                STG[:, SD:SD + 1024], axis=AX), "pv")
        done(("stg", SD), vt)
        W("s", vt)
        s_op(lambda e, c_=c8: e.activation(sm[:, c_ + 6:c_ + 7], sm[:, c_ + 5:c_ + 6],
                                           AF.Ln, bias=eps_c[:, 0:1], scale=1.0 / D))
        st_ = s_op(lambda e, c_=c8: e.activation(sm[:, c_ + 7:c_ + 8], sm[:, c_ + 6:c_ + 7],
                                           AF.Exp, bias=0.0, scale=-0.5), "ps")
        W("v", st_)
        st_ = v_op(lambda e, c_=c8: e.tensor_scalar(
            STG[:, SB_:SB_ + 1024], STG[:, SB_:SB_ + 1024],
            sm[:, c_ + 7:c_ + 8], None, op0=ALU.mult), "pv")
        W("t", st_)
        need("t", "psB0", "psB1", "psA0")
        for c in range(8):
            tk1 = t_op(lambda e, c_=c: e.transpose(
                PS_B[:, c_ * 128:(c_ + 1) * 128],
                STG[:, SC_ + c_ * 128:SC_ + (c_ + 1) * 128], idn[:, :]),
                "pt" if c == 7 else None)
        for c in range(8):
            tk2 = t_op(lambda e, c_=c: e.transpose(
                PS_A[:, c_ * 128:(c_ + 1) * 128],
                STG[:, SB_ + c_ * 128:SB_ + (c_ + 1) * 128], idn[:, :]),
                "pt" if c == 7 else None)
        W("v", tk1)
        vt = v_op(lambda e, i_=i: e.tensor_copy(
            AR1.rearrange("p (c t) -> p c t", c=8)[:, :, i_ * 128:(i_ + 1) * 128],
            PS_B.rearrange("p (c t) -> p c t", c=8)[:, :, :]), "pv")
        done("psB0", vt); done("psB1", vt)
        done(("stg", SC_), vt)
        if "min0" in tap_req:
            done("psA0", tk2)
            done(("stg", SB_), tk2)
            continue
        W("v", tk2)
        for c in range(8):
            vt = v_op(lambda e, i_=i, c_=c: e.tensor_scalar(
                AR2[:, c_ * 2048 + i_ * 128: c_ * 2048 + (i_ + 1) * 128],
                PS_A[:, c_ * 128:(c_ + 1) * 128],
                ln1g[:, c_:c_ + 1], ln1b[:, c_:c_ + 1], op0=ALU.mult, op1=ALU.add),
                "pv" if c == 7 else None)
        done("psA0", vt)
        done(("stg", SB_), vt)
        if "serial" in tap_req:
            kb.barrier()

    def tap_fm(name, src, rows=128):
        if name not in tap_req:
            return
        nchunks = {1024: 1}.get(src.shape[1] // 16, DC)
        total = src.shape[1]
        ncol = total // 1024
        for cj in range(ncol):
            need("v", ("stg", SA))
            vt = v_op(lambda e, c_=cj: e.tensor_copy(
                STG[0:rows, SA:SA + 1024], src[0:rows, c_ * 1024:(c_ + 1) * 1024]), "pv")
            W("y", vt)
            r0 = (cj // 2) * 128
            col = (cj % 2) * 1024
            if src.shape[1] == T:
                dt_ = y_dma(tap_d[name][0:rows, cj * 1024:(cj + 1) * 1024],
                            STG[0:rows, SA:SA + 1024])
            else:
                dt_ = y_dma(tap_d[name][r0:r0 + 128, col:col + 1024],
                            STG[0:rows, SA:SA + 1024])
            done(("stg", SA), dt_)

    if "stm" in tap_req:
        W("y", kb.tok("pv"))
        y_dma(tap_d["stm"][:, :], stats_m[:, :])
        y_dma(tap_d["str"][:, :], stats_r[:, :])
    tap_fm("xn", AR1)
    tap_fm("h", AR2)
    if stages <= 0:
        return _finish(nc, kb, ctx)

    W("t", w1w_tok)
    W("t", kb.tok("pv"))
    for nb in range(4):
        slot = "psC%d" % (nb % 2)
        need("t", slot)
        for c in range(8):
            tk = t_op(lambda e, c_=c, nb_=nb: e.matmul(
                PS_C[:, (nb_ % 2) * 512:(nb_ % 2) * 512 + 512],
                WAR[:, O_WKV + c_ * 128:O_WKV + (c_ + 1) * 128],
                AR1[:, c_ * 2048 + nb_ * 512: c_ * 2048 + nb_ * 512 + 512],
                start=(c_ == 0), stop=(c_ == 7)), "pt" if c == 7 else None)
        W("v", tk)
        vt = v_op(lambda e, nb_=nb: e.tensor_copy(
            CKV[:, nb_ * 512:nb_ * 512 + 512],
            PS_C[:, (nb_ % 2) * 512:(nb_ % 2) * 512 + 512]), "pv")
        done(slot, vt)
    tap_fm("ckv", CKV)

    for m in range(8):
        for nb in range(4):
            slot = "psC%d" % (nb % 2)
            need("t", slot)
            for c in range(8):
                tk = t_op(lambda e, c_=c, m_=m, nb_=nb: e.matmul(
                    PS_C[:, (nb_ % 2) * 512:(nb_ % 2) * 512 + 512],
                    WAR[:, O_WQ + c_ * 1024 + m_ * 128: O_WQ + c_ * 1024 + (m_ + 1) * 128],
                    AR1[:, c_ * 2048 + nb_ * 512: c_ * 2048 + nb_ * 512 + 512],
                    start=(c_ == 0), stop=(c_ == 7)), "pt" if c == 7 else None)
            W("v", tk)
            vt = v_op(lambda e, m_=m, nb_=nb: e.tensor_single_scalar(
                AR3[:, m_ * 2048 + nb_ * 512: m_ * 2048 + nb_ * 512 + 512],
                PS_C[:, (nb_ % 2) * 512:(nb_ % 2) * 512 + 512], SC_MLA, op=ALU.mult), "pv")
            done(slot, vt)
    tap_fm("q", AR3)

    QR, ROT, TA, TB = SE, SE + 32, SE + 64, SE + 80
    for i in range(NC):
        for which in range(2):
            need("t", "psC0")
            if which == 0:
                for c in range(8):
                    tk = t_op(lambda e, c_=c, i_=i: e.matmul(
                        PS_C[:, 0:32],
                        AR1[:, c_ * 2048 + i_ * 128: c_ * 2048 + (i_ + 1) * 128],
                        WAR[:, O_WQR + c_ * 32:O_WQR + (c_ + 1) * 32],
                        start=(c_ == 0), stop=(c_ == 7)), "pt" if c == 7 else None)
            else:
                tk = t_op(lambda e, i_=i: e.matmul(
                    PS_C[:, 0:32],
                    CKV[:, i_ * 128:(i_ + 1) * 128],
                    WAR[:, O_WKR:O_WKR + 32], start=True, stop=True), "pt")
            W("v", tk)
            need("v", ("stg", SE))
            vt = v_op(lambda e: e.tensor_copy(STG[:, QR:QR + 32], PS_C[:, 0:32]), "pv")
            done("psC0", vt)
            v_op(lambda e, i_=i: e.tensor_mul(STG[:, TA:TA + 16], STG[:, QR:QR + 16],
                                              cost[:, i_ * 16:(i_ + 1) * 16]))
            v_op(lambda e, i_=i: e.tensor_mul(STG[:, TB:TB + 16], STG[:, QR + 16:QR + 32],
                                              sint[:, i_ * 16:(i_ + 1) * 16]))
            v_op(lambda e: e.tensor_sub(STG[:, ROT:ROT + 16], STG[:, TA:TA + 16],
                                        STG[:, TB:TB + 16]))
            v_op(lambda e, i_=i: e.tensor_mul(STG[:, TA:TA + 16], STG[:, QR + 16:QR + 32],
                                              cost[:, i_ * 16:(i_ + 1) * 16]))
            v_op(lambda e, i_=i: e.tensor_mul(STG[:, TB:TB + 16], STG[:, QR:QR + 16],
                                              sint[:, i_ * 16:(i_ + 1) * 16]))
            vt = v_op(lambda e: e.tensor_add(STG[:, ROT + 16:ROT + 32], STG[:, TA:TA + 16],
                                             STG[:, TB:TB + 16]), "pv")
            W("t", vt)
            need("t", "psC1")
            tk = t_op(lambda e: e.transpose(PS_C[0:32, 512:640], STG[:, ROT:ROT + 32],
                                            idn[:, :]), "pt")
            W("v", tk)
            base = 0 if which == 0 else 64
            scale = SC_MLA if which == 0 else 1.0
            vt = v_op(lambda e, i_=i, b_=base, s2=scale: e.tensor_single_scalar(
                ROPE[b_:b_ + 32, i_ * 128:(i_ + 1) * 128],
                PS_C[0:32, 512:640], s2, op=ALU.mult), "pv")
            done("psC1", vt)
            done(("stg", SE), vt)
    rope_tok = kb.tok("pv")

    if stages <= 1:
        return _finish(nc, kb, ctx)

    def attention(kind):
        mla = kind == "mla"
        KQ = 96 if mla else 64
        KBr = 97 if mla else 65
        brow = 96 if mla else 64

        if mla:
            W("v", rope_tok)
            vt = v_op(lambda e: e.tensor_copy(k_st[64:96, :], ROPE[64:96, :]), "pv")
            W("g", rope_tok)
            rtok = g_dma(q_st[64:96, :], ROPE[0:32, :])
            done("nstg", rtok)
            W("y", vt)
            y_dma(k_st[96:97, :], onesr[:, :])
        else:
            W("y", kb.tok("pt"))
            y_dma(k_st[64:65, :], onesr[:, :])
            W("g", kb.tok("pt"))
        aug_tok = kb.tok("dy")

        for h in range(NH):
            pe_prev = kb.tok("pt")
            if mla:
                W("g", pe_prev)
                vt = g_dma(q_st[0:64, :], AR3[(h % 2) * 64:(h % 2) * 64 + 64,
                                              (h // 2) * 2048:(h // 2) * 2048 + 2048])
                W("t", vt)
                W("v", pe_prev)
                for nb in range(4):
                    slot = "psC%d" % (nb % 2)
                    need("t", slot)
                    tk = t_op(lambda e, h_=h, nb_=nb: e.matmul(
                        PS_C[0:64, (nb_ % 2) * 512:(nb_ % 2) * 512 + 512],
                        WAR[:, O_WKUP + h_ * 64:O_WKUP + (h_ + 1) * 64],
                        CKV[:, nb_ * 512:nb_ * 512 + 512], start=True, stop=True), "pt")
                    W("v", tk)
                    vk = v_op(lambda e, nb_=nb: e.tensor_copy(
                        k_st[0:64, nb_ * 512:nb_ * 512 + 512],
                        PS_C[0:64, (nb_ % 2) * 512:(nb_ % 2) * 512 + 512]), "pv")
                    done(slot, vk)
                for kc in range(NC):
                    slot = "psC%d" % (kc % 2)
                    need("t", slot)
                    tk = t_op(lambda e, h_=h, kc_=kc: e.matmul(
                        PS_C[:, (kc_ % 2) * 512:(kc_ % 2) * 512 + 64],
                        CKV[:, kc_ * 128:(kc_ + 1) * 128],
                        WAR[:, O_WVUP + h_ * 64:O_WVUP + (h_ + 1) * 64],
                        start=True, stop=True), "pt")
                    W("v", tk)
                    W("v", vst_ones_tok)
                    vk = v_op(lambda e, kc_=kc: e.tensor_copy(
                        v_st[:, kc_ * 65:kc_ * 65 + 64],
                        PS_C[:, (kc_ % 2) * 512:(kc_ % 2) * 512 + 64]), "pv")
                    done(slot, vk)
            else:
                sl = h % 2
                off = sl * 1536
                need("g", ("whb", sl))
                g_dma(whb[:, off:off + 512].rearrange("p (c f) -> p c f", c=8),
                      dr["w_qkv"][:, h * 64:(h + 1) * 64].rearrange("(c p) f -> p c f", p=128))
                g_dma(whb[:, off + 512:off + 1024].rearrange("p (c f) -> p c f", c=8),
                      dr["w_qkv"][:, D + h * 64:D + (h + 1) * 64].rearrange("(c p) f -> p c f", p=128))
                gt = g_dma(whb[:, off + 1024:off + 1536].rearrange("p (c f) -> p c f", c=8),
                           dr["w_qkv"][:, 2 * D + h * 64:2 * D + (h + 1) * 64].rearrange("(c p) f -> p c f", p=128))
                W("t", gt)
                W("t", pe_prev)
                for nb in range(4):
                    slot = "psC%d" % (nb % 2)
                    need("t", slot)
                    for c in range(8):
                        tk = t_op(lambda e, nb_=nb, c_=c, o_=off: e.matmul(
                            PS_C[0:64, (nb_ % 2) * 512:(nb_ % 2) * 512 + 512],
                            whb[:, o_ + c_ * 64:o_ + (c_ + 1) * 64],
                            AR2[:, c_ * 2048 + nb_ * 512: c_ * 2048 + nb_ * 512 + 512],
                            start=(c_ == 0), stop=(c_ == 7)), "pt" if c == 7 else None)
                    W("v", tk)
                    W("v", pe_prev)
                    vk = v_op(lambda e, h_=h, nb_=nb: e.tensor_scalar(
                        q_st[0:64, nb_ * 512:nb_ * 512 + 512],
                        PS_C[0:64, (nb_ % 2) * 512:(nb_ % 2) * 512 + 512],
                        bqq[:, h_:h_ + 1], SC_FFN, op0=ALU.add, op1=ALU.mult), "pv")
                    done(slot, vk)
                for nb in range(4):
                    slot = "psC%d" % (nb % 2)
                    need("t", slot)
                    for c in range(8):
                        tk = t_op(lambda e, nb_=nb, c_=c, o_=off: e.matmul(
                            PS_C[0:64, (nb_ % 2) * 512:(nb_ % 2) * 512 + 512],
                            whb[:, o_ + 512 + c_ * 64:o_ + 512 + (c_ + 1) * 64],
                            AR2[:, c_ * 2048 + nb_ * 512: c_ * 2048 + nb_ * 512 + 512],
                            start=(c_ == 0), stop=(c_ == 7)), "pt" if c == 7 else None)
                    W("v", tk)
                    vk = v_op(lambda e, h_=h, nb_=nb: e.tensor_scalar(
                        k_st[0:64, nb_ * 512:nb_ * 512 + 512],
                        PS_C[0:64, (nb_ % 2) * 512:(nb_ % 2) * 512 + 512],
                        bqk[:, h_:h_ + 1], None, op0=ALU.add), "pv")
                    done(slot, vk)
                for kc in range(NC):
                    slot = "psC%d" % (kc % 2)
                    need("t", slot)
                    for c in range(8):
                        t_op(lambda e, kc_=kc, c_=c, o_=off: e.matmul(
                            PS_C[:, (kc_ % 2) * 512:(kc_ % 2) * 512 + 64],
                            AR2[:, c_ * 2048 + kc_ * 128: c_ * 2048 + (kc_ + 1) * 128],
                            whb[:, o_ + 1024 + c_ * 64:o_ + 1024 + (c_ + 1) * 64],
                            start=(c_ == 0), stop=False))
                    tk = t_op(lambda e, h_=h, kc_=kc: e.matmul(
                        PS_C[:, (kc_ % 2) * 512:(kc_ % 2) * 512 + 64],
                        onesr[0:1, kc_ * 128:(kc_ + 1) * 128],
                        bqv[0:1, h_ * 64:(h_ + 1) * 64], start=False, stop=True), "pt")
                    W("v", tk)
                    W("v", vst_ones_tok)
                    vk = v_op(lambda e, kc_=kc: e.tensor_copy(
                        v_st[:, kc_ * 65:kc_ * 65 + 64],
                        PS_C[:, (kc_ % 2) * 512:(kc_ % 2) * 512 + 64]), "pv")
                    done(slot, vk)
                done(("whb", sl), kb.tok("pt"))
            qkv_tok = kb.tok("pv")

            W("t", qkv_tok)
            units = []
            for qc in range(NC):
                nb = 4 if not mla else min(4, qc // 4 + 1)
                units.append((qc, 0, min(nb, 2)))
                if nb > 2:
                    units.append((qc, 1, nb - 2))
            for uidx, (qc, hf, nblk) in enumerate(units):
                reg = uidx % 2
                rk = "psA%d" % reg
                need("t", rk)
                for b in range(nblk):
                    tk = t_op(lambda e, qc_=qc, hf_=hf, b_=b, r_=reg, K_=KQ: e.matmul(
                        PS_A[:, r_ * 1024 + b_ * 512:r_ * 1024 + (b_ + 1) * 512],
                        q_st[0:K_, qc_ * 128:(qc_ + 1) * 128],
                        k_st[0:K_, hf_ * 1024 + b_ * 512:hf_ * 1024 + (b_ + 1) * 512],
                        start=True, stop=True), "pt" if b == nblk - 1 else None)
                W("v", tk)
                if hf == 0:
                    vt = v_op(lambda e, qc_=qc, r_=reg, n_=nblk: e.reduce_max(
                        m_all[:, qc_:qc_ + 1], PS_A[:, r_ * 1024:r_ * 1024 + n_ * 512],
                        axis=AX, negate=True), "pv")
                else:
                    rt = v_op(lambda e, r_=reg, n_=nblk: e.reduce_max(
                        sm[:, 0:1], PS_A[:, r_ * 1024:r_ * 1024 + n_ * 512],
                        axis=AX, negate=True), "pv")
                    W("v", rt)
                    vt = v_op(lambda e, qc_=qc: e.tensor_tensor(
                        m_all[:, qc_:qc_ + 1], m_all[:, qc_:qc_ + 1], sm[:, 0:1],
                        op=ALU.min), "pv")
                done(rk, vt)
            W("t", kb.tok("pv"))
            need("t", "psC0")
            tk = t_op(lambda e: e.transpose(PS_C[0:16, 0:128], m_all[:, :], idn[:, :]), "pt")
            W("v", tk)
            vt = v_op(lambda e: e.tensor_copy(mT[:, :], PS_C[0:16, 0:128]), "pv")
            done("psC0", vt)
            W("g", vt)
            mrow_tok = g_dma(q_st[brow:brow + 1, :].rearrange("o (c q) -> o c q", c=16),
                             mT[:, :])

            for qb in range(4):
                kcs = list(range((qb + 1) * 4)) if mla else list(range(16))
                pvk = "psB%d" % (qb % 2)
                need("t", pvk)
                pairs = [kcs[j:j + 2] for j in range(0, len(kcs), 2)]
                for pj, pair in enumerate(pairs):
                    ssl = pj % 2
                    need("t", "psA%d" % ssl)
                    if pj == 0:
                        W("t", mrow_tok)
                        W("t", aug_tok)
                    for z, kc in enumerate(pair):
                        tk = t_op(lambda e, kc_=kc, qb_=qb, ssl_=ssl, z_=z, K_=KBr:
                                  e.matmul(
                            PS_A[:, ssl_ * 1024 + z_ * 512:ssl_ * 1024 + z_ * 512 + 512],
                            k_st[0:K_, kc_ * 128:(kc_ + 1) * 128],
                            q_st[0:K_, qb_ * 512:qb_ * 512 + 512],
                            start=True, stop=True), "pt" if z == len(pair) - 1 else None)
                    W("v", tk)
                    wid = len(pair) * 512
                    need("v", ("sst", ssl))
                    vt = v_op(lambda e, ssl_=ssl, w_=wid: e.tensor_copy(
                        sst[:, ssl_ * 1024:ssl_ * 1024 + w_],
                        PS_A[:, ssl_ * 1024:ssl_ * 1024 + w_]), "pv")
                    done("psA%d" % ssl, vt)
                    W("s", vt)
                    need("s", ("ptb", ssl))
                    st_ = s_op(lambda e, ssl_=ssl, w_=wid: e.activation(
                        ptb[:, ssl_ * 1024:ssl_ * 1024 + w_],
                        sst[:, ssl_ * 1024:ssl_ * 1024 + w_], AF.Exp,
                        bias=0.0, scale=1.0), "ps")
                    done(("sst", ssl), st_)
                    gtk = st_
                    if mla:
                        for z, kc in enumerate(pair):
                            if kc >= qb * 4:
                                a = kc - qb * 4
                                W("v", st_)
                                vt = v_op(lambda e, ssl_=ssl, z_=z, a_=a: e.tensor_mul(
                                    ptb[:, ssl_ * 1024 + z_ * 512:ssl_ * 1024 + z_ * 512 + 512],
                                    ptb[:, ssl_ * 1024 + z_ * 512:ssl_ * 1024 + z_ * 512 + 512],
                                    maskT[:, a_ * 512:(a_ + 1) * 512]), "pv")
                                gtk = vt
                    W("t", gtk)
                    for z, kc in enumerate(pair):
                        tk = t_op(lambda e, kc_=kc, q_=qb, ssl_=ssl, z_=z,
                                  f_=(pj == 0 and z == 0),
                                  l_=(pj == len(pairs) - 1 and z == len(pair) - 1):
                                  e.matmul(
                            PS_B[0:65, (q_ % 2) * 512:(q_ % 2) * 512 + 512],
                            v_st[:, kc_ * 65:(kc_ + 1) * 65],
                            ptb[:, ssl_ * 1024 + z_ * 512:ssl_ * 1024 + z_ * 512 + 512],
                            start=f_, stop=l_), "pt" if z == len(pair) - 1 else None)
                    done(("ptb", ssl), tk)
                pv_done = tk
                W("v", pv_done)
                need("v", "invb", ("stg", SD))
                vt = v_op(lambda e, q_=qb: e.tensor_copy(
                    inv_h[:, :], PS_B[64:65, (q_ % 2) * 512:(q_ % 2) * 512 + 512]), "pv")
                W("s", vt)
                s_op(lambda e: e.activation(invu2[:, :], inv_h[:, :], AF.Ln,
                                            bias=0.0, scale=1.0))
                st_ = s_op(lambda e: e.activation(inv_h[:, :], invu2[:, :], AF.Exp,
                                                  bias=0.0, scale=-1.0), "ps")
                W("v", st_)
                v_op(lambda e: e.tensor_copy(invhb[0:1, :], inv_h[:, :]))
                v_op(lambda e: e.tensor_copy(invu2[:, :], invhb[0:1, :]))
                vt = v_op(lambda e: e.tensor_sub(invlb[0:1, :], inv_h[:, :],
                                                 invu2[:, :]), "pv")
                W("t", vt)
                need("t", "psC1")
                t_op(lambda e: e.matmul(PS_C[0:64, 512:1024], onesr[0:1, 0:64],
                                        invhb[0:1, :], start=True, stop=False))
                tk = t_op(lambda e: e.matmul(PS_C[0:64, 512:1024], onesr[0:1, 0:64],
                                             invlb[0:1, :], start=False, stop=True), "pt")
                W("v", tk)
                vt = v_op(lambda e: e.tensor_copy(invbc[:, :], PS_C[0:64, 512:1024]), "pv")
                done("psC1", vt)
                if h % 2 == 0:
                    vt2 = v_op(lambda e, h_=h, q_=qb: e.tensor_mul(
                        AR1[0:64,
                            (h_ // 2) * 2048 + q_ * 512:(h_ // 2) * 2048 + q_ * 512 + 512],
                        PS_B[0:64, (q_ % 2) * 512:(q_ % 2) * 512 + 512], invbc[:, :]), "pv")
                    done(pvk, vt2)
                else:
                    need("v", "nstg")
                    vt2 = v_op(lambda e, q_=qb: e.tensor_mul(
                        nstg[:, :],
                        PS_B[0:64, (q_ % 2) * 512:(q_ % 2) * 512 + 512], invbc[:, :]), "pv")
                    done(pvk, vt2)
                    W("g", vt2)
                    gt2 = g_dma(AR1[64:128,
                                    (h // 2) * 2048 + qb * 512:(h // 2) * 2048 + qb * 512 + 512],
                                nstg[:, :])
                    done("nstg", gt2)
                done("invb", vt2)
                done(("stg", SD), vt2)

    attention("mla")
    if stages <= 2:
        return _finish(nc, kb, ctx)

    mla_tok = kb.tok("pv")
    W("t", kb.tok("dg"))
    for i in range(NC):
        need("t", "psB0", "psB1")
        W("t", mla_tok)
        for nb in range(2):
            for c in range(8):
                tk = t_op(lambda e, i_=i, nb_=nb, c_=c: e.matmul(
                    PS_B[:, nb_ * 512:nb_ * 512 + 512],
                    AR1[:, c_ * 2048 + i_ * 128: c_ * 2048 + (i_ + 1) * 128],
                    WAR[:, O_WO1 + c_ * 1024 + nb_ * 512: O_WO1 + c_ * 1024 + nb_ * 512 + 512],
                    start=(c_ == 0), stop=(c_ == 7)), "pt" if c == 7 else None)
        W("v", tk)
        need("v", ("stg", SF))
        vt = v_op(lambda e: e.tensor_copy(STG[:, SF:SF + 1024], PS_B[:, :]), "pv")
        done("psB0", vt); done("psB1", vt)
        W("y", vt)
        dt_ = y_dma(s_dram[i * 128:(i + 1) * 128, :], STG[:, SF:SF + 1024])
        if "s" in tap_req:
            dt_ = y_dma(tap_d["s"][i * 128:(i + 1) * 128, :], STG[:, SF:SF + 1024])
        done(("stg", SF), dt_)
    s_mm_tok = kb.tok("pt")

    if stages <= 3:
        return _finish(nc, kb, ctx)

    W("g", s_mm_tok)
    wo2_tok = g_dma(WAR[:, O_WO2:O_WO2 + 8192].rearrange("p (c m) -> p c m", c=8),
                    dr["w_o2"].rearrange("(c p) m -> p c m", p=128))

    W("v", s_mm_tok)
    attention("ffn")
    ffn_tok = kb.tok("pv")
    tap_fm("o", AR1)
    if stages <= 4:
        return _finish(nc, kb, ctx)

    W("v", consts_tok)
    need("v", ("stg", SE))
    for c in range(8):
        v_op(lambda e, c_=c: e.tensor_scalar(
            STG[:, SE + c_ * 8:SE + (c_ + 1) * 8], gw32[:, c_ * 8:(c_ + 1) * 8],
            ln2g[:, c_:c_ + 1], None, op0=ALU.mult))
    v_op(lambda e: e.tensor_copy(gwhi[:, :], STG[:, SE:SE + 64]))
    v_op(lambda e: e.tensor_copy(STG[:, SE + 64:SE + 128], gwhi[:, :]))
    v_op(lambda e: e.tensor_sub(gwlo[:, :], STG[:, SE:SE + 64], STG[:, SE + 64:SE + 128]))
    v_op(lambda e: e.tensor_copy(grhi[:, :], gw32[:, :]))
    v_op(lambda e: e.tensor_copy(STG[:, SE + 64:SE + 128], grhi[:, :]))
    v_op(lambda e: e.tensor_sub(grlo[:, :], gw32[:, :], STG[:, SE + 64:SE + 128]))
    v_op(lambda e: e.tensor_copy(bchi[:, :], ln2b[:, :]))
    v_op(lambda e: e.tensor_copy(STG[:, SE + 64:SE + 72], bchi[:, :]))
    vt = v_op(lambda e: e.tensor_sub(bclo[:, :], ln2b[:, :], STG[:, SE + 64:SE + 72]), "pv")
    done(("stg", SE), vt)
    W("t", vt)
    need("t", "psC0")
    np_ = 0
    for c in range(8):
        for bb, gg in ((bchi, grhi), (bchi, grlo), (bclo, grhi)):
            np_ += 1
            tk = t_op(lambda e, c_=c, b_=bb, g_=gg, f_=(np_ == 1), l_=(np_ == 24): e.matmul(
                PS_C[0:1, 0:8], b_[:, c_:c_ + 1], g_[:, c_ * 8:(c_ + 1) * 8],
                start=f_, stop=l_), "pt" if np_ == 24 else None)
    W("v", tk)
    v_op(lambda e: e.tensor_copy(bpr[0:1, 0:8], PS_C[0:1, 0:8]))
    v_op(lambda e: e.tensor_add(bpr[0:1, 0:8], bpr[0:1, 0:8], gateb[0:1, :]))
    v_op(lambda e: e.tensor_copy(bprh[0:1, :], bpr[0:1, 0:8]))
    v_op(lambda e: e.tensor_copy(bpr[0:1, 8:16], bprh[0:1, :]))
    vt = v_op(lambda e: e.tensor_sub(bprl[0:1, :], bpr[0:1, 0:8], bpr[0:1, 8:16]), "pv")
    done("psC0", vt)
    W("t", vt)
    need("t", "psC0")
    t_op(lambda e: e.matmul(PS_C[:, 0:8], onesr[0:1, 0:128], bprh[0:1, :],
                            start=True, stop=False))
    tk = t_op(lambda e: e.matmul(PS_C[:, 0:8], onesr[0:1, 0:128], bprl[0:1, :],
                                 start=False, stop=True), "pt")
    W("v", tk)
    gp_tok = v_op(lambda e: e.tensor_copy(bprbc[:, :], PS_C[:, 0:8]), "pv")
    done("psC0", gp_tok)

    W("t", kb.tok("dg"))
    for i in range(NC):
        c8 = i * 8
        need("y", ("stg", SA))
        xtok = y_dma(STG[:, SA:SA + 1024], dr["x"][i * 128:(i + 1) * 128, :])
        need("t", "psB0", "psB1")
        W("t", ffn_tok)
        W("t", wo2_tok)
        for nb in range(2):
            for c in range(8):
                t_op(lambda e, i_=i, nb_=nb, c_=c: e.matmul(
                    PS_B[:, nb_ * 512:nb_ * 512 + 512],
                    AR1[:, c_ * 2048 + i_ * 128: c_ * 2048 + (i_ + 1) * 128],
                    WAR[:, O_WO2 + c_ * 1024 + nb_ * 512: O_WO2 + c_ * 1024 + nb_ * 512 + 512],
                    start=(c_ == 0), stop=False))
            tk = t_op(lambda e, i_=i, nb_=nb: e.matmul(
                PS_B[:, nb_ * 512:nb_ * 512 + 512],
                onesr[0:1, i_ * 128:(i_ + 1) * 128],
                bo_row[0:1, nb_ * 512:nb_ * 512 + 512], start=False, stop=True), "pt")
        W("v", xtok)
        need("v", ("stg", SD))
        st_ = v_op(lambda e, i_=i: e.tensor_scalar(
            STG[:, SD:SD + 1024], STG[:, SA:SA + 1024],
            stats_r[:, i_:i_ + 1], stats_mr[:, i_:i_ + 1],
            op0=ALU.mult, op1=ALU.add), "pv")
        done(("stg", SA), st_)
        W("v", tk)
        need("v", ("stg", SB_))
        vt = v_op(lambda e: e.tensor_add(STG[:, SB_:SB_ + 1024], PS_B[:, :],
                                         STG[:, SD:SD + 1024]), "pv")
        done("psB0", vt); done("psB1", vt)
        done(("stg", SD), vt)
        W("y", vt)
        dt_ = y_dma(x2_dram[i * 128:(i + 1) * 128, :], STG[:, SB_:SB_ + 1024])
        if "x2" in tap_req:
            dt_ = y_dma(tap_d["x2"][i * 128:(i + 1) * 128, :], STG[:, SB_:SB_ + 1024])
        rt = v_op(lambda e, c_=c8: e.reduce_sum(sm[:, c_:c_ + 1], STG[:, SB_:SB_ + 1024],
                                                axis=AX), "pv")
        W("v", rt)
        vt = v_op(lambda e, c_=c8: e.tensor_scalar_mul(sm[:, c_ + 1:c_ + 2],
                                                       sm[:, c_:c_ + 1], -1.0 / D), "pv")
        need("v", ("stg", SC_))
        st_ = v_op(lambda e, c_=c8: e.tensor_scalar(
            STG[:, SC_:SC_ + 1024], STG[:, SB_:SB_ + 1024],
            sm[:, c_ + 1:c_ + 2], None, op0=ALU.add), "pv")
        done(("stg", SB_), (("dy", dt_[1])))
        need("v", ("stg", SD))
        v_op(lambda e: e.tensor_mul(STG[:, SD:SD + 1024], STG[:, SC_:SC_ + 1024],
                                    STG[:, SC_:SC_ + 1024]))
        vt = v_op(lambda e, c_=c8: e.reduce_sum(sm[:, c_ + 2:c_ + 3],
                                                STG[:, SD:SD + 1024], axis=AX), "pv")
        done(("stg", SD), vt)
        W("s", vt)
        s_op(lambda e, c_=c8: e.activation(sm[:, c_ + 3:c_ + 4], sm[:, c_ + 2:c_ + 3],
                                           AF.Ln, bias=eps_c[:, 0:1], scale=1.0 / D))
        st2 = s_op(lambda e, c_=c8: e.activation(sm[:, c_ + 4:c_ + 5], sm[:, c_ + 3:c_ + 4],
                                           AF.Exp, bias=0.0, scale=-0.5), "ps")
        W("v", st2)
        st_ = v_op(lambda e, c_=c8: e.tensor_scalar(
            STG[:, SC_:SC_ + 1024], STG[:, SC_:SC_ + 1024],
            sm[:, c_ + 4:c_ + 5], None, op0=ALU.mult), "pv")
        W("t", st_)
        need("t", "psA0")
        for c in range(8):
            tk = t_op(lambda e, c_=c: e.transpose(
                PS_A[:, c_ * 128:(c_ + 1) * 128],
                STG[:, SC_ + c_ * 128:SC_ + (c_ + 1) * 128], idn[:, :]),
                "pt" if c == 7 else None)
        done(("stg", SC_), tk)
        W("v", tk)
        for c in range(8):
            v_op(lambda e, i_=i, c_=c: e.tensor_scalar(
                AR2[:, c_ * 2048 + i_ * 128: c_ * 2048 + (i_ + 1) * 128],
                PS_A[:, c_ * 128:(c_ + 1) * 128],
                ln2g[:, c_:c_ + 1], ln2b[:, c_:c_ + 1], op0=ALU.mult, op1=ALU.add))
        v_op(lambda e: e.tensor_copy(lghi[:, :], PS_A[:, 0:1024]))
        need("v", ("stg", SD))
        v_op(lambda e: e.tensor_copy(STG[:, SD:SD + 1024], lghi[:, :]))
        vt = v_op(lambda e: e.tensor_sub(lglo[:, :], PS_A[:, 0:1024],
                                         STG[:, SD:SD + 1024]), "pv")
        done("psA0", vt)
        done(("stg", SD), vt)
        W("t", vt); W("t", gp_tok)
        need("t", "psC0")
        np2 = 0
        for c in range(8):
            for lhs, rhs in ((lghi, gwhi), (lghi, gwlo), (lglo, gwhi)):
                np2 += 1
                tk = t_op(lambda e, c_=c, l_=lhs, r_=rhs, f_=(np2 == 1),
                          z_=(np2 == 24): e.matmul(
                    PS_C[:, 0:8], l_[:, c_ * 128:(c_ + 1) * 128],
                    r_[:, c_ * 8:(c_ + 1) * 8], start=f_, stop=z_),
                    "pt" if np2 == 24 else None)
        W("v", tk)
        need("v", ("stg", SE))
        v_op(lambda e: e.tensor_add(STG[:, SE:SE + 8], PS_C[:, 0:8], bprbc[:, :]))
        vt = v_op(lambda e, c_=c8: e.reduce_max(sm[:, c_ + 5:c_ + 6],
                                                STG[:, SE:SE + 8], axis=AX, negate=True), "pv")
        done("psC0", vt)
        W("v", vt)
        v_op(lambda e, c_=c8: e.tensor_scalar(sm[:, c_ + 6:c_ + 7], sm[:, c_ + 5:c_ + 6],
                                              -1.0, None, op0=ALU.mult))
        v_op(lambda e, c_=c8: e.tensor_scalar(
            STG[:, SE + 8:SE + 16], STG[:, SE:SE + 8],
            sm[:, c_ + 6:c_ + 7], None, op0=ALU.is_equal))
        v_op(lambda e, c_=c8: e.scalar_tensor_tensor(
            STG[:, SE + 16:SE + 24], STG[:, SE + 8:SE + 16], -1e30,
            STG[:, SE:SE + 8], op0=ALU.mult, op1=ALU.add))
        vt = v_op(lambda e, c_=c8: e.reduce_max(sm[:, c_ + 7:c_ + 8],
                                                STG[:, SE + 16:SE + 24], axis=AX), "pv")
        W("v", vt)
        v_op(lambda e, c_=c8: e.tensor_scalar(
            STG[:, SE + 24:SE + 32], STG[:, SE + 16:SE + 24],
            sm[:, c_ + 7:c_ + 8], None, op0=ALU.is_equal))
        W("s", kb.tok("pv"))
        st_ = s_op(lambda e, c_=c8: e.activation(
            sm[:, c_ + 2:c_ + 3], sm[:, c_ + 7:c_ + 8], AF.Exp,
            bias=sm[:, c_ + 5:c_ + 6], scale=1.0), "ps")
        W("v", st_)
        v_op(lambda e, c_=c8: e.tensor_scalar(sm[:, c_ + 3:c_ + 4], sm[:, c_ + 2:c_ + 3],
                                              1.0, None, op0=ALU.add))
        rt = v_op(lambda e, c_=c8: e.reciprocal(sm[:, c_ + 0:c_ + 1],
                                                sm[:, c_ + 3:c_ + 4]), "pv")
        W("v", rt)
        v_op(lambda e, c_=c8: e.tensor_mul(sm[:, c_ + 1:c_ + 2], sm[:, c_ + 2:c_ + 3],
                                           sm[:, c_ + 0:c_ + 1]))
        v_op(lambda e, c_=c8: e.tensor_scalar(
            STG[:, SE + 24:SE + 32], STG[:, SE + 24:SE + 32],
            sm[:, c_ + 1:c_ + 2], None, op0=ALU.mult))
        vt = v_op(lambda e, i_=i, c_=c8: e.scalar_tensor_tensor(
            ew_all[:, i_ * 8:(i_ + 1) * 8], STG[:, SE + 8:SE + 16],
            sm[:, c_ + 0:c_ + 1], STG[:, SE + 24:SE + 32],
            op0=ALU.mult, op1=ALU.add), "pv")
        done(("stg", SE), vt)
    h2_tok = kb.tok("pv")
    tap_fm("h2", AR2)
    if "ew" in tap_req:
        for i in range(NC):
            W("y", h2_tok)
            y_dma(tap_d["ew"][i * 128:(i + 1) * 128, :], ew_all[:, i * 8:(i + 1) * 8])
    if stages <= 5:
        return _finish(nc, kb, ctx)

    for i in range(NC):
        need("t", "psC0")
        W("t", h2_tok)
        tk = t_op(lambda e, i_=i: e.transpose(
            PS_C[0:8, 0:128], ew_all[:, i_ * 8:(i_ + 1) * 8], idn[:, :]), "pt")
        W("v", tk)
        vt = v_op(lambda e, i_=i: e.tensor_copy(
            ewCb[:, i_ * 128:(i_ + 1) * 128], PS_C[0:8, 0:128]), "pv")
        done("psC0", vt)
    ew_tok = kb.tok("pv")

    W("g", kb.tok("pt"))
    b2t = g_dma(b2bf, dr["b2"][:, :])
    for tile in range(2):
        first_acc = True
        for e_ in range(E):
            W("g", ew_tok)
            need("g", "ewrbf")
            gt = g_dma(ew_rbf[0:1, :], ewCb[e_:e_ + 1, tile * 1024:(tile + 1) * 1024])
            W("t", gt)
            need("t", "psC0", "psC1")
            t_op(lambda e: e.matmul(PS_C[:, 0:512], onesr[0:1, 0:128],
                                    ew_rbf[0:1, 0:512], start=True, stop=True))
            tk = t_op(lambda e: e.matmul(PS_C[:, 512:1024], onesr[0:1, 0:128],
                                         ew_rbf[0:1, 512:1024], start=True, stop=True), "pt")
            W("v", tk)
            need("v", "ewbc")
            vt = v_op(lambda e: e.tensor_copy(ew_bc[:, :], PS_C[:, 0:1024]), "pv")
            done("psC0", vt); done("psC1", vt)
            need("v", "h2s")
            for c in range(8):
                vt = v_op(lambda e, c_=c, t_=tile: e.tensor_mul(
                    h2s[:, c_ * 1024:(c_ + 1) * 1024],
                    AR2[:, c_ * 2048 + t_ * 1024: c_ * 2048 + t_ * 1024 + 1024],
                    ew_bc[:, :]), "pv" if c == 7 else None)
            h2s_tok = vt
            done("ewbc", vt)

            for grp in range(2):
                need("g", "b1row")
                b1t = g_dma(b1row[0:1, :],
                            dr["b1"][e_:e_ + 1, grp * 2048:(grp + 1) * 2048])
                for fi in range(16):
                    slab = fi // 4
                    if fi % 4 == 0:
                        sl1 = slab % 2
                        fbase = grp * 2048 + slab * 512
                        need("g", ("w1s", sl1))
                        w1t = g_dma(
                            w1buf[:, sl1 * 4096:sl1 * 4096 + 4096].rearrange(
                                "p (c f) -> p c f", c=8),
                            dr["W1"][e_, :, fbase:fbase + 512].rearrange(
                                "(c p) f -> p c f", p=128))
                    psk = "psB0" if fi % 2 == 0 else "psA1"
                    ps = PS_B if fi % 2 == 0 else PS_A
                    pso = 0 if fi % 2 == 0 else 1024
                    need("t", psk)
                    if fi % 2 == 0:
                        need("t", "psB1")
                    W("t", w1t)
                    W("t", h2s_tok)
                    W("t", b1t)
                    o1 = (slab % 2) * 4096
                    fr = fi % 4
                    for nb2 in range(2):
                        for c in range(8):
                            t_op(lambda e, c_=c, o_=o1, fr_=fr, nb_=nb2, p_=ps, po_=pso:
                                 e.matmul(
                                p_[:, po_ + nb_ * 512:po_ + nb_ * 512 + 512],
                                w1buf[:, o_ + c_ * 512 + fr_ * 128:
                                      o_ + c_ * 512 + fr_ * 128 + 128],
                                h2s[:, c_ * 1024 + nb_ * 512:c_ * 1024 + nb_ * 512 + 512],
                                start=(c_ == 0), stop=False))
                        tk = t_op(lambda e, fi_=fi, nb_=nb2, p_=ps, po_=pso:
                                  e.matmul(
                            p_[:, po_ + nb_ * 512:po_ + nb_ * 512 + 512],
                            b1row[0:1, fi_ * 128:(fi_ + 1) * 128],
                            ew_rbf[0:1, nb_ * 512:nb_ * 512 + 512],
                            start=False, stop=True), "pt")
                    if fi % 4 == 3:
                        done(("w1s", slab % 2), tk)
                    W("v", tk)
                    need("v", "hid")
                    vt = v_op(lambda e, fi_=fi, p_=ps, po_=pso: e.tensor_single_scalar(
                        AR1[:, fi_ * 1024:(fi_ + 1) * 1024],
                        p_[:, po_:po_ + 1024], 0.0, op=ALU.max), "pv")
                    done(psk, vt)
                    if fi % 2 == 0:
                        done("psB1", vt)
                hid_tok = kb.tok("pv")

                for dk in range(8):
                    sl2 = dk % 2
                    need("g", ("w2s", sl2))
                    fb = grp * 2048
                    w2t1 = g_dma(
                        w2buf[:, sl2 * 2048:sl2 * 2048 + 1024].rearrange(
                            "p (c d) -> p c d", c=8),
                        dr["W2"][e_, fb:fb + 1024, dk * 128:(dk + 1) * 128].rearrange(
                            "(c p) d -> p c d", p=128))
                    w2t = g_dma(
                        w2buf[:, sl2 * 2048 + 1024:sl2 * 2048 + 2048].rearrange(
                            "p (c d) -> p c d", c=8),
                        dr["W2"][e_, fb + 1024:fb + 2048, dk * 128:(dk + 1) * 128].rearrange(
                            "(c p) d -> p c d", p=128))
                    psk = "psA0" if dk % 2 == 0 else "psC0"
                    ps = PS_A if dk % 2 == 0 else PS_C
                    need("t", psk)
                    if dk % 2 == 1:
                        need("t", "psC1")
                    W("t", w2t)
                    W("t", hid_tok)
                    for fi in range(16):
                        for nb2 in range(2):
                            tk = t_op(lambda e, fi_=fi, nb_=nb2, o_=sl2 * 2048, p_=ps:
                                      e.matmul(
                                p_[:, nb_ * 512:nb_ * 512 + 512],
                                w2buf[:, o_ + fi_ * 128:o_ + (fi_ + 1) * 128],
                                AR1[:, fi_ * 1024 + nb_ * 512:fi_ * 1024 + nb_ * 512 + 512],
                                start=(fi_ == 0), stop=(fi_ == 15 and nb_ == 1)),
                                "pt" if (fi == 15 and nb2 == 1) else None)
                    done(("w2s", sl2), tk)
                    W("v", tk)
                    if first_acc:
                        need("v", "moeT")
                        vt = v_op(lambda e, dk_=dk, p_=ps: e.tensor_copy(
                            MOE[:, dk_ * 1024:(dk_ + 1) * 1024], p_[:, 0:1024]), "pv")
                    else:
                        vt = v_op(lambda e, dk_=dk, p_=ps: e.tensor_add(
                            MOE[:, dk_ * 1024:(dk_ + 1) * 1024], p_[:, 0:1024],
                            MOE[:, dk_ * 1024:(dk_ + 1) * 1024]), "pv")
                    done(psk, vt)
                    if dk % 2 == 1:
                        done("psC1", vt)
                done("hid", kb.tok("pt"))
                done("b1row", kb.tok("pt"))
                first_acc = False
            done("h2s", kb.tok("pt"))
            done("ewrbf", kb.tok("pt"))
        moe_tok = kb.tok("pv")

        for j in range(8):
            i = tile * 8 + j
            need("y", ("stg", SA), ("stg", SB_), ("stg", SC_))
            x2t = y_dma(STG[:, SA:SA + 1024], x2_dram[i * 128:(i + 1) * 128, :])
            stt_ = y_dma(STG[:, SB_:SB_ + 1024], s_dram[i * 128:(i + 1) * 128, :])
            xt = y_dma(STG[:, SC_:SC_ + 1024], dr["x"][i * 128:(i + 1) * 128, :])
            need("t", "psC0", "psC1")
            W("t", ew_tok)
            W("t", b2t)
            tk0 = t_op(lambda e, i_=i: e.matmul(
                PS_C[:, 0:512], ewCb[0:8, i_ * 128:(i_ + 1) * 128],
                b2bf[:, 0:512], start=True, stop=True))
            tk0 = t_op(lambda e, i_=i: e.matmul(
                PS_C[:, 512:1024], ewCb[0:8, i_ * 128:(i_ + 1) * 128],
                b2bf[:, 512:1024], start=True, stop=True), "pt")
            need("t", "psB0", "psB1")
            W("t", moe_tok)
            for dk in range(8):
                tk1 = t_op(lambda e, dk_=dk, j_=j: e.transpose(
                    PS_B[:, dk_ * 128:(dk_ + 1) * 128],
                    MOE[:, dk_ * 1024 + j_ * 128:dk_ * 1024 + (j_ + 1) * 128],
                    idn[:, :]), "pt" if dk == 7 else None)
            W("v", tk0); W("v", x2t)
            need("v", ("stg", SD))
            v_op(lambda e: e.tensor_add(STG[:, SD:SD + 1024], PS_C[:, 0:1024],
                                        STG[:, SA:SA + 1024]))
            W("v", tk1)
            vt = v_op(lambda e: e.tensor_add(STG[:, SD:SD + 1024], PS_B[:, 0:1024],
                                             STG[:, SD:SD + 1024]), "pv")
            done("moeT", kb.tok("pt"))
            done("psB0", vt); done("psB1", vt); done("psC0", vt); done("psC1", vt)
            done(("stg", SA), vt)
            if "gpre" in tap_req:
                W("y", vt)
                y_dma(tap_d["gpre"][i * 128:(i + 1) * 128, :], STG[:, SD:SD + 1024])
            W("s", vt)
            need("s", ("stg", SF))
            st2 = s_op(lambda e: e.activation(STG[:, SF:SF + 1024], STG[:, SD:SD + 1024],
                                              AF.Silu, bias=0.0, scale=1.0), "ps")
            done(("stg", SD), st2)
            W("v", st2); W("v", stt_); W("v", xt)
            v_op(lambda e: e.tensor_mul(STG[:, SF:SF + 1024], STG[:, SF:SF + 1024],
                                        STG[:, SB_:SB_ + 1024]))
            vt = v_op(lambda e: e.tensor_add(STG[:, SF:SF + 1024], STG[:, SF:SF + 1024],
                                             STG[:, SC_:SC_ + 1024]), "pv")
            done(("stg", SB_), vt)
            done(("stg", SC_), vt)
            W("y", vt)
            ot = y_dma(out_d[i * 128:(i + 1) * 128, :], STG[:, SF:SF + 1024], inc="od")
            done(("stg", SF), ot)

    return _finish(nc, kb, ctx)


def _finish(nc, kb, ctx):
    kb.wait("y", kb.tok("od"))
    kb.wait("y", kb.tok("dy"))
    kb.wait("g", kb.tok("dg"))
    with nc.Block() as block:
        @block.tensor
        def _(e):
            kb.replay("t", e)
        @block.vector
        def _(e):
            kb.replay("v", e)
        @block.scalar
        def _(e):
            kb.replay("s", e)
        @block.gpsimd
        def _(e):
            kb.replay("g", e)
        @block.sync
        def _(e):
            kb.replay("y", e)
    ctx.close()
    return nc


def _consts():
    inv = 1.0 / 10000.0 ** (np.arange(0, DR, 2, dtype=np.float32) / DR)
    tt = np.arange(T, dtype=np.float32)
    fr = np.outer(tt, inv)
    cos, sin = np.cos(fr).astype(np.float32), np.sin(fr).astype(np.float32)
    idn = np.eye(128, dtype=np.float32)
    masks = np.zeros((128, 4 * 512), dtype=ml_dtypes.bfloat16)
    for a, r in enumerate((0, 128, 256, 384)):
        ii = np.arange(128)[:, None] + r
        jj = np.arange(512)[None, :]
        masks[:, a * 512:(a + 1) * 512] = (ii <= jj).astype(ml_dtypes.bfloat16)
    return {"c_cos": cos, "c_sin": sin, "c_idn": idn, "c_maskT": masks,
            "c_ones": np.ones((1, T), dtype=ml_dtypes.bfloat16),
            "c_onescol": np.ones((128, 1), dtype=ml_dtypes.bfloat16)}


def _prep_shared(inputs):
    f = np.ascontiguousarray
    sh = {
        "w_kv_down": f(inputs["L0_attn_W_kv_down"]),
        "w_k_up": f(inputs["L0_attn_W_k_up"]),
        "w_v_up": f(inputs["L0_attn_W_v_up"]),
        "w_q": f(inputs["L0_attn_W_q"]),
        "w_q_rope": f(inputs["L0_attn_W_q_rope"]),
        "w_k_rope": f(inputs["L0_attn_W_k_rope"]),
        "w_o1": f(inputs["L0_attn_W_o"]),
        "ln1_g": f(inputs["L0_ffn_ln1_g"].reshape(8, 128).T),
        "ln1_b": f(inputs["L0_ffn_ln1_b"].reshape(8, 128).T),
        "w_qkv": f(inputs["L0_ffn_w_qkv"]),
        "b_qkv_q": f(inputs["L0_ffn_b_qkv"][0:D].reshape(16, 64).T),
        "b_qkv_k": f(inputs["L0_ffn_b_qkv"][D:2 * D].reshape(16, 64).T),
        "b_qkv_v": f(inputs["L0_ffn_b_qkv"][2 * D:3 * D].reshape(1, D)),
        "w_o2": f(inputs["L0_ffn_w_o"]),
        "b_o": f(inputs["L0_ffn_b_o"].reshape(1, D)),
        "ln2_g": f(inputs["L0_ffn_ln2_g"].reshape(8, 128).T),
        "ln2_b": f(inputs["L0_ffn_ln2_b"].reshape(8, 128).T),
        "gate_w": f(inputs["L0_ffn_gate_w"].reshape(8, 128, 8).transpose(1, 0, 2).reshape(128, 64)),
        "gate_b": f(inputs["L0_ffn_gate_b"].reshape(1, 8)),
        "W1": f(inputs["L0_ffn_W1"]),
        "b1": f(inputs["L0_ffn_b1"]),
        "W2": f(inputs["L0_ffn_W2"]),
        "b2": f(inputs["L0_ffn_b2"]),
    }
    sh.update(_consts())
    return sh


def kernel(**inputs):
    x = np.ascontiguousarray(np.asarray(inputs["L0_attn_x"], dtype=np.float32))
    shared = _prep_shared({k: np.asarray(v) for k, v in inputs.items()})
    nc = build_program(taps=TAPS, stages=STAGES)
    in_maps = []
    for c in range(8):
        m = dict(shared)
        m["x"] = x[c]
        in_maps.append(m)
    res = run_bass_kernel_spmd(nc, in_maps, core_ids=list(range(8)))
    out = np.stack([np.asarray(r["out"], dtype=np.float32) for r in res.results], axis=0)
    if TAPS:
        kernel.tap_results = res.results
    return out


# version 03
# speedup vs baseline: 1.1528x; 1.0067x over previous
import math
import numpy as np
import ml_dtypes
import concourse.bass as bass
import concourse.mybir as mybir
from concourse.bass_utils import run_bass_kernel_spmd

F32 = mybir.dt.float32
BF16 = mybir.dt.bfloat16
AX = mybir.AxisListType.X
ALU = mybir.AluOpType
AF = mybir.ActivationFunctionType

T, D, NC, DC = 2048, 1024, 16, 8
NH, HD, DR, DKV = 16, 64, 32, 128
E, DFF = 8, 4096
SC_MLA = 1.0 / math.sqrt(HD + DR)
SC_FFN = 1.0 / math.sqrt(HD)
EPS = 1e-5

TAPS = ()
STAGES = 99


class KB:
    def __init__(self, nc):
        self.nc = nc
        self.streams = {n: [] for n in ("t", "v", "s", "g", "y")}
        self.cnt = {}
        self.handles = {}
        self.lastw = {}
        self.free = {}

    def sem(self, name):
        self.handles[name] = self.nc.semaphore(name).__enter__()
        self.cnt[name] = 0

    def wait(self, st, tok):
        if tok is None:
            return
        sem, val = tok
        assert val <= self.cnt[sem], f"wait {sem}>={val} > emitted {self.cnt[sem]}"
        if self.lastw.get((st, sem), 0) < val:
            self.streams[st].append(("w", sem, val))
            self.lastw[(st, sem)] = val

    AUTO_SERIAL = {"v": "pv", "s": "ps"}

    def op(self, st, fn, inc=None, n=1):
        auto = self.AUTO_SERIAL.get(st)
        if auto is not None:
            self.wait(st, (auto, self.cnt[auto]))
            if inc is None:
                inc, n = auto, 1
        self.streams[st].append(("c", fn, inc, n))
        if inc is not None:
            self.cnt[inc] += n
            return (inc, self.cnt[inc])
        return None

    def tok(self, sem):
        return (sem, self.cnt[sem])

    def need(self, st, *keys):
        for k in keys:
            self.wait(st, self.free.get(k))

    def done(self, key, tok):
        self.free[key] = tok

    def barrier(self):
        for st in ("t", "v", "s", "g", "y"):
            for sem in ("pt", "pv", "ps", "dy", "dg"):
                self.wait(st, (sem, self.cnt[sem]))

    def replay(self, st, eng):
        for item in self.streams[st]:
            if item[0] == "w":
                eng.wait_ge(self.handles[item[1]], item[2])
            else:
                _, fn, inc, n = item
                ins = fn(eng)
                if inc is not None:
                    ins.then_inc(self.handles[inc], n)


def build_program(taps=(), stages=99):
    nc = bass.Bass("TRN2", target_bir_lowering=False)
    tap_req = set(taps)

    dr = {}
    def din(name, shape, dt=F32):
        dr[name] = nc.dram_tensor(name, list(shape), dt, kind="ExternalInput").ap()
    din("x", (T, D))
    din("w_kv_down", (D, DKV)); din("w_k_up", (DKV, D)); din("w_v_up", (DKV, D))
    din("w_q", (D, D)); din("w_q_rope", (D, DR)); din("w_k_rope", (DKV, DR))
    din("w_o1", (D, D))
    din("ln1_g", (128, 8)); din("ln1_b", (128, 8))
    din("w_qkv", (D, 3 * D))
    din("b_qkv_q", (64, 16)); din("b_qkv_k", (64, 16)); din("b_qkv_v", (1, D))
    din("w_o2", (D, D)); din("b_o", (1, D))
    din("ln2_g", (128, 8)); din("ln2_b", (128, 8))
    din("gate_w", (128, 64)); din("gate_b", (1, 8))
    din("W1", (E, D, DFF)); din("b1", (E, DFF))
    din("W2", (E, DFF, D)); din("b2", (E, D))
    din("c_idn", (128, 128))
    din("c_cos", (T, 16)); din("c_sin", (T, 16))
    din("c_maskT", (128, 4 * 512), BF16)
    din("c_ones", (1, T), BF16)
    din("c_onescol", (128, 1), BF16)
    out_d = nc.dram_tensor("out", [T, D], F32, kind="ExternalOutput").ap()
    tap_d = {}
    for tp in tap_req - {"min0", "noW", "serial"}:
        shp = {"xn": (D, T), "h": (D, T), "ckv": (DKV, T), "q": (D, T),
               "s": (T, D), "o": (D, T), "x2": (T, D), "h2": (D, T),
               "ew": (T, 8), "gpre": (T, D), "xnt": (T, D),
               "stm": (128, 16), "str": (128, 16)}[tp]
        tap_d[tp] = nc.dram_tensor("tap_" + tp, list(shp), F32, kind="ExternalOutput").ap()
    s_dram = nc.dram_tensor("s_scr", [T, D], F32, kind="Internal").ap()
    x2_dram = nc.dram_tensor("x2_scr", [T, D], F32, kind="Internal").ap()

    from contextlib import ExitStack
    ctx = ExitStack()
    def sb(name, shape, dt):
        return ctx.enter_context(nc.sbuf_tensor(name, list(shape), dt)).ap()

    AR1 = sb("AR1", (128, 16384), BF16)
    AR2 = sb("AR2", (128, 16384), BF16)
    AR3 = sb("AR3", (128, 16384), BF16)
    MOE = AR3.bitcast(F32)
    WAR = sb("WAR", (128, 20480), BF16)
    CKV = sb("CKV", (128, T), BF16)
    ROPE = sb("ROPE", (128, T), BF16)
    STG = sb("STG", (128, 6144), F32)
    q_st = sb("q_st", (128, T), BF16)
    k_st = sb("k_st", (128, T), BF16)
    v_st = sb("v_st", (128, NC * 65), BF16)
    sst = sb("sst", (128, 2 * 1024), BF16)
    ptb = sb("ptb", (128, 2 * 1024), BF16)
    m_all = sb("m_all", (128, 16), F32)
    mT = sb("mT", (16, 128), F32)
    stats_m = sb("stats_m", (128, 16), F32)
    stats_r = sb("stats_r", (128, 16), F32)
    stats_mr = sb("stats_mr", (128, 16), F32)
    sm = sb("sm", (128, 128), F32)
    eps_c = sb("eps_c", (128, 1), F32)
    idn = sb("idn", (128, 128), F32)
    cost = sb("cost", (128, NC * 16), F32)
    sint = sb("sint", (128, NC * 16), F32)
    maskT = sb("maskT", (128, 4 * 512), BF16)
    onesr = sb("onesr", (1, T), BF16)
    onesc = sb("onesc", (128, 1), BF16)
    ln1g = sb("ln1g", (128, 8), F32); ln1b = sb("ln1b", (128, 8), F32)
    ln2g = sb("ln2g", (128, 8), F32); ln2b = sb("ln2b", (128, 8), F32)
    bqq = sb("bqq", (64, 16), F32); bqk = sb("bqk", (64, 16), F32)
    bqv = sb("bqv", (1, D), BF16)
    bo_row = sb("bo_row", (1, D), BF16)
    gw32 = sb("gw32", (128, 64), F32)
    gwhi = sb("gwhi", (128, 64), BF16); gwlo = sb("gwlo", (128, 64), BF16)
    grhi = sb("grhi", (128, 64), BF16); grlo = sb("grlo", (128, 64), BF16)
    bchi = sb("bchi", (128, 8), BF16); bclo = sb("bclo", (128, 8), BF16)
    bpr = sb("bpr", (1, 16), F32)
    bprh = sb("bprh", (1, 8), BF16); bprl = sb("bprl", (1, 8), BF16)
    bprbc = sb("bprbc", (128, 8), F32)
    gateb = sb("gateb", (1, 8), F32)
    ew_all = sb("ew_all", (128, 128), F32)
    ewCb = ROPE[0:8, 0:2048]
    nstg = ROPE[0:64, 0:512]
    ew_rbf = CKV[0:1, 1024:2048]
    ew_bc = CKV[:, 0:1024]
    whb = AR3[:, 0:4096]
    w1buf = WAR[:, 0:8192]
    w2buf = WAR[:, 8192:12288]
    h2s = WAR[:, 12288:20480]
    sstb = None
    b2bf = q_st[0:8, 0:1024]
    invhb = sb("invhb", (1, 512), BF16)
    invlb = sb("invlb", (1, 512), BF16)
    invu2 = sb("invu2", (1, 512), F32)

    PS_A = ctx.enter_context(nc.psum_tensor("PS_A", [128, 2048], F32)).ap()
    PS_B = ctx.enter_context(nc.psum_tensor("PS_B", [128, 1024], F32)).ap()
    PS_C = ctx.enter_context(nc.psum_tensor("PS_C", [128, 1024], F32)).ap()

    kb = KB(nc)
    for s_ in ("pt", "pv", "ps", "dy", "dg", "od"):
        kb.sem(s_)
    W, need, done = kb.wait, kb.need, kb.done

    def t_op(fn, inc=None):
        return kb.op("t", fn, inc)
    def v_op(fn, inc=None):
        return kb.op("v", fn, inc)
    def s_op(fn, inc=None):
        return kb.op("s", fn, inc)
    def y_dma(dst, src, inc="dy"):
        return kb.op("y", lambda e, d=dst, s2=src: e.dma_start(out=d, in_=s2), inc, 16)
    def g_dma(dst, src, inc="dg"):
        return kb.op("g", lambda e, d=dst, s2=src: e.dma_start(out=d, in_=s2), inc, 16)

    SA, SB_, SC_, SD, SE, SF = 0, 1024, 2048, 3072, 4096, 5120
    inv_h = STG[0:1, SD + 512:SD + 1024]
    invbc = STG[0:64, SD:SD + 512]
    lghi = sst.bitcast(BF16)[:, 0:1024]
    lglo = sst.bitcast(BF16)[:, 1024:2048]
    b1row = sst.bitcast(BF16)[0:1, 0:2048]

    y_dma(idn[:, :], dr["c_idn"][:, :])
    y_dma(cost.rearrange("p (c f) -> p c f", c=16), dr["c_cos"].rearrange("(c p) f -> p c f", p=128))
    y_dma(sint.rearrange("p (c f) -> p c f", c=16), dr["c_sin"].rearrange("(c p) f -> p c f", p=128))
    y_dma(maskT[:, :], dr["c_maskT"][:, :])
    y_dma(onesr[:, :], dr["c_ones"][:, :])
    y_dma(onesc[:, :], dr["c_onescol"][:, :])
    y_dma(ln1g[:, :], dr["ln1_g"][:, :]); y_dma(ln1b[:, :], dr["ln1_b"][:, :])
    y_dma(ln2g[:, :], dr["ln2_g"][:, :]); y_dma(ln2b[:, :], dr["ln2_b"][:, :])
    y_dma(bqq[:, :], dr["b_qkv_q"][:, :]); y_dma(bqk[:, :], dr["b_qkv_k"][:, :])
    y_dma(gw32[:, :], dr["gate_w"][:, :])
    y_dma(gateb[:, :], dr["gate_b"][:, :])
    consts_tok = kb.tok("dy")

    O_WO2 = 0
    O_WKV, O_WQ, O_WQR, O_WKR = 0, 1024, 9216, 9472
    O_WKUP, O_WVUP, O_WO1 = 9504, 10528, 11552
    if "noW" not in tap_req:
        g_dma(WAR[:, O_WKV:O_WKV + 1024].rearrange("p (c m) -> p c m", c=8), dr["w_kv_down"].rearrange("(c p) m -> p c m", p=128))
        g_dma(WAR[:, O_WQ:O_WQ + 8192].rearrange("p (c m) -> p c m", c=8), dr["w_q"].rearrange("(c p) m -> p c m", p=128))
        g_dma(WAR[:, O_WQR:O_WQR + 256].rearrange("p (c m) -> p c m", c=8), dr["w_q_rope"].rearrange("(c p) m -> p c m", p=128))
        g_dma(WAR[:, O_WKR:O_WKR + 32], dr["w_k_rope"][:, :])
        g_dma(WAR[:, O_WKUP:O_WKUP + 1024], dr["w_k_up"][:, :])
        g_dma(WAR[:, O_WVUP:O_WVUP + 1024], dr["w_v_up"][:, :])
        g_dma(WAR[:, O_WO1:O_WO1 + 8192].rearrange("p (c m) -> p c m", c=8), dr["w_o1"].rearrange("(c p) m -> p c m", p=128))
        g_dma(bqv[:, :], dr["b_qkv_v"][:, :])
        g_dma(bo_row[:, :], dr["b_o"][:, :])
    w1w_tok = kb.tok("dg")

    W("v", consts_tok)
    v_op(lambda e: e.memset(eps_c[:, :], EPS))
    for kc in range(NC):
        vt = v_op(lambda e, kc_=kc: e.tensor_copy(
            v_st[:, kc_ * 65 + 64:kc_ * 65 + 65], onesc[:, 0:1]),
            "pv" if kc == NC - 1 else None)
    vst_ones_tok = vt

    W("t", consts_tok)
    for i in range(NC):
        c8 = i * 8
        need("y", ("stg", SA))
        xtok = y_dma(STG[:, SA:SA + 1024], dr["x"][i * 128:(i + 1) * 128, :])
        W("v", xtok)
        rt = v_op(lambda e, c_=c8: e.reduce_sum(sm[:, c_:c_ + 1], STG[:, SA:SA + 1024],
                                                axis=AX), "pv")
        W("v", rt)
        vt = v_op(lambda e, i_=i, c_=c8: e.tensor_scalar_mul(
            stats_m[:, i_:i_ + 1], sm[:, c_:c_ + 1], -1.0 / D), "pv")
        need("v", ("stg", SB_))
        st_ = v_op(lambda e, i_=i: e.tensor_scalar(
            STG[:, SB_:SB_ + 1024], STG[:, SA:SA + 1024],
            stats_m[:, i_:i_ + 1], None, op0=ALU.add), "pv")
        done(("stg", SA), st_)
        need("v", ("stg", SD))
        v_op(lambda e: e.tensor_mul(STG[:, SD:SD + 1024], STG[:, SB_:SB_ + 1024],
                                    STG[:, SB_:SB_ + 1024]))
        vt = v_op(lambda e, c_=c8: e.reduce_sum(sm[:, c_ + 1:c_ + 2],
                                                STG[:, SD:SD + 1024], axis=AX), "pv")
        W("s", vt)
        s_op(lambda e, c_=c8: e.activation(sm[:, c_ + 2:c_ + 3], sm[:, c_ + 1:c_ + 2],
                                           AF.Ln, bias=eps_c[:, 0:1], scale=1.0 / D))
        st_ = s_op(lambda e, i_=i, c_=c8: e.activation(
            stats_r[:, i_:i_ + 1], sm[:, c_ + 2:c_ + 3], AF.Exp, bias=0.0, scale=-0.5), "ps")
        W("v", st_)
        v_op(lambda e, i_=i: e.tensor_mul(stats_mr[:, i_:i_ + 1],
                                          stats_m[:, i_:i_ + 1], stats_r[:, i_:i_ + 1]))
        need("v", ("stg", SC_))
        st_ = v_op(lambda e, i_=i: e.tensor_scalar(
            STG[:, SC_:SC_ + 1024], STG[:, SB_:SB_ + 1024],
            stats_r[:, i_:i_ + 1], None, op0=ALU.mult), "pv")
        if "xnt" in tap_req:
            W("y", st_)
            xnt_t = y_dma(tap_d["xnt"][i * 128:(i + 1) * 128, :], STG[:, SC_:SC_ + 1024])
        rt = v_op(lambda e, c_=c8: e.reduce_sum(sm[:, c_ + 3:c_ + 4],
                                                STG[:, SC_:SC_ + 1024], axis=AX), "pv")
        W("v", rt)
        vt = v_op(lambda e, c_=c8: e.tensor_scalar_mul(
            sm[:, c_ + 4:c_ + 5], sm[:, c_ + 3:c_ + 4], -1.0 / D), "pv")
        st_ = v_op(lambda e, c_=c8: e.tensor_scalar(
            STG[:, SB_:SB_ + 1024], STG[:, SC_:SC_ + 1024],
            sm[:, c_ + 4:c_ + 5], None, op0=ALU.add), "pv")
        v_op(lambda e: e.tensor_mul(STG[:, SD:SD + 1024], STG[:, SB_:SB_ + 1024],
                                    STG[:, SB_:SB_ + 1024]))
        vt = v_op(lambda e, c_=c8: e.reduce_sum(sm[:, c_ + 5:c_ + 6],
                                                STG[:, SD:SD + 1024], axis=AX), "pv")
        done(("stg", SD), vt)
        W("s", vt)
        s_op(lambda e, c_=c8: e.activation(sm[:, c_ + 6:c_ + 7], sm[:, c_ + 5:c_ + 6],
                                           AF.Ln, bias=eps_c[:, 0:1], scale=1.0 / D))
        st_ = s_op(lambda e, c_=c8: e.activation(sm[:, c_ + 7:c_ + 8], sm[:, c_ + 6:c_ + 7],
                                           AF.Exp, bias=0.0, scale=-0.5), "ps")
        W("v", st_)
        st_ = v_op(lambda e, c_=c8: e.tensor_scalar(
            STG[:, SB_:SB_ + 1024], STG[:, SB_:SB_ + 1024],
            sm[:, c_ + 7:c_ + 8], None, op0=ALU.mult), "pv")
        W("t", st_)
        need("t", "psB0", "psB1", "psA0")
        for c in range(8):
            tk1 = t_op(lambda e, c_=c: e.transpose(
                PS_B[:, c_ * 128:(c_ + 1) * 128],
                STG[:, SC_ + c_ * 128:SC_ + (c_ + 1) * 128], idn[:, :]),
                "pt" if c == 7 else None)
        for c in range(8):
            tk2 = t_op(lambda e, c_=c: e.transpose(
                PS_A[:, c_ * 128:(c_ + 1) * 128],
                STG[:, SB_ + c_ * 128:SB_ + (c_ + 1) * 128], idn[:, :]),
                "pt" if c == 7 else None)
        W("v", tk1)
        vt = v_op(lambda e, i_=i: e.tensor_copy(
            AR1.rearrange("p (c t) -> p c t", c=8)[:, :, i_ * 128:(i_ + 1) * 128],
            PS_B.rearrange("p (c t) -> p c t", c=8)[:, :, :]), "pv")
        done("psB0", vt); done("psB1", vt)
        done(("stg", SC_), vt)
        if "min0" in tap_req:
            done("psA0", tk2)
            done(("stg", SB_), tk2)
            continue
        W("v", tk2)
        for c in range(8):
            vt = v_op(lambda e, i_=i, c_=c: e.tensor_scalar(
                AR2[:, c_ * 2048 + i_ * 128: c_ * 2048 + (i_ + 1) * 128],
                PS_A[:, c_ * 128:(c_ + 1) * 128],
                ln1g[:, c_:c_ + 1], ln1b[:, c_:c_ + 1], op0=ALU.mult, op1=ALU.add),
                "pv" if c == 7 else None)
        done("psA0", vt)
        done(("stg", SB_), vt)
        if "serial" in tap_req:
            kb.barrier()

    def tap_fm(name, src, rows=128):
        if name not in tap_req:
            return
        nchunks = {1024: 1}.get(src.shape[1] // 16, DC)
        total = src.shape[1]
        ncol = total // 1024
        for cj in range(ncol):
            need("v", ("stg", SA))
            vt = v_op(lambda e, c_=cj: e.tensor_copy(
                STG[0:rows, SA:SA + 1024], src[0:rows, c_ * 1024:(c_ + 1) * 1024]), "pv")
            W("y", vt)
            r0 = (cj // 2) * 128
            col = (cj % 2) * 1024
            if src.shape[1] == T:
                dt_ = y_dma(tap_d[name][0:rows, cj * 1024:(cj + 1) * 1024],
                            STG[0:rows, SA:SA + 1024])
            else:
                dt_ = y_dma(tap_d[name][r0:r0 + 128, col:col + 1024],
                            STG[0:rows, SA:SA + 1024])
            done(("stg", SA), dt_)

    if "stm" in tap_req:
        W("y", kb.tok("pv"))
        y_dma(tap_d["stm"][:, :], stats_m[:, :])
        y_dma(tap_d["str"][:, :], stats_r[:, :])
    tap_fm("xn", AR1)
    tap_fm("h", AR2)
    if stages <= 0:
        return _finish(nc, kb, ctx)

    W("t", w1w_tok)
    W("t", kb.tok("pv"))
    for nb in range(4):
        slot = "psC%d" % (nb % 2)
        need("t", slot)
        for c in range(8):
            tk = t_op(lambda e, c_=c, nb_=nb: e.matmul(
                PS_C[:, (nb_ % 2) * 512:(nb_ % 2) * 512 + 512],
                WAR[:, O_WKV + c_ * 128:O_WKV + (c_ + 1) * 128],
                AR1[:, c_ * 2048 + nb_ * 512: c_ * 2048 + nb_ * 512 + 512],
                start=(c_ == 0), stop=(c_ == 7)), "pt" if c == 7 else None)
        W("v", tk)
        vt = v_op(lambda e, nb_=nb: e.tensor_copy(
            CKV[:, nb_ * 512:nb_ * 512 + 512],
            PS_C[:, (nb_ % 2) * 512:(nb_ % 2) * 512 + 512]), "pv")
        done(slot, vt)
    tap_fm("ckv", CKV)

    for m in range(8):
        for nb in range(4):
            slot = "psC%d" % (nb % 2)
            need("t", slot)
            for c in range(8):
                tk = t_op(lambda e, c_=c, m_=m, nb_=nb: e.matmul(
                    PS_C[:, (nb_ % 2) * 512:(nb_ % 2) * 512 + 512],
                    WAR[:, O_WQ + c_ * 1024 + m_ * 128: O_WQ + c_ * 1024 + (m_ + 1) * 128],
                    AR1[:, c_ * 2048 + nb_ * 512: c_ * 2048 + nb_ * 512 + 512],
                    start=(c_ == 0), stop=(c_ == 7)), "pt" if c == 7 else None)
            W("v", tk)
            vt = v_op(lambda e, m_=m, nb_=nb: e.tensor_single_scalar(
                AR3[:, m_ * 2048 + nb_ * 512: m_ * 2048 + nb_ * 512 + 512],
                PS_C[:, (nb_ % 2) * 512:(nb_ % 2) * 512 + 512], SC_MLA, op=ALU.mult), "pv")
            done(slot, vt)
    tap_fm("q", AR3)

    QR, ROT, TA, TB = SE, SE + 32, SE + 64, SE + 80
    for i in range(NC):
        for which in range(2):
            need("t", "psC0")
            if which == 0:
                for c in range(8):
                    tk = t_op(lambda e, c_=c, i_=i: e.matmul(
                        PS_C[:, 0:32],
                        AR1[:, c_ * 2048 + i_ * 128: c_ * 2048 + (i_ + 1) * 128],
                        WAR[:, O_WQR + c_ * 32:O_WQR + (c_ + 1) * 32],
                        start=(c_ == 0), stop=(c_ == 7)), "pt" if c == 7 else None)
            else:
                tk = t_op(lambda e, i_=i: e.matmul(
                    PS_C[:, 0:32],
                    CKV[:, i_ * 128:(i_ + 1) * 128],
                    WAR[:, O_WKR:O_WKR + 32], start=True, stop=True), "pt")
            W("v", tk)
            need("v", ("stg", SE))
            vt = v_op(lambda e: e.tensor_copy(STG[:, QR:QR + 32], PS_C[:, 0:32]), "pv")
            done("psC0", vt)
            v_op(lambda e, i_=i: e.tensor_mul(STG[:, TA:TA + 16], STG[:, QR:QR + 16],
                                              cost[:, i_ * 16:(i_ + 1) * 16]))
            v_op(lambda e, i_=i: e.tensor_mul(STG[:, TB:TB + 16], STG[:, QR + 16:QR + 32],
                                              sint[:, i_ * 16:(i_ + 1) * 16]))
            v_op(lambda e: e.tensor_sub(STG[:, ROT:ROT + 16], STG[:, TA:TA + 16],
                                        STG[:, TB:TB + 16]))
            v_op(lambda e, i_=i: e.tensor_mul(STG[:, TA:TA + 16], STG[:, QR + 16:QR + 32],
                                              cost[:, i_ * 16:(i_ + 1) * 16]))
            v_op(lambda e, i_=i: e.tensor_mul(STG[:, TB:TB + 16], STG[:, QR:QR + 16],
                                              sint[:, i_ * 16:(i_ + 1) * 16]))
            vt = v_op(lambda e: e.tensor_add(STG[:, ROT + 16:ROT + 32], STG[:, TA:TA + 16],
                                             STG[:, TB:TB + 16]), "pv")
            W("t", vt)
            need("t", "psC1")
            tk = t_op(lambda e: e.transpose(PS_C[0:32, 512:640], STG[:, ROT:ROT + 32],
                                            idn[:, :]), "pt")
            W("v", tk)
            base = 0 if which == 0 else 64
            scale = SC_MLA if which == 0 else 1.0
            vt = v_op(lambda e, i_=i, b_=base, s2=scale: e.tensor_single_scalar(
                ROPE[b_:b_ + 32, i_ * 128:(i_ + 1) * 128],
                PS_C[0:32, 512:640], s2, op=ALU.mult), "pv")
            done("psC1", vt)
            done(("stg", SE), vt)
    rope_tok = kb.tok("pv")

    if stages <= 1:
        return _finish(nc, kb, ctx)

    def attention(kind):
        mla = kind == "mla"
        KQ = 96 if mla else 64
        KBr = 97 if mla else 65
        brow = 96 if mla else 64

        if mla:
            W("v", rope_tok)
            vt = v_op(lambda e: e.tensor_copy(k_st[64:96, :], ROPE[64:96, :]), "pv")
            W("g", rope_tok)
            rtok = g_dma(q_st[64:96, :], ROPE[0:32, :])
            done("nstg", rtok)
            W("y", vt)
            y_dma(k_st[96:97, :], onesr[:, :])
        else:
            W("y", kb.tok("pt"))
            y_dma(k_st[64:65, :], onesr[:, :])
            W("g", kb.tok("pt"))
        aug_tok = kb.tok("dy")

        for h in range(NH):
            pe_prev = kb.tok("pt")
            if mla:
                W("g", pe_prev)
                vt = g_dma(q_st[0:64, :], AR3[(h % 2) * 64:(h % 2) * 64 + 64,
                                              (h // 2) * 2048:(h // 2) * 2048 + 2048])
                W("t", vt)
                W("v", pe_prev)
                BS = [(PS_C, 0, "psC0"), (PS_C, 512, "psC1"), (PS_A, 0, "bA0"),
                      (PS_A, 512, "bA1"), (PS_A, 1024, "bA2"), (PS_A, 1536, "bA3")]
                bi = 0
                for nb in range(4):
                    ps_, off_, key = BS[bi % 6]; bi += 1
                    need("t", key)
                    if key in ("bA0", "bA1"):
                        need("t", "psA0")
                    elif key in ("bA2", "bA3"):
                        need("t", "psA1")
                    tk = t_op(lambda e, h_=h, nb_=nb, p_=ps_, o_=off_: e.matmul(
                        p_[0:64, o_:o_ + 512],
                        WAR[:, O_WKUP + h_ * 64:O_WKUP + (h_ + 1) * 64],
                        CKV[:, nb_ * 512:nb_ * 512 + 512], start=True, stop=True), "pt")
                    W("v", tk)
                    vk = v_op(lambda e, nb_=nb, p_=ps_, o_=off_: e.tensor_copy(
                        k_st[0:64, nb_ * 512:nb_ * 512 + 512],
                        p_[0:64, o_:o_ + 512]), "pv")
                    done(key, vk)
                for g4 in range(4):
                    ps_, off_, key = BS[bi % 6]; bi += 1
                    need("t", key)
                    if key in ("bA0", "bA1"):
                        need("t", "psA0")
                    elif key in ("bA2", "bA3"):
                        need("t", "psA1")
                    for z in range(4):
                        tk = t_op(lambda e, h_=h, kc_=g4 * 4 + z, z_=z, p_=ps_, o_=off_:
                                  e.matmul(
                            p_[:, o_ + z_ * 64:o_ + z_ * 64 + 64],
                            CKV[:, kc_ * 128:(kc_ + 1) * 128],
                            WAR[:, O_WVUP + h_ * 64:O_WVUP + (h_ + 1) * 64],
                            start=True, stop=True), "pt" if z == 3 else None)
                    W("v", tk)
                    W("v", vst_ones_tok)
                    vk = v_op(lambda e, g_=g4, p_=ps_, o_=off_: e.tensor_copy(
                        v_st.rearrange("p (k w) -> p k w", w=65)[:, g_ * 4:(g_ + 1) * 4, 0:64],
                        p_[:, o_:o_ + 256].rearrange("p (k w) -> p k w", w=64)), "pv")
                    done(key, vk)
            else:
                sl = h % 2
                off = sl * 1536
                need("g", ("whb", sl))
                g_dma(whb[:, off:off + 512].rearrange("p (c f) -> p c f", c=8),
                      dr["w_qkv"][:, h * 64:(h + 1) * 64].rearrange("(c p) f -> p c f", p=128))
                g_dma(whb[:, off + 512:off + 1024].rearrange("p (c f) -> p c f", c=8),
                      dr["w_qkv"][:, D + h * 64:D + (h + 1) * 64].rearrange("(c p) f -> p c f", p=128))
                gt = g_dma(whb[:, off + 1024:off + 1536].rearrange("p (c f) -> p c f", c=8),
                           dr["w_qkv"][:, 2 * D + h * 64:2 * D + (h + 1) * 64].rearrange("(c p) f -> p c f", p=128))
                W("t", gt)
                W("t", pe_prev)
                BS = [(PS_C, 0, "psC0"), (PS_C, 512, "psC1"), (PS_A, 0, "bA0"),
                      (PS_A, 512, "bA1"), (PS_A, 1024, "bA2"), (PS_A, 1536, "bA3")]
                def bneed(key):
                    need("t", key)
                    if key in ("bA0", "bA1"):
                        need("t", "psA0")
                    elif key in ("bA2", "bA3"):
                        need("t", "psA1")
                bi = 0
                for nb in range(4):
                    ps_, off_, key = BS[bi % 6]; bi += 1
                    bneed(key)
                    for c in range(8):
                        tk = t_op(lambda e, nb_=nb, c_=c, o_=off, p_=ps_, q2_=off_:
                                  e.matmul(
                            p_[0:64, q2_:q2_ + 512],
                            whb[:, o_ + c_ * 64:o_ + (c_ + 1) * 64],
                            AR2[:, c_ * 2048 + nb_ * 512: c_ * 2048 + nb_ * 512 + 512],
                            start=(c_ == 0), stop=(c_ == 7)), "pt" if c == 7 else None)
                    W("v", tk)
                    W("v", pe_prev)
                    vk = v_op(lambda e, h_=h, nb_=nb, p_=ps_, q2_=off_: e.tensor_scalar(
                        q_st[0:64, nb_ * 512:nb_ * 512 + 512],
                        p_[0:64, q2_:q2_ + 512],
                        bqq[:, h_:h_ + 1], SC_FFN, op0=ALU.add, op1=ALU.mult), "pv")
                    done(key, vk)
                for nb in range(4):
                    ps_, off_, key = BS[bi % 6]; bi += 1
                    bneed(key)
                    for c in range(8):
                        tk = t_op(lambda e, nb_=nb, c_=c, o_=off, p_=ps_, q2_=off_:
                                  e.matmul(
                            p_[0:64, q2_:q2_ + 512],
                            whb[:, o_ + 512 + c_ * 64:o_ + 512 + (c_ + 1) * 64],
                            AR2[:, c_ * 2048 + nb_ * 512: c_ * 2048 + nb_ * 512 + 512],
                            start=(c_ == 0), stop=(c_ == 7)), "pt" if c == 7 else None)
                    W("v", tk)
                    vk = v_op(lambda e, h_=h, nb_=nb, p_=ps_, q2_=off_: e.tensor_scalar(
                        k_st[0:64, nb_ * 512:nb_ * 512 + 512],
                        p_[0:64, q2_:q2_ + 512],
                        bqk[:, h_:h_ + 1], None, op0=ALU.add), "pv")
                    done(key, vk)
                for g4 in range(4):
                    ps_, off_, key = BS[bi % 6]; bi += 1
                    bneed(key)
                    for z in range(4):
                        kc = g4 * 4 + z
                        for c in range(8):
                            t_op(lambda e, kc_=kc, c_=c, o_=off, z_=z, p_=ps_, q2_=off_:
                                 e.matmul(
                                p_[:, q2_ + z_ * 64:q2_ + z_ * 64 + 64],
                                AR2[:, c_ * 2048 + kc_ * 128: c_ * 2048 + (kc_ + 1) * 128],
                                whb[:, o_ + 1024 + c_ * 64:o_ + 1024 + (c_ + 1) * 64],
                                start=(c_ == 0), stop=False))
                        tk = t_op(lambda e, h_=h, kc_=kc, z_=z, p_=ps_, q2_=off_:
                                  e.matmul(
                            p_[:, q2_ + z_ * 64:q2_ + z_ * 64 + 64],
                            onesr[0:1, kc_ * 128:(kc_ + 1) * 128],
                            bqv[0:1, h_ * 64:(h_ + 1) * 64], start=False, stop=True),
                            "pt" if z == 3 else None)
                    W("v", tk)
                    W("v", vst_ones_tok)
                    vk = v_op(lambda e, g_=g4, p_=ps_, q2_=off_: e.tensor_copy(
                        v_st.rearrange("p (k w) -> p k w", w=65)[:, g_ * 4:(g_ + 1) * 4, 0:64],
                        p_[:, q2_:q2_ + 256].rearrange("p (k w) -> p k w", w=64)), "pv")
                    done(key, vk)
                done(("whb", sl), kb.tok("pt"))
            qkv_tok = kb.tok("pv")

            W("t", qkv_tok)
            units = []
            for qc in range(NC):
                nb = 4 if not mla else min(4, qc // 4 + 1)
                units.append((qc, 0, min(nb, 2)))
                if nb > 2:
                    units.append((qc, 1, nb - 2))
            for uidx, (qc, hf, nblk) in enumerate(units):
                reg = uidx % 2
                rk = "psA%d" % reg
                need("t", rk, *( ("bA0", "bA1") if reg == 0 else ("bA2", "bA3") ))
                for b in range(nblk):
                    tk = t_op(lambda e, qc_=qc, hf_=hf, b_=b, r_=reg, K_=KQ: e.matmul(
                        PS_A[:, r_ * 1024 + b_ * 512:r_ * 1024 + (b_ + 1) * 512],
                        q_st[0:K_, qc_ * 128:(qc_ + 1) * 128],
                        k_st[0:K_, hf_ * 1024 + b_ * 512:hf_ * 1024 + (b_ + 1) * 512],
                        start=True, stop=True), "pt" if b == nblk - 1 else None)
                W("v", tk)
                if hf == 0:
                    vt = v_op(lambda e, qc_=qc, r_=reg, n_=nblk: e.reduce_max(
                        m_all[:, qc_:qc_ + 1], PS_A[:, r_ * 1024:r_ * 1024 + n_ * 512],
                        axis=AX, negate=True), "pv")
                else:
                    rt = v_op(lambda e, r_=reg, n_=nblk: e.reduce_max(
                        sm[:, 0:1], PS_A[:, r_ * 1024:r_ * 1024 + n_ * 512],
                        axis=AX, negate=True), "pv")
                    W("v", rt)
                    vt = v_op(lambda e, qc_=qc: e.tensor_tensor(
                        m_all[:, qc_:qc_ + 1], m_all[:, qc_:qc_ + 1], sm[:, 0:1],
                        op=ALU.min), "pv")
                done(rk, vt)
            W("t", kb.tok("pv"))
            need("t", "psC0")
            tk = t_op(lambda e: e.transpose(PS_C[0:16, 0:128], m_all[:, :], idn[:, :]), "pt")
            W("v", tk)
            vt = v_op(lambda e: e.tensor_copy(mT[:, :], PS_C[0:16, 0:128]), "pv")
            done("psC0", vt)
            W("g", vt)
            mrow_tok = g_dma(q_st[brow:brow + 1, :].rearrange("o (c q) -> o c q", c=16),
                             mT[:, :])

            for qb in range(4):
                kcs = list(range((qb + 1) * 4)) if mla else list(range(16))
                pvk = "psB%d" % (qb % 2)
                need("t", pvk)
                pairs = [kcs[j:j + 2] for j in range(0, len(kcs), 2)]
                for pj, pair in enumerate(pairs):
                    ssl = pj % 2
                    need("t", "psA%d" % ssl,
                         *( ("bA0", "bA1") if ssl == 0 else ("bA2", "bA3") ))
                    if pj == 0:
                        W("t", mrow_tok)
                        W("t", aug_tok)
                    for z, kc in enumerate(pair):
                        tk = t_op(lambda e, kc_=kc, qb_=qb, ssl_=ssl, z_=z, K_=KBr:
                                  e.matmul(
                            PS_A[:, ssl_ * 1024 + z_ * 512:ssl_ * 1024 + z_ * 512 + 512],
                            k_st[0:K_, kc_ * 128:(kc_ + 1) * 128],
                            q_st[0:K_, qb_ * 512:qb_ * 512 + 512],
                            start=True, stop=True), "pt" if z == len(pair) - 1 else None)
                    W("v", tk)
                    wid = len(pair) * 512
                    need("v", ("sst", ssl))
                    vt = v_op(lambda e, ssl_=ssl, w_=wid: e.tensor_copy(
                        sst[:, ssl_ * 1024:ssl_ * 1024 + w_],
                        PS_A[:, ssl_ * 1024:ssl_ * 1024 + w_]), "pv")
                    done("psA%d" % ssl, vt)
                    W("s", vt)
                    need("s", ("ptb", ssl))
                    st_ = s_op(lambda e, ssl_=ssl, w_=wid: e.activation(
                        ptb[:, ssl_ * 1024:ssl_ * 1024 + w_],
                        sst[:, ssl_ * 1024:ssl_ * 1024 + w_], AF.Exp,
                        bias=0.0, scale=1.0), "ps")
                    done(("sst", ssl), st_)
                    gtk = st_
                    if mla:
                        for z, kc in enumerate(pair):
                            if kc >= qb * 4:
                                a = kc - qb * 4
                                W("v", st_)
                                vt = v_op(lambda e, ssl_=ssl, z_=z, a_=a: e.tensor_mul(
                                    ptb[:, ssl_ * 1024 + z_ * 512:ssl_ * 1024 + z_ * 512 + 512],
                                    ptb[:, ssl_ * 1024 + z_ * 512:ssl_ * 1024 + z_ * 512 + 512],
                                    maskT[:, a_ * 512:(a_ + 1) * 512]), "pv")
                                gtk = vt
                    W("t", gtk)
                    for z, kc in enumerate(pair):
                        tk = t_op(lambda e, kc_=kc, q_=qb, ssl_=ssl, z_=z,
                                  f_=(pj == 0 and z == 0),
                                  l_=(pj == len(pairs) - 1 and z == len(pair) - 1):
                                  e.matmul(
                            PS_B[0:65, (q_ % 2) * 512:(q_ % 2) * 512 + 512],
                            v_st[:, kc_ * 65:(kc_ + 1) * 65],
                            ptb[:, ssl_ * 1024 + z_ * 512:ssl_ * 1024 + z_ * 512 + 512],
                            start=f_, stop=l_), "pt" if z == len(pair) - 1 else None)
                    done(("ptb", ssl), tk)
                pv_done = tk
                W("v", pv_done)
                need("v", "invb", ("stg", SD))
                vt = v_op(lambda e, q_=qb: e.tensor_copy(
                    inv_h[:, :], PS_B[64:65, (q_ % 2) * 512:(q_ % 2) * 512 + 512]), "pv")
                W("s", vt)
                s_op(lambda e: e.activation(invu2[:, :], inv_h[:, :], AF.Ln,
                                            bias=0.0, scale=1.0))
                st_ = s_op(lambda e: e.activation(inv_h[:, :], invu2[:, :], AF.Exp,
                                                  bias=0.0, scale=-1.0), "ps")
                W("v", st_)
                v_op(lambda e: e.tensor_copy(invhb[0:1, :], inv_h[:, :]))
                v_op(lambda e: e.tensor_copy(invu2[:, :], invhb[0:1, :]))
                vt = v_op(lambda e: e.tensor_sub(invlb[0:1, :], inv_h[:, :],
                                                 invu2[:, :]), "pv")
                W("t", vt)
                need("t", "psC1")
                t_op(lambda e: e.matmul(PS_C[0:64, 512:1024], onesr[0:1, 0:64],
                                        invhb[0:1, :], start=True, stop=False))
                tk = t_op(lambda e: e.matmul(PS_C[0:64, 512:1024], onesr[0:1, 0:64],
                                             invlb[0:1, :], start=False, stop=True), "pt")
                W("v", tk)
                vt = v_op(lambda e: e.tensor_copy(invbc[:, :], PS_C[0:64, 512:1024]), "pv")
                done("psC1", vt)
                if h % 2 == 0:
                    vt2 = v_op(lambda e, h_=h, q_=qb: e.tensor_mul(
                        AR1[0:64,
                            (h_ // 2) * 2048 + q_ * 512:(h_ // 2) * 2048 + q_ * 512 + 512],
                        PS_B[0:64, (q_ % 2) * 512:(q_ % 2) * 512 + 512], invbc[:, :]), "pv")
                    done(pvk, vt2)
                else:
                    need("v", "nstg")
                    vt2 = v_op(lambda e, q_=qb: e.tensor_mul(
                        nstg[:, :],
                        PS_B[0:64, (q_ % 2) * 512:(q_ % 2) * 512 + 512], invbc[:, :]), "pv")
                    done(pvk, vt2)
                    W("g", vt2)
                    gt2 = g_dma(AR1[64:128,
                                    (h // 2) * 2048 + qb * 512:(h // 2) * 2048 + qb * 512 + 512],
                                nstg[:, :])
                    done("nstg", gt2)
                done("invb", vt2)
                done(("stg", SD), vt2)

    attention("mla")
    if stages <= 2:
        return _finish(nc, kb, ctx)

    mla_tok = kb.tok("pv")
    W("t", kb.tok("dg"))
    for i in range(NC):
        need("t", "psB0", "psB1")
        W("t", mla_tok)
        for nb in range(2):
            for c in range(8):
                tk = t_op(lambda e, i_=i, nb_=nb, c_=c: e.matmul(
                    PS_B[:, nb_ * 512:nb_ * 512 + 512],
                    AR1[:, c_ * 2048 + i_ * 128: c_ * 2048 + (i_ + 1) * 128],
                    WAR[:, O_WO1 + c_ * 1024 + nb_ * 512: O_WO1 + c_ * 1024 + nb_ * 512 + 512],
                    start=(c_ == 0), stop=(c_ == 7)), "pt" if c == 7 else None)
        W("v", tk)
        need("v", ("stg", SF))
        vt = v_op(lambda e: e.tensor_copy(STG[:, SF:SF + 1024], PS_B[:, :]), "pv")
        done("psB0", vt); done("psB1", vt)
        W("y", vt)
        dt_ = y_dma(s_dram[i * 128:(i + 1) * 128, :], STG[:, SF:SF + 1024])
        if "s" in tap_req:
            dt_ = y_dma(tap_d["s"][i * 128:(i + 1) * 128, :], STG[:, SF:SF + 1024])
        done(("stg", SF), dt_)
    s_mm_tok = kb.tok("pt")

    if stages <= 3:
        return _finish(nc, kb, ctx)

    W("g", s_mm_tok)
    wo2_tok = g_dma(WAR[:, O_WO2:O_WO2 + 8192].rearrange("p (c m) -> p c m", c=8),
                    dr["w_o2"].rearrange("(c p) m -> p c m", p=128))

    W("v", s_mm_tok)
    attention("ffn")
    ffn_tok = kb.tok("pv")
    tap_fm("o", AR1)
    if stages <= 4:
        return _finish(nc, kb, ctx)

    W("v", consts_tok)
    need("v", ("stg", SE))
    for c in range(8):
        v_op(lambda e, c_=c: e.tensor_scalar(
            STG[:, SE + c_ * 8:SE + (c_ + 1) * 8], gw32[:, c_ * 8:(c_ + 1) * 8],
            ln2g[:, c_:c_ + 1], None, op0=ALU.mult))
    v_op(lambda e: e.tensor_copy(gwhi[:, :], STG[:, SE:SE + 64]))
    v_op(lambda e: e.tensor_copy(STG[:, SE + 64:SE + 128], gwhi[:, :]))
    v_op(lambda e: e.tensor_sub(gwlo[:, :], STG[:, SE:SE + 64], STG[:, SE + 64:SE + 128]))
    v_op(lambda e: e.tensor_copy(grhi[:, :], gw32[:, :]))
    v_op(lambda e: e.tensor_copy(STG[:, SE + 64:SE + 128], grhi[:, :]))
    v_op(lambda e: e.tensor_sub(grlo[:, :], gw32[:, :], STG[:, SE + 64:SE + 128]))
    v_op(lambda e: e.tensor_copy(bchi[:, :], ln2b[:, :]))
    v_op(lambda e: e.tensor_copy(STG[:, SE + 64:SE + 72], bchi[:, :]))
    vt = v_op(lambda e: e.tensor_sub(bclo[:, :], ln2b[:, :], STG[:, SE + 64:SE + 72]), "pv")
    done(("stg", SE), vt)
    W("t", vt)
    need("t", "psC0")
    np_ = 0
    for c in range(8):
        for bb, gg in ((bchi, grhi), (bchi, grlo), (bclo, grhi)):
            np_ += 1
            tk = t_op(lambda e, c_=c, b_=bb, g_=gg, f_=(np_ == 1), l_=(np_ == 24): e.matmul(
                PS_C[0:1, 0:8], b_[:, c_:c_ + 1], g_[:, c_ * 8:(c_ + 1) * 8],
                start=f_, stop=l_), "pt" if np_ == 24 else None)
    W("v", tk)
    v_op(lambda e: e.tensor_copy(bpr[0:1, 0:8], PS_C[0:1, 0:8]))
    v_op(lambda e: e.tensor_add(bpr[0:1, 0:8], bpr[0:1, 0:8], gateb[0:1, :]))
    v_op(lambda e: e.tensor_copy(bprh[0:1, :], bpr[0:1, 0:8]))
    v_op(lambda e: e.tensor_copy(bpr[0:1, 8:16], bprh[0:1, :]))
    vt = v_op(lambda e: e.tensor_sub(bprl[0:1, :], bpr[0:1, 0:8], bpr[0:1, 8:16]), "pv")
    done("psC0", vt)
    W("t", vt)
    need("t", "psC0")
    t_op(lambda e: e.matmul(PS_C[:, 0:8], onesr[0:1, 0:128], bprh[0:1, :],
                            start=True, stop=False))
    tk = t_op(lambda e: e.matmul(PS_C[:, 0:8], onesr[0:1, 0:128], bprl[0:1, :],
                                 start=False, stop=True), "pt")
    W("v", tk)
    gp_tok = v_op(lambda e: e.tensor_copy(bprbc[:, :], PS_C[:, 0:8]), "pv")
    done("psC0", gp_tok)

    W("t", kb.tok("dg"))
    for i in range(NC):
        c8 = i * 8
        need("y", ("stg", SA))
        xtok = y_dma(STG[:, SA:SA + 1024], dr["x"][i * 128:(i + 1) * 128, :])
        need("t", "psB0", "psB1")
        W("t", ffn_tok)
        W("t", wo2_tok)
        for nb in range(2):
            for c in range(8):
                t_op(lambda e, i_=i, nb_=nb, c_=c: e.matmul(
                    PS_B[:, nb_ * 512:nb_ * 512 + 512],
                    AR1[:, c_ * 2048 + i_ * 128: c_ * 2048 + (i_ + 1) * 128],
                    WAR[:, O_WO2 + c_ * 1024 + nb_ * 512: O_WO2 + c_ * 1024 + nb_ * 512 + 512],
                    start=(c_ == 0), stop=False))
            tk = t_op(lambda e, i_=i, nb_=nb: e.matmul(
                PS_B[:, nb_ * 512:nb_ * 512 + 512],
                onesr[0:1, i_ * 128:(i_ + 1) * 128],
                bo_row[0:1, nb_ * 512:nb_ * 512 + 512], start=False, stop=True), "pt")
        W("v", xtok)
        need("v", ("stg", SD))
        st_ = v_op(lambda e, i_=i: e.tensor_scalar(
            STG[:, SD:SD + 1024], STG[:, SA:SA + 1024],
            stats_r[:, i_:i_ + 1], stats_mr[:, i_:i_ + 1],
            op0=ALU.mult, op1=ALU.add), "pv")
        done(("stg", SA), st_)
        W("v", tk)
        need("v", ("stg", SB_))
        vt = v_op(lambda e: e.tensor_add(STG[:, SB_:SB_ + 1024], PS_B[:, :],
                                         STG[:, SD:SD + 1024]), "pv")
        done("psB0", vt); done("psB1", vt)
        done(("stg", SD), vt)
        W("y", vt)
        dt_ = y_dma(x2_dram[i * 128:(i + 1) * 128, :], STG[:, SB_:SB_ + 1024])
        if "x2" in tap_req:
            dt_ = y_dma(tap_d["x2"][i * 128:(i + 1) * 128, :], STG[:, SB_:SB_ + 1024])
        rt = v_op(lambda e, c_=c8: e.reduce_sum(sm[:, c_:c_ + 1], STG[:, SB_:SB_ + 1024],
                                                axis=AX), "pv")
        W("v", rt)
        vt = v_op(lambda e, c_=c8: e.tensor_scalar_mul(sm[:, c_ + 1:c_ + 2],
                                                       sm[:, c_:c_ + 1], -1.0 / D), "pv")
        need("v", ("stg", SC_))
        st_ = v_op(lambda e, c_=c8: e.tensor_scalar(
            STG[:, SC_:SC_ + 1024], STG[:, SB_:SB_ + 1024],
            sm[:, c_ + 1:c_ + 2], None, op0=ALU.add), "pv")
        done(("stg", SB_), (("dy", dt_[1])))
        need("v", ("stg", SD))
        v_op(lambda e: e.tensor_mul(STG[:, SD:SD + 1024], STG[:, SC_:SC_ + 1024],
                                    STG[:, SC_:SC_ + 1024]))
        vt = v_op(lambda e, c_=c8: e.reduce_sum(sm[:, c_ + 2:c_ + 3],
                                                STG[:, SD:SD + 1024], axis=AX), "pv")
        done(("stg", SD), vt)
        W("s", vt)
        s_op(lambda e, c_=c8: e.activation(sm[:, c_ + 3:c_ + 4], sm[:, c_ + 2:c_ + 3],
                                           AF.Ln, bias=eps_c[:, 0:1], scale=1.0 / D))
        st2 = s_op(lambda e, c_=c8: e.activation(sm[:, c_ + 4:c_ + 5], sm[:, c_ + 3:c_ + 4],
                                           AF.Exp, bias=0.0, scale=-0.5), "ps")
        W("v", st2)
        st_ = v_op(lambda e, c_=c8: e.tensor_scalar(
            STG[:, SC_:SC_ + 1024], STG[:, SC_:SC_ + 1024],
            sm[:, c_ + 4:c_ + 5], None, op0=ALU.mult), "pv")
        W("t", st_)
        need("t", "psA0")
        for c in range(8):
            tk = t_op(lambda e, c_=c: e.transpose(
                PS_A[:, c_ * 128:(c_ + 1) * 128],
                STG[:, SC_ + c_ * 128:SC_ + (c_ + 1) * 128], idn[:, :]),
                "pt" if c == 7 else None)
        done(("stg", SC_), tk)
        W("v", tk)
        for c in range(8):
            v_op(lambda e, i_=i, c_=c: e.tensor_scalar(
                AR2[:, c_ * 2048 + i_ * 128: c_ * 2048 + (i_ + 1) * 128],
                PS_A[:, c_ * 128:(c_ + 1) * 128],
                ln2g[:, c_:c_ + 1], ln2b[:, c_:c_ + 1], op0=ALU.mult, op1=ALU.add))
        v_op(lambda e: e.tensor_copy(lghi[:, :], PS_A[:, 0:1024]))
        need("v", ("stg", SD))
        v_op(lambda e: e.tensor_copy(STG[:, SD:SD + 1024], lghi[:, :]))
        vt = v_op(lambda e: e.tensor_sub(lglo[:, :], PS_A[:, 0:1024],
                                         STG[:, SD:SD + 1024]), "pv")
        done("psA0", vt)
        done(("stg", SD), vt)
        W("t", vt); W("t", gp_tok)
        need("t", "psC0")
        np2 = 0
        for c in range(8):
            for lhs, rhs in ((lghi, gwhi), (lghi, gwlo), (lglo, gwhi)):
                np2 += 1
                tk = t_op(lambda e, c_=c, l_=lhs, r_=rhs, f_=(np2 == 1),
                          z_=(np2 == 24): e.matmul(
                    PS_C[:, 0:8], l_[:, c_ * 128:(c_ + 1) * 128],
                    r_[:, c_ * 8:(c_ + 1) * 8], start=f_, stop=z_),
                    "pt" if np2 == 24 else None)
        W("v", tk)
        need("v", ("stg", SE))
        v_op(lambda e: e.tensor_add(STG[:, SE:SE + 8], PS_C[:, 0:8], bprbc[:, :]))
        vt = v_op(lambda e, c_=c8: e.reduce_max(sm[:, c_ + 5:c_ + 6],
                                                STG[:, SE:SE + 8], axis=AX, negate=True), "pv")
        done("psC0", vt)
        W("v", vt)
        v_op(lambda e, c_=c8: e.tensor_scalar(sm[:, c_ + 6:c_ + 7], sm[:, c_ + 5:c_ + 6],
                                              -1.0, None, op0=ALU.mult))
        v_op(lambda e, c_=c8: e.tensor_scalar(
            STG[:, SE + 8:SE + 16], STG[:, SE:SE + 8],
            sm[:, c_ + 6:c_ + 7], None, op0=ALU.is_equal))
        v_op(lambda e, c_=c8: e.scalar_tensor_tensor(
            STG[:, SE + 16:SE + 24], STG[:, SE + 8:SE + 16], -1e30,
            STG[:, SE:SE + 8], op0=ALU.mult, op1=ALU.add))
        vt = v_op(lambda e, c_=c8: e.reduce_max(sm[:, c_ + 7:c_ + 8],
                                                STG[:, SE + 16:SE + 24], axis=AX), "pv")
        W("v", vt)
        v_op(lambda e, c_=c8: e.tensor_scalar(
            STG[:, SE + 24:SE + 32], STG[:, SE + 16:SE + 24],
            sm[:, c_ + 7:c_ + 8], None, op0=ALU.is_equal))
        W("s", kb.tok("pv"))
        st_ = s_op(lambda e, c_=c8: e.activation(
            sm[:, c_ + 2:c_ + 3], sm[:, c_ + 7:c_ + 8], AF.Exp,
            bias=sm[:, c_ + 5:c_ + 6], scale=1.0), "ps")
        W("v", st_)
        v_op(lambda e, c_=c8: e.tensor_scalar(sm[:, c_ + 3:c_ + 4], sm[:, c_ + 2:c_ + 3],
                                              1.0, None, op0=ALU.add))
        rt = v_op(lambda e, c_=c8: e.reciprocal(sm[:, c_ + 0:c_ + 1],
                                                sm[:, c_ + 3:c_ + 4]), "pv")
        W("v", rt)
        v_op(lambda e, c_=c8: e.tensor_mul(sm[:, c_ + 1:c_ + 2], sm[:, c_ + 2:c_ + 3],
                                           sm[:, c_ + 0:c_ + 1]))
        v_op(lambda e, c_=c8: e.tensor_scalar(
            STG[:, SE + 24:SE + 32], STG[:, SE + 24:SE + 32],
            sm[:, c_ + 1:c_ + 2], None, op0=ALU.mult))
        vt = v_op(lambda e, i_=i, c_=c8: e.scalar_tensor_tensor(
            ew_all[:, i_ * 8:(i_ + 1) * 8], STG[:, SE + 8:SE + 16],
            sm[:, c_ + 0:c_ + 1], STG[:, SE + 24:SE + 32],
            op0=ALU.mult, op1=ALU.add), "pv")
        done(("stg", SE), vt)
    h2_tok = kb.tok("pv")
    tap_fm("h2", AR2)
    if "ew" in tap_req:
        for i in range(NC):
            W("y", h2_tok)
            y_dma(tap_d["ew"][i * 128:(i + 1) * 128, :], ew_all[:, i * 8:(i + 1) * 8])
    if stages <= 5:
        return _finish(nc, kb, ctx)

    for i in range(NC):
        need("t", "psC0")
        W("t", h2_tok)
        tk = t_op(lambda e, i_=i: e.transpose(
            PS_C[0:8, 0:128], ew_all[:, i_ * 8:(i_ + 1) * 8], idn[:, :]), "pt")
        W("v", tk)
        vt = v_op(lambda e, i_=i: e.tensor_copy(
            ewCb[:, i_ * 128:(i_ + 1) * 128], PS_C[0:8, 0:128]), "pv")
        done("psC0", vt)
    ew_tok = kb.tok("pv")

    W("g", kb.tok("pt"))
    b2t = g_dma(b2bf, dr["b2"][:, :])
    for tile in range(2):
        first_acc = True
        for e_ in range(E):
            W("g", ew_tok)
            need("g", "ewrbf")
            gt = g_dma(ew_rbf[0:1, :], ewCb[e_:e_ + 1, tile * 1024:(tile + 1) * 1024])
            W("t", gt)
            need("t", "psC0", "psC1")
            t_op(lambda e: e.matmul(PS_C[:, 0:512], onesr[0:1, 0:128],
                                    ew_rbf[0:1, 0:512], start=True, stop=True))
            tk = t_op(lambda e: e.matmul(PS_C[:, 512:1024], onesr[0:1, 0:128],
                                         ew_rbf[0:1, 512:1024], start=True, stop=True), "pt")
            W("v", tk)
            need("v", "ewbc")
            vt = v_op(lambda e: e.tensor_copy(ew_bc[:, :], PS_C[:, 0:1024]), "pv")
            done("psC0", vt); done("psC1", vt)
            need("v", "h2s")
            for c in range(8):
                vt = v_op(lambda e, c_=c, t_=tile: e.tensor_mul(
                    h2s[:, c_ * 1024:(c_ + 1) * 1024],
                    AR2[:, c_ * 2048 + t_ * 1024: c_ * 2048 + t_ * 1024 + 1024],
                    ew_bc[:, :]), "pv" if c == 7 else None)
            h2s_tok = vt
            done("ewbc", vt)

            for grp in range(2):
                need("g", "b1row")
                b1t = g_dma(b1row[0:1, :],
                            dr["b1"][e_:e_ + 1, grp * 2048:(grp + 1) * 2048])
                for fi in range(16):
                    slab = fi // 4
                    if fi % 4 == 0:
                        sl1 = slab % 2
                        fbase = grp * 2048 + slab * 512
                        need("g", ("w1s", sl1))
                        w1t = g_dma(
                            w1buf[:, sl1 * 4096:sl1 * 4096 + 4096].rearrange(
                                "p (c f) -> p c f", c=8),
                            dr["W1"][e_, :, fbase:fbase + 512].rearrange(
                                "(c p) f -> p c f", p=128))
                    psk = "psB0" if fi % 2 == 0 else "psA1"
                    ps = PS_B if fi % 2 == 0 else PS_A
                    pso = 0 if fi % 2 == 0 else 1024
                    need("t", psk)
                    if fi % 2 == 0:
                        need("t", "psB1")
                    W("t", w1t)
                    W("t", h2s_tok)
                    W("t", b1t)
                    o1 = (slab % 2) * 4096
                    fr = fi % 4
                    for nb2 in range(2):
                        for c in range(8):
                            t_op(lambda e, c_=c, o_=o1, fr_=fr, nb_=nb2, p_=ps, po_=pso:
                                 e.matmul(
                                p_[:, po_ + nb_ * 512:po_ + nb_ * 512 + 512],
                                w1buf[:, o_ + c_ * 512 + fr_ * 128:
                                      o_ + c_ * 512 + fr_ * 128 + 128],
                                h2s[:, c_ * 1024 + nb_ * 512:c_ * 1024 + nb_ * 512 + 512],
                                start=(c_ == 0), stop=False))
                        tk = t_op(lambda e, fi_=fi, nb_=nb2, p_=ps, po_=pso:
                                  e.matmul(
                            p_[:, po_ + nb_ * 512:po_ + nb_ * 512 + 512],
                            b1row[0:1, fi_ * 128:(fi_ + 1) * 128],
                            ew_rbf[0:1, nb_ * 512:nb_ * 512 + 512],
                            start=False, stop=True), "pt")
                    if fi % 4 == 3:
                        done(("w1s", slab % 2), tk)
                    W("v", tk)
                    need("v", "hid")
                    vt = v_op(lambda e, fi_=fi, p_=ps, po_=pso: e.tensor_single_scalar(
                        AR1[:, fi_ * 1024:(fi_ + 1) * 1024],
                        p_[:, po_:po_ + 1024], 0.0, op=ALU.max), "pv")
                    done(psk, vt)
                    if fi % 2 == 0:
                        done("psB1", vt)
                hid_tok = kb.tok("pv")

                for dk in range(8):
                    sl2 = dk % 2
                    need("g", ("w2s", sl2))
                    fb = grp * 2048
                    w2t1 = g_dma(
                        w2buf[:, sl2 * 2048:sl2 * 2048 + 1024].rearrange(
                            "p (c d) -> p c d", c=8),
                        dr["W2"][e_, fb:fb + 1024, dk * 128:(dk + 1) * 128].rearrange(
                            "(c p) d -> p c d", p=128))
                    w2t = g_dma(
                        w2buf[:, sl2 * 2048 + 1024:sl2 * 2048 + 2048].rearrange(
                            "p (c d) -> p c d", c=8),
                        dr["W2"][e_, fb + 1024:fb + 2048, dk * 128:(dk + 1) * 128].rearrange(
                            "(c p) d -> p c d", p=128))
                    psk = "psA0" if dk % 2 == 0 else "psC0"
                    ps = PS_A if dk % 2 == 0 else PS_C
                    need("t", psk)
                    if dk % 2 == 1:
                        need("t", "psC1")
                    W("t", w2t)
                    W("t", hid_tok)
                    for fi in range(16):
                        for nb2 in range(2):
                            tk = t_op(lambda e, fi_=fi, nb_=nb2, o_=sl2 * 2048, p_=ps:
                                      e.matmul(
                                p_[:, nb_ * 512:nb_ * 512 + 512],
                                w2buf[:, o_ + fi_ * 128:o_ + (fi_ + 1) * 128],
                                AR1[:, fi_ * 1024 + nb_ * 512:fi_ * 1024 + nb_ * 512 + 512],
                                start=(fi_ == 0), stop=(fi_ == 15 and nb_ == 1)),
                                "pt" if (fi == 15 and nb2 == 1) else None)
                    done(("w2s", sl2), tk)
                    W("v", tk)
                    if first_acc:
                        need("v", "moeT")
                        vt = v_op(lambda e, dk_=dk, p_=ps: e.tensor_copy(
                            MOE[:, dk_ * 1024:(dk_ + 1) * 1024], p_[:, 0:1024]), "pv")
                    else:
                        vt = v_op(lambda e, dk_=dk, p_=ps: e.tensor_add(
                            MOE[:, dk_ * 1024:(dk_ + 1) * 1024], p_[:, 0:1024],
                            MOE[:, dk_ * 1024:(dk_ + 1) * 1024]), "pv")
                    done(psk, vt)
                    if dk % 2 == 1:
                        done("psC1", vt)
                done("hid", kb.tok("pt"))
                done("b1row", kb.tok("pt"))
                first_acc = False
            done("h2s", kb.tok("pt"))
            done("ewrbf", kb.tok("pt"))
        moe_tok = kb.tok("pv")

        for j in range(8):
            i = tile * 8 + j
            need("y", ("stg", SA), ("stg", SB_), ("stg", SC_))
            x2t = y_dma(STG[:, SA:SA + 1024], x2_dram[i * 128:(i + 1) * 128, :])
            stt_ = y_dma(STG[:, SB_:SB_ + 1024], s_dram[i * 128:(i + 1) * 128, :])
            xt = y_dma(STG[:, SC_:SC_ + 1024], dr["x"][i * 128:(i + 1) * 128, :])
            need("t", "psC0", "psC1")
            W("t", ew_tok)
            W("t", b2t)
            tk0 = t_op(lambda e, i_=i: e.matmul(
                PS_C[:, 0:512], ewCb[0:8, i_ * 128:(i_ + 1) * 128],
                b2bf[:, 0:512], start=True, stop=True))
            tk0 = t_op(lambda e, i_=i: e.matmul(
                PS_C[:, 512:1024], ewCb[0:8, i_ * 128:(i_ + 1) * 128],
                b2bf[:, 512:1024], start=True, stop=True), "pt")
            need("t", "psB0", "psB1")
            W("t", moe_tok)
            for dk in range(8):
                tk1 = t_op(lambda e, dk_=dk, j_=j: e.transpose(
                    PS_B[:, dk_ * 128:(dk_ + 1) * 128],
                    MOE[:, dk_ * 1024 + j_ * 128:dk_ * 1024 + (j_ + 1) * 128],
                    idn[:, :]), "pt" if dk == 7 else None)
            W("v", tk0); W("v", x2t)
            need("v", ("stg", SD))
            v_op(lambda e: e.tensor_add(STG[:, SD:SD + 1024], PS_C[:, 0:1024],
                                        STG[:, SA:SA + 1024]))
            W("v", tk1)
            vt = v_op(lambda e: e.tensor_add(STG[:, SD:SD + 1024], PS_B[:, 0:1024],
                                             STG[:, SD:SD + 1024]), "pv")
            done("moeT", kb.tok("pt"))
            done("psB0", vt); done("psB1", vt); done("psC0", vt); done("psC1", vt)
            done(("stg", SA), vt)
            if "gpre" in tap_req:
                W("y", vt)
                y_dma(tap_d["gpre"][i * 128:(i + 1) * 128, :], STG[:, SD:SD + 1024])
            W("s", vt)
            need("s", ("stg", SF))
            st2 = s_op(lambda e: e.activation(STG[:, SF:SF + 1024], STG[:, SD:SD + 1024],
                                              AF.Silu, bias=0.0, scale=1.0), "ps")
            done(("stg", SD), st2)
            W("v", st2); W("v", stt_); W("v", xt)
            v_op(lambda e: e.tensor_mul(STG[:, SF:SF + 1024], STG[:, SF:SF + 1024],
                                        STG[:, SB_:SB_ + 1024]))
            vt = v_op(lambda e: e.tensor_add(STG[:, SF:SF + 1024], STG[:, SF:SF + 1024],
                                             STG[:, SC_:SC_ + 1024]), "pv")
            done(("stg", SB_), vt)
            done(("stg", SC_), vt)
            W("y", vt)
            ot = y_dma(out_d[i * 128:(i + 1) * 128, :], STG[:, SF:SF + 1024], inc="od")
            done(("stg", SF), ot)

    return _finish(nc, kb, ctx)


def _finish(nc, kb, ctx):
    kb.wait("y", kb.tok("od"))
    kb.wait("y", kb.tok("dy"))
    kb.wait("g", kb.tok("dg"))
    with nc.Block() as block:
        @block.tensor
        def _(e):
            kb.replay("t", e)
        @block.vector
        def _(e):
            kb.replay("v", e)
        @block.scalar
        def _(e):
            kb.replay("s", e)
        @block.gpsimd
        def _(e):
            kb.replay("g", e)
        @block.sync
        def _(e):
            kb.replay("y", e)
    ctx.close()
    return nc


def _consts():
    inv = 1.0 / 10000.0 ** (np.arange(0, DR, 2, dtype=np.float32) / DR)
    tt = np.arange(T, dtype=np.float32)
    fr = np.outer(tt, inv)
    cos, sin = np.cos(fr).astype(np.float32), np.sin(fr).astype(np.float32)
    idn = np.eye(128, dtype=np.float32)
    masks = np.zeros((128, 4 * 512), dtype=ml_dtypes.bfloat16)
    for a, r in enumerate((0, 128, 256, 384)):
        ii = np.arange(128)[:, None] + r
        jj = np.arange(512)[None, :]
        masks[:, a * 512:(a + 1) * 512] = (ii <= jj).astype(ml_dtypes.bfloat16)
    return {"c_cos": cos, "c_sin": sin, "c_idn": idn, "c_maskT": masks,
            "c_ones": np.ones((1, T), dtype=ml_dtypes.bfloat16),
            "c_onescol": np.ones((128, 1), dtype=ml_dtypes.bfloat16)}


def _prep_shared(inputs):
    f = np.ascontiguousarray
    sh = {
        "w_kv_down": f(inputs["L0_attn_W_kv_down"]),
        "w_k_up": f(inputs["L0_attn_W_k_up"]),
        "w_v_up": f(inputs["L0_attn_W_v_up"]),
        "w_q": f(inputs["L0_attn_W_q"]),
        "w_q_rope": f(inputs["L0_attn_W_q_rope"]),
        "w_k_rope": f(inputs["L0_attn_W_k_rope"]),
        "w_o1": f(inputs["L0_attn_W_o"]),
        "ln1_g": f(inputs["L0_ffn_ln1_g"].reshape(8, 128).T),
        "ln1_b": f(inputs["L0_ffn_ln1_b"].reshape(8, 128).T),
        "w_qkv": f(inputs["L0_ffn_w_qkv"]),
        "b_qkv_q": f(inputs["L0_ffn_b_qkv"][0:D].reshape(16, 64).T),
        "b_qkv_k": f(inputs["L0_ffn_b_qkv"][D:2 * D].reshape(16, 64).T),
        "b_qkv_v": f(inputs["L0_ffn_b_qkv"][2 * D:3 * D].reshape(1, D)),
        "w_o2": f(inputs["L0_ffn_w_o"]),
        "b_o": f(inputs["L0_ffn_b_o"].reshape(1, D)),
        "ln2_g": f(inputs["L0_ffn_ln2_g"].reshape(8, 128).T),
        "ln2_b": f(inputs["L0_ffn_ln2_b"].reshape(8, 128).T),
        "gate_w": f(inputs["L0_ffn_gate_w"].reshape(8, 128, 8).transpose(1, 0, 2).reshape(128, 64)),
        "gate_b": f(inputs["L0_ffn_gate_b"].reshape(1, 8)),
        "W1": f(inputs["L0_ffn_W1"]),
        "b1": f(inputs["L0_ffn_b1"]),
        "W2": f(inputs["L0_ffn_W2"]),
        "b2": f(inputs["L0_ffn_b2"]),
    }
    sh.update(_consts())
    return sh


def kernel(**inputs):
    x = np.ascontiguousarray(np.asarray(inputs["L0_attn_x"], dtype=np.float32))
    shared = _prep_shared({k: np.asarray(v) for k, v in inputs.items()})
    nc = build_program(taps=TAPS, stages=STAGES)
    in_maps = []
    for c in range(8):
        m = dict(shared)
        m["x"] = x[c]
        in_maps.append(m)
    res = run_bass_kernel_spmd(nc, in_maps, core_ids=list(range(8)))
    out = np.stack([np.asarray(r["out"], dtype=np.float32) for r in res.results], axis=0)
    if TAPS:
        kernel.tap_results = res.results
    return out


# version 04
# speedup vs baseline: 1.2157x; 1.0545x over previous
import math
import numpy as np
import ml_dtypes
import concourse.bass as bass
import concourse.mybir as mybir
from concourse.bass_utils import run_bass_kernel_spmd

F32 = mybir.dt.float32
BF16 = mybir.dt.bfloat16
AX = mybir.AxisListType.X
ALU = mybir.AluOpType
AF = mybir.ActivationFunctionType

T, D, NC, DC = 2048, 1024, 16, 8
NH, HD, DR, DKV = 16, 64, 32, 128
E, DFF = 8, 4096
SC_MLA = 1.0 / math.sqrt(HD + DR)
SC_FFN = 1.0 / math.sqrt(HD)
EPS = 1e-5

TAPS = ()
STAGES = 99


class KB:
    def __init__(self, nc):
        self.nc = nc
        self.streams = {n: [] for n in ("t", "v", "s", "g", "y")}
        self.cnt = {}
        self.handles = {}
        self.lastw = {}
        self.free = {}

    def sem(self, name):
        self.handles[name] = self.nc.semaphore(name).__enter__()
        self.cnt[name] = 0

    def wait(self, st, tok):
        if tok is None:
            return
        sem, val = tok
        assert val <= self.cnt[sem], f"wait {sem}>={val} > emitted {self.cnt[sem]}"
        if self.lastw.get((st, sem), 0) < val:
            self.streams[st].append(("w", sem, val))
            self.lastw[(st, sem)] = val

    AUTO_SERIAL = {"v": "pv", "s": "ps"}

    def op(self, st, fn, inc=None, n=1):
        auto = self.AUTO_SERIAL.get(st)
        if auto is not None:
            self.wait(st, (auto, self.cnt[auto]))
            if inc is None:
                inc, n = auto, 1
        self.streams[st].append(("c", fn, inc, n))
        if inc is not None:
            self.cnt[inc] += n
            return (inc, self.cnt[inc])
        return None

    def tok(self, sem):
        return (sem, self.cnt[sem])

    def need(self, st, *keys):
        for k in keys:
            self.wait(st, self.free.get(k))

    def done(self, key, tok):
        self.free[key] = tok

    def barrier(self):
        for st in ("t", "v", "s", "g", "y"):
            for sem in ("pt", "pv", "ps", "dy", "dg"):
                self.wait(st, (sem, self.cnt[sem]))

    def replay(self, st, eng):
        for item in self.streams[st]:
            if item[0] == "w":
                eng.wait_ge(self.handles[item[1]], item[2])
            else:
                _, fn, inc, n = item
                ins = fn(eng)
                if inc is not None:
                    ins.then_inc(self.handles[inc], n)


def build_program(taps=(), stages=99):
    nc = bass.Bass("TRN2", target_bir_lowering=False)
    tap_req = set(taps)

    dr = {}
    def din(name, shape, dt=F32):
        dr[name] = nc.dram_tensor(name, list(shape), dt, kind="ExternalInput").ap()
    din("x", (T, D))
    din("w_kv_down", (D, DKV)); din("w_k_up", (DKV, D)); din("w_v_up", (DKV, D))
    din("w_q", (D, D)); din("w_q_rope", (D, DR)); din("w_k_rope", (DKV, DR))
    din("w_o1", (D, D))
    din("ln1_g", (128, 8)); din("ln1_b", (128, 8))
    din("w_qkv", (D, 3 * D))
    din("b_qkv_q", (64, 16)); din("b_qkv_k", (64, 16)); din("b_qkv_v", (1, D))
    din("w_o2", (D, D)); din("b_o", (1, D))
    din("ln2_g", (128, 8)); din("ln2_b", (128, 8))
    din("gate_w", (128, 64)); din("gate_b", (1, 8))
    din("W1", (E, D, DFF)); din("b1", (E, DFF))
    din("W2", (E, DFF, D)); din("b2", (E, D))
    din("c_idn", (128, 128))
    din("c_cos", (T, 16)); din("c_sin", (T, 16))
    din("c_maskT", (128, 4 * 512), BF16)
    din("c_ones", (1, T), BF16)
    din("c_onescol", (128, 1), BF16)
    out_d = nc.dram_tensor("out", [T, D], F32, kind="ExternalOutput").ap()
    tap_d = {}
    for tp in tap_req - {"min0", "noW", "serial"}:
        shp = {"xn": (D, T), "h": (D, T), "ckv": (DKV, T), "q": (D, T),
               "s": (T, D), "o": (D, T), "x2": (T, D), "h2": (D, T),
               "ew": (T, 8), "gpre": (T, D), "xnt": (T, D),
               "stm": (128, 16), "str": (128, 16)}[tp]
        tap_d[tp] = nc.dram_tensor("tap_" + tp, list(shp), F32, kind="ExternalOutput").ap()
    s_dram = nc.dram_tensor("s_scr", [T, D], F32, kind="Internal").ap()
    x2_dram = nc.dram_tensor("x2_scr", [T, D], F32, kind="Internal").ap()

    from contextlib import ExitStack
    ctx = ExitStack()
    def sb(name, shape, dt):
        return ctx.enter_context(nc.sbuf_tensor(name, list(shape), dt)).ap()

    AR1 = sb("AR1", (128, 16384), BF16)
    AR2 = sb("AR2", (128, 16384), BF16)
    AR3 = sb("AR3", (128, 16384), BF16)
    MOE = AR3.bitcast(F32)
    WAR = sb("WAR", (128, 20480), BF16)
    CKV = sb("CKV", (128, T), BF16)
    ROPE = sb("ROPE", (128, T), BF16)
    STG = sb("STG", (128, 6144), F32)
    q_st = sb("q_st", (128, T), BF16)
    k_st = sb("k_st", (128, T), BF16)
    v_st = sb("v_st", (128, NC * 65), BF16)
    sst = sb("sst", (128, 2 * 1024), BF16)
    ptb = sb("ptb", (128, 2 * 1024), BF16)
    m_all = sb("m_all", (128, 16), F32)
    mT = sb("mT", (16, 128), F32)
    stats_m = sb("stats_m", (128, 16), F32)
    stats_r = sb("stats_r", (128, 16), F32)
    stats_mr = sb("stats_mr", (128, 16), F32)
    sm = sb("sm", (128, 128), F32)
    eps_c = sb("eps_c", (128, 1), F32)
    idn = sb("idn", (128, 128), F32)
    cost = sb("cost", (128, NC * 16), F32)
    sint = sb("sint", (128, NC * 16), F32)
    maskT = sb("maskT", (128, 4 * 512), BF16)
    onesr = sb("onesr", (1, T), BF16)
    onesc = sb("onesc", (128, 1), BF16)
    ln1g = sb("ln1g", (128, 8), F32); ln1b = sb("ln1b", (128, 8), F32)
    ln2g = sb("ln2g", (128, 8), F32); ln2b = sb("ln2b", (128, 8), F32)
    bqq = sb("bqq", (64, 16), F32); bqk = sb("bqk", (64, 16), F32)
    bqv = sb("bqv", (1, D), BF16)
    bo_row = sb("bo_row", (1, D), BF16)
    gw32 = sb("gw32", (128, 64), F32)
    gwhi = sb("gwhi", (128, 64), BF16); gwlo = sb("gwlo", (128, 64), BF16)
    grhi = sb("grhi", (128, 64), BF16); grlo = sb("grlo", (128, 64), BF16)
    bchi = sb("bchi", (128, 8), BF16); bclo = sb("bclo", (128, 8), BF16)
    bpr = sb("bpr", (1, 16), F32)
    bprh = sb("bprh", (1, 8), BF16); bprl = sb("bprl", (1, 8), BF16)
    bprbc = sb("bprbc", (128, 8), F32)
    gateb = sb("gateb", (1, 8), F32)
    ew_all = sb("ew_all", (128, 128), F32)
    ewCb = ROPE[0:8, 0:2048]
    nstg = ROPE[0:64, 0:512]
    ew_rbf = CKV[0:1, 1024:2048]
    ew_bc = CKV[:, 0:1024]
    whb = AR3[:, 0:4096]
    w1buf = WAR[:, 0:8192]
    w2buf = WAR[:, 8192:12288]
    h2s = WAR[:, 12288:20480]
    sstb = None
    b2bf = q_st[0:8, 0:1024]
    invhb = sb("invhb", (1, 512), BF16)
    invlb = sb("invlb", (1, 512), BF16)
    invu2 = sb("invu2", (1, 512), F32)

    PS_A = ctx.enter_context(nc.psum_tensor("PS_A", [128, 2048], F32)).ap()
    PS_B = ctx.enter_context(nc.psum_tensor("PS_B", [128, 1024], F32)).ap()
    PS_C = ctx.enter_context(nc.psum_tensor("PS_C", [128, 1024], F32)).ap()

    kb = KB(nc)
    for s_ in ("pt", "pv", "ps", "dy", "dg", "od"):
        kb.sem(s_)
    W, need, done = kb.wait, kb.need, kb.done

    def t_op(fn, inc=None):
        return kb.op("t", fn, inc)
    def v_op(fn, inc=None):
        return kb.op("v", fn, inc)
    def s_op(fn, inc=None):
        return kb.op("s", fn, inc)
    def y_dma(dst, src, inc="dy"):
        return kb.op("y", lambda e, d=dst, s2=src: e.dma_start(out=d, in_=s2), inc, 16)
    def g_dma(dst, src, inc="dg"):
        return kb.op("g", lambda e, d=dst, s2=src: e.dma_start(out=d, in_=s2), inc, 16)

    SA, SB_, SC_, SD, SE, SF = 0, 1024, 2048, 3072, 4096, 5120
    inv_h = STG[0:1, SD + 512:SD + 1024]
    invbc = STG[0:64, SD:SD + 512]
    lghi = sst.bitcast(BF16)[:, 0:1024]
    lglo = sst.bitcast(BF16)[:, 1024:2048]
    b1row = sst.bitcast(BF16)[0:1, 0:2048]

    y_dma(idn[:, :], dr["c_idn"][:, :])
    y_dma(cost.rearrange("p (c f) -> p c f", c=16), dr["c_cos"].rearrange("(c p) f -> p c f", p=128))
    y_dma(sint.rearrange("p (c f) -> p c f", c=16), dr["c_sin"].rearrange("(c p) f -> p c f", p=128))
    y_dma(maskT[:, :], dr["c_maskT"][:, :])
    y_dma(onesr[:, :], dr["c_ones"][:, :])
    y_dma(onesc[:, :], dr["c_onescol"][:, :])
    y_dma(ln1g[:, :], dr["ln1_g"][:, :]); y_dma(ln1b[:, :], dr["ln1_b"][:, :])
    y_dma(ln2g[:, :], dr["ln2_g"][:, :]); y_dma(ln2b[:, :], dr["ln2_b"][:, :])
    y_dma(bqq[:, :], dr["b_qkv_q"][:, :]); y_dma(bqk[:, :], dr["b_qkv_k"][:, :])
    y_dma(gw32[:, :], dr["gate_w"][:, :])
    y_dma(gateb[:, :], dr["gate_b"][:, :])
    consts_tok = kb.tok("dy")

    O_WO2 = 0
    O_WKV, O_WQ, O_WQR, O_WKR = 0, 1024, 9216, 9472
    O_WKUP, O_WVUP, O_WO1 = 9504, 10528, 11552
    if "noW" not in tap_req:
        g_dma(WAR[:, O_WKV:O_WKV + 1024].rearrange("p (c m) -> p c m", c=8), dr["w_kv_down"].rearrange("(c p) m -> p c m", p=128))
        g_dma(WAR[:, O_WQ:O_WQ + 8192].rearrange("p (c m) -> p c m", c=8), dr["w_q"].rearrange("(c p) m -> p c m", p=128))
        g_dma(WAR[:, O_WQR:O_WQR + 256].rearrange("p (c m) -> p c m", c=8), dr["w_q_rope"].rearrange("(c p) m -> p c m", p=128))
        g_dma(WAR[:, O_WKR:O_WKR + 32], dr["w_k_rope"][:, :])
        g_dma(WAR[:, O_WKUP:O_WKUP + 1024], dr["w_k_up"][:, :])
        g_dma(WAR[:, O_WVUP:O_WVUP + 1024], dr["w_v_up"][:, :])
        g_dma(WAR[:, O_WO1:O_WO1 + 8192].rearrange("p (c m) -> p c m", c=8), dr["w_o1"].rearrange("(c p) m -> p c m", p=128))
        g_dma(bqv[:, :], dr["b_qkv_v"][:, :])
        g_dma(bo_row[:, :], dr["b_o"][:, :])
    w1w_tok = kb.tok("dg")

    W("v", consts_tok)
    v_op(lambda e: e.memset(eps_c[:, :], EPS))
    for kc in range(NC):
        vt = v_op(lambda e, kc_=kc: e.tensor_copy(
            v_st[:, kc_ * 65 + 64:kc_ * 65 + 65], onesc[:, 0:1]),
            "pv" if kc == NC - 1 else None)
    vst_ones_tok = vt

    W("t", consts_tok)
    for i in range(NC):
        c8 = i * 8
        need("y", ("stg", SA))
        xtok = y_dma(STG[:, SA:SA + 1024], dr["x"][i * 128:(i + 1) * 128, :])
        W("v", xtok)
        rt = v_op(lambda e, c_=c8: e.reduce_sum(sm[:, c_:c_ + 1], STG[:, SA:SA + 1024],
                                                axis=AX), "pv")
        W("v", rt)
        vt = v_op(lambda e, i_=i, c_=c8: e.tensor_scalar_mul(
            stats_m[:, i_:i_ + 1], sm[:, c_:c_ + 1], -1.0 / D), "pv")
        need("v", ("stg", SB_))
        st_ = v_op(lambda e, i_=i: e.tensor_scalar(
            STG[:, SB_:SB_ + 1024], STG[:, SA:SA + 1024],
            stats_m[:, i_:i_ + 1], None, op0=ALU.add), "pv")
        done(("stg", SA), st_)
        need("v", ("stg", SD))
        v_op(lambda e: e.tensor_mul(STG[:, SD:SD + 1024], STG[:, SB_:SB_ + 1024],
                                    STG[:, SB_:SB_ + 1024]))
        vt = v_op(lambda e, c_=c8: e.reduce_sum(sm[:, c_ + 1:c_ + 2],
                                                STG[:, SD:SD + 1024], axis=AX), "pv")
        W("s", vt)
        s_op(lambda e, c_=c8: e.activation(sm[:, c_ + 2:c_ + 3], sm[:, c_ + 1:c_ + 2],
                                           AF.Ln, bias=eps_c[:, 0:1], scale=1.0 / D))
        st_ = s_op(lambda e, i_=i, c_=c8: e.activation(
            stats_r[:, i_:i_ + 1], sm[:, c_ + 2:c_ + 3], AF.Exp, bias=0.0, scale=-0.5), "ps")
        W("v", st_)
        v_op(lambda e, i_=i: e.tensor_mul(stats_mr[:, i_:i_ + 1],
                                          stats_m[:, i_:i_ + 1], stats_r[:, i_:i_ + 1]))
        need("v", ("stg", SC_))
        st_ = v_op(lambda e, i_=i: e.tensor_scalar(
            STG[:, SC_:SC_ + 1024], STG[:, SB_:SB_ + 1024],
            stats_r[:, i_:i_ + 1], None, op0=ALU.mult), "pv")
        if "xnt" in tap_req:
            W("y", st_)
            xnt_t = y_dma(tap_d["xnt"][i * 128:(i + 1) * 128, :], STG[:, SC_:SC_ + 1024])
        rt = v_op(lambda e, c_=c8: e.reduce_sum(sm[:, c_ + 3:c_ + 4],
                                                STG[:, SC_:SC_ + 1024], axis=AX), "pv")
        W("v", rt)
        vt = v_op(lambda e, c_=c8: e.tensor_scalar_mul(
            sm[:, c_ + 4:c_ + 5], sm[:, c_ + 3:c_ + 4], -1.0 / D), "pv")
        st_ = v_op(lambda e, c_=c8: e.tensor_scalar(
            STG[:, SB_:SB_ + 1024], STG[:, SC_:SC_ + 1024],
            sm[:, c_ + 4:c_ + 5], None, op0=ALU.add), "pv")
        v_op(lambda e: e.tensor_mul(STG[:, SD:SD + 1024], STG[:, SB_:SB_ + 1024],
                                    STG[:, SB_:SB_ + 1024]))
        vt = v_op(lambda e, c_=c8: e.reduce_sum(sm[:, c_ + 5:c_ + 6],
                                                STG[:, SD:SD + 1024], axis=AX), "pv")
        done(("stg", SD), vt)
        W("s", vt)
        s_op(lambda e, c_=c8: e.activation(sm[:, c_ + 6:c_ + 7], sm[:, c_ + 5:c_ + 6],
                                           AF.Ln, bias=eps_c[:, 0:1], scale=1.0 / D))
        st_ = s_op(lambda e, c_=c8: e.activation(sm[:, c_ + 7:c_ + 8], sm[:, c_ + 6:c_ + 7],
                                           AF.Exp, bias=0.0, scale=-0.5), "ps")
        W("v", st_)
        st_ = v_op(lambda e, c_=c8: e.tensor_scalar(
            STG[:, SB_:SB_ + 1024], STG[:, SB_:SB_ + 1024],
            sm[:, c_ + 7:c_ + 8], None, op0=ALU.mult), "pv")
        W("t", st_)
        need("t", "psB0", "psB1", "psA0")
        for c in range(8):
            tk1 = t_op(lambda e, c_=c: e.transpose(
                PS_B[:, c_ * 128:(c_ + 1) * 128],
                STG[:, SC_ + c_ * 128:SC_ + (c_ + 1) * 128], idn[:, :]),
                "pt" if c == 7 else None)
        for c in range(8):
            tk2 = t_op(lambda e, c_=c: e.transpose(
                PS_A[:, c_ * 128:(c_ + 1) * 128],
                STG[:, SB_ + c_ * 128:SB_ + (c_ + 1) * 128], idn[:, :]),
                "pt" if c == 7 else None)
        W("v", tk1)
        vt = v_op(lambda e, i_=i: e.tensor_copy(
            AR1.rearrange("p (c t) -> p c t", c=8)[:, :, i_ * 128:(i_ + 1) * 128],
            PS_B.rearrange("p (c t) -> p c t", c=8)[:, :, :]), "pv")
        done("psB0", vt); done("psB1", vt)
        done(("stg", SC_), vt)
        if "min0" in tap_req:
            done("psA0", tk2)
            done(("stg", SB_), tk2)
            continue
        W("v", tk2)
        for c in range(8):
            vt = v_op(lambda e, i_=i, c_=c: e.tensor_scalar(
                AR2[:, c_ * 2048 + i_ * 128: c_ * 2048 + (i_ + 1) * 128],
                PS_A[:, c_ * 128:(c_ + 1) * 128],
                ln1g[:, c_:c_ + 1], ln1b[:, c_:c_ + 1], op0=ALU.mult, op1=ALU.add),
                "pv" if c == 7 else None)
        done("psA0", vt)
        done(("stg", SB_), vt)
        if "serial" in tap_req:
            kb.barrier()

    def tap_fm(name, src, rows=128):
        if name not in tap_req:
            return
        nchunks = {1024: 1}.get(src.shape[1] // 16, DC)
        total = src.shape[1]
        ncol = total // 1024
        for cj in range(ncol):
            need("v", ("stg", SA))
            vt = v_op(lambda e, c_=cj: e.tensor_copy(
                STG[0:rows, SA:SA + 1024], src[0:rows, c_ * 1024:(c_ + 1) * 1024]), "pv")
            W("y", vt)
            r0 = (cj // 2) * 128
            col = (cj % 2) * 1024
            if src.shape[1] == T:
                dt_ = y_dma(tap_d[name][0:rows, cj * 1024:(cj + 1) * 1024],
                            STG[0:rows, SA:SA + 1024])
            else:
                dt_ = y_dma(tap_d[name][r0:r0 + 128, col:col + 1024],
                            STG[0:rows, SA:SA + 1024])
            done(("stg", SA), dt_)

    if "stm" in tap_req:
        W("y", kb.tok("pv"))
        y_dma(tap_d["stm"][:, :], stats_m[:, :])
        y_dma(tap_d["str"][:, :], stats_r[:, :])
    tap_fm("xn", AR1)
    tap_fm("h", AR2)
    if stages <= 0:
        return _finish(nc, kb, ctx)

    W("t", w1w_tok)
    W("t", kb.tok("pv"))
    for nb in range(4):
        slot = "psC%d" % (nb % 2)
        need("t", slot)
        for c in range(8):
            tk = t_op(lambda e, c_=c, nb_=nb: e.matmul(
                PS_C[:, (nb_ % 2) * 512:(nb_ % 2) * 512 + 512],
                WAR[:, O_WKV + c_ * 128:O_WKV + (c_ + 1) * 128],
                AR1[:, c_ * 2048 + nb_ * 512: c_ * 2048 + nb_ * 512 + 512],
                start=(c_ == 0), stop=(c_ == 7)), "pt" if c == 7 else None)
        W("v", tk)
        vt = v_op(lambda e, nb_=nb: e.tensor_copy(
            CKV[:, nb_ * 512:nb_ * 512 + 512],
            PS_C[:, (nb_ % 2) * 512:(nb_ % 2) * 512 + 512]), "pv")
        done(slot, vt)
    tap_fm("ckv", CKV)

    for m in range(8):
        for nb in range(4):
            slot = "psC%d" % (nb % 2)
            need("t", slot)
            for c in range(8):
                tk = t_op(lambda e, c_=c, m_=m, nb_=nb: e.matmul(
                    PS_C[:, (nb_ % 2) * 512:(nb_ % 2) * 512 + 512],
                    WAR[:, O_WQ + c_ * 1024 + m_ * 128: O_WQ + c_ * 1024 + (m_ + 1) * 128],
                    AR1[:, c_ * 2048 + nb_ * 512: c_ * 2048 + nb_ * 512 + 512],
                    start=(c_ == 0), stop=(c_ == 7)), "pt" if c == 7 else None)
            W("v", tk)
            vt = v_op(lambda e, m_=m, nb_=nb: e.tensor_single_scalar(
                AR3[:, m_ * 2048 + nb_ * 512: m_ * 2048 + nb_ * 512 + 512],
                PS_C[:, (nb_ % 2) * 512:(nb_ % 2) * 512 + 512], SC_MLA, op=ALU.mult), "pv")
            done(slot, vt)
    tap_fm("q", AR3)

    QR, ROT, TA, TB = SE, SE + 32, SE + 64, SE + 80
    for i in range(NC):
        for which in range(2):
            need("t", "psC0")
            if which == 0:
                for c in range(8):
                    tk = t_op(lambda e, c_=c, i_=i: e.matmul(
                        PS_C[:, 0:32],
                        AR1[:, c_ * 2048 + i_ * 128: c_ * 2048 + (i_ + 1) * 128],
                        WAR[:, O_WQR + c_ * 32:O_WQR + (c_ + 1) * 32],
                        start=(c_ == 0), stop=(c_ == 7)), "pt" if c == 7 else None)
            else:
                tk = t_op(lambda e, i_=i: e.matmul(
                    PS_C[:, 0:32],
                    CKV[:, i_ * 128:(i_ + 1) * 128],
                    WAR[:, O_WKR:O_WKR + 32], start=True, stop=True), "pt")
            W("v", tk)
            need("v", ("stg", SE))
            vt = v_op(lambda e: e.tensor_copy(STG[:, QR:QR + 32], PS_C[:, 0:32]), "pv")
            done("psC0", vt)
            v_op(lambda e, i_=i: e.tensor_mul(STG[:, TA:TA + 16], STG[:, QR:QR + 16],
                                              cost[:, i_ * 16:(i_ + 1) * 16]))
            v_op(lambda e, i_=i: e.tensor_mul(STG[:, TB:TB + 16], STG[:, QR + 16:QR + 32],
                                              sint[:, i_ * 16:(i_ + 1) * 16]))
            v_op(lambda e: e.tensor_sub(STG[:, ROT:ROT + 16], STG[:, TA:TA + 16],
                                        STG[:, TB:TB + 16]))
            v_op(lambda e, i_=i: e.tensor_mul(STG[:, TA:TA + 16], STG[:, QR + 16:QR + 32],
                                              cost[:, i_ * 16:(i_ + 1) * 16]))
            v_op(lambda e, i_=i: e.tensor_mul(STG[:, TB:TB + 16], STG[:, QR:QR + 16],
                                              sint[:, i_ * 16:(i_ + 1) * 16]))
            vt = v_op(lambda e: e.tensor_add(STG[:, ROT + 16:ROT + 32], STG[:, TA:TA + 16],
                                             STG[:, TB:TB + 16]), "pv")
            W("t", vt)
            need("t", "psC1")
            tk = t_op(lambda e: e.transpose(PS_C[0:32, 512:640], STG[:, ROT:ROT + 32],
                                            idn[:, :]), "pt")
            W("v", tk)
            base = 0 if which == 0 else 64
            scale = SC_MLA if which == 0 else 1.0
            vt = v_op(lambda e, i_=i, b_=base, s2=scale: e.tensor_single_scalar(
                ROPE[b_:b_ + 32, i_ * 128:(i_ + 1) * 128],
                PS_C[0:32, 512:640], s2, op=ALU.mult), "pv")
            done("psC1", vt)
            done(("stg", SE), vt)
    rope_tok = kb.tok("pv")

    if stages <= 1:
        return _finish(nc, kb, ctx)

    def attention(kind):
        mla = kind == "mla"
        KQ = 96 if mla else 64
        KBr = 97 if mla else 65
        brow = 96 if mla else 64

        if mla:
            W("v", rope_tok)
            vt = v_op(lambda e: e.tensor_copy(k_st[64:96, :], ROPE[64:96, :]), "pv")
            W("g", rope_tok)
            rtok = g_dma(q_st[64:96, :], ROPE[0:32, :])
            done("nstg", rtok)
            W("y", vt)
            y_dma(k_st[96:97, :], onesr[:, :])
        else:
            W("y", kb.tok("pt"))
            y_dma(k_st[64:65, :], onesr[:, :])
            W("g", kb.tok("pt"))
        aug_tok = kb.tok("dy")

        for h in range(NH):
            pe_prev = kb.tok("pt")
            if mla:
                W("g", pe_prev)
                vt = g_dma(q_st[0:64, :], AR3[(h % 2) * 64:(h % 2) * 64 + 64,
                                              (h // 2) * 2048:(h // 2) * 2048 + 2048])
                W("t", vt)
                W("v", pe_prev)
                BS = [(PS_C, 0, "psC0"), (PS_C, 512, "psC1"), (PS_A, 0, "bA0"),
                      (PS_A, 512, "bA1"), (PS_A, 1024, "bA2"), (PS_A, 1536, "bA3")]
                bi = 0
                for nb in range(4):
                    ps_, off_, key = BS[bi % 6]; bi += 1
                    need("t", key)
                    if key in ("bA0", "bA1"):
                        need("t", "psA0")
                    elif key in ("bA2", "bA3"):
                        need("t", "psA1")
                    tk = t_op(lambda e, h_=h, nb_=nb, p_=ps_, o_=off_: e.matmul(
                        p_[0:64, o_:o_ + 512],
                        WAR[:, O_WKUP + h_ * 64:O_WKUP + (h_ + 1) * 64],
                        CKV[:, nb_ * 512:nb_ * 512 + 512], start=True, stop=True), "pt")
                    W("v", tk)
                    vk = v_op(lambda e, nb_=nb, p_=ps_, o_=off_: e.tensor_copy(
                        k_st[0:64, nb_ * 512:nb_ * 512 + 512],
                        p_[0:64, o_:o_ + 512]), "pv")
                    done(key, vk)
                for g4 in range(4):
                    ps_, off_, key = BS[bi % 6]; bi += 1
                    need("t", key)
                    if key in ("bA0", "bA1"):
                        need("t", "psA0")
                    elif key in ("bA2", "bA3"):
                        need("t", "psA1")
                    for z in range(4):
                        tk = t_op(lambda e, h_=h, kc_=g4 * 4 + z, z_=z, p_=ps_, o_=off_:
                                  e.matmul(
                            p_[:, o_ + z_ * 64:o_ + z_ * 64 + 64],
                            CKV[:, kc_ * 128:(kc_ + 1) * 128],
                            WAR[:, O_WVUP + h_ * 64:O_WVUP + (h_ + 1) * 64],
                            start=True, stop=True), "pt" if z == 3 else None)
                    W("v", tk)
                    W("v", vst_ones_tok)
                    vk = v_op(lambda e, g_=g4, p_=ps_, o_=off_: e.tensor_copy(
                        v_st.rearrange("p (k w) -> p k w", w=65)[:, g_ * 4:(g_ + 1) * 4, 0:64],
                        p_[:, o_:o_ + 256].rearrange("p (k w) -> p k w", w=64)), "pv")
                    done(key, vk)
            else:
                sl = h % 2
                off = sl * 1536
                need("g", ("whb", sl))
                g_dma(whb[:, off:off + 512].rearrange("p (c f) -> p c f", c=8),
                      dr["w_qkv"][:, h * 64:(h + 1) * 64].rearrange("(c p) f -> p c f", p=128))
                g_dma(whb[:, off + 512:off + 1024].rearrange("p (c f) -> p c f", c=8),
                      dr["w_qkv"][:, D + h * 64:D + (h + 1) * 64].rearrange("(c p) f -> p c f", p=128))
                gt = g_dma(whb[:, off + 1024:off + 1536].rearrange("p (c f) -> p c f", c=8),
                           dr["w_qkv"][:, 2 * D + h * 64:2 * D + (h + 1) * 64].rearrange("(c p) f -> p c f", p=128))
                W("t", gt)
                W("t", pe_prev)
                BS = [(PS_C, 0, "psC0"), (PS_C, 512, "psC1"), (PS_A, 0, "bA0"),
                      (PS_A, 512, "bA1"), (PS_A, 1024, "bA2"), (PS_A, 1536, "bA3")]
                def bneed(key):
                    need("t", key)
                    if key in ("bA0", "bA1"):
                        need("t", "psA0")
                    elif key in ("bA2", "bA3"):
                        need("t", "psA1")
                bi = 0
                for nb in range(4):
                    ps_, off_, key = BS[bi % 6]; bi += 1
                    bneed(key)
                    for c in range(8):
                        tk = t_op(lambda e, nb_=nb, c_=c, o_=off, p_=ps_, q2_=off_:
                                  e.matmul(
                            p_[0:64, q2_:q2_ + 512],
                            whb[:, o_ + c_ * 64:o_ + (c_ + 1) * 64],
                            AR2[:, c_ * 2048 + nb_ * 512: c_ * 2048 + nb_ * 512 + 512],
                            start=(c_ == 0), stop=(c_ == 7)), "pt" if c == 7 else None)
                    W("v", tk)
                    W("v", pe_prev)
                    vk = v_op(lambda e, h_=h, nb_=nb, p_=ps_, q2_=off_: e.tensor_scalar(
                        q_st[0:64, nb_ * 512:nb_ * 512 + 512],
                        p_[0:64, q2_:q2_ + 512],
                        bqq[:, h_:h_ + 1], SC_FFN, op0=ALU.add, op1=ALU.mult), "pv")
                    done(key, vk)
                for nb in range(4):
                    ps_, off_, key = BS[bi % 6]; bi += 1
                    bneed(key)
                    for c in range(8):
                        tk = t_op(lambda e, nb_=nb, c_=c, o_=off, p_=ps_, q2_=off_:
                                  e.matmul(
                            p_[0:64, q2_:q2_ + 512],
                            whb[:, o_ + 512 + c_ * 64:o_ + 512 + (c_ + 1) * 64],
                            AR2[:, c_ * 2048 + nb_ * 512: c_ * 2048 + nb_ * 512 + 512],
                            start=(c_ == 0), stop=(c_ == 7)), "pt" if c == 7 else None)
                    W("v", tk)
                    vk = v_op(lambda e, h_=h, nb_=nb, p_=ps_, q2_=off_: e.tensor_scalar(
                        k_st[0:64, nb_ * 512:nb_ * 512 + 512],
                        p_[0:64, q2_:q2_ + 512],
                        bqk[:, h_:h_ + 1], None, op0=ALU.add), "pv")
                    done(key, vk)
                for g4 in range(4):
                    ps_, off_, key = BS[bi % 6]; bi += 1
                    bneed(key)
                    for z in range(4):
                        kc = g4 * 4 + z
                        for c in range(8):
                            t_op(lambda e, kc_=kc, c_=c, o_=off, z_=z, p_=ps_, q2_=off_:
                                 e.matmul(
                                p_[:, q2_ + z_ * 64:q2_ + z_ * 64 + 64],
                                AR2[:, c_ * 2048 + kc_ * 128: c_ * 2048 + (kc_ + 1) * 128],
                                whb[:, o_ + 1024 + c_ * 64:o_ + 1024 + (c_ + 1) * 64],
                                start=(c_ == 0), stop=False))
                        tk = t_op(lambda e, h_=h, kc_=kc, z_=z, p_=ps_, q2_=off_:
                                  e.matmul(
                            p_[:, q2_ + z_ * 64:q2_ + z_ * 64 + 64],
                            onesr[0:1, kc_ * 128:(kc_ + 1) * 128],
                            bqv[0:1, h_ * 64:(h_ + 1) * 64], start=False, stop=True),
                            "pt" if z == 3 else None)
                    W("v", tk)
                    W("v", vst_ones_tok)
                    vk = v_op(lambda e, g_=g4, p_=ps_, q2_=off_: e.tensor_copy(
                        v_st.rearrange("p (k w) -> p k w", w=65)[:, g_ * 4:(g_ + 1) * 4, 0:64],
                        p_[:, q2_:q2_ + 256].rearrange("p (k w) -> p k w", w=64)), "pv")
                    done(key, vk)
                done(("whb", sl), kb.tok("pt"))
            qkv_tok = kb.tok("pv")

            W("t", qkv_tok)
            for qc in range(NC):
                nb = 4 if not mla else min(4, qc // 4 + 1)
                wid = nb * 256
                reg = qc % 2
                rk = "psA%d" % reg
                need("t", rk, *( ("bA0", "bA1") if reg == 0 else ("bA2", "bA3") ))
                for b in range(nb):
                    tk = t_op(lambda e, qc_=qc, b_=b, r_=reg, K_=KQ: e.matmul(
                        PS_A[:, r_ * 1024 + b_ * 256:r_ * 1024 + (b_ + 1) * 256],
                        q_st[0:K_, qc_ * 128:(qc_ + 1) * 128],
                        k_st.rearrange("p (t two) -> p t two", two=2)
                            [0:K_, b_ * 256:(b_ + 1) * 256, 0],
                        start=True, stop=True), "pt" if b == nb - 1 else None)
                W("v", tk)
                vt = v_op(lambda e, qc_=qc, r_=reg, w_=wid: e.reduce_max(
                    m_all[:, qc_:qc_ + 1], PS_A[:, r_ * 1024:r_ * 1024 + w_],
                    axis=AX, negate=True), "pv")
                done(rk, vt)
            W("t", kb.tok("pv"))
            need("t", "psC0")
            tk = t_op(lambda e: e.transpose(PS_C[0:16, 0:128], m_all[:, :], idn[:, :]), "pt")
            W("v", tk)
            vt = v_op(lambda e: e.tensor_copy(mT[:, :], PS_C[0:16, 0:128]), "pv")
            done("psC0", vt)
            W("g", vt)
            mrow_tok = g_dma(q_st[brow:brow + 1, :].rearrange("o (c q) -> o c q", c=16),
                             mT[:, :])

            for qb in range(4):
                kcs = list(range((qb + 1) * 4)) if mla else list(range(16))
                pvk = "psB%d" % (qb % 2)
                need("t", pvk)
                pairs = [kcs[j:j + 2] for j in range(0, len(kcs), 2)]
                for pj, pair in enumerate(pairs):
                    ssl = pj % 2
                    need("t", "psA%d" % ssl,
                         *( ("bA0", "bA1") if ssl == 0 else ("bA2", "bA3") ))
                    if pj == 0:
                        W("t", mrow_tok)
                        W("t", aug_tok)
                    for z, kc in enumerate(pair):
                        tk = t_op(lambda e, kc_=kc, qb_=qb, ssl_=ssl, z_=z, K_=KBr:
                                  e.matmul(
                            PS_A[:, ssl_ * 1024 + z_ * 512:ssl_ * 1024 + z_ * 512 + 512],
                            k_st[0:K_, kc_ * 128:(kc_ + 1) * 128],
                            q_st[0:K_, qb_ * 512:qb_ * 512 + 512],
                            start=True, stop=True), "pt" if z == len(pair) - 1 else None)
                    W("v", tk)
                    wid = len(pair) * 512
                    need("v", ("sst", ssl))
                    vt = v_op(lambda e, ssl_=ssl, w_=wid: e.tensor_copy(
                        sst[:, ssl_ * 1024:ssl_ * 1024 + w_],
                        PS_A[:, ssl_ * 1024:ssl_ * 1024 + w_]), "pv")
                    done("psA%d" % ssl, vt)
                    W("s", vt)
                    need("s", ("ptb", ssl))
                    st_ = s_op(lambda e, ssl_=ssl, w_=wid: e.activation(
                        ptb[:, ssl_ * 1024:ssl_ * 1024 + w_],
                        sst[:, ssl_ * 1024:ssl_ * 1024 + w_], AF.Exp,
                        bias=0.0, scale=1.0), "ps")
                    done(("sst", ssl), st_)
                    gtk = st_
                    if mla:
                        for z, kc in enumerate(pair):
                            if kc >= qb * 4:
                                a = kc - qb * 4
                                W("v", st_)
                                vt = v_op(lambda e, ssl_=ssl, z_=z, a_=a: e.tensor_mul(
                                    ptb[:, ssl_ * 1024 + z_ * 512:ssl_ * 1024 + z_ * 512 + 512],
                                    ptb[:, ssl_ * 1024 + z_ * 512:ssl_ * 1024 + z_ * 512 + 512],
                                    maskT[:, a_ * 512:(a_ + 1) * 512]), "pv")
                                gtk = vt
                    W("t", gtk)
                    for z, kc in enumerate(pair):
                        tk = t_op(lambda e, kc_=kc, q_=qb, ssl_=ssl, z_=z,
                                  f_=(pj == 0 and z == 0),
                                  l_=(pj == len(pairs) - 1 and z == len(pair) - 1):
                                  e.matmul(
                            PS_B[0:65, (q_ % 2) * 512:(q_ % 2) * 512 + 512],
                            v_st[:, kc_ * 65:(kc_ + 1) * 65],
                            ptb[:, ssl_ * 1024 + z_ * 512:ssl_ * 1024 + z_ * 512 + 512],
                            start=f_, stop=l_), "pt" if z == len(pair) - 1 else None)
                    done(("ptb", ssl), tk)
                pv_done = tk
                W("v", pv_done)
                need("v", "invb", ("stg", SD))
                vt = v_op(lambda e, q_=qb: e.tensor_copy(
                    inv_h[:, :], PS_B[64:65, (q_ % 2) * 512:(q_ % 2) * 512 + 512]), "pv")
                W("s", vt)
                s_op(lambda e: e.activation(invu2[:, :], inv_h[:, :], AF.Ln,
                                            bias=0.0, scale=1.0))
                st_ = s_op(lambda e: e.activation(inv_h[:, :], invu2[:, :], AF.Exp,
                                                  bias=0.0, scale=-1.0), "ps")
                W("v", st_)
                v_op(lambda e: e.tensor_copy(invhb[0:1, :], inv_h[:, :]))
                v_op(lambda e: e.tensor_copy(invu2[:, :], invhb[0:1, :]))
                vt = v_op(lambda e: e.tensor_sub(invlb[0:1, :], inv_h[:, :],
                                                 invu2[:, :]), "pv")
                W("t", vt)
                need("t", "psC1")
                t_op(lambda e: e.matmul(PS_C[0:64, 512:1024], onesr[0:1, 0:64],
                                        invhb[0:1, :], start=True, stop=False))
                tk = t_op(lambda e: e.matmul(PS_C[0:64, 512:1024], onesr[0:1, 0:64],
                                             invlb[0:1, :], start=False, stop=True), "pt")
                W("v", tk)
                vt = v_op(lambda e: e.tensor_copy(invbc[:, :], PS_C[0:64, 512:1024]), "pv")
                done("psC1", vt)
                if h % 2 == 0:
                    vt2 = v_op(lambda e, h_=h, q_=qb: e.tensor_mul(
                        AR1[0:64,
                            (h_ // 2) * 2048 + q_ * 512:(h_ // 2) * 2048 + q_ * 512 + 512],
                        PS_B[0:64, (q_ % 2) * 512:(q_ % 2) * 512 + 512], invbc[:, :]), "pv")
                    done(pvk, vt2)
                else:
                    need("v", "nstg")
                    vt2 = v_op(lambda e, q_=qb: e.tensor_mul(
                        nstg[:, :],
                        PS_B[0:64, (q_ % 2) * 512:(q_ % 2) * 512 + 512], invbc[:, :]), "pv")
                    done(pvk, vt2)
                    W("g", vt2)
                    gt2 = g_dma(AR1[64:128,
                                    (h // 2) * 2048 + qb * 512:(h // 2) * 2048 + qb * 512 + 512],
                                nstg[:, :])
                    done("nstg", gt2)
                done("invb", vt2)
                done(("stg", SD), vt2)

    attention("mla")
    if stages <= 2:
        return _finish(nc, kb, ctx)

    mla_tok = kb.tok("pv")
    W("t", kb.tok("dg"))
    for i in range(NC):
        need("t", "psB0", "psB1")
        W("t", mla_tok)
        for nb in range(2):
            for c in range(8):
                tk = t_op(lambda e, i_=i, nb_=nb, c_=c: e.matmul(
                    PS_B[:, nb_ * 512:nb_ * 512 + 512],
                    AR1[:, c_ * 2048 + i_ * 128: c_ * 2048 + (i_ + 1) * 128],
                    WAR[:, O_WO1 + c_ * 1024 + nb_ * 512: O_WO1 + c_ * 1024 + nb_ * 512 + 512],
                    start=(c_ == 0), stop=(c_ == 7)), "pt" if c == 7 else None)
        W("v", tk)
        need("v", ("stg", SF))
        vt = v_op(lambda e: e.tensor_copy(STG[:, SF:SF + 1024], PS_B[:, :]), "pv")
        done("psB0", vt); done("psB1", vt)
        W("y", vt)
        dt_ = y_dma(s_dram[i * 128:(i + 1) * 128, :], STG[:, SF:SF + 1024])
        if "s" in tap_req:
            dt_ = y_dma(tap_d["s"][i * 128:(i + 1) * 128, :], STG[:, SF:SF + 1024])
        done(("stg", SF), dt_)
    s_mm_tok = kb.tok("pt")

    if stages <= 3:
        return _finish(nc, kb, ctx)

    W("g", s_mm_tok)
    wo2_tok = g_dma(WAR[:, O_WO2:O_WO2 + 8192].rearrange("p (c m) -> p c m", c=8),
                    dr["w_o2"].rearrange("(c p) m -> p c m", p=128))

    W("v", s_mm_tok)
    attention("ffn")
    ffn_tok = kb.tok("pv")
    tap_fm("o", AR1)
    if stages <= 4:
        return _finish(nc, kb, ctx)

    W("v", consts_tok)
    need("v", ("stg", SE))
    for c in range(8):
        v_op(lambda e, c_=c: e.tensor_scalar(
            STG[:, SE + c_ * 8:SE + (c_ + 1) * 8], gw32[:, c_ * 8:(c_ + 1) * 8],
            ln2g[:, c_:c_ + 1], None, op0=ALU.mult))
    v_op(lambda e: e.tensor_copy(gwhi[:, :], STG[:, SE:SE + 64]))
    v_op(lambda e: e.tensor_copy(STG[:, SE + 64:SE + 128], gwhi[:, :]))
    v_op(lambda e: e.tensor_sub(gwlo[:, :], STG[:, SE:SE + 64], STG[:, SE + 64:SE + 128]))
    v_op(lambda e: e.tensor_copy(grhi[:, :], gw32[:, :]))
    v_op(lambda e: e.tensor_copy(STG[:, SE + 64:SE + 128], grhi[:, :]))
    v_op(lambda e: e.tensor_sub(grlo[:, :], gw32[:, :], STG[:, SE + 64:SE + 128]))
    v_op(lambda e: e.tensor_copy(bchi[:, :], ln2b[:, :]))
    v_op(lambda e: e.tensor_copy(STG[:, SE + 64:SE + 72], bchi[:, :]))
    vt = v_op(lambda e: e.tensor_sub(bclo[:, :], ln2b[:, :], STG[:, SE + 64:SE + 72]), "pv")
    done(("stg", SE), vt)
    W("t", vt)
    need("t", "psC0")
    np_ = 0
    for c in range(8):
        for bb, gg in ((bchi, grhi), (bchi, grlo), (bclo, grhi)):
            np_ += 1
            tk = t_op(lambda e, c_=c, b_=bb, g_=gg, f_=(np_ == 1), l_=(np_ == 24): e.matmul(
                PS_C[0:1, 0:8], b_[:, c_:c_ + 1], g_[:, c_ * 8:(c_ + 1) * 8],
                start=f_, stop=l_), "pt" if np_ == 24 else None)
    W("v", tk)
    v_op(lambda e: e.tensor_copy(bpr[0:1, 0:8], PS_C[0:1, 0:8]))
    v_op(lambda e: e.tensor_add(bpr[0:1, 0:8], bpr[0:1, 0:8], gateb[0:1, :]))
    v_op(lambda e: e.tensor_copy(bprh[0:1, :], bpr[0:1, 0:8]))
    v_op(lambda e: e.tensor_copy(bpr[0:1, 8:16], bprh[0:1, :]))
    vt = v_op(lambda e: e.tensor_sub(bprl[0:1, :], bpr[0:1, 0:8], bpr[0:1, 8:16]), "pv")
    done("psC0", vt)
    W("t", vt)
    need("t", "psC0")
    t_op(lambda e: e.matmul(PS_C[:, 0:8], onesr[0:1, 0:128], bprh[0:1, :],
                            start=True, stop=False))
    tk = t_op(lambda e: e.matmul(PS_C[:, 0:8], onesr[0:1, 0:128], bprl[0:1, :],
                                 start=False, stop=True), "pt")
    W("v", tk)
    gp_tok = v_op(lambda e: e.tensor_copy(bprbc[:, :], PS_C[:, 0:8]), "pv")
    done("psC0", gp_tok)

    W("t", kb.tok("dg"))
    for i in range(NC):
        c8 = i * 8
        need("y", ("stg", SA))
        xtok = y_dma(STG[:, SA:SA + 1024], dr["x"][i * 128:(i + 1) * 128, :])
        need("t", "psB0", "psB1")
        W("t", ffn_tok)
        W("t", wo2_tok)
        for nb in range(2):
            for c in range(8):
                t_op(lambda e, i_=i, nb_=nb, c_=c: e.matmul(
                    PS_B[:, nb_ * 512:nb_ * 512 + 512],
                    AR1[:, c_ * 2048 + i_ * 128: c_ * 2048 + (i_ + 1) * 128],
                    WAR[:, O_WO2 + c_ * 1024 + nb_ * 512: O_WO2 + c_ * 1024 + nb_ * 512 + 512],
                    start=(c_ == 0), stop=False))
            tk = t_op(lambda e, i_=i, nb_=nb: e.matmul(
                PS_B[:, nb_ * 512:nb_ * 512 + 512],
                onesr[0:1, i_ * 128:(i_ + 1) * 128],
                bo_row[0:1, nb_ * 512:nb_ * 512 + 512], start=False, stop=True), "pt")
        W("v", xtok)
        need("v", ("stg", SD))
        st_ = v_op(lambda e, i_=i: e.tensor_scalar(
            STG[:, SD:SD + 1024], STG[:, SA:SA + 1024],
            stats_r[:, i_:i_ + 1], stats_mr[:, i_:i_ + 1],
            op0=ALU.mult, op1=ALU.add), "pv")
        done(("stg", SA), st_)
        W("v", tk)
        need("v", ("stg", SB_))
        vt = v_op(lambda e: e.tensor_add(STG[:, SB_:SB_ + 1024], PS_B[:, :],
                                         STG[:, SD:SD + 1024]), "pv")
        done("psB0", vt); done("psB1", vt)
        done(("stg", SD), vt)
        W("y", vt)
        dt_ = y_dma(x2_dram[i * 128:(i + 1) * 128, :], STG[:, SB_:SB_ + 1024])
        if "x2" in tap_req:
            dt_ = y_dma(tap_d["x2"][i * 128:(i + 1) * 128, :], STG[:, SB_:SB_ + 1024])
        rt = v_op(lambda e, c_=c8: e.reduce_sum(sm[:, c_:c_ + 1], STG[:, SB_:SB_ + 1024],
                                                axis=AX), "pv")
        W("v", rt)
        vt = v_op(lambda e, c_=c8: e.tensor_scalar_mul(sm[:, c_ + 1:c_ + 2],
                                                       sm[:, c_:c_ + 1], -1.0 / D), "pv")
        need("v", ("stg", SC_))
        st_ = v_op(lambda e, c_=c8: e.tensor_scalar(
            STG[:, SC_:SC_ + 1024], STG[:, SB_:SB_ + 1024],
            sm[:, c_ + 1:c_ + 2], None, op0=ALU.add), "pv")
        done(("stg", SB_), (("dy", dt_[1])))
        need("v", ("stg", SD))
        v_op(lambda e: e.tensor_mul(STG[:, SD:SD + 1024], STG[:, SC_:SC_ + 1024],
                                    STG[:, SC_:SC_ + 1024]))
        vt = v_op(lambda e, c_=c8: e.reduce_sum(sm[:, c_ + 2:c_ + 3],
                                                STG[:, SD:SD + 1024], axis=AX), "pv")
        done(("stg", SD), vt)
        W("s", vt)
        s_op(lambda e, c_=c8: e.activation(sm[:, c_ + 3:c_ + 4], sm[:, c_ + 2:c_ + 3],
                                           AF.Ln, bias=eps_c[:, 0:1], scale=1.0 / D))
        st2 = s_op(lambda e, c_=c8: e.activation(sm[:, c_ + 4:c_ + 5], sm[:, c_ + 3:c_ + 4],
                                           AF.Exp, bias=0.0, scale=-0.5), "ps")
        W("v", st2)
        st_ = v_op(lambda e, c_=c8: e.tensor_scalar(
            STG[:, SC_:SC_ + 1024], STG[:, SC_:SC_ + 1024],
            sm[:, c_ + 4:c_ + 5], None, op0=ALU.mult), "pv")
        W("t", st_)
        need("t", "psA0")
        for c in range(8):
            tk = t_op(lambda e, c_=c: e.transpose(
                PS_A[:, c_ * 128:(c_ + 1) * 128],
                STG[:, SC_ + c_ * 128:SC_ + (c_ + 1) * 128], idn[:, :]),
                "pt" if c == 7 else None)
        done(("stg", SC_), tk)
        W("v", tk)
        for c in range(8):
            v_op(lambda e, i_=i, c_=c: e.tensor_scalar(
                AR2[:, c_ * 2048 + i_ * 128: c_ * 2048 + (i_ + 1) * 128],
                PS_A[:, c_ * 128:(c_ + 1) * 128],
                ln2g[:, c_:c_ + 1], ln2b[:, c_:c_ + 1], op0=ALU.mult, op1=ALU.add))
        v_op(lambda e: e.tensor_copy(lghi[:, :], PS_A[:, 0:1024]))
        need("v", ("stg", SD))
        v_op(lambda e: e.tensor_copy(STG[:, SD:SD + 1024], lghi[:, :]))
        vt = v_op(lambda e: e.tensor_sub(lglo[:, :], PS_A[:, 0:1024],
                                         STG[:, SD:SD + 1024]), "pv")
        done("psA0", vt)
        done(("stg", SD), vt)
        W("t", vt); W("t", gp_tok)
        need("t", "psC0")
        np2 = 0
        for c in range(8):
            for lhs, rhs in ((lghi, gwhi), (lghi, gwlo), (lglo, gwhi)):
                np2 += 1
                tk = t_op(lambda e, c_=c, l_=lhs, r_=rhs, f_=(np2 == 1),
                          z_=(np2 == 24): e.matmul(
                    PS_C[:, 0:8], l_[:, c_ * 128:(c_ + 1) * 128],
                    r_[:, c_ * 8:(c_ + 1) * 8], start=f_, stop=z_),
                    "pt" if np2 == 24 else None)
        W("v", tk)
        need("v", ("stg", SE))
        v_op(lambda e: e.tensor_add(STG[:, SE:SE + 8], PS_C[:, 0:8], bprbc[:, :]))
        vt = v_op(lambda e, c_=c8: e.reduce_max(sm[:, c_ + 5:c_ + 6],
                                                STG[:, SE:SE + 8], axis=AX, negate=True), "pv")
        done("psC0", vt)
        W("v", vt)
        v_op(lambda e, c_=c8: e.tensor_scalar(sm[:, c_ + 6:c_ + 7], sm[:, c_ + 5:c_ + 6],
                                              -1.0, None, op0=ALU.mult))
        v_op(lambda e, c_=c8: e.tensor_scalar(
            STG[:, SE + 8:SE + 16], STG[:, SE:SE + 8],
            sm[:, c_ + 6:c_ + 7], None, op0=ALU.is_equal))
        v_op(lambda e, c_=c8: e.scalar_tensor_tensor(
            STG[:, SE + 16:SE + 24], STG[:, SE + 8:SE + 16], -1e30,
            STG[:, SE:SE + 8], op0=ALU.mult, op1=ALU.add))
        vt = v_op(lambda e, c_=c8: e.reduce_max(sm[:, c_ + 7:c_ + 8],
                                                STG[:, SE + 16:SE + 24], axis=AX), "pv")
        W("v", vt)
        v_op(lambda e, c_=c8: e.tensor_scalar(
            STG[:, SE + 24:SE + 32], STG[:, SE + 16:SE + 24],
            sm[:, c_ + 7:c_ + 8], None, op0=ALU.is_equal))
        W("s", kb.tok("pv"))
        st_ = s_op(lambda e, c_=c8: e.activation(
            sm[:, c_ + 2:c_ + 3], sm[:, c_ + 7:c_ + 8], AF.Exp,
            bias=sm[:, c_ + 5:c_ + 6], scale=1.0), "ps")
        W("v", st_)
        v_op(lambda e, c_=c8: e.tensor_scalar(sm[:, c_ + 3:c_ + 4], sm[:, c_ + 2:c_ + 3],
                                              1.0, None, op0=ALU.add))
        rt = v_op(lambda e, c_=c8: e.reciprocal(sm[:, c_ + 0:c_ + 1],
                                                sm[:, c_ + 3:c_ + 4]), "pv")
        W("v", rt)
        v_op(lambda e, c_=c8: e.tensor_mul(sm[:, c_ + 1:c_ + 2], sm[:, c_ + 2:c_ + 3],
                                           sm[:, c_ + 0:c_ + 1]))
        v_op(lambda e, c_=c8: e.tensor_scalar(
            STG[:, SE + 24:SE + 32], STG[:, SE + 24:SE + 32],
            sm[:, c_ + 1:c_ + 2], None, op0=ALU.mult))
        vt = v_op(lambda e, i_=i, c_=c8: e.scalar_tensor_tensor(
            ew_all[:, i_ * 8:(i_ + 1) * 8], STG[:, SE + 8:SE + 16],
            sm[:, c_ + 0:c_ + 1], STG[:, SE + 24:SE + 32],
            op0=ALU.mult, op1=ALU.add), "pv")
        done(("stg", SE), vt)
    h2_tok = kb.tok("pv")
    tap_fm("h2", AR2)
    if "ew" in tap_req:
        for i in range(NC):
            W("y", h2_tok)
            y_dma(tap_d["ew"][i * 128:(i + 1) * 128, :], ew_all[:, i * 8:(i + 1) * 8])
    if stages <= 5:
        return _finish(nc, kb, ctx)

    for i in range(NC):
        need("t", "psC0")
        W("t", h2_tok)
        tk = t_op(lambda e, i_=i: e.transpose(
            PS_C[0:8, 0:128], ew_all[:, i_ * 8:(i_ + 1) * 8], idn[:, :]), "pt")
        W("v", tk)
        vt = v_op(lambda e, i_=i: e.tensor_copy(
            ewCb[:, i_ * 128:(i_ + 1) * 128], PS_C[0:8, 0:128]), "pv")
        done("psC0", vt)
    ew_tok = kb.tok("pv")

    W("g", kb.tok("pt"))
    b2t = g_dma(b2bf, dr["b2"][:, :])
    for tile in range(2):
        first_acc = True
        for e_ in range(E):
            W("g", ew_tok)
            need("g", "ewrbf")
            gt = g_dma(ew_rbf[0:1, :], ewCb[e_:e_ + 1, tile * 1024:(tile + 1) * 1024])
            W("t", gt)
            need("t", "psC0", "psC1")
            t_op(lambda e: e.matmul(PS_C[:, 0:512], onesr[0:1, 0:128],
                                    ew_rbf[0:1, 0:512], start=True, stop=True))
            tk = t_op(lambda e: e.matmul(PS_C[:, 512:1024], onesr[0:1, 0:128],
                                         ew_rbf[0:1, 512:1024], start=True, stop=True), "pt")
            W("v", tk)
            need("v", "ewbc")
            vt = v_op(lambda e: e.tensor_copy(ew_bc[:, :], PS_C[:, 0:1024]), "pv")
            done("psC0", vt); done("psC1", vt)
            need("v", "h2s")
            for c in range(8):
                vt = v_op(lambda e, c_=c, t_=tile: e.tensor_mul(
                    h2s[:, c_ * 1024:(c_ + 1) * 1024],
                    AR2[:, c_ * 2048 + t_ * 1024: c_ * 2048 + t_ * 1024 + 1024],
                    ew_bc[:, :]), "pv" if c == 7 else None)
            h2s_tok = vt
            done("ewbc", vt)

            for grp in range(2):
                need("g", "b1row")
                b1t = g_dma(b1row[0:1, :],
                            dr["b1"][e_:e_ + 1, grp * 2048:(grp + 1) * 2048])
                for fi in range(16):
                    slab = fi // 4
                    if fi % 4 == 0:
                        sl1 = slab % 2
                        fbase = grp * 2048 + slab * 512
                        need("g", ("w1s", sl1))
                        w1t = g_dma(
                            w1buf[:, sl1 * 4096:sl1 * 4096 + 4096].rearrange(
                                "p (c f) -> p c f", c=8),
                            dr["W1"][e_, :, fbase:fbase + 512].rearrange(
                                "(c p) f -> p c f", p=128))
                    psk = "psB0" if fi % 2 == 0 else "psA1"
                    ps = PS_B if fi % 2 == 0 else PS_A
                    pso = 0 if fi % 2 == 0 else 1024
                    need("t", psk)
                    if fi % 2 == 0:
                        need("t", "psB1")
                    W("t", w1t)
                    W("t", h2s_tok)
                    W("t", b1t)
                    o1 = (slab % 2) * 4096
                    fr = fi % 4
                    for nb2 in range(2):
                        for c in range(8):
                            t_op(lambda e, c_=c, o_=o1, fr_=fr, nb_=nb2, p_=ps, po_=pso:
                                 e.matmul(
                                p_[:, po_ + nb_ * 512:po_ + nb_ * 512 + 512],
                                w1buf[:, o_ + c_ * 512 + fr_ * 128:
                                      o_ + c_ * 512 + fr_ * 128 + 128],
                                h2s[:, c_ * 1024 + nb_ * 512:c_ * 1024 + nb_ * 512 + 512],
                                start=(c_ == 0), stop=False))
                        tk = t_op(lambda e, fi_=fi, nb_=nb2, p_=ps, po_=pso:
                                  e.matmul(
                            p_[:, po_ + nb_ * 512:po_ + nb_ * 512 + 512],
                            b1row[0:1, fi_ * 128:(fi_ + 1) * 128],
                            ew_rbf[0:1, nb_ * 512:nb_ * 512 + 512],
                            start=False, stop=True), "pt")
                    if fi % 4 == 3:
                        done(("w1s", slab % 2), tk)
                    W("v", tk)
                    need("v", "hid")
                    vt = v_op(lambda e, fi_=fi, p_=ps, po_=pso: e.tensor_single_scalar(
                        AR1[:, fi_ * 1024:(fi_ + 1) * 1024],
                        p_[:, po_:po_ + 1024], 0.0, op=ALU.max), "pv")
                    done(psk, vt)
                    if fi % 2 == 0:
                        done("psB1", vt)
                hid_tok = kb.tok("pv")

                for dk in range(8):
                    sl2 = dk % 2
                    need("g", ("w2s", sl2))
                    fb = grp * 2048
                    w2t1 = g_dma(
                        w2buf[:, sl2 * 2048:sl2 * 2048 + 1024].rearrange(
                            "p (c d) -> p c d", c=8),
                        dr["W2"][e_, fb:fb + 1024, dk * 128:(dk + 1) * 128].rearrange(
                            "(c p) d -> p c d", p=128))
                    w2t = g_dma(
                        w2buf[:, sl2 * 2048 + 1024:sl2 * 2048 + 2048].rearrange(
                            "p (c d) -> p c d", c=8),
                        dr["W2"][e_, fb + 1024:fb + 2048, dk * 128:(dk + 1) * 128].rearrange(
                            "(c p) d -> p c d", p=128))
                    psk = "psA0" if dk % 2 == 0 else "psC0"
                    ps = PS_A if dk % 2 == 0 else PS_C
                    need("t", psk)
                    if dk % 2 == 1:
                        need("t", "psC1")
                    W("t", w2t)
                    W("t", hid_tok)
                    for fi in range(16):
                        for nb2 in range(2):
                            tk = t_op(lambda e, fi_=fi, nb_=nb2, o_=sl2 * 2048, p_=ps:
                                      e.matmul(
                                p_[:, nb_ * 512:nb_ * 512 + 512],
                                w2buf[:, o_ + fi_ * 128:o_ + (fi_ + 1) * 128],
                                AR1[:, fi_ * 1024 + nb_ * 512:fi_ * 1024 + nb_ * 512 + 512],
                                start=(fi_ == 0), stop=(fi_ == 15 and nb_ == 1)),
                                "pt" if (fi == 15 and nb2 == 1) else None)
                    done(("w2s", sl2), tk)
                    W("v", tk)
                    if first_acc:
                        need("v", "moeT")
                        vt = v_op(lambda e, dk_=dk, p_=ps: e.tensor_copy(
                            MOE[:, dk_ * 1024:(dk_ + 1) * 1024], p_[:, 0:1024]), "pv")
                    else:
                        vt = v_op(lambda e, dk_=dk, p_=ps: e.tensor_add(
                            MOE[:, dk_ * 1024:(dk_ + 1) * 1024], p_[:, 0:1024],
                            MOE[:, dk_ * 1024:(dk_ + 1) * 1024]), "pv")
                    done(psk, vt)
                    if dk % 2 == 1:
                        done("psC1", vt)
                done("hid", kb.tok("pt"))
                done("b1row", kb.tok("pt"))
                first_acc = False
            done("h2s", kb.tok("pt"))
            done("ewrbf", kb.tok("pt"))
        moe_tok = kb.tok("pv")

        for j in range(8):
            i = tile * 8 + j
            need("y", ("stg", SA), ("stg", SB_), ("stg", SC_))
            x2t = y_dma(STG[:, SA:SA + 1024], x2_dram[i * 128:(i + 1) * 128, :])
            stt_ = y_dma(STG[:, SB_:SB_ + 1024], s_dram[i * 128:(i + 1) * 128, :])
            xt = y_dma(STG[:, SC_:SC_ + 1024], dr["x"][i * 128:(i + 1) * 128, :])
            need("t", "psC0", "psC1")
            W("t", ew_tok)
            W("t", b2t)
            tk0 = t_op(lambda e, i_=i: e.matmul(
                PS_C[:, 0:512], ewCb[0:8, i_ * 128:(i_ + 1) * 128],
                b2bf[:, 0:512], start=True, stop=True))
            tk0 = t_op(lambda e, i_=i: e.matmul(
                PS_C[:, 512:1024], ewCb[0:8, i_ * 128:(i_ + 1) * 128],
                b2bf[:, 512:1024], start=True, stop=True), "pt")
            need("t", "psB0", "psB1")
            W("t", moe_tok)
            for dk in range(8):
                tk1 = t_op(lambda e, dk_=dk, j_=j: e.transpose(
                    PS_B[:, dk_ * 128:(dk_ + 1) * 128],
                    MOE[:, dk_ * 1024 + j_ * 128:dk_ * 1024 + (j_ + 1) * 128],
                    idn[:, :]), "pt" if dk == 7 else None)
            W("v", tk0); W("v", x2t)
            need("v", ("stg", SD))
            v_op(lambda e: e.tensor_add(STG[:, SD:SD + 1024], PS_C[:, 0:1024],
                                        STG[:, SA:SA + 1024]))
            W("v", tk1)
            vt = v_op(lambda e: e.tensor_add(STG[:, SD:SD + 1024], PS_B[:, 0:1024],
                                             STG[:, SD:SD + 1024]), "pv")
            done("moeT", kb.tok("pt"))
            done("psB0", vt); done("psB1", vt); done("psC0", vt); done("psC1", vt)
            done(("stg", SA), vt)
            if "gpre" in tap_req:
                W("y", vt)
                y_dma(tap_d["gpre"][i * 128:(i + 1) * 128, :], STG[:, SD:SD + 1024])
            W("s", vt)
            need("s", ("stg", SF))
            st2 = s_op(lambda e: e.activation(STG[:, SF:SF + 1024], STG[:, SD:SD + 1024],
                                              AF.Silu, bias=0.0, scale=1.0), "ps")
            done(("stg", SD), st2)
            W("v", st2); W("v", stt_); W("v", xt)
            v_op(lambda e: e.tensor_mul(STG[:, SF:SF + 1024], STG[:, SF:SF + 1024],
                                        STG[:, SB_:SB_ + 1024]))
            vt = v_op(lambda e: e.tensor_add(STG[:, SF:SF + 1024], STG[:, SF:SF + 1024],
                                             STG[:, SC_:SC_ + 1024]), "pv")
            done(("stg", SB_), vt)
            done(("stg", SC_), vt)
            W("y", vt)
            ot = y_dma(out_d[i * 128:(i + 1) * 128, :], STG[:, SF:SF + 1024], inc="od")
            done(("stg", SF), ot)

    return _finish(nc, kb, ctx)


def _finish(nc, kb, ctx):
    kb.wait("y", kb.tok("od"))
    kb.wait("y", kb.tok("dy"))
    kb.wait("g", kb.tok("dg"))
    with nc.Block() as block:
        @block.tensor
        def _(e):
            kb.replay("t", e)
        @block.vector
        def _(e):
            kb.replay("v", e)
        @block.scalar
        def _(e):
            kb.replay("s", e)
        @block.gpsimd
        def _(e):
            kb.replay("g", e)
        @block.sync
        def _(e):
            kb.replay("y", e)
    ctx.close()
    return nc


def _consts():
    inv = 1.0 / 10000.0 ** (np.arange(0, DR, 2, dtype=np.float32) / DR)
    tt = np.arange(T, dtype=np.float32)
    fr = np.outer(tt, inv)
    cos, sin = np.cos(fr).astype(np.float32), np.sin(fr).astype(np.float32)
    idn = np.eye(128, dtype=np.float32)
    masks = np.zeros((128, 4 * 512), dtype=ml_dtypes.bfloat16)
    for a, r in enumerate((0, 128, 256, 384)):
        ii = np.arange(128)[:, None] + r
        jj = np.arange(512)[None, :]
        masks[:, a * 512:(a + 1) * 512] = (ii <= jj).astype(ml_dtypes.bfloat16)
    return {"c_cos": cos, "c_sin": sin, "c_idn": idn, "c_maskT": masks,
            "c_ones": np.ones((1, T), dtype=ml_dtypes.bfloat16),
            "c_onescol": np.ones((128, 1), dtype=ml_dtypes.bfloat16)}


def _prep_shared(inputs):
    f = np.ascontiguousarray
    sh = {
        "w_kv_down": f(inputs["L0_attn_W_kv_down"]),
        "w_k_up": f(inputs["L0_attn_W_k_up"]),
        "w_v_up": f(inputs["L0_attn_W_v_up"]),
        "w_q": f(inputs["L0_attn_W_q"]),
        "w_q_rope": f(inputs["L0_attn_W_q_rope"]),
        "w_k_rope": f(inputs["L0_attn_W_k_rope"]),
        "w_o1": f(inputs["L0_attn_W_o"]),
        "ln1_g": f(inputs["L0_ffn_ln1_g"].reshape(8, 128).T),
        "ln1_b": f(inputs["L0_ffn_ln1_b"].reshape(8, 128).T),
        "w_qkv": f(inputs["L0_ffn_w_qkv"]),
        "b_qkv_q": f(inputs["L0_ffn_b_qkv"][0:D].reshape(16, 64).T),
        "b_qkv_k": f(inputs["L0_ffn_b_qkv"][D:2 * D].reshape(16, 64).T),
        "b_qkv_v": f(inputs["L0_ffn_b_qkv"][2 * D:3 * D].reshape(1, D)),
        "w_o2": f(inputs["L0_ffn_w_o"]),
        "b_o": f(inputs["L0_ffn_b_o"].reshape(1, D)),
        "ln2_g": f(inputs["L0_ffn_ln2_g"].reshape(8, 128).T),
        "ln2_b": f(inputs["L0_ffn_ln2_b"].reshape(8, 128).T),
        "gate_w": f(inputs["L0_ffn_gate_w"].reshape(8, 128, 8).transpose(1, 0, 2).reshape(128, 64)),
        "gate_b": f(inputs["L0_ffn_gate_b"].reshape(1, 8)),
        "W1": f(inputs["L0_ffn_W1"]),
        "b1": f(inputs["L0_ffn_b1"]),
        "W2": f(inputs["L0_ffn_W2"]),
        "b2": f(inputs["L0_ffn_b2"]),
    }
    sh.update(_consts())
    return sh


def kernel(**inputs):
    x = np.ascontiguousarray(np.asarray(inputs["L0_attn_x"], dtype=np.float32))
    shared = _prep_shared({k: np.asarray(v) for k, v in inputs.items()})
    nc = build_program(taps=TAPS, stages=STAGES)
    in_maps = []
    for c in range(8):
        m = dict(shared)
        m["x"] = x[c]
        in_maps.append(m)
    res = run_bass_kernel_spmd(nc, in_maps, core_ids=list(range(8)))
    out = np.stack([np.asarray(r["out"], dtype=np.float32) for r in res.results], axis=0)
    if TAPS:
        kernel.tap_results = res.results
    return out


# version 06
# speedup vs baseline: 1.2166x; 1.0007x over previous
import math
import numpy as np
import ml_dtypes
import concourse.bass as bass
import concourse.mybir as mybir
from concourse.bass_utils import run_bass_kernel_spmd

F32 = mybir.dt.float32
BF16 = mybir.dt.bfloat16
AX = mybir.AxisListType.X
ALU = mybir.AluOpType
AF = mybir.ActivationFunctionType

T, D, NC, DC = 2048, 1024, 16, 8
NH, HD, DR, DKV = 16, 64, 32, 128
E, DFF = 8, 4096
SC_MLA = 1.0 / math.sqrt(HD + DR)
SC_FFN = 1.0 / math.sqrt(HD)
EPS = 1e-5

TAPS = ()
STAGES = 99


class KB:
    def __init__(self, nc):
        self.nc = nc
        self.streams = {n: [] for n in ("t", "v", "s", "g", "y")}
        self.cnt = {}
        self.handles = {}
        self.lastw = {}
        self.free = {}

    def sem(self, name):
        self.handles[name] = self.nc.semaphore(name).__enter__()
        self.cnt[name] = 0

    def wait(self, st, tok):
        if tok is None:
            return
        sem, val = tok
        assert val <= self.cnt[sem], f"wait {sem}>={val} > emitted {self.cnt[sem]}"
        if self.lastw.get((st, sem), 0) < val:
            self.streams[st].append(("w", sem, val))
            self.lastw[(st, sem)] = val

    AUTO_SERIAL = {"v": "pv", "s": "ps"}

    def op(self, st, fn, inc=None, n=1, fence=False):
        auto = self.AUTO_SERIAL.get(st)
        if auto is not None:
            self.wait(st, (auto, self.cnt[auto]))
            if inc is None:
                inc, n = auto, 1
        self.streams[st].append(("c", fn, inc, n))
        if inc is not None:
            self.cnt[inc] += n
            return (inc, self.cnt[inc])
        return None

    def tok(self, sem):
        return (sem, self.cnt[sem])

    def need(self, st, *keys):
        for k in keys:
            self.wait(st, self.free.get(k))

    def done(self, key, tok):
        self.free[key] = tok

    def barrier(self):
        for st in ("t", "v", "s", "g", "y"):
            for sem in ("pt", "pv", "ps", "dy", "dg"):
                self.wait(st, (sem, self.cnt[sem]))

    def replay(self, st, eng):
        for item in self.streams[st]:
            if item[0] == "w":
                eng.wait_ge(self.handles[item[1]], item[2])
            else:
                _, fn, inc, n = item
                ins = fn(eng)
                if inc is not None:
                    ins.then_inc(self.handles[inc], n)


def build_program(taps=(), stages=99):
    nc = bass.Bass("TRN2", target_bir_lowering=False)
    tap_req = set(taps)

    dr = {}
    def din(name, shape, dt=F32):
        dr[name] = nc.dram_tensor(name, list(shape), dt, kind="ExternalInput").ap()
    din("x", (T, D))
    din("w_kv_down", (D, DKV)); din("w_k_up", (DKV, D)); din("w_v_up", (DKV, D))
    din("w_q", (D, D)); din("w_q_rope", (D, DR)); din("w_k_rope", (DKV, DR))
    din("w_o1", (D, D))
    din("ln1_g", (128, 8)); din("ln1_b", (128, 8))
    din("w_qkv", (D, 3 * D))
    din("b_qkv_q", (64, 16)); din("b_qkv_k", (64, 16)); din("b_qkv_v", (1, D))
    din("w_o2", (D, D)); din("b_o", (1, D))
    din("ln2_g", (128, 8)); din("ln2_b", (128, 8))
    din("gate_w", (128, 64)); din("gate_b", (1, 8))
    din("W1", (E, D, DFF)); din("b1", (E, DFF))
    din("W2", (E, DFF, D)); din("b2", (E, D))
    din("c_idn", (128, 128))
    din("c_cos", (T, 16)); din("c_sin", (T, 16))
    din("c_maskT", (128, 4 * 512), BF16)
    din("c_ones", (1, T), BF16)
    din("c_onescol", (128, 1), BF16)
    out_d = nc.dram_tensor("out", [T, D], F32, kind="ExternalOutput").ap()
    tap_d = {}
    for tp in tap_req - {"min0", "noW", "serial"}:
        shp = {"xn": (D, T), "h": (D, T), "ckv": (DKV, T), "q": (D, T),
               "s": (T, D), "o": (D, T), "x2": (T, D), "h2": (D, T),
               "ew": (T, 8), "gpre": (T, D), "xnt": (T, D),
               "stm": (128, 16), "str": (128, 16)}[tp]
        tap_d[tp] = nc.dram_tensor("tap_" + tp, list(shp), F32, kind="ExternalOutput").ap()
    s_dram = nc.dram_tensor("s_scr", [T, D], F32, kind="Internal").ap()
    x2_dram = nc.dram_tensor("x2_scr", [T, D], F32, kind="Internal").ap()

    from contextlib import ExitStack
    ctx = ExitStack()
    def sb(name, shape, dt):
        return ctx.enter_context(nc.sbuf_tensor(name, list(shape), dt)).ap()

    AR1 = sb("AR1", (128, 16384), BF16)
    AR2 = sb("AR2", (128, 16384), BF16)
    AR3 = sb("AR3", (128, 16384), BF16)
    MOE = AR3.bitcast(F32)
    WAR = sb("WAR", (128, 20480), BF16)
    CKV = sb("CKV", (128, T), BF16)
    ROPE = sb("ROPE", (128, T), BF16)
    STG = sb("STG", (128, 6144), F32)
    q_st = sb("q_st", (128, T), BF16)
    k_st = sb("k_st", (128, T), BF16)
    v_st = sb("v_st", (128, NC * 65), BF16)
    sst = sb("sst", (128, 2 * 1024), BF16)
    ptb = sb("ptb", (128, 2 * 1024), BF16)
    m_all = sb("m_all", (128, 16), F32)
    mT = sb("mT", (16, 128), F32)
    stats_m = sb("stats_m", (128, 16), F32)
    stats_r = sb("stats_r", (128, 16), F32)
    stats_mr = sb("stats_mr", (128, 16), F32)
    sm = sb("sm", (128, 128), F32)
    eps_c = sb("eps_c", (128, 1), F32)
    idn = sb("idn", (128, 128), F32)
    cost = sb("cost", (128, NC * 16), F32)
    sint = sb("sint", (128, NC * 16), F32)
    maskT = sb("maskT", (128, 4 * 512), BF16)
    onesr = sb("onesr", (1, T), BF16)
    onesc = sb("onesc", (128, 1), BF16)
    ln1g = sb("ln1g", (128, 8), F32); ln1b = sb("ln1b", (128, 8), F32)
    ln2g = sb("ln2g", (128, 8), F32); ln2b = sb("ln2b", (128, 8), F32)
    bqq = sb("bqq", (64, 16), F32); bqk = sb("bqk", (64, 16), F32)
    bqv = sb("bqv", (1, D), BF16)
    bo_row = sb("bo_row", (1, D), BF16)
    gw32 = sb("gw32", (128, 64), F32)
    gwhi = sb("gwhi", (128, 64), BF16); gwlo = sb("gwlo", (128, 64), BF16)
    grhi = sb("grhi", (128, 64), BF16); grlo = sb("grlo", (128, 64), BF16)
    bchi = sb("bchi", (128, 8), BF16); bclo = sb("bclo", (128, 8), BF16)
    bpr = sb("bpr", (1, 16), F32)
    bprh = sb("bprh", (1, 8), BF16); bprl = sb("bprl", (1, 8), BF16)
    bprbc = sb("bprbc", (128, 8), F32)
    gateb = sb("gateb", (1, 8), F32)
    ew_all = sb("ew_all", (128, 128), F32)
    ewCb = ROPE[0:8, 0:2048]
    nstg = ROPE[0:64, 0:512]
    ew_rbf = CKV[0:1, 1024:2048]
    ew_bc = CKV[:, 0:1024]
    whb = AR3[:, 0:4096]
    w1buf = WAR[:, 0:8192]
    w2buf = WAR[:, 8192:12288]
    h2s = WAR[:, 12288:20480]
    sstb = None
    b2bf = q_st[0:8, 0:1024]
    invhb = sb("invhb", (1, 512), BF16)
    invlb = sb("invlb", (1, 512), BF16)
    invu2 = sb("invu2", (1, 512), F32)

    PS_A = ctx.enter_context(nc.psum_tensor("PS_A", [128, 2048], F32)).ap()
    PS_B = ctx.enter_context(nc.psum_tensor("PS_B", [128, 1024], F32)).ap()
    PS_C = ctx.enter_context(nc.psum_tensor("PS_C", [128, 1024], F32)).ap()

    kb = KB(nc)
    for s_ in ("pt", "pv", "ps", "pg", "dy", "dg", "od"):
        kb.sem(s_)
    W, need, done = kb.wait, kb.need, kb.done

    def t_op(fn, inc=None):
        return kb.op("t", fn, inc)
    def v_op(fn, inc=None, fence=False):
        return kb.op("v", fn, inc, fence=fence)
    def s_op(fn, inc=None, fence=False):
        return kb.op("s", fn, inc, fence=fence)
    def y_dma(dst, src, inc="dy"):
        return kb.op("y", lambda e, d=dst, s2=src: e.dma_start(out=d, in_=s2), inc, 16)
    def g_dma(dst, src, inc="dg"):
        return kb.op("g", lambda e, d=dst, s2=src: e.dma_start(out=d, in_=s2), inc, 16)

    SA, SB_, SC_, SD, SE, SF = 0, 1024, 2048, 3072, 4096, 5120
    inv_h = STG[0:1, SD + 512:SD + 1024]
    invbc = STG[0:64, SD:SD + 512]
    lghi = sst.bitcast(BF16)[:, 0:1024]
    lglo = sst.bitcast(BF16)[:, 1024:2048]
    b1row = sst.bitcast(BF16)[0:1, 0:2048]

    y_dma(idn[:, :], dr["c_idn"][:, :])
    y_dma(cost.rearrange("p (c f) -> p c f", c=16), dr["c_cos"].rearrange("(c p) f -> p c f", p=128))
    y_dma(sint.rearrange("p (c f) -> p c f", c=16), dr["c_sin"].rearrange("(c p) f -> p c f", p=128))
    y_dma(maskT[:, :], dr["c_maskT"][:, :])
    y_dma(onesr[:, :], dr["c_ones"][:, :])
    y_dma(onesc[:, :], dr["c_onescol"][:, :])
    y_dma(ln1g[:, :], dr["ln1_g"][:, :]); y_dma(ln1b[:, :], dr["ln1_b"][:, :])
    y_dma(ln2g[:, :], dr["ln2_g"][:, :]); y_dma(ln2b[:, :], dr["ln2_b"][:, :])
    y_dma(bqq[:, :], dr["b_qkv_q"][:, :]); y_dma(bqk[:, :], dr["b_qkv_k"][:, :])
    y_dma(gw32[:, :], dr["gate_w"][:, :])
    y_dma(gateb[:, :], dr["gate_b"][:, :])
    consts_tok = kb.tok("dy")

    O_WO2 = 0
    O_WKV, O_WQ, O_WQR, O_WKR = 0, 1024, 9216, 9472
    O_WKUP, O_WVUP, O_WO1 = 9504, 10528, 11552
    if "noW" not in tap_req:
        g_dma(WAR[:, O_WKV:O_WKV + 1024].rearrange("p (c m) -> p c m", c=8), dr["w_kv_down"].rearrange("(c p) m -> p c m", p=128))
        g_dma(WAR[:, O_WQ:O_WQ + 8192].rearrange("p (c m) -> p c m", c=8), dr["w_q"].rearrange("(c p) m -> p c m", p=128))
        g_dma(WAR[:, O_WQR:O_WQR + 256].rearrange("p (c m) -> p c m", c=8), dr["w_q_rope"].rearrange("(c p) m -> p c m", p=128))
        g_dma(WAR[:, O_WKR:O_WKR + 32], dr["w_k_rope"][:, :])
        g_dma(WAR[:, O_WKUP:O_WKUP + 1024], dr["w_k_up"][:, :])
        g_dma(WAR[:, O_WVUP:O_WVUP + 1024], dr["w_v_up"][:, :])
        g_dma(WAR[:, O_WO1:O_WO1 + 8192].rearrange("p (c m) -> p c m", c=8), dr["w_o1"].rearrange("(c p) m -> p c m", p=128))
        g_dma(bqv[:, :], dr["b_qkv_v"][:, :])
        g_dma(bo_row[:, :], dr["b_o"][:, :])
    w1w_tok = kb.tok("dg")

    W("v", consts_tok)
    v_op(lambda e: e.memset(eps_c[:, :], EPS))
    for kc in range(NC):
        vt = v_op(lambda e, kc_=kc: e.tensor_copy(
            v_st[:, kc_ * 65 + 64:kc_ * 65 + 65], onesc[:, 0:1]),
            "pv" if kc == NC - 1 else None)
    vst_ones_tok = vt

    W("t", consts_tok)
    for i in range(NC):
        c8 = i * 8
        need("y", ("stg", SA))
        xtok = y_dma(STG[:, SA:SA + 1024], dr["x"][i * 128:(i + 1) * 128, :])
        W("v", xtok)
        rt = v_op(lambda e, c_=c8: e.reduce_sum(sm[:, c_:c_ + 1], STG[:, SA:SA + 1024],
                                                axis=AX), "pv", fence=True)
        W("v", rt)
        vt = v_op(lambda e, i_=i, c_=c8: e.tensor_scalar_mul(
            stats_m[:, i_:i_ + 1], sm[:, c_:c_ + 1], -1.0 / D), "pv")
        need("v", ("stg", SB_))
        st_ = v_op(lambda e, i_=i: e.tensor_scalar(
            STG[:, SB_:SB_ + 1024], STG[:, SA:SA + 1024],
            stats_m[:, i_:i_ + 1], None, op0=ALU.add), "pv")
        done(("stg", SA), st_)
        need("v", ("stg", SD))
        v_op(lambda e: e.tensor_mul(STG[:, SD:SD + 1024], STG[:, SB_:SB_ + 1024],
                                    STG[:, SB_:SB_ + 1024]))
        vt = v_op(lambda e, c_=c8: e.reduce_sum(sm[:, c_ + 1:c_ + 2],
                                                STG[:, SD:SD + 1024], axis=AX), "pv", fence=True)
        W("s", vt)
        s_op(lambda e, c_=c8: e.activation(sm[:, c_ + 2:c_ + 3], sm[:, c_ + 1:c_ + 2],
                                           AF.Ln, bias=eps_c[:, 0:1], scale=1.0 / D))
        st_ = s_op(lambda e, i_=i, c_=c8: e.activation(
            stats_r[:, i_:i_ + 1], sm[:, c_ + 2:c_ + 3], AF.Exp, bias=0.0, scale=-0.5), "ps")
        W("v", st_)
        v_op(lambda e, i_=i: e.tensor_mul(stats_mr[:, i_:i_ + 1],
                                          stats_m[:, i_:i_ + 1], stats_r[:, i_:i_ + 1]))
        need("v", ("stg", SC_))
        st_ = v_op(lambda e, i_=i: e.tensor_scalar(
            STG[:, SC_:SC_ + 1024], STG[:, SB_:SB_ + 1024],
            stats_r[:, i_:i_ + 1], None, op0=ALU.mult), "pv")
        if "xnt" in tap_req:
            W("y", st_)
            xnt_t = y_dma(tap_d["xnt"][i * 128:(i + 1) * 128, :], STG[:, SC_:SC_ + 1024])
        rt = v_op(lambda e, c_=c8: e.reduce_sum(sm[:, c_ + 3:c_ + 4],
                                                STG[:, SC_:SC_ + 1024], axis=AX), "pv", fence=True)
        W("v", rt)
        vt = v_op(lambda e, c_=c8: e.tensor_scalar_mul(
            sm[:, c_ + 4:c_ + 5], sm[:, c_ + 3:c_ + 4], -1.0 / D), "pv")
        st_ = v_op(lambda e, c_=c8: e.tensor_scalar(
            STG[:, SB_:SB_ + 1024], STG[:, SC_:SC_ + 1024],
            sm[:, c_ + 4:c_ + 5], None, op0=ALU.add), "pv")
        v_op(lambda e: e.tensor_mul(STG[:, SD:SD + 1024], STG[:, SB_:SB_ + 1024],
                                    STG[:, SB_:SB_ + 1024]))
        vt = v_op(lambda e, c_=c8: e.reduce_sum(sm[:, c_ + 5:c_ + 6],
                                                STG[:, SD:SD + 1024], axis=AX), "pv", fence=True)
        done(("stg", SD), vt)
        W("s", vt)
        s_op(lambda e, c_=c8: e.activation(sm[:, c_ + 6:c_ + 7], sm[:, c_ + 5:c_ + 6],
                                           AF.Ln, bias=eps_c[:, 0:1], scale=1.0 / D))
        st_ = s_op(lambda e, c_=c8: e.activation(sm[:, c_ + 7:c_ + 8], sm[:, c_ + 6:c_ + 7],
                                           AF.Exp, bias=0.0, scale=-0.5), "ps")
        W("v", st_)
        st_ = v_op(lambda e, c_=c8: e.tensor_scalar(
            STG[:, SB_:SB_ + 1024], STG[:, SB_:SB_ + 1024],
            sm[:, c_ + 7:c_ + 8], None, op0=ALU.mult), "pv")
        W("t", st_)
        need("t", "psB0", "psB1", "psA0")
        for c in range(8):
            tk1 = t_op(lambda e, c_=c: e.transpose(
                PS_B[:, c_ * 128:(c_ + 1) * 128],
                STG[:, SC_ + c_ * 128:SC_ + (c_ + 1) * 128], idn[:, :]),
                "pt" if c == 7 else None)
        for c in range(8):
            tk2 = t_op(lambda e, c_=c: e.transpose(
                PS_A[:, c_ * 128:(c_ + 1) * 128],
                STG[:, SB_ + c_ * 128:SB_ + (c_ + 1) * 128], idn[:, :]),
                "pt" if c == 7 else None)
        W("v", tk1)
        vt = v_op(lambda e, i_=i: e.tensor_copy(
            AR1.rearrange("p (c t) -> p c t", c=8)[:, :, i_ * 128:(i_ + 1) * 128],
            PS_B.rearrange("p (c t) -> p c t", c=8)[:, :, :]), "pv")
        done("psB0", vt); done("psB1", vt)
        done(("stg", SC_), vt)
        if "min0" in tap_req:
            done("psA0", tk2)
            done(("stg", SB_), tk2)
            continue
        W("v", tk2)
        for c in range(8):
            vt = v_op(lambda e, i_=i, c_=c: e.tensor_scalar(
                AR2[:, c_ * 2048 + i_ * 128: c_ * 2048 + (i_ + 1) * 128],
                PS_A[:, c_ * 128:(c_ + 1) * 128],
                ln1g[:, c_:c_ + 1], ln1b[:, c_:c_ + 1], op0=ALU.mult, op1=ALU.add),
                "pv" if c == 7 else None)
        done("psA0", vt)
        done(("stg", SB_), vt)
        if "serial" in tap_req:
            kb.barrier()

    def tap_fm(name, src, rows=128):
        if name not in tap_req:
            return
        nchunks = {1024: 1}.get(src.shape[1] // 16, DC)
        total = src.shape[1]
        ncol = total // 1024
        for cj in range(ncol):
            need("v", ("stg", SA))
            vt = v_op(lambda e, c_=cj: e.tensor_copy(
                STG[0:rows, SA:SA + 1024], src[0:rows, c_ * 1024:(c_ + 1) * 1024]), "pv")
            W("y", vt)
            r0 = (cj // 2) * 128
            col = (cj % 2) * 1024
            if src.shape[1] == T:
                dt_ = y_dma(tap_d[name][0:rows, cj * 1024:(cj + 1) * 1024],
                            STG[0:rows, SA:SA + 1024])
            else:
                dt_ = y_dma(tap_d[name][r0:r0 + 128, col:col + 1024],
                            STG[0:rows, SA:SA + 1024])
            done(("stg", SA), dt_)

    if "stm" in tap_req:
        W("y", kb.tok("pv"))
        y_dma(tap_d["stm"][:, :], stats_m[:, :])
        y_dma(tap_d["str"][:, :], stats_r[:, :])
    tap_fm("xn", AR1)
    tap_fm("h", AR2)
    if stages <= 0:
        return _finish(nc, kb, ctx)

    W("t", w1w_tok)
    W("t", kb.tok("pv"))
    for nb in range(4):
        slot = "psC%d" % (nb % 2)
        need("t", slot)
        for c in range(8):
            tk = t_op(lambda e, c_=c, nb_=nb: e.matmul(
                PS_C[:, (nb_ % 2) * 512:(nb_ % 2) * 512 + 512],
                WAR[:, O_WKV + c_ * 128:O_WKV + (c_ + 1) * 128],
                AR1[:, c_ * 2048 + nb_ * 512: c_ * 2048 + nb_ * 512 + 512],
                start=(c_ == 0), stop=(c_ == 7)), "pt" if c == 7 else None)
        W("v", tk)
        vt = v_op(lambda e, nb_=nb: e.tensor_copy(
            CKV[:, nb_ * 512:nb_ * 512 + 512],
            PS_C[:, (nb_ % 2) * 512:(nb_ % 2) * 512 + 512]), "pv")
        done(slot, vt)
    tap_fm("ckv", CKV)

    for m in range(8):
        for nb in range(4):
            slot = "psC%d" % (nb % 2)
            need("t", slot)
            for c in range(8):
                tk = t_op(lambda e, c_=c, m_=m, nb_=nb: e.matmul(
                    PS_C[:, (nb_ % 2) * 512:(nb_ % 2) * 512 + 512],
                    WAR[:, O_WQ + c_ * 1024 + m_ * 128: O_WQ + c_ * 1024 + (m_ + 1) * 128],
                    AR1[:, c_ * 2048 + nb_ * 512: c_ * 2048 + nb_ * 512 + 512],
                    start=(c_ == 0), stop=(c_ == 7)), "pt" if c == 7 else None)
            W("v", tk)
            vt = v_op(lambda e, m_=m, nb_=nb: e.tensor_single_scalar(
                AR3[:, m_ * 2048 + nb_ * 512: m_ * 2048 + nb_ * 512 + 512],
                PS_C[:, (nb_ % 2) * 512:(nb_ % 2) * 512 + 512], SC_MLA, op=ALU.mult), "pv")
            done(slot, vt)
    tap_fm("q", AR3)

    QR, ROT, TA, TB = SE, SE + 32, SE + 64, SE + 80
    for i in range(NC):
        for which in range(2):
            need("t", "psC0")
            if which == 0:
                for c in range(8):
                    tk = t_op(lambda e, c_=c, i_=i: e.matmul(
                        PS_C[:, 0:32],
                        AR1[:, c_ * 2048 + i_ * 128: c_ * 2048 + (i_ + 1) * 128],
                        WAR[:, O_WQR + c_ * 32:O_WQR + (c_ + 1) * 32],
                        start=(c_ == 0), stop=(c_ == 7)), "pt" if c == 7 else None)
            else:
                tk = t_op(lambda e, i_=i: e.matmul(
                    PS_C[:, 0:32],
                    CKV[:, i_ * 128:(i_ + 1) * 128],
                    WAR[:, O_WKR:O_WKR + 32], start=True, stop=True), "pt")
            W("v", tk)
            need("v", ("stg", SE))
            vt = v_op(lambda e: e.tensor_copy(STG[:, QR:QR + 32], PS_C[:, 0:32]), "pv")
            done("psC0", vt)
            v_op(lambda e, i_=i: e.tensor_mul(STG[:, TA:TA + 16], STG[:, QR:QR + 16],
                                              cost[:, i_ * 16:(i_ + 1) * 16]))
            v_op(lambda e, i_=i: e.tensor_mul(STG[:, TB:TB + 16], STG[:, QR + 16:QR + 32],
                                              sint[:, i_ * 16:(i_ + 1) * 16]))
            v_op(lambda e: e.tensor_sub(STG[:, ROT:ROT + 16], STG[:, TA:TA + 16],
                                        STG[:, TB:TB + 16]))
            v_op(lambda e, i_=i: e.tensor_mul(STG[:, TA:TA + 16], STG[:, QR + 16:QR + 32],
                                              cost[:, i_ * 16:(i_ + 1) * 16]))
            v_op(lambda e, i_=i: e.tensor_mul(STG[:, TB:TB + 16], STG[:, QR:QR + 16],
                                              sint[:, i_ * 16:(i_ + 1) * 16]))
            vt = v_op(lambda e: e.tensor_add(STG[:, ROT + 16:ROT + 32], STG[:, TA:TA + 16],
                                             STG[:, TB:TB + 16]), "pv")
            W("t", vt)
            need("t", "psC1")
            tk = t_op(lambda e: e.transpose(PS_C[0:32, 512:640], STG[:, ROT:ROT + 32],
                                            idn[:, :]), "pt")
            W("v", tk)
            base = 0 if which == 0 else 64
            scale = SC_MLA if which == 0 else 1.0
            vt = v_op(lambda e, i_=i, b_=base, s2=scale: e.tensor_single_scalar(
                ROPE[b_:b_ + 32, i_ * 128:(i_ + 1) * 128],
                PS_C[0:32, 512:640], s2, op=ALU.mult), "pv")
            done("psC1", vt)
            done(("stg", SE), vt)
    rope_tok = kb.tok("pv")

    if stages <= 1:
        return _finish(nc, kb, ctx)

    def attention(kind):
        mla = kind == "mla"
        KQ = 96 if mla else 64
        KBr = 97 if mla else 65
        brow = 96 if mla else 64

        if mla:
            W("v", rope_tok)
            vt = v_op(lambda e: e.tensor_copy(k_st[64:96, :], ROPE[64:96, :]), "pv")
            W("g", rope_tok)
            rtok = g_dma(q_st[64:96, :], ROPE[0:32, :])
            done("nstg", rtok)
            W("y", vt)
            y_dma(k_st[96:97, :], onesr[:, :])
        else:
            W("y", kb.tok("pt"))
            y_dma(k_st[64:65, :], onesr[:, :])
            W("g", kb.tok("pt"))
        aug_tok = kb.tok("dy")

        for h in range(NH):
            pe_prev = kb.tok("pt")
            if mla:
                W("g", pe_prev)
                vt = g_dma(q_st[0:64, :], AR3[(h % 2) * 64:(h % 2) * 64 + 64,
                                              (h // 2) * 2048:(h // 2) * 2048 + 2048])
                W("t", vt)
                W("v", pe_prev)
                BS = [(PS_C, 0, "psC0"), (PS_C, 512, "psC1"), (PS_A, 0, "bA0"),
                      (PS_A, 512, "bA1"), (PS_A, 1024, "bA2"), (PS_A, 1536, "bA3")]
                bi = 0
                for nb in range(4):
                    ps_, off_, key = BS[bi % 6]; bi += 1
                    need("t", key)
                    if key in ("bA0", "bA1"):
                        need("t", "psA0")
                    elif key in ("bA2", "bA3"):
                        need("t", "psA1")
                    tk = t_op(lambda e, h_=h, nb_=nb, p_=ps_, o_=off_: e.matmul(
                        p_[0:64, o_:o_ + 512],
                        WAR[:, O_WKUP + h_ * 64:O_WKUP + (h_ + 1) * 64],
                        CKV[:, nb_ * 512:nb_ * 512 + 512], start=True, stop=True), "pt")
                    W("v", tk)
                    vk = v_op(lambda e, nb_=nb, p_=ps_, o_=off_: e.tensor_copy(
                        k_st[0:64, nb_ * 512:nb_ * 512 + 512],
                        p_[0:64, o_:o_ + 512]), "pv")
                    done(key, vk)
                for g4 in range(4):
                    ps_, off_, key = BS[bi % 6]; bi += 1
                    need("t", key)
                    if key in ("bA0", "bA1"):
                        need("t", "psA0")
                    elif key in ("bA2", "bA3"):
                        need("t", "psA1")
                    for z in range(4):
                        tk = t_op(lambda e, h_=h, kc_=g4 * 4 + z, z_=z, p_=ps_, o_=off_:
                                  e.matmul(
                            p_[:, o_ + z_ * 64:o_ + z_ * 64 + 64],
                            CKV[:, kc_ * 128:(kc_ + 1) * 128],
                            WAR[:, O_WVUP + h_ * 64:O_WVUP + (h_ + 1) * 64],
                            start=True, stop=True), "pt" if z == 3 else None)
                    W("v", tk)
                    W("v", vst_ones_tok)
                    vk = v_op(lambda e, g_=g4, p_=ps_, o_=off_: e.tensor_copy(
                        v_st.rearrange("p (k w) -> p k w", w=65)[:, g_ * 4:(g_ + 1) * 4, 0:64],
                        p_[:, o_:o_ + 256].rearrange("p (k w) -> p k w", w=64)), "pv")
                    done(key, vk)
            else:
                sl = h % 2
                off = sl * 1536
                need("g", ("whb", sl))
                g_dma(whb[:, off:off + 512].rearrange("p (c f) -> p c f", c=8),
                      dr["w_qkv"][:, h * 64:(h + 1) * 64].rearrange("(c p) f -> p c f", p=128))
                g_dma(whb[:, off + 512:off + 1024].rearrange("p (c f) -> p c f", c=8),
                      dr["w_qkv"][:, D + h * 64:D + (h + 1) * 64].rearrange("(c p) f -> p c f", p=128))
                gt = g_dma(whb[:, off + 1024:off + 1536].rearrange("p (c f) -> p c f", c=8),
                           dr["w_qkv"][:, 2 * D + h * 64:2 * D + (h + 1) * 64].rearrange("(c p) f -> p c f", p=128))
                W("t", gt)
                W("t", pe_prev)
                BS = [(PS_C, 0, "psC0"), (PS_C, 512, "psC1"), (PS_A, 0, "bA0"),
                      (PS_A, 512, "bA1"), (PS_A, 1024, "bA2"), (PS_A, 1536, "bA3")]
                def bneed(key):
                    need("t", key)
                    if key in ("bA0", "bA1"):
                        need("t", "psA0")
                    elif key in ("bA2", "bA3"):
                        need("t", "psA1")
                bi = 0
                for nb in range(4):
                    ps_, off_, key = BS[bi % 6]; bi += 1
                    bneed(key)
                    for c in range(8):
                        tk = t_op(lambda e, nb_=nb, c_=c, o_=off, p_=ps_, q2_=off_:
                                  e.matmul(
                            p_[0:64, q2_:q2_ + 512],
                            whb[:, o_ + c_ * 64:o_ + (c_ + 1) * 64],
                            AR2[:, c_ * 2048 + nb_ * 512: c_ * 2048 + nb_ * 512 + 512],
                            start=(c_ == 0), stop=(c_ == 7)), "pt" if c == 7 else None)
                    W("v", tk)
                    W("v", pe_prev)
                    vk = v_op(lambda e, h_=h, nb_=nb, p_=ps_, q2_=off_: e.tensor_scalar(
                        q_st[0:64, nb_ * 512:nb_ * 512 + 512],
                        p_[0:64, q2_:q2_ + 512],
                        bqq[:, h_:h_ + 1], SC_FFN, op0=ALU.add, op1=ALU.mult), "pv")
                    done(key, vk)
                for nb in range(4):
                    ps_, off_, key = BS[bi % 6]; bi += 1
                    bneed(key)
                    for c in range(8):
                        tk = t_op(lambda e, nb_=nb, c_=c, o_=off, p_=ps_, q2_=off_:
                                  e.matmul(
                            p_[0:64, q2_:q2_ + 512],
                            whb[:, o_ + 512 + c_ * 64:o_ + 512 + (c_ + 1) * 64],
                            AR2[:, c_ * 2048 + nb_ * 512: c_ * 2048 + nb_ * 512 + 512],
                            start=(c_ == 0), stop=(c_ == 7)), "pt" if c == 7 else None)
                    W("v", tk)
                    vk = v_op(lambda e, h_=h, nb_=nb, p_=ps_, q2_=off_: e.tensor_scalar(
                        k_st[0:64, nb_ * 512:nb_ * 512 + 512],
                        p_[0:64, q2_:q2_ + 512],
                        bqk[:, h_:h_ + 1], None, op0=ALU.add), "pv")
                    done(key, vk)
                for g4 in range(4):
                    ps_, off_, key = BS[bi % 6]; bi += 1
                    bneed(key)
                    for z in range(4):
                        kc = g4 * 4 + z
                        for c in range(8):
                            t_op(lambda e, kc_=kc, c_=c, o_=off, z_=z, p_=ps_, q2_=off_:
                                 e.matmul(
                                p_[:, q2_ + z_ * 64:q2_ + z_ * 64 + 64],
                                AR2[:, c_ * 2048 + kc_ * 128: c_ * 2048 + (kc_ + 1) * 128],
                                whb[:, o_ + 1024 + c_ * 64:o_ + 1024 + (c_ + 1) * 64],
                                start=(c_ == 0), stop=False))
                        tk = t_op(lambda e, h_=h, kc_=kc, z_=z, p_=ps_, q2_=off_:
                                  e.matmul(
                            p_[:, q2_ + z_ * 64:q2_ + z_ * 64 + 64],
                            onesr[0:1, kc_ * 128:(kc_ + 1) * 128],
                            bqv[0:1, h_ * 64:(h_ + 1) * 64], start=False, stop=True),
                            "pt" if z == 3 else None)
                    W("v", tk)
                    W("v", vst_ones_tok)
                    vk = v_op(lambda e, g_=g4, p_=ps_, q2_=off_: e.tensor_copy(
                        v_st.rearrange("p (k w) -> p k w", w=65)[:, g_ * 4:(g_ + 1) * 4, 0:64],
                        p_[:, q2_:q2_ + 256].rearrange("p (k w) -> p k w", w=64)), "pv")
                    done(key, vk)
                done(("whb", sl), kb.tok("pt"))
            qkv_tok = kb.tok("pv")

            W("t", qkv_tok)
            for qc in range(NC):
                nb = 4 if not mla else min(4, qc // 4 + 1)
                wid = nb * 256
                reg = qc % 2
                rk = "psA%d" % reg
                need("t", rk, *( ("bA0", "bA1") if reg == 0 else ("bA2", "bA3") ))
                for b in range(nb):
                    tk = t_op(lambda e, qc_=qc, b_=b, r_=reg, K_=KQ: e.matmul(
                        PS_A[:, r_ * 1024 + b_ * 256:r_ * 1024 + (b_ + 1) * 256],
                        q_st[0:K_, qc_ * 128:(qc_ + 1) * 128],
                        k_st.rearrange("p (t two) -> p t two", two=2)
                            [0:K_, b_ * 256:(b_ + 1) * 256, 0],
                        start=True, stop=True), "pt" if b == nb - 1 else None)
                W("v", tk)
                vt = v_op(lambda e, qc_=qc, r_=reg, w_=wid: e.reduce_max(
                    m_all[:, qc_:qc_ + 1], PS_A[:, r_ * 1024:r_ * 1024 + w_],
                    axis=AX, negate=True), "pv", fence=True)
                done(rk, vt)
            W("t", kb.tok("pv"))
            need("t", "psC0")
            tk = t_op(lambda e: e.transpose(PS_C[0:16, 0:128], m_all[:, :], idn[:, :]), "pt")
            W("v", tk)
            vt = v_op(lambda e: e.tensor_copy(mT[:, :], PS_C[0:16, 0:128]), "pv")
            done("psC0", vt)
            W("g", vt)
            mrow_tok = g_dma(q_st[brow:brow + 1, :].rearrange("o (c q) -> o c q", c=16),
                             mT[:, :])

            for qpair in range(2):
                qbs = [qpair * 2, qpair * 2 + 1]
                chains = []
                for ci, qb in enumerate(qbs):
                    kcs = list(range((qb + 1) * 4)) if mla else list(range(16))
                    chains.append([kcs[j:j + 2] for j in range(0, len(kcs), 2)])
                nsteps = max(len(c) for c in chains)
                pv_last = {}
                for stp in range(nsteps):
                    for ci, qb in enumerate(qbs):
                        if stp >= len(chains[ci]):
                            continue
                        pair = chains[ci][stp]
                        ssl = ci
                        need("t", "psA%d" % ssl,
                             *( ("bA0", "bA1") if ssl == 0 else ("bA2", "bA3") ))
                        if stp == 0:
                            W("t", mrow_tok)
                            W("t", aug_tok)
                            need("t", "psB%d" % ci)
                        for z, kc in enumerate(pair):
                            tk = t_op(lambda e, kc_=kc, qb_=qb, ssl_=ssl, z_=z, K_=KBr:
                                      e.matmul(
                                PS_A[:, ssl_ * 1024 + z_ * 512:ssl_ * 1024 + z_ * 512 + 512],
                                k_st[0:K_, kc_ * 128:(kc_ + 1) * 128],
                                q_st[0:K_, qb_ * 512:qb_ * 512 + 512],
                                start=True, stop=True), "pt" if z == len(pair) - 1 else None)
                        W("v", tk)
                        wid = len(pair) * 512
                        need("v", ("sst", ssl))
                        vt = v_op(lambda e, ssl_=ssl, w_=wid: e.tensor_copy(
                            sst[:, ssl_ * 1024:ssl_ * 1024 + w_],
                            PS_A[:, ssl_ * 1024:ssl_ * 1024 + w_]), "pv")
                        done("psA%d" % ssl, vt)
                        W("s", vt)
                        need("s", ("ptb", ssl))
                        st_ = s_op(lambda e, ssl_=ssl, w_=wid: e.activation(
                            ptb[:, ssl_ * 1024:ssl_ * 1024 + w_],
                            sst[:, ssl_ * 1024:ssl_ * 1024 + w_], AF.Exp,
                            bias=0.0, scale=1.0), "ps")
                        done(("sst", ssl), st_)
                        gtk = st_
                        if mla:
                            for z, kc in enumerate(pair):
                                if kc >= qb * 4:
                                    a = kc - qb * 4
                                    kb.wait("g", st_)
                                    gk = kb.op("g", lambda e, ssl_=ssl, z_=z, a_=a:
                                               e.tensor_mul(
                                        ptb[:, ssl_ * 1024 + z_ * 512:
                                            ssl_ * 1024 + z_ * 512 + 512],
                                        ptb[:, ssl_ * 1024 + z_ * 512:
                                            ssl_ * 1024 + z_ * 512 + 512],
                                        maskT[:, a_ * 512:(a_ + 1) * 512]), "pg")
                                    gtk = gk
                        W("t", gtk)
                        for z, kc in enumerate(pair):
                            tk = t_op(lambda e, kc_=kc, q_=qb, ssl_=ssl, z_=z, ci_=ci,
                                      f_=(stp == 0 and z == 0),
                                      l_=(stp == len(chains[ci]) - 1 and z == len(pair) - 1):
                                      e.matmul(
                                PS_B[0:65, ci_ * 512:ci_ * 512 + 512],
                                v_st[:, kc_ * 65:(kc_ + 1) * 65],
                                ptb[:, ssl_ * 1024 + z_ * 512:ssl_ * 1024 + z_ * 512 + 512],
                                start=f_, stop=l_), "pt" if z == len(pair) - 1 else None)
                        done(("ptb", ssl), tk)
                        pv_last[ci] = tk
                for ci, qb in enumerate(qbs):
                    tk = pv_last[ci]
                    pv_done = tk
                    pvk = "psB%d" % ci
                    W("v", pv_done)
                    need("v", "invb", ("stg", SD))
                    vt = v_op(lambda e, q_=qb: e.tensor_copy(
                        inv_h[:, :], PS_B[64:65, (q_ % 2) * 512:(q_ % 2) * 512 + 512]), "pv")
                    W("s", vt)
                    s_op(lambda e: e.activation(invu2[:, :], inv_h[:, :], AF.Ln,
                                                    bias=0.0, scale=1.0))
                    st_ = s_op(lambda e: e.activation(inv_h[:, :], invu2[:, :], AF.Exp,
                                                              bias=0.0, scale=-1.0), "ps")
                    W("v", st_)
                    v_op(lambda e: e.tensor_copy(invhb[0:1, :], inv_h[:, :]))
                    v_op(lambda e: e.tensor_copy(invu2[:, :], invhb[0:1, :]))
                    vt = v_op(lambda e: e.tensor_sub(invlb[0:1, :], inv_h[:, :],
                                                             invu2[:, :]), "pv")
                    W("t", vt)
                    need("t", "psC1")
                    t_op(lambda e: e.matmul(PS_C[0:64, 512:1024], onesr[0:1, 0:64],
                                                invhb[0:1, :], start=True, stop=False))
                    tk = t_op(lambda e: e.matmul(PS_C[0:64, 512:1024], onesr[0:1, 0:64],
                                                     invlb[0:1, :], start=False, stop=True), "pt")
                    W("v", tk)
                    vt = v_op(lambda e: e.tensor_copy(invbc[:, :], PS_C[0:64, 512:1024]), "pv")
                    done("psC1", vt)
                    if h % 2 == 0:
                        vt2 = v_op(lambda e, h_=h, q_=qb: e.tensor_mul(
                            AR1[0:64,
                                (h_ // 2) * 2048 + q_ * 512:(h_ // 2) * 2048 + q_ * 512 + 512],
                            PS_B[0:64, (q_ % 2) * 512:(q_ % 2) * 512 + 512], invbc[:, :]), "pv")
                        done(pvk, vt2)
                    else:
                        need("v", "nstg")
                        vt2 = v_op(lambda e, q_=qb: e.tensor_mul(
                            nstg[:, :],
                            PS_B[0:64, (q_ % 2) * 512:(q_ % 2) * 512 + 512], invbc[:, :]), "pv")
                        done(pvk, vt2)
                        W("g", vt2)
                        gt2 = g_dma(AR1[64:128,
                                            (h // 2) * 2048 + qb * 512:(h // 2) * 2048 + qb * 512 + 512],
                                        nstg[:, :])
                        done("nstg", gt2)
                    done("invb", vt2)
                    done(("stg", SD), vt2)

    attention("mla")
    if stages <= 2:
        return _finish(nc, kb, ctx)

    mla_tok = kb.tok("pv")
    W("t", kb.tok("dg"))
    for i in range(NC):
        need("t", "psB0", "psB1")
        W("t", mla_tok)
        for nb in range(2):
            for c in range(8):
                tk = t_op(lambda e, i_=i, nb_=nb, c_=c: e.matmul(
                    PS_B[:, nb_ * 512:nb_ * 512 + 512],
                    AR1[:, c_ * 2048 + i_ * 128: c_ * 2048 + (i_ + 1) * 128],
                    WAR[:, O_WO1 + c_ * 1024 + nb_ * 512: O_WO1 + c_ * 1024 + nb_ * 512 + 512],
                    start=(c_ == 0), stop=(c_ == 7)), "pt" if c == 7 else None)
        W("v", tk)
        need("v", ("stg", SF))
        vt = v_op(lambda e: e.tensor_copy(STG[:, SF:SF + 1024], PS_B[:, :]), "pv")
        done("psB0", vt); done("psB1", vt)
        W("y", vt)
        dt_ = y_dma(s_dram[i * 128:(i + 1) * 128, :], STG[:, SF:SF + 1024])
        if "s" in tap_req:
            dt_ = y_dma(tap_d["s"][i * 128:(i + 1) * 128, :], STG[:, SF:SF + 1024])
        done(("stg", SF), dt_)
    s_mm_tok = kb.tok("pt")

    if stages <= 3:
        return _finish(nc, kb, ctx)

    W("g", s_mm_tok)
    wo2_tok = g_dma(WAR[:, O_WO2:O_WO2 + 8192].rearrange("p (c m) -> p c m", c=8),
                    dr["w_o2"].rearrange("(c p) m -> p c m", p=128))

    W("v", s_mm_tok)
    attention("ffn")
    ffn_tok = kb.tok("pv")
    tap_fm("o", AR1)
    if stages <= 4:
        return _finish(nc, kb, ctx)

    W("v", consts_tok)
    need("v", ("stg", SE))
    for c in range(8):
        v_op(lambda e, c_=c: e.tensor_scalar(
            STG[:, SE + c_ * 8:SE + (c_ + 1) * 8], gw32[:, c_ * 8:(c_ + 1) * 8],
            ln2g[:, c_:c_ + 1], None, op0=ALU.mult))
    v_op(lambda e: e.tensor_copy(gwhi[:, :], STG[:, SE:SE + 64]))
    v_op(lambda e: e.tensor_copy(STG[:, SE + 64:SE + 128], gwhi[:, :]))
    v_op(lambda e: e.tensor_sub(gwlo[:, :], STG[:, SE:SE + 64], STG[:, SE + 64:SE + 128]))
    v_op(lambda e: e.tensor_copy(grhi[:, :], gw32[:, :]))
    v_op(lambda e: e.tensor_copy(STG[:, SE + 64:SE + 128], grhi[:, :]))
    v_op(lambda e: e.tensor_sub(grlo[:, :], gw32[:, :], STG[:, SE + 64:SE + 128]))
    v_op(lambda e: e.tensor_copy(bchi[:, :], ln2b[:, :]))
    v_op(lambda e: e.tensor_copy(STG[:, SE + 64:SE + 72], bchi[:, :]))
    vt = v_op(lambda e: e.tensor_sub(bclo[:, :], ln2b[:, :], STG[:, SE + 64:SE + 72]), "pv")
    done(("stg", SE), vt)
    W("t", vt)
    need("t", "psC0")
    np_ = 0
    for c in range(8):
        for bb, gg in ((bchi, grhi), (bchi, grlo), (bclo, grhi)):
            np_ += 1
            tk = t_op(lambda e, c_=c, b_=bb, g_=gg, f_=(np_ == 1), l_=(np_ == 24): e.matmul(
                PS_C[0:1, 0:8], b_[:, c_:c_ + 1], g_[:, c_ * 8:(c_ + 1) * 8],
                start=f_, stop=l_), "pt" if np_ == 24 else None)
    W("v", tk)
    v_op(lambda e: e.tensor_copy(bpr[0:1, 0:8], PS_C[0:1, 0:8]))
    v_op(lambda e: e.tensor_add(bpr[0:1, 0:8], bpr[0:1, 0:8], gateb[0:1, :]))
    v_op(lambda e: e.tensor_copy(bprh[0:1, :], bpr[0:1, 0:8]))
    v_op(lambda e: e.tensor_copy(bpr[0:1, 8:16], bprh[0:1, :]))
    vt = v_op(lambda e: e.tensor_sub(bprl[0:1, :], bpr[0:1, 0:8], bpr[0:1, 8:16]), "pv")
    done("psC0", vt)
    W("t", vt)
    need("t", "psC0")
    t_op(lambda e: e.matmul(PS_C[:, 0:8], onesr[0:1, 0:128], bprh[0:1, :],
                            start=True, stop=False))
    tk = t_op(lambda e: e.matmul(PS_C[:, 0:8], onesr[0:1, 0:128], bprl[0:1, :],
                                 start=False, stop=True), "pt")
    W("v", tk)
    gp_tok = v_op(lambda e: e.tensor_copy(bprbc[:, :], PS_C[:, 0:8]), "pv")
    done("psC0", gp_tok)

    W("t", kb.tok("dg"))
    for i in range(NC):
        c8 = i * 8
        need("y", ("stg", SA))
        xtok = y_dma(STG[:, SA:SA + 1024], dr["x"][i * 128:(i + 1) * 128, :])
        need("t", "psB0", "psB1")
        W("t", ffn_tok)
        W("t", wo2_tok)
        for nb in range(2):
            for c in range(8):
                t_op(lambda e, i_=i, nb_=nb, c_=c: e.matmul(
                    PS_B[:, nb_ * 512:nb_ * 512 + 512],
                    AR1[:, c_ * 2048 + i_ * 128: c_ * 2048 + (i_ + 1) * 128],
                    WAR[:, O_WO2 + c_ * 1024 + nb_ * 512: O_WO2 + c_ * 1024 + nb_ * 512 + 512],
                    start=(c_ == 0), stop=False))
            tk = t_op(lambda e, i_=i, nb_=nb: e.matmul(
                PS_B[:, nb_ * 512:nb_ * 512 + 512],
                onesr[0:1, i_ * 128:(i_ + 1) * 128],
                bo_row[0:1, nb_ * 512:nb_ * 512 + 512], start=False, stop=True), "pt")
        W("v", xtok)
        need("v", ("stg", SD))
        st_ = v_op(lambda e, i_=i: e.tensor_scalar(
            STG[:, SD:SD + 1024], STG[:, SA:SA + 1024],
            stats_r[:, i_:i_ + 1], stats_mr[:, i_:i_ + 1],
            op0=ALU.mult, op1=ALU.add), "pv")
        done(("stg", SA), st_)
        W("v", tk)
        need("v", ("stg", SB_))
        vt = v_op(lambda e: e.tensor_add(STG[:, SB_:SB_ + 1024], PS_B[:, :],
                                         STG[:, SD:SD + 1024]), "pv")
        done("psB0", vt); done("psB1", vt)
        done(("stg", SD), vt)
        W("y", vt)
        dt_ = y_dma(x2_dram[i * 128:(i + 1) * 128, :], STG[:, SB_:SB_ + 1024])
        if "x2" in tap_req:
            dt_ = y_dma(tap_d["x2"][i * 128:(i + 1) * 128, :], STG[:, SB_:SB_ + 1024])
        rt = v_op(lambda e, c_=c8: e.reduce_sum(sm[:, c_:c_ + 1], STG[:, SB_:SB_ + 1024],
                                                axis=AX), "pv", fence=True)
        W("v", rt)
        vt = v_op(lambda e, c_=c8: e.tensor_scalar_mul(sm[:, c_ + 1:c_ + 2],
                                                       sm[:, c_:c_ + 1], -1.0 / D), "pv")
        need("v", ("stg", SC_))
        st_ = v_op(lambda e, c_=c8: e.tensor_scalar(
            STG[:, SC_:SC_ + 1024], STG[:, SB_:SB_ + 1024],
            sm[:, c_ + 1:c_ + 2], None, op0=ALU.add), "pv")
        done(("stg", SB_), (("dy", dt_[1])))
        need("v", ("stg", SD))
        v_op(lambda e: e.tensor_mul(STG[:, SD:SD + 1024], STG[:, SC_:SC_ + 1024],
                                    STG[:, SC_:SC_ + 1024]))
        vt = v_op(lambda e, c_=c8: e.reduce_sum(sm[:, c_ + 2:c_ + 3],
                                                STG[:, SD:SD + 1024], axis=AX), "pv", fence=True)
        done(("stg", SD), vt)
        W("s", vt)
        s_op(lambda e, c_=c8: e.activation(sm[:, c_ + 3:c_ + 4], sm[:, c_ + 2:c_ + 3],
                                           AF.Ln, bias=eps_c[:, 0:1], scale=1.0 / D))
        st2 = s_op(lambda e, c_=c8: e.activation(sm[:, c_ + 4:c_ + 5], sm[:, c_ + 3:c_ + 4],
                                           AF.Exp, bias=0.0, scale=-0.5), "ps")
        W("v", st2)
        st_ = v_op(lambda e, c_=c8: e.tensor_scalar(
            STG[:, SC_:SC_ + 1024], STG[:, SC_:SC_ + 1024],
            sm[:, c_ + 4:c_ + 5], None, op0=ALU.mult), "pv")
        W("t", st_)
        need("t", "psA0")
        for c in range(8):
            tk = t_op(lambda e, c_=c: e.transpose(
                PS_A[:, c_ * 128:(c_ + 1) * 128],
                STG[:, SC_ + c_ * 128:SC_ + (c_ + 1) * 128], idn[:, :]),
                "pt" if c == 7 else None)
        done(("stg", SC_), tk)
        W("v", tk)
        for c in range(8):
            v_op(lambda e, i_=i, c_=c: e.tensor_scalar(
                AR2[:, c_ * 2048 + i_ * 128: c_ * 2048 + (i_ + 1) * 128],
                PS_A[:, c_ * 128:(c_ + 1) * 128],
                ln2g[:, c_:c_ + 1], ln2b[:, c_:c_ + 1], op0=ALU.mult, op1=ALU.add))
        v_op(lambda e: e.tensor_copy(lghi[:, :], PS_A[:, 0:1024]))
        need("v", ("stg", SD))
        v_op(lambda e: e.tensor_copy(STG[:, SD:SD + 1024], lghi[:, :]))
        vt = v_op(lambda e: e.tensor_sub(lglo[:, :], PS_A[:, 0:1024],
                                         STG[:, SD:SD + 1024]), "pv")
        done("psA0", vt)
        done(("stg", SD), vt)
        W("t", vt); W("t", gp_tok)
        need("t", "psC0")
        np2 = 0
        for c in range(8):
            for lhs, rhs in ((lghi, gwhi), (lghi, gwlo), (lglo, gwhi)):
                np2 += 1
                tk = t_op(lambda e, c_=c, l_=lhs, r_=rhs, f_=(np2 == 1),
                          z_=(np2 == 24): e.matmul(
                    PS_C[:, 0:8], l_[:, c_ * 128:(c_ + 1) * 128],
                    r_[:, c_ * 8:(c_ + 1) * 8], start=f_, stop=z_),
                    "pt" if np2 == 24 else None)
        W("v", tk)
        need("v", ("stg", SE))
        v_op(lambda e: e.tensor_add(STG[:, SE:SE + 8], PS_C[:, 0:8], bprbc[:, :]))
        vt = v_op(lambda e, c_=c8: e.reduce_max(sm[:, c_ + 5:c_ + 6],
                                                STG[:, SE:SE + 8], axis=AX, negate=True), "pv", fence=True)
        done("psC0", vt)
        W("v", vt)
        v_op(lambda e, c_=c8: e.tensor_scalar(sm[:, c_ + 6:c_ + 7], sm[:, c_ + 5:c_ + 6],
                                              -1.0, None, op0=ALU.mult))
        v_op(lambda e, c_=c8: e.tensor_scalar(
            STG[:, SE + 8:SE + 16], STG[:, SE:SE + 8],
            sm[:, c_ + 6:c_ + 7], None, op0=ALU.is_equal))
        v_op(lambda e, c_=c8: e.scalar_tensor_tensor(
            STG[:, SE + 16:SE + 24], STG[:, SE + 8:SE + 16], -1e30,
            STG[:, SE:SE + 8], op0=ALU.mult, op1=ALU.add))
        vt = v_op(lambda e, c_=c8: e.reduce_max(sm[:, c_ + 7:c_ + 8],
                                                STG[:, SE + 16:SE + 24], axis=AX), "pv", fence=True)
        W("v", vt)
        v_op(lambda e, c_=c8: e.tensor_scalar(
            STG[:, SE + 24:SE + 32], STG[:, SE + 16:SE + 24],
            sm[:, c_ + 7:c_ + 8], None, op0=ALU.is_equal))
        W("s", kb.tok("pv"))
        st_ = s_op(lambda e, c_=c8: e.activation(
            sm[:, c_ + 2:c_ + 3], sm[:, c_ + 7:c_ + 8], AF.Exp,
            bias=sm[:, c_ + 5:c_ + 6], scale=1.0), "ps")
        W("v", st_)
        v_op(lambda e, c_=c8: e.tensor_scalar(sm[:, c_ + 3:c_ + 4], sm[:, c_ + 2:c_ + 3],
                                              1.0, None, op0=ALU.add))
        rt = v_op(lambda e, c_=c8: e.reciprocal(sm[:, c_ + 0:c_ + 1],
                                                sm[:, c_ + 3:c_ + 4]), "pv", fence=True)
        W("v", rt)
        v_op(lambda e, c_=c8: e.tensor_mul(sm[:, c_ + 1:c_ + 2], sm[:, c_ + 2:c_ + 3],
                                           sm[:, c_ + 0:c_ + 1]))
        v_op(lambda e, c_=c8: e.tensor_scalar(
            STG[:, SE + 24:SE + 32], STG[:, SE + 24:SE + 32],
            sm[:, c_ + 1:c_ + 2], None, op0=ALU.mult))
        vt = v_op(lambda e, i_=i, c_=c8: e.scalar_tensor_tensor(
            ew_all[:, i_ * 8:(i_ + 1) * 8], STG[:, SE + 8:SE + 16],
            sm[:, c_ + 0:c_ + 1], STG[:, SE + 24:SE + 32],
            op0=ALU.mult, op1=ALU.add), "pv")
        done(("stg", SE), vt)
    h2_tok = kb.tok("pv")
    tap_fm("h2", AR2)
    if "ew" in tap_req:
        for i in range(NC):
            W("y", h2_tok)
            y_dma(tap_d["ew"][i * 128:(i + 1) * 128, :], ew_all[:, i * 8:(i + 1) * 8])
    if stages <= 5:
        return _finish(nc, kb, ctx)

    for i in range(NC):
        need("t", "psC0")
        W("t", h2_tok)
        tk = t_op(lambda e, i_=i: e.transpose(
            PS_C[0:8, 0:128], ew_all[:, i_ * 8:(i_ + 1) * 8], idn[:, :]), "pt")
        W("v", tk)
        vt = v_op(lambda e, i_=i: e.tensor_copy(
            ewCb[:, i_ * 128:(i_ + 1) * 128], PS_C[0:8, 0:128]), "pv")
        done("psC0", vt)
    ew_tok = kb.tok("pv")

    W("g", kb.tok("pt"))
    b2t = g_dma(b2bf, dr["b2"][:, :])
    for tile in range(2):
        first_acc = True
        for e_ in range(E):
            W("g", ew_tok)
            need("g", "ewrbf")
            gt = g_dma(ew_rbf[0:1, :], ewCb[e_:e_ + 1, tile * 1024:(tile + 1) * 1024])
            W("t", gt)
            need("t", "psC0", "psC1")
            t_op(lambda e: e.matmul(PS_C[:, 0:512], onesr[0:1, 0:128],
                                    ew_rbf[0:1, 0:512], start=True, stop=True))
            tk = t_op(lambda e: e.matmul(PS_C[:, 512:1024], onesr[0:1, 0:128],
                                         ew_rbf[0:1, 512:1024], start=True, stop=True), "pt")
            W("v", tk)
            need("v", "ewbc")
            vt = v_op(lambda e: e.tensor_copy(ew_bc[:, :], PS_C[:, 0:1024]), "pv")
            done("psC0", vt); done("psC1", vt)
            need("v", "h2s")
            for c in range(8):
                vt = v_op(lambda e, c_=c, t_=tile: e.tensor_mul(
                    h2s[:, c_ * 1024:(c_ + 1) * 1024],
                    AR2[:, c_ * 2048 + t_ * 1024: c_ * 2048 + t_ * 1024 + 1024],
                    ew_bc[:, :]), "pv" if c == 7 else None)
            h2s_tok = vt
            done("ewbc", vt)

            for grp in range(2):
                need("g", "b1row")
                b1t = g_dma(b1row[0:1, :],
                            dr["b1"][e_:e_ + 1, grp * 2048:(grp + 1) * 2048])
                for fi in range(16):
                    slab = fi // 4
                    if fi % 4 == 0:
                        sl1 = slab % 2
                        fbase = grp * 2048 + slab * 512
                        need("g", ("w1s", sl1))
                        w1t = g_dma(
                            w1buf[:, sl1 * 4096:sl1 * 4096 + 4096].rearrange(
                                "p (c f) -> p c f", c=8),
                            dr["W1"][e_, :, fbase:fbase + 512].rearrange(
                                "(c p) f -> p c f", p=128))
                    psk = "psB0" if fi % 2 == 0 else "psA1"
                    ps = PS_B if fi % 2 == 0 else PS_A
                    pso = 0 if fi % 2 == 0 else 1024
                    need("t", psk)
                    if fi % 2 == 0:
                        need("t", "psB1")
                    W("t", w1t)
                    W("t", h2s_tok)
                    W("t", b1t)
                    o1 = (slab % 2) * 4096
                    fr = fi % 4
                    for nb2 in range(2):
                        for c in range(8):
                            t_op(lambda e, c_=c, o_=o1, fr_=fr, nb_=nb2, p_=ps, po_=pso:
                                 e.matmul(
                                p_[:, po_ + nb_ * 512:po_ + nb_ * 512 + 512],
                                w1buf[:, o_ + c_ * 512 + fr_ * 128:
                                      o_ + c_ * 512 + fr_ * 128 + 128],
                                h2s[:, c_ * 1024 + nb_ * 512:c_ * 1024 + nb_ * 512 + 512],
                                start=(c_ == 0), stop=False))
                        tk = t_op(lambda e, fi_=fi, nb_=nb2, p_=ps, po_=pso:
                                  e.matmul(
                            p_[:, po_ + nb_ * 512:po_ + nb_ * 512 + 512],
                            b1row[0:1, fi_ * 128:(fi_ + 1) * 128],
                            ew_rbf[0:1, nb_ * 512:nb_ * 512 + 512],
                            start=False, stop=True), "pt")
                    if fi % 4 == 3:
                        done(("w1s", slab % 2), tk)
                    W("v", tk)
                    need("v", "hid")
                    vt = v_op(lambda e, fi_=fi, p_=ps, po_=pso: e.tensor_single_scalar(
                        AR1[:, fi_ * 1024:(fi_ + 1) * 1024],
                        p_[:, po_:po_ + 1024], 0.0, op=ALU.max), "pv")
                    done(psk, vt)
                    if fi % 2 == 0:
                        done("psB1", vt)
                hid_tok = kb.tok("pv")

                for dk in range(8):
                    sl2 = dk % 2
                    need("g", ("w2s", sl2))
                    fb = grp * 2048
                    w2t1 = g_dma(
                        w2buf[:, sl2 * 2048:sl2 * 2048 + 1024].rearrange(
                            "p (c d) -> p c d", c=8),
                        dr["W2"][e_, fb:fb + 1024, dk * 128:(dk + 1) * 128].rearrange(
                            "(c p) d -> p c d", p=128))
                    w2t = g_dma(
                        w2buf[:, sl2 * 2048 + 1024:sl2 * 2048 + 2048].rearrange(
                            "p (c d) -> p c d", c=8),
                        dr["W2"][e_, fb + 1024:fb + 2048, dk * 128:(dk + 1) * 128].rearrange(
                            "(c p) d -> p c d", p=128))
                    psk = "psA0" if dk % 2 == 0 else "psC0"
                    ps = PS_A if dk % 2 == 0 else PS_C
                    need("t", psk)
                    if dk % 2 == 1:
                        need("t", "psC1")
                    W("t", w2t)
                    W("t", hid_tok)
                    for fi in range(16):
                        for nb2 in range(2):
                            tk = t_op(lambda e, fi_=fi, nb_=nb2, o_=sl2 * 2048, p_=ps:
                                      e.matmul(
                                p_[:, nb_ * 512:nb_ * 512 + 512],
                                w2buf[:, o_ + fi_ * 128:o_ + (fi_ + 1) * 128],
                                AR1[:, fi_ * 1024 + nb_ * 512:fi_ * 1024 + nb_ * 512 + 512],
                                start=(fi_ == 0), stop=(fi_ == 15 and nb_ == 1)),
                                "pt" if (fi == 15 and nb2 == 1) else None)
                    done(("w2s", sl2), tk)
                    W("v", tk)
                    if first_acc:
                        need("v", "moeT")
                        vt = v_op(lambda e, dk_=dk, p_=ps: e.tensor_copy(
                            MOE[:, dk_ * 1024:(dk_ + 1) * 1024], p_[:, 0:1024]), "pv")
                    else:
                        vt = v_op(lambda e, dk_=dk, p_=ps: e.tensor_add(
                            MOE[:, dk_ * 1024:(dk_ + 1) * 1024], p_[:, 0:1024],
                            MOE[:, dk_ * 1024:(dk_ + 1) * 1024]), "pv")
                    done(psk, vt)
                    if dk % 2 == 1:
                        done("psC1", vt)
                done("hid", kb.tok("pt"))
                done("b1row", kb.tok("pt"))
                first_acc = False
            done("h2s", kb.tok("pt"))
            done("ewrbf", kb.tok("pt"))
        moe_tok = kb.tok("pv")

        for j in range(8):
            i = tile * 8 + j
            need("y", ("stg", SA), ("stg", SB_), ("stg", SC_))
            x2t = y_dma(STG[:, SA:SA + 1024], x2_dram[i * 128:(i + 1) * 128, :])
            stt_ = y_dma(STG[:, SB_:SB_ + 1024], s_dram[i * 128:(i + 1) * 128, :])
            xt = y_dma(STG[:, SC_:SC_ + 1024], dr["x"][i * 128:(i + 1) * 128, :])
            need("t", "psC0", "psC1")
            W("t", ew_tok)
            W("t", b2t)
            tk0 = t_op(lambda e, i_=i: e.matmul(
                PS_C[:, 0:512], ewCb[0:8, i_ * 128:(i_ + 1) * 128],
                b2bf[:, 0:512], start=True, stop=True))
            tk0 = t_op(lambda e, i_=i: e.matmul(
                PS_C[:, 512:1024], ewCb[0:8, i_ * 128:(i_ + 1) * 128],
                b2bf[:, 512:1024], start=True, stop=True), "pt")
            need("t", "psB0", "psB1")
            W("t", moe_tok)
            for dk in range(8):
                tk1 = t_op(lambda e, dk_=dk, j_=j: e.transpose(
                    PS_B[:, dk_ * 128:(dk_ + 1) * 128],
                    MOE[:, dk_ * 1024 + j_ * 128:dk_ * 1024 + (j_ + 1) * 128],
                    idn[:, :]), "pt" if dk == 7 else None)
            W("v", tk0); W("v", x2t)
            need("v", ("stg", SD))
            v_op(lambda e: e.tensor_add(STG[:, SD:SD + 1024], PS_C[:, 0:1024],
                                        STG[:, SA:SA + 1024]))
            W("v", tk1)
            vt = v_op(lambda e: e.tensor_add(STG[:, SD:SD + 1024], PS_B[:, 0:1024],
                                             STG[:, SD:SD + 1024]), "pv")
            done("moeT", kb.tok("pt"))
            done("psB0", vt); done("psB1", vt); done("psC0", vt); done("psC1", vt)
            done(("stg", SA), vt)
            if "gpre" in tap_req:
                W("y", vt)
                y_dma(tap_d["gpre"][i * 128:(i + 1) * 128, :], STG[:, SD:SD + 1024])
            W("s", vt)
            need("s", ("stg", SF))
            st2 = s_op(lambda e: e.activation(STG[:, SF:SF + 1024], STG[:, SD:SD + 1024],
                                              AF.Silu, bias=0.0, scale=1.0), "ps")
            done(("stg", SD), st2)
            W("v", st2); W("v", stt_); W("v", xt)
            v_op(lambda e: e.tensor_mul(STG[:, SF:SF + 1024], STG[:, SF:SF + 1024],
                                        STG[:, SB_:SB_ + 1024]))
            vt = v_op(lambda e: e.tensor_add(STG[:, SF:SF + 1024], STG[:, SF:SF + 1024],
                                             STG[:, SC_:SC_ + 1024]), "pv")
            done(("stg", SB_), vt)
            done(("stg", SC_), vt)
            W("y", vt)
            ot = y_dma(out_d[i * 128:(i + 1) * 128, :], STG[:, SF:SF + 1024], inc="od")
            done(("stg", SF), ot)

    return _finish(nc, kb, ctx)


def _finish(nc, kb, ctx):
    kb.wait("y", kb.tok("od"))
    kb.wait("y", kb.tok("dy"))
    kb.wait("g", kb.tok("dg"))
    with nc.Block() as block:
        @block.tensor
        def _(e):
            kb.replay("t", e)
        @block.vector
        def _(e):
            kb.replay("v", e)
        @block.scalar
        def _(e):
            kb.replay("s", e)
        @block.gpsimd
        def _(e):
            kb.replay("g", e)
        @block.sync
        def _(e):
            kb.replay("y", e)
    ctx.close()
    return nc


def _consts():
    inv = 1.0 / 10000.0 ** (np.arange(0, DR, 2, dtype=np.float32) / DR)
    tt = np.arange(T, dtype=np.float32)
    fr = np.outer(tt, inv)
    cos, sin = np.cos(fr).astype(np.float32), np.sin(fr).astype(np.float32)
    idn = np.eye(128, dtype=np.float32)
    masks = np.zeros((128, 4 * 512), dtype=ml_dtypes.bfloat16)
    for a, r in enumerate((0, 128, 256, 384)):
        ii = np.arange(128)[:, None] + r
        jj = np.arange(512)[None, :]
        masks[:, a * 512:(a + 1) * 512] = (ii <= jj).astype(ml_dtypes.bfloat16)
    return {"c_cos": cos, "c_sin": sin, "c_idn": idn, "c_maskT": masks,
            "c_ones": np.ones((1, T), dtype=ml_dtypes.bfloat16),
            "c_onescol": np.ones((128, 1), dtype=ml_dtypes.bfloat16)}


def _prep_shared(inputs):
    f = np.ascontiguousarray
    sh = {
        "w_kv_down": f(inputs["L0_attn_W_kv_down"]),
        "w_k_up": f(inputs["L0_attn_W_k_up"]),
        "w_v_up": f(inputs["L0_attn_W_v_up"]),
        "w_q": f(inputs["L0_attn_W_q"]),
        "w_q_rope": f(inputs["L0_attn_W_q_rope"]),
        "w_k_rope": f(inputs["L0_attn_W_k_rope"]),
        "w_o1": f(inputs["L0_attn_W_o"]),
        "ln1_g": f(inputs["L0_ffn_ln1_g"].reshape(8, 128).T),
        "ln1_b": f(inputs["L0_ffn_ln1_b"].reshape(8, 128).T),
        "w_qkv": f(inputs["L0_ffn_w_qkv"]),
        "b_qkv_q": f(inputs["L0_ffn_b_qkv"][0:D].reshape(16, 64).T),
        "b_qkv_k": f(inputs["L0_ffn_b_qkv"][D:2 * D].reshape(16, 64).T),
        "b_qkv_v": f(inputs["L0_ffn_b_qkv"][2 * D:3 * D].reshape(1, D)),
        "w_o2": f(inputs["L0_ffn_w_o"]),
        "b_o": f(inputs["L0_ffn_b_o"].reshape(1, D)),
        "ln2_g": f(inputs["L0_ffn_ln2_g"].reshape(8, 128).T),
        "ln2_b": f(inputs["L0_ffn_ln2_b"].reshape(8, 128).T),
        "gate_w": f(inputs["L0_ffn_gate_w"].reshape(8, 128, 8).transpose(1, 0, 2).reshape(128, 64)),
        "gate_b": f(inputs["L0_ffn_gate_b"].reshape(1, 8)),
        "W1": f(inputs["L0_ffn_W1"]),
        "b1": f(inputs["L0_ffn_b1"]),
        "W2": f(inputs["L0_ffn_W2"]),
        "b2": f(inputs["L0_ffn_b2"]),
    }
    sh.update(_consts())
    return sh


def kernel(**inputs):
    x = np.ascontiguousarray(np.asarray(inputs["L0_attn_x"], dtype=np.float32))
    shared = _prep_shared({k: np.asarray(v) for k, v in inputs.items()})
    nc = build_program(taps=TAPS, stages=STAGES)
    in_maps = []
    for c in range(8):
        m = dict(shared)
        m["x"] = x[c]
        in_maps.append(m)
    res = run_bass_kernel_spmd(nc, in_maps, core_ids=list(range(8)))
    out = np.stack([np.asarray(r["out"], dtype=np.float32) for r in res.results], axis=0)
    if TAPS:
        kernel.tap_results = res.results
    return out


# version 09
# speedup vs baseline: 1.5413x; 1.2669x over previous
import math
import numpy as np
import ml_dtypes
import concourse.bass as bass
import concourse.mybir as mybir
from concourse.bass_utils import run_bass_kernel_spmd

F32 = mybir.dt.float32
BF16 = mybir.dt.bfloat16
AX = mybir.AxisListType.X
ALU = mybir.AluOpType
AF = mybir.ActivationFunctionType

T, D, NC, DC = 2048, 1024, 16, 8
NH, HD, DR, DKV = 16, 64, 32, 128
E, DFF = 8, 4096
SC_MLA = 1.0 / math.sqrt(HD + DR)
SC_FFN = 1.0 / math.sqrt(HD)
EPS = 1e-5

TAPS = ()
STAGES = 99


class KB:
    def __init__(self, nc):
        self.nc = nc
        self.streams = {n: [] for n in ("t", "v", "s", "g", "y")}
        self.cnt = {}
        self.handles = {}
        self.lastw = {}
        self.free = {}

    def sem(self, name):
        self.handles[name] = self.nc.semaphore(name).__enter__()
        self.cnt[name] = 0

    def wait(self, st, tok):
        if tok is None:
            return
        sem, val = tok
        assert val <= self.cnt[sem], f"wait {sem}>={val} > emitted {self.cnt[sem]}"
        if self.lastw.get((st, sem), 0) < val:
            self.streams[st].append(("w", sem, val))
            self.lastw[(st, sem)] = val

    AUTO_SERIAL = {"v": "pv", "s": "ps"}

    def op(self, st, fn, inc=None, n=1, fence=False):
        auto = self.AUTO_SERIAL.get(st)
        if auto is not None:
            self.wait(st, (auto, self.cnt[auto]))
            if inc is None:
                inc, n = auto, 1
        self.streams[st].append(("c", fn, inc, n))
        if inc is not None:
            self.cnt[inc] += n
            return (inc, self.cnt[inc])
        return None

    def tok(self, sem):
        return (sem, self.cnt[sem])

    def need(self, st, *keys):
        for k in keys:
            self.wait(st, self.free.get(k))

    def done(self, key, tok):
        self.free[key] = tok

    def barrier(self):
        for st in ("t", "v", "s", "g", "y"):
            for sem in ("pt", "pv", "ps", "dy", "dg"):
                self.wait(st, (sem, self.cnt[sem]))

    def replay(self, st, eng):
        for item in self.streams[st]:
            if item[0] == "w":
                eng.wait_ge(self.handles[item[1]], item[2])
            else:
                _, fn, inc, n = item
                ins = fn(eng)
                if inc is not None:
                    ins.then_inc(self.handles[inc], n)


def build_program(taps=(), stages=99):
    nc = bass.Bass("TRN2", target_bir_lowering=False)
    tap_req = set(taps)

    dr = {}
    def din(name, shape, dt=F32):
        dr[name] = nc.dram_tensor(name, list(shape), dt, kind="ExternalInput").ap()
    din("x", (T, D))
    din("w_kv_down", (D, DKV)); din("w_k_up", (DKV, D)); din("w_v_up", (DKV, D))
    din("w_q", (D, D)); din("w_q_rope", (D, DR)); din("w_k_rope", (DKV, DR))
    din("w_o1", (D, D))
    din("ln1_g", (128, 8)); din("ln1_b", (128, 8))
    din("w_qkv", (D, 3 * D))
    din("b_qkv_q", (64, 16)); din("b_qkv_k", (64, 16)); din("b_qkv_v", (1, D))
    din("w_o2", (D, D)); din("b_o", (1, D))
    din("ln2_g", (128, 8)); din("ln2_b", (128, 8))
    din("gate_w", (128, 64)); din("gate_b", (1, 8))
    din("W1", (E, D, DFF)); din("b1", (E, DFF))
    din("W2", (E, DFF, D)); din("b2", (E, D))
    din("c_idn", (128, 128))
    din("c_cos", (T, 16)); din("c_sin", (T, 16))
    din("c_maskT", (128, 4 * 512), BF16)
    din("c_ones", (1, T), BF16)
    din("c_onescol", (128, 1), BF16)
    out_d = nc.dram_tensor("out", [T, D], F32, kind="ExternalOutput").ap()
    tap_d = {}
    for tp in tap_req - {"min0", "noW", "serial"}:
        shp = {"xn": (D, T), "h": (D, T), "ckv": (DKV, T), "q": (D, T),
               "s": (T, D), "o": (D, T), "x2": (T, D), "h2": (D, T),
               "ew": (T, 8), "gpre": (T, D), "xnt": (T, D),
               "stm": (128, 16), "str": (128, 16)}[tp]
        tap_d[tp] = nc.dram_tensor("tap_" + tp, list(shp), F32, kind="ExternalOutput").ap()
    s_dram = nc.dram_tensor("s_scr", [T, D], F32, kind="Internal").ap()
    x2_dram = nc.dram_tensor("x2_scr", [T, D], F32, kind="Internal").ap()

    from contextlib import ExitStack
    ctx = ExitStack()
    def sb(name, shape, dt):
        return ctx.enter_context(nc.sbuf_tensor(name, list(shape), dt)).ap()

    AR1 = sb("AR1", (128, 16384), BF16)
    AR2 = sb("AR2", (128, 16384), BF16)
    AR3 = sb("AR3", (128, 16384), BF16)
    MOE = AR3.bitcast(F32)
    WAR = sb("WAR", (128, 20480), BF16)
    CKV = sb("CKV", (128, T), BF16)
    ROPE = sb("ROPE", (128, T), BF16)
    STG = sb("STG", (128, 5632), F32)
    q_st = sb("q_st", (128, T), BF16)
    k_st = sb("k_st", (128, T), BF16)
    v_st = sb("v_st", (128, NC * 65), BF16)
    sst = sb("sst", (128, 2 * 1024), BF16)
    ptb = sb("ptb", (128, 4 * 1024), BF16)
    m_all = sb("m_all", (128, 16), F32)
    mT = sb("mT", (16, 128), F32)
    stats_m = sb("stats_m", (128, 16), F32)
    stats_r = sb("stats_r", (128, 16), F32)
    stats_mr = sb("stats_mr", (128, 16), F32)
    sm = sb("sm", (128, 128), F32)
    eps_c = sb("eps_c", (128, 1), F32)
    idn = sb("idn", (128, 128), F32)
    cost = sb("cost", (128, NC * 16), F32)
    sint = sb("sint", (128, NC * 16), F32)
    maskT = sb("maskT", (128, 4 * 512), BF16)
    onesr = sb("onesr", (1, T), BF16)
    onesc = sb("onesc", (128, 1), BF16)
    ln1g = sb("ln1g", (128, 8), F32); ln1b = sb("ln1b", (128, 8), F32)
    ln2g = sb("ln2g", (128, 8), F32); ln2b = sb("ln2b", (128, 8), F32)
    bqq = sb("bqq", (64, 16), F32); bqk = sb("bqk", (64, 16), F32)
    bqv = sb("bqv", (1, D), BF16)
    bo_row = sb("bo_row", (1, D), BF16)
    gw32 = sb("gw32", (128, 64), F32)
    gwhi = sb("gwhi", (128, 64), BF16); gwlo = sb("gwlo", (128, 64), BF16)
    grhi = sb("grhi", (128, 64), BF16); grlo = sb("grlo", (128, 64), BF16)
    bchi = sb("bchi", (128, 8), BF16); bclo = sb("bclo", (128, 8), BF16)
    bpr = sb("bpr", (1, 16), F32)
    bprh = sb("bprh", (1, 8), BF16); bprl = sb("bprl", (1, 8), BF16)
    bprbc = sb("bprbc", (128, 8), F32)
    gateb = sb("gateb", (1, 8), F32)
    ew_all = sb("ew_all", (128, 128), F32)
    ewCb = ROPE[0:8, 0:2048]
    nstg = ROPE[0:64, 0:512]
    ew_rbf = CKV[0:1, 1024:2048]
    ew_bc = CKV[:, 0:1024]
    whb = AR3[:, 0:4096]
    w1buf = WAR[:, 0:8192]
    w2buf = WAR[:, 8192:12288]
    h2s = WAR[:, 12288:20480]
    sstb = None
    b2bf = q_st[0:8, 0:1024]
    invhb = sb("invhb", (1, 512), BF16)
    invlb = sb("invlb", (1, 512), BF16)

    PS_A = ctx.enter_context(nc.psum_tensor("PS_A", [128, 2048], F32)).ap()
    PS_B = ctx.enter_context(nc.psum_tensor("PS_B", [128, 1024], F32)).ap()
    PS_C = ctx.enter_context(nc.psum_tensor("PS_C", [128, 1024], F32)).ap()

    kb = KB(nc)
    for s_ in ("pt", "pv", "ps", "pg", "dy", "dg", "od"):
        kb.sem(s_)
    W, need, done = kb.wait, kb.need, kb.done

    def t_op(fn, inc=None):
        return kb.op("t", fn, inc)
    def v_op(fn, inc=None, fence=False):
        return kb.op("v", fn, inc, fence=fence)
    def s_op(fn, inc=None, fence=False):
        return kb.op("s", fn, inc, fence=fence)
    def y_dma(dst, src, inc="dy"):
        return kb.op("y", lambda e, d=dst, s2=src: e.dma_start(out=d, in_=s2), inc, 16)
    def g_dma(dst, src, inc="dg"):
        return kb.op("g", lambda e, d=dst, s2=src: e.dma_start(out=d, in_=s2), inc, 16)

    SA, SB_, SC_, SD, SE, SF = 0, 1024, 2048, 3072, 4096, 4608
    inv_h = STG[0:1, SD + 512:SD + 1024]
    invu2 = STG[0:1, SE:SE + 512]
    invbc = STG[0:64, SD:SD + 512]
    lghi = sst.bitcast(BF16)[:, 0:1024]
    lglo = sst.bitcast(BF16)[:, 1024:2048]
    b1row = sst.bitcast(BF16)[0:1, 0:2048]

    y_dma(idn[:, :], dr["c_idn"][:, :])
    y_dma(cost.rearrange("p (c f) -> p c f", c=16), dr["c_cos"].rearrange("(c p) f -> p c f", p=128))
    y_dma(sint.rearrange("p (c f) -> p c f", c=16), dr["c_sin"].rearrange("(c p) f -> p c f", p=128))
    y_dma(maskT[:, :], dr["c_maskT"][:, :])
    y_dma(onesr[:, :], dr["c_ones"][:, :])
    y_dma(onesc[:, :], dr["c_onescol"][:, :])
    y_dma(ln1g[:, :], dr["ln1_g"][:, :]); y_dma(ln1b[:, :], dr["ln1_b"][:, :])
    y_dma(ln2g[:, :], dr["ln2_g"][:, :]); y_dma(ln2b[:, :], dr["ln2_b"][:, :])
    y_dma(bqq[:, :], dr["b_qkv_q"][:, :]); y_dma(bqk[:, :], dr["b_qkv_k"][:, :])
    y_dma(gw32[:, :], dr["gate_w"][:, :])
    y_dma(gateb[:, :], dr["gate_b"][:, :])
    consts_tok = kb.tok("dy")

    O_WO2 = 0
    O_WKV, O_WQ, O_WQR, O_WKR = 0, 1024, 9216, 9472
    O_WKUP, O_WVUP, O_WO1 = 9504, 10528, 11552
    if "noW" not in tap_req:
        g_dma(WAR[:, O_WKV:O_WKV + 1024].rearrange("p (c m) -> p c m", c=8), dr["w_kv_down"].rearrange("(c p) m -> p c m", p=128))
        g_dma(WAR[:, O_WQ:O_WQ + 8192].rearrange("p (c m) -> p c m", c=8), dr["w_q"].rearrange("(c p) m -> p c m", p=128))
        g_dma(WAR[:, O_WQR:O_WQR + 256].rearrange("p (c m) -> p c m", c=8), dr["w_q_rope"].rearrange("(c p) m -> p c m", p=128))
        g_dma(WAR[:, O_WKR:O_WKR + 32], dr["w_k_rope"][:, :])
        g_dma(WAR[:, O_WKUP:O_WKUP + 1024], dr["w_k_up"][:, :])
        g_dma(WAR[:, O_WVUP:O_WVUP + 1024], dr["w_v_up"][:, :])
        g_dma(WAR[:, O_WO1:O_WO1 + 8192].rearrange("p (c m) -> p c m", c=8), dr["w_o1"].rearrange("(c p) m -> p c m", p=128))
        g_dma(bqv[:, :], dr["b_qkv_v"][:, :])
        g_dma(bo_row[:, :], dr["b_o"][:, :])
    w1w_tok = kb.tok("dg")

    W("v", consts_tok)
    v_op(lambda e: e.memset(eps_c[:, :], EPS))
    for kc in range(NC):
        vt = v_op(lambda e, kc_=kc: e.tensor_copy(
            v_st[:, kc_ * 65 + 64:kc_ * 65 + 65], onesc[:, 0:1]),
            "pv" if kc == NC - 1 else None)
    vst_ones_tok = vt

    W("t", consts_tok)
    for i in range(NC):
        c8 = i * 8
        need("y", ("stg", SA))
        xtok = y_dma(STG[:, SA:SA + 1024], dr["x"][i * 128:(i + 1) * 128, :])
        W("v", xtok)
        rt = v_op(lambda e, c_=c8: e.reduce_sum(sm[:, c_:c_ + 1], STG[:, SA:SA + 1024],
                                                axis=AX), "pv", fence=True)
        W("v", rt)
        vt = v_op(lambda e, i_=i, c_=c8: e.tensor_scalar_mul(
            stats_m[:, i_:i_ + 1], sm[:, c_:c_ + 1], -1.0 / D), "pv")
        need("v", ("stg", SB_))
        st_ = v_op(lambda e, i_=i: e.tensor_scalar(
            STG[:, SB_:SB_ + 1024], STG[:, SA:SA + 1024],
            stats_m[:, i_:i_ + 1], None, op0=ALU.add), "pv")
        done(("stg", SA), st_)
        need("v", ("stg", SD))
        v_op(lambda e: e.tensor_mul(STG[:, SD:SD + 1024], STG[:, SB_:SB_ + 1024],
                                    STG[:, SB_:SB_ + 1024]))
        vt = v_op(lambda e, c_=c8: e.reduce_sum(sm[:, c_ + 1:c_ + 2],
                                                STG[:, SD:SD + 1024], axis=AX), "pv", fence=True)
        W("s", vt)
        s_op(lambda e, c_=c8: e.activation(sm[:, c_ + 2:c_ + 3], sm[:, c_ + 1:c_ + 2],
                                           AF.Ln, bias=eps_c[:, 0:1], scale=1.0 / D))
        st_ = s_op(lambda e, i_=i, c_=c8: e.activation(
            stats_r[:, i_:i_ + 1], sm[:, c_ + 2:c_ + 3], AF.Exp, bias=0.0, scale=-0.5), "ps")
        W("v", st_)
        v_op(lambda e, i_=i: e.tensor_mul(stats_mr[:, i_:i_ + 1],
                                          stats_m[:, i_:i_ + 1], stats_r[:, i_:i_ + 1]))
        need("v", ("stg", SC_))
        st_ = v_op(lambda e, i_=i: e.tensor_scalar(
            STG[:, SC_:SC_ + 1024], STG[:, SB_:SB_ + 1024],
            stats_r[:, i_:i_ + 1], None, op0=ALU.mult), "pv")
        if "xnt" in tap_req:
            W("y", st_)
            xnt_t = y_dma(tap_d["xnt"][i * 128:(i + 1) * 128, :], STG[:, SC_:SC_ + 1024])
        rt = v_op(lambda e, c_=c8: e.reduce_sum(sm[:, c_ + 3:c_ + 4],
                                                STG[:, SC_:SC_ + 1024], axis=AX), "pv", fence=True)
        W("v", rt)
        vt = v_op(lambda e, c_=c8: e.tensor_scalar_mul(
            sm[:, c_ + 4:c_ + 5], sm[:, c_ + 3:c_ + 4], -1.0 / D), "pv")
        st_ = v_op(lambda e, c_=c8: e.tensor_scalar(
            STG[:, SB_:SB_ + 1024], STG[:, SC_:SC_ + 1024],
            sm[:, c_ + 4:c_ + 5], None, op0=ALU.add), "pv")
        v_op(lambda e: e.tensor_mul(STG[:, SD:SD + 1024], STG[:, SB_:SB_ + 1024],
                                    STG[:, SB_:SB_ + 1024]))
        vt = v_op(lambda e, c_=c8: e.reduce_sum(sm[:, c_ + 5:c_ + 6],
                                                STG[:, SD:SD + 1024], axis=AX), "pv", fence=True)
        done(("stg", SD), vt)
        W("s", vt)
        s_op(lambda e, c_=c8: e.activation(sm[:, c_ + 6:c_ + 7], sm[:, c_ + 5:c_ + 6],
                                           AF.Ln, bias=eps_c[:, 0:1], scale=1.0 / D))
        st_ = s_op(lambda e, c_=c8: e.activation(sm[:, c_ + 7:c_ + 8], sm[:, c_ + 6:c_ + 7],
                                           AF.Exp, bias=0.0, scale=-0.5), "ps")
        W("v", st_)
        st_ = v_op(lambda e, c_=c8: e.tensor_scalar(
            STG[:, SB_:SB_ + 1024], STG[:, SB_:SB_ + 1024],
            sm[:, c_ + 7:c_ + 8], None, op0=ALU.mult), "pv")
        W("t", st_)
        need("t", "psB0", "psB1", "psA0")
        for c in range(8):
            tk1 = t_op(lambda e, c_=c: e.transpose(
                PS_B[:, c_ * 128:(c_ + 1) * 128],
                STG[:, SC_ + c_ * 128:SC_ + (c_ + 1) * 128], idn[:, :]),
                "pt" if c == 7 else None)
        for c in range(8):
            tk2 = t_op(lambda e, c_=c: e.transpose(
                PS_A[:, c_ * 128:(c_ + 1) * 128],
                STG[:, SB_ + c_ * 128:SB_ + (c_ + 1) * 128], idn[:, :]),
                "pt" if c == 7 else None)
        W("v", tk1)
        vt = v_op(lambda e, i_=i: e.tensor_copy(
            AR1.rearrange("p (c t) -> p c t", c=8)[:, :, i_ * 128:(i_ + 1) * 128],
            PS_B.rearrange("p (c t) -> p c t", c=8)[:, :, :]), "pv")
        done("psB0", vt); done("psB1", vt)
        done(("stg", SC_), vt)
        if "min0" in tap_req:
            done("psA0", tk2)
            done(("stg", SB_), tk2)
            continue
        W("v", tk2)
        for c in range(8):
            vt = v_op(lambda e, i_=i, c_=c: e.tensor_scalar(
                AR2[:, c_ * 2048 + i_ * 128: c_ * 2048 + (i_ + 1) * 128],
                PS_A[:, c_ * 128:(c_ + 1) * 128],
                ln1g[:, c_:c_ + 1], ln1b[:, c_:c_ + 1], op0=ALU.mult, op1=ALU.add),
                "pv" if c == 7 else None)
        done("psA0", vt)
        done(("stg", SB_), vt)
        if "serial" in tap_req:
            kb.barrier()

    def tap_fm(name, src, rows=128):
        if name not in tap_req:
            return
        nchunks = {1024: 1}.get(src.shape[1] // 16, DC)
        total = src.shape[1]
        ncol = total // 1024
        for cj in range(ncol):
            need("v", ("stg", SA))
            vt = v_op(lambda e, c_=cj: e.tensor_copy(
                STG[0:rows, SA:SA + 1024], src[0:rows, c_ * 1024:(c_ + 1) * 1024]), "pv")
            W("y", vt)
            r0 = (cj // 2) * 128
            col = (cj % 2) * 1024
            if src.shape[1] == T:
                dt_ = y_dma(tap_d[name][0:rows, cj * 1024:(cj + 1) * 1024],
                            STG[0:rows, SA:SA + 1024])
            else:
                dt_ = y_dma(tap_d[name][r0:r0 + 128, col:col + 1024],
                            STG[0:rows, SA:SA + 1024])
            done(("stg", SA), dt_)

    if "stm" in tap_req:
        W("y", kb.tok("pv"))
        y_dma(tap_d["stm"][:, :], stats_m[:, :])
        y_dma(tap_d["str"][:, :], stats_r[:, :])
    tap_fm("xn", AR1)
    tap_fm("h", AR2)
    if stages <= 0:
        return _finish(nc, kb, ctx)

    W("t", w1w_tok)
    W("t", kb.tok("pv"))
    for nb in range(4):
        slot = "psC%d" % (nb % 2)
        need("t", slot)
        for c in range(8):
            tk = t_op(lambda e, c_=c, nb_=nb: e.matmul(
                PS_C[:, (nb_ % 2) * 512:(nb_ % 2) * 512 + 512],
                WAR[:, O_WKV + c_ * 128:O_WKV + (c_ + 1) * 128],
                AR1[:, c_ * 2048 + nb_ * 512: c_ * 2048 + nb_ * 512 + 512],
                start=(c_ == 0), stop=(c_ == 7)), "pt" if c == 7 else None)
        W("v", tk)
        vt = v_op(lambda e, nb_=nb: e.tensor_copy(
            CKV[:, nb_ * 512:nb_ * 512 + 512],
            PS_C[:, (nb_ % 2) * 512:(nb_ % 2) * 512 + 512]), "pv")
        done(slot, vt)
    tap_fm("ckv", CKV)

    for m in range(8):
        for nb in range(4):
            slot = "psC%d" % (nb % 2)
            need("t", slot)
            for c in range(8):
                tk = t_op(lambda e, c_=c, m_=m, nb_=nb: e.matmul(
                    PS_C[:, (nb_ % 2) * 512:(nb_ % 2) * 512 + 512],
                    WAR[:, O_WQ + c_ * 1024 + m_ * 128: O_WQ + c_ * 1024 + (m_ + 1) * 128],
                    AR1[:, c_ * 2048 + nb_ * 512: c_ * 2048 + nb_ * 512 + 512],
                    start=(c_ == 0), stop=(c_ == 7)), "pt" if c == 7 else None)
            W("v", tk)
            vt = v_op(lambda e, m_=m, nb_=nb: e.tensor_single_scalar(
                AR3[:, m_ * 2048 + nb_ * 512: m_ * 2048 + nb_ * 512 + 512],
                PS_C[:, (nb_ % 2) * 512:(nb_ % 2) * 512 + 512], SC_MLA, op=ALU.mult), "pv")
            done(slot, vt)
    tap_fm("q", AR3)

    QR, ROT, TA, TB = SE, SE + 32, SE + 64, SE + 80
    for i in range(NC):
        for which in range(2):
            need("t", "psC0")
            if which == 0:
                for c in range(8):
                    tk = t_op(lambda e, c_=c, i_=i: e.matmul(
                        PS_C[:, 0:32],
                        AR1[:, c_ * 2048 + i_ * 128: c_ * 2048 + (i_ + 1) * 128],
                        WAR[:, O_WQR + c_ * 32:O_WQR + (c_ + 1) * 32],
                        start=(c_ == 0), stop=(c_ == 7)), "pt" if c == 7 else None)
            else:
                tk = t_op(lambda e, i_=i: e.matmul(
                    PS_C[:, 0:32],
                    CKV[:, i_ * 128:(i_ + 1) * 128],
                    WAR[:, O_WKR:O_WKR + 32], start=True, stop=True), "pt")
            W("v", tk)
            need("v", ("stg", SE))
            vt = v_op(lambda e: e.tensor_copy(STG[:, QR:QR + 32], PS_C[:, 0:32]), "pv")
            done("psC0", vt)
            v_op(lambda e, i_=i: e.tensor_mul(STG[:, TA:TA + 16], STG[:, QR:QR + 16],
                                              cost[:, i_ * 16:(i_ + 1) * 16]))
            v_op(lambda e, i_=i: e.tensor_mul(STG[:, TB:TB + 16], STG[:, QR + 16:QR + 32],
                                              sint[:, i_ * 16:(i_ + 1) * 16]))
            v_op(lambda e: e.tensor_sub(STG[:, ROT:ROT + 16], STG[:, TA:TA + 16],
                                        STG[:, TB:TB + 16]))
            v_op(lambda e, i_=i: e.tensor_mul(STG[:, TA:TA + 16], STG[:, QR + 16:QR + 32],
                                              cost[:, i_ * 16:(i_ + 1) * 16]))
            v_op(lambda e, i_=i: e.tensor_mul(STG[:, TB:TB + 16], STG[:, QR:QR + 16],
                                              sint[:, i_ * 16:(i_ + 1) * 16]))
            vt = v_op(lambda e: e.tensor_add(STG[:, ROT + 16:ROT + 32], STG[:, TA:TA + 16],
                                             STG[:, TB:TB + 16]), "pv")
            W("t", vt)
            need("t", "psC1")
            tk = t_op(lambda e: e.transpose(PS_C[0:32, 512:640], STG[:, ROT:ROT + 32],
                                            idn[:, :]), "pt")
            W("v", tk)
            base = 0 if which == 0 else 64
            scale = SC_MLA if which == 0 else 1.0
            vt = v_op(lambda e, i_=i, b_=base, s2=scale: e.tensor_single_scalar(
                ROPE[b_:b_ + 32, i_ * 128:(i_ + 1) * 128],
                PS_C[0:32, 512:640], s2, op=ALU.mult), "pv")
            done("psC1", vt)
            done(("stg", SE), vt)
    rope_tok = kb.tok("pv")

    if stages <= 1:
        return _finish(nc, kb, ctx)

    def attention(kind):
        mla = kind == "mla"
        KQ = 96 if mla else 64
        KBr = 97 if mla else 65
        brow = 96 if mla else 64

        if mla:
            W("v", rope_tok)
            vt = v_op(lambda e: e.tensor_copy(k_st[64:96, :], ROPE[64:96, :]), "pv")
            W("g", rope_tok)
            rtok = g_dma(q_st[64:96, :], ROPE[0:32, :])
            done("nstg", rtok)
            W("y", vt)
            y_dma(k_st[96:97, :], onesr[:, :])
        else:
            W("y", kb.tok("pt"))
            y_dma(k_st[64:65, :], onesr[:, :])
            W("g", kb.tok("pt"))
        aug_tok = kb.tok("dy")

        for h in range(NH):
            pe_prev = kb.tok("pt")
            if mla:
                W("g", pe_prev)
                vt = g_dma(q_st[0:64, :], AR3[(h % 2) * 64:(h % 2) * 64 + 64,
                                              (h // 2) * 2048:(h // 2) * 2048 + 2048])
                W("t", vt)
                W("v", pe_prev)
                BS = [(PS_C, 0, "psC0"), (PS_C, 512, "psC1"), (PS_A, 0, "bA0"),
                      (PS_A, 512, "bA1"), (PS_A, 1024, "bA2"), (PS_A, 1536, "bA3")]
                bi = 0
                for nb in range(4):
                    ps_, off_, key = BS[bi % 6]; bi += 1
                    need("t", key)
                    if key in ("bA0", "bA1"):
                        need("t", "psA0")
                    elif key in ("bA2", "bA3"):
                        need("t", "psA1")
                    tk = t_op(lambda e, h_=h, nb_=nb, p_=ps_, o_=off_: e.matmul(
                        p_[0:64, o_:o_ + 512],
                        WAR[:, O_WKUP + h_ * 64:O_WKUP + (h_ + 1) * 64],
                        CKV[:, nb_ * 512:nb_ * 512 + 512], start=True, stop=True), "pt")
                    W("v", tk)
                    vk = v_op(lambda e, nb_=nb, p_=ps_, o_=off_: e.tensor_copy(
                        k_st[0:64, nb_ * 512:nb_ * 512 + 512],
                        p_[0:64, o_:o_ + 512]), "pv")
                    done(key, vk)
                for g4 in range(4):
                    ps_, off_, key = BS[bi % 6]; bi += 1
                    need("t", key)
                    if key in ("bA0", "bA1"):
                        need("t", "psA0")
                    elif key in ("bA2", "bA3"):
                        need("t", "psA1")
                    for z in range(4):
                        tk = t_op(lambda e, h_=h, kc_=g4 * 4 + z, z_=z, p_=ps_, o_=off_:
                                  e.matmul(
                            p_[:, o_ + z_ * 64:o_ + z_ * 64 + 64],
                            CKV[:, kc_ * 128:(kc_ + 1) * 128],
                            WAR[:, O_WVUP + h_ * 64:O_WVUP + (h_ + 1) * 64],
                            start=True, stop=True), "pt" if z == 3 else None)
                    W("v", tk)
                    W("v", vst_ones_tok)
                    vk = v_op(lambda e, g_=g4, p_=ps_, o_=off_: e.tensor_copy(
                        v_st.rearrange("p (k w) -> p k w", w=65)[:, g_ * 4:(g_ + 1) * 4, 0:64],
                        p_[:, o_:o_ + 256].rearrange("p (k w) -> p k w", w=64)), "pv")
                    done(key, vk)
            else:
                sl = h % 2
                off = sl * 1536
                need("g", ("whb", sl))
                g_dma(whb[:, off:off + 512].rearrange("p (c f) -> p c f", c=8),
                      dr["w_qkv"][:, h * 64:(h + 1) * 64].rearrange("(c p) f -> p c f", p=128))
                g_dma(whb[:, off + 512:off + 1024].rearrange("p (c f) -> p c f", c=8),
                      dr["w_qkv"][:, D + h * 64:D + (h + 1) * 64].rearrange("(c p) f -> p c f", p=128))
                gt = g_dma(whb[:, off + 1024:off + 1536].rearrange("p (c f) -> p c f", c=8),
                           dr["w_qkv"][:, 2 * D + h * 64:2 * D + (h + 1) * 64].rearrange("(c p) f -> p c f", p=128))
                W("t", gt)
                W("t", pe_prev)
                BS = [(PS_C, 0, "psC0"), (PS_C, 512, "psC1"), (PS_A, 0, "bA0"),
                      (PS_A, 512, "bA1"), (PS_A, 1024, "bA2"), (PS_A, 1536, "bA3")]
                def bneed(key):
                    need("t", key)
                    if key in ("bA0", "bA1"):
                        need("t", "psA0")
                    elif key in ("bA2", "bA3"):
                        need("t", "psA1")
                bi = 0
                for nb in range(4):
                    ps_, off_, key = BS[bi % 6]; bi += 1
                    bneed(key)
                    for c in range(8):
                        tk = t_op(lambda e, nb_=nb, c_=c, o_=off, p_=ps_, q2_=off_:
                                  e.matmul(
                            p_[0:64, q2_:q2_ + 512],
                            whb[:, o_ + c_ * 64:o_ + (c_ + 1) * 64],
                            AR2[:, c_ * 2048 + nb_ * 512: c_ * 2048 + nb_ * 512 + 512],
                            start=(c_ == 0), stop=(c_ == 7)), "pt" if c == 7 else None)
                    W("v", tk)
                    W("v", pe_prev)
                    vk = v_op(lambda e, h_=h, nb_=nb, p_=ps_, q2_=off_: e.tensor_scalar(
                        q_st[0:64, nb_ * 512:nb_ * 512 + 512],
                        p_[0:64, q2_:q2_ + 512],
                        bqq[:, h_:h_ + 1], SC_FFN, op0=ALU.add, op1=ALU.mult), "pv")
                    done(key, vk)
                for nb in range(4):
                    ps_, off_, key = BS[bi % 6]; bi += 1
                    bneed(key)
                    for c in range(8):
                        tk = t_op(lambda e, nb_=nb, c_=c, o_=off, p_=ps_, q2_=off_:
                                  e.matmul(
                            p_[0:64, q2_:q2_ + 512],
                            whb[:, o_ + 512 + c_ * 64:o_ + 512 + (c_ + 1) * 64],
                            AR2[:, c_ * 2048 + nb_ * 512: c_ * 2048 + nb_ * 512 + 512],
                            start=(c_ == 0), stop=(c_ == 7)), "pt" if c == 7 else None)
                    W("v", tk)
                    vk = v_op(lambda e, h_=h, nb_=nb, p_=ps_, q2_=off_: e.tensor_scalar(
                        k_st[0:64, nb_ * 512:nb_ * 512 + 512],
                        p_[0:64, q2_:q2_ + 512],
                        bqk[:, h_:h_ + 1], None, op0=ALU.add), "pv")
                    done(key, vk)
                for g4 in range(4):
                    ps_, off_, key = BS[bi % 6]; bi += 1
                    bneed(key)
                    for z in range(4):
                        kc = g4 * 4 + z
                        for c in range(8):
                            t_op(lambda e, kc_=kc, c_=c, o_=off, z_=z, p_=ps_, q2_=off_:
                                 e.matmul(
                                p_[:, q2_ + z_ * 64:q2_ + z_ * 64 + 64],
                                AR2[:, c_ * 2048 + kc_ * 128: c_ * 2048 + (kc_ + 1) * 128],
                                whb[:, o_ + 1024 + c_ * 64:o_ + 1024 + (c_ + 1) * 64],
                                start=(c_ == 0), stop=False))
                        tk = t_op(lambda e, h_=h, kc_=kc, z_=z, p_=ps_, q2_=off_:
                                  e.matmul(
                            p_[:, q2_ + z_ * 64:q2_ + z_ * 64 + 64],
                            onesr[0:1, kc_ * 128:(kc_ + 1) * 128],
                            bqv[0:1, h_ * 64:(h_ + 1) * 64], start=False, stop=True),
                            "pt" if z == 3 else None)
                    W("v", tk)
                    W("v", vst_ones_tok)
                    vk = v_op(lambda e, g_=g4, p_=ps_, q2_=off_: e.tensor_copy(
                        v_st.rearrange("p (k w) -> p k w", w=65)[:, g_ * 4:(g_ + 1) * 4, 0:64],
                        p_[:, q2_:q2_ + 256].rearrange("p (k w) -> p k w", w=64)), "pv")
                    done(key, vk)
                done(("whb", sl), kb.tok("pt"))
            qkv_tok = kb.tok("pv")

            W("t", qkv_tok)
            for qc in range(NC):
                nb = 4 if not mla else min(4, qc // 4 + 1)
                wid = nb * 256
                reg = qc % 2
                rk = "psA%d" % reg
                need("t", rk, *( ("bA0", "bA1") if reg == 0 else ("bA2", "bA3") ))
                for b in range(nb):
                    tk = t_op(lambda e, qc_=qc, b_=b, r_=reg, K_=KQ: e.matmul(
                        PS_A[:, r_ * 1024 + b_ * 256:r_ * 1024 + (b_ + 1) * 256],
                        q_st[0:K_, qc_ * 128:(qc_ + 1) * 128],
                        k_st.rearrange("p (t two) -> p t two", two=2)
                            [0:K_, b_ * 256:(b_ + 1) * 256, 0],
                        start=True, stop=True), "pt" if b == nb - 1 else None)
                W("v", tk)
                vt = v_op(lambda e, qc_=qc, r_=reg, w_=wid: e.reduce_max(
                    m_all[:, qc_:qc_ + 1], PS_A[:, r_ * 1024:r_ * 1024 + w_],
                    axis=AX, negate=True), "pv", fence=True)
                done(rk, vt)
            W("t", kb.tok("pv"))
            need("t", "psC0")
            tk = t_op(lambda e: e.transpose(PS_C[0:16, 0:128], m_all[:, :], idn[:, :]), "pt")
            W("v", tk)
            vt = v_op(lambda e: e.tensor_copy(mT[:, :], PS_C[0:16, 0:128]), "pv")
            done("psC0", vt)
            W("g", vt)
            mrow_tok = g_dma(q_st[brow:brow + 1, :].rearrange("o (c q) -> o c q", c=16),
                             mT[:, :])

            for qpair in range(2):
                qbs = [qpair * 2, qpair * 2 + 1]
                chains = []
                for ci, qb in enumerate(qbs):
                    kcs = list(range((qb + 1) * 4)) if mla else list(range(16))
                    chains.append([kcs[j:j + 2] for j in range(0, len(kcs), 2)])
                nsteps = max(len(c) for c in chains)
                pv_last = {}
                pending = {0: None, 1: None}

                def emit_pv(ci, qb, pair, stp, pslot, gtk):
                    W("t", gtk)
                    last_step = len(chains[ci]) - 1
                    for z, kc in enumerate(pair):
                        tk = t_op(lambda e, kc_=kc, p2_=pslot, z_=z, ci_=ci,
                                  f_=(stp == 0 and z == 0),
                                  l_=(stp == last_step and z == len(pair) - 1):
                                  e.matmul(
                            PS_B[0:65, ci_ * 512:ci_ * 512 + 512],
                            v_st[:, kc_ * 65:(kc_ + 1) * 65],
                            ptb[:, p2_ * 1024 + z_ * 512:p2_ * 1024 + z_ * 512 + 512],
                            start=f_, stop=l_), "pt" if z == len(pair) - 1 else None)
                    done(("ptb", pslot), tk)
                    pv_last[ci] = tk
                    return tk

                for stp in range(nsteps):
                    for ci, qb in enumerate(qbs):
                        if stp >= len(chains[ci]):
                            continue
                        pair = chains[ci][stp]
                        ssl = ci
                        need("t", "psA%d" % ssl,
                             *( ("bA0", "bA1") if ssl == 0 else ("bA2", "bA3") ))
                        if stp == 0:
                            W("t", mrow_tok)
                            W("t", aug_tok)
                            need("t", "psB%d" % ci)
                        for z, kc in enumerate(pair):
                            tk = t_op(lambda e, kc_=kc, qb_=qb, ssl_=ssl, z_=z, K_=KBr:
                                      e.matmul(
                                PS_A[:, ssl_ * 1024 + z_ * 512:ssl_ * 1024 + z_ * 512 + 512],
                                k_st[0:K_, kc_ * 128:(kc_ + 1) * 128],
                                q_st[0:K_, qb_ * 512:qb_ * 512 + 512],
                                start=True, stop=True), "pt" if z == len(pair) - 1 else None)
                        W("v", tk)
                        wid = len(pair) * 512
                        need("v", ("sst", ssl))
                        vt = v_op(lambda e, ssl_=ssl, w_=wid: e.tensor_copy(
                            sst[:, ssl_ * 1024:ssl_ * 1024 + w_],
                            PS_A[:, ssl_ * 1024:ssl_ * 1024 + w_]), "pv")
                        done("psA%d" % ssl, vt)
                        W("s", vt)
                        pslot = ci * 2 + stp % 2
                        need("s", ("ptb", pslot))
                        st_ = s_op(lambda e, p2_=pslot, ssl_=ssl, w_=wid: e.activation(
                            ptb[:, p2_ * 1024:p2_ * 1024 + w_],
                            sst[:, ssl_ * 1024:ssl_ * 1024 + w_], AF.Exp,
                            bias=0.0, scale=1.0), "ps")
                        done(("sst", ssl), st_)
                        gtk = st_
                        if mla:
                            for z, kc in enumerate(pair):
                                if kc >= qb * 4:
                                    a = kc - qb * 4
                                    kb.wait("g", st_)
                                    gk = kb.op("g", lambda e, p2_=pslot, z_=z, a_=a:
                                               e.tensor_mul(
                                        ptb[:, p2_ * 1024 + z_ * 512:
                                            p2_ * 1024 + z_ * 512 + 512],
                                        ptb[:, p2_ * 1024 + z_ * 512:
                                            p2_ * 1024 + z_ * 512 + 512],
                                        maskT[:, a_ * 512:(a_ + 1) * 512]), "pg")
                                    gtk = gk
                        if pending[ci] is not None:
                            emit_pv(*pending[ci])
                        pending[ci] = (ci, qb, pair, stp, pslot, gtk)
                for ci in (0, 1):
                    if pending[ci] is not None:
                        emit_pv(*pending[ci])
                        pending[ci] = None
                for ci, qb in enumerate(qbs):
                    tk = pv_last[ci]
                    pv_done = tk
                    pvk = "psB%d" % ci
                    W("v", pv_done)
                    need("v", "invb", ("stg", SD))
                    vt = v_op(lambda e, q_=qb: e.tensor_copy(
                        inv_h[:, :], PS_B[64:65, (q_ % 2) * 512:(q_ % 2) * 512 + 512]), "pv")
                    W("s", vt)
                    s_op(lambda e: e.activation(invu2[:, :], inv_h[:, :], AF.Ln,
                                                    bias=0.0, scale=1.0))
                    st_ = s_op(lambda e: e.activation(inv_h[:, :], invu2[:, :], AF.Exp,
                                                              bias=0.0, scale=-1.0), "ps")
                    W("v", st_)
                    v_op(lambda e: e.tensor_copy(invhb[0:1, :], inv_h[:, :]))
                    v_op(lambda e: e.tensor_copy(invu2[:, :], invhb[0:1, :]))
                    vt = v_op(lambda e: e.tensor_sub(invlb[0:1, :], inv_h[:, :],
                                                             invu2[:, :]), "pv")
                    W("t", vt)
                    need("t", "psC1")
                    t_op(lambda e: e.matmul(PS_C[0:64, 512:1024], onesr[0:1, 0:64],
                                                invhb[0:1, :], start=True, stop=False))
                    tk = t_op(lambda e: e.matmul(PS_C[0:64, 512:1024], onesr[0:1, 0:64],
                                                     invlb[0:1, :], start=False, stop=True), "pt")
                    W("v", tk)
                    vt = v_op(lambda e: e.tensor_copy(invbc[:, :], PS_C[0:64, 512:1024]), "pv")
                    done("psC1", vt)
                    if h % 2 == 0:
                        vt2 = v_op(lambda e, h_=h, q_=qb: e.tensor_mul(
                            AR1[0:64,
                                (h_ // 2) * 2048 + q_ * 512:(h_ // 2) * 2048 + q_ * 512 + 512],
                            PS_B[0:64, (q_ % 2) * 512:(q_ % 2) * 512 + 512], invbc[:, :]), "pv")
                        done(pvk, vt2)
                    else:
                        need("v", "nstg")
                        vt2 = v_op(lambda e, q_=qb: e.tensor_mul(
                            nstg[:, :],
                            PS_B[0:64, (q_ % 2) * 512:(q_ % 2) * 512 + 512], invbc[:, :]), "pv")
                        done(pvk, vt2)
                        W("g", vt2)
                        gt2 = g_dma(AR1[64:128,
                                            (h // 2) * 2048 + qb * 512:(h // 2) * 2048 + qb * 512 + 512],
                                        nstg[:, :])
                        done("nstg", gt2)
                    done("invb", vt2)
                    done(("stg", SD), vt2)

    attention("mla")
    if stages <= 2:
        return _finish(nc, kb, ctx)

    mla_tok = kb.tok("pv")
    W("t", kb.tok("dg"))
    for i in range(NC):
        need("t", "psB0", "psB1")
        W("t", mla_tok)
        for nb in range(2):
            for c in range(8):
                tk = t_op(lambda e, i_=i, nb_=nb, c_=c: e.matmul(
                    PS_B[:, nb_ * 512:nb_ * 512 + 512],
                    AR1[:, c_ * 2048 + i_ * 128: c_ * 2048 + (i_ + 1) * 128],
                    WAR[:, O_WO1 + c_ * 1024 + nb_ * 512: O_WO1 + c_ * 1024 + nb_ * 512 + 512],
                    start=(c_ == 0), stop=(c_ == 7)), "pt" if c == 7 else None)
        W("v", tk)
        need("v", ("stg", SF))
        vt = v_op(lambda e: e.tensor_copy(STG[:, SF:SF + 1024], PS_B[:, :]), "pv")
        done("psB0", vt); done("psB1", vt)
        W("y", vt)
        dt_ = y_dma(s_dram[i * 128:(i + 1) * 128, :], STG[:, SF:SF + 1024])
        if "s" in tap_req:
            dt_ = y_dma(tap_d["s"][i * 128:(i + 1) * 128, :], STG[:, SF:SF + 1024])
        done(("stg", SF), dt_)
    s_mm_tok = kb.tok("pt")

    if stages <= 3:
        return _finish(nc, kb, ctx)

    W("g", s_mm_tok)
    wo2_tok = g_dma(WAR[:, O_WO2:O_WO2 + 8192].rearrange("p (c m) -> p c m", c=8),
                    dr["w_o2"].rearrange("(c p) m -> p c m", p=128))

    W("v", s_mm_tok)
    attention("ffn")
    ffn_tok = kb.tok("pv")
    tap_fm("o", AR1)
    if stages <= 4:
        return _finish(nc, kb, ctx)

    W("v", consts_tok)
    need("v", ("stg", SE))
    for c in range(8):
        v_op(lambda e, c_=c: e.tensor_scalar(
            STG[:, SE + c_ * 8:SE + (c_ + 1) * 8], gw32[:, c_ * 8:(c_ + 1) * 8],
            ln2g[:, c_:c_ + 1], None, op0=ALU.mult))
    v_op(lambda e: e.tensor_copy(gwhi[:, :], STG[:, SE:SE + 64]))
    v_op(lambda e: e.tensor_copy(STG[:, SE + 64:SE + 128], gwhi[:, :]))
    v_op(lambda e: e.tensor_sub(gwlo[:, :], STG[:, SE:SE + 64], STG[:, SE + 64:SE + 128]))
    v_op(lambda e: e.tensor_copy(grhi[:, :], gw32[:, :]))
    v_op(lambda e: e.tensor_copy(STG[:, SE + 64:SE + 128], grhi[:, :]))
    v_op(lambda e: e.tensor_sub(grlo[:, :], gw32[:, :], STG[:, SE + 64:SE + 128]))
    v_op(lambda e: e.tensor_copy(bchi[:, :], ln2b[:, :]))
    v_op(lambda e: e.tensor_copy(STG[:, SE + 64:SE + 72], bchi[:, :]))
    vt = v_op(lambda e: e.tensor_sub(bclo[:, :], ln2b[:, :], STG[:, SE + 64:SE + 72]), "pv")
    done(("stg", SE), vt)
    W("t", vt)
    need("t", "psC0")
    np_ = 0
    for c in range(8):
        for bb, gg in ((bchi, grhi), (bchi, grlo), (bclo, grhi)):
            np_ += 1
            tk = t_op(lambda e, c_=c, b_=bb, g_=gg, f_=(np_ == 1), l_=(np_ == 24): e.matmul(
                PS_C[0:1, 0:8], b_[:, c_:c_ + 1], g_[:, c_ * 8:(c_ + 1) * 8],
                start=f_, stop=l_), "pt" if np_ == 24 else None)
    W("v", tk)
    v_op(lambda e: e.tensor_copy(bpr[0:1, 0:8], PS_C[0:1, 0:8]))
    v_op(lambda e: e.tensor_add(bpr[0:1, 0:8], bpr[0:1, 0:8], gateb[0:1, :]))
    v_op(lambda e: e.tensor_copy(bprh[0:1, :], bpr[0:1, 0:8]))
    v_op(lambda e: e.tensor_copy(bpr[0:1, 8:16], bprh[0:1, :]))
    vt = v_op(lambda e: e.tensor_sub(bprl[0:1, :], bpr[0:1, 0:8], bpr[0:1, 8:16]), "pv")
    done("psC0", vt)
    W("t", vt)
    need("t", "psC0")
    t_op(lambda e: e.matmul(PS_C[:, 0:8], onesr[0:1, 0:128], bprh[0:1, :],
                            start=True, stop=False))
    tk = t_op(lambda e: e.matmul(PS_C[:, 0:8], onesr[0:1, 0:128], bprl[0:1, :],
                                 start=False, stop=True), "pt")
    W("v", tk)
    gp_tok = v_op(lambda e: e.tensor_copy(bprbc[:, :], PS_C[:, 0:8]), "pv")
    done("psC0", gp_tok)

    W("t", kb.tok("dg"))
    for i in range(NC):
        c8 = i * 8
        need("y", ("stg", SA))
        xtok = y_dma(STG[:, SA:SA + 1024], dr["x"][i * 128:(i + 1) * 128, :])
        need("t", "psB0", "psB1")
        W("t", ffn_tok)
        W("t", wo2_tok)
        for nb in range(2):
            for c in range(8):
                t_op(lambda e, i_=i, nb_=nb, c_=c: e.matmul(
                    PS_B[:, nb_ * 512:nb_ * 512 + 512],
                    AR1[:, c_ * 2048 + i_ * 128: c_ * 2048 + (i_ + 1) * 128],
                    WAR[:, O_WO2 + c_ * 1024 + nb_ * 512: O_WO2 + c_ * 1024 + nb_ * 512 + 512],
                    start=(c_ == 0), stop=False))
            tk = t_op(lambda e, i_=i, nb_=nb: e.matmul(
                PS_B[:, nb_ * 512:nb_ * 512 + 512],
                onesr[0:1, i_ * 128:(i_ + 1) * 128],
                bo_row[0:1, nb_ * 512:nb_ * 512 + 512], start=False, stop=True), "pt")
        W("v", xtok)
        need("v", ("stg", SD))
        st_ = v_op(lambda e, i_=i: e.tensor_scalar(
            STG[:, SD:SD + 1024], STG[:, SA:SA + 1024],
            stats_r[:, i_:i_ + 1], stats_mr[:, i_:i_ + 1],
            op0=ALU.mult, op1=ALU.add), "pv")
        done(("stg", SA), st_)
        W("v", tk)
        need("v", ("stg", SB_))
        vt = v_op(lambda e: e.tensor_add(STG[:, SB_:SB_ + 1024], PS_B[:, :],
                                         STG[:, SD:SD + 1024]), "pv")
        done("psB0", vt); done("psB1", vt)
        done(("stg", SD), vt)
        W("y", vt)
        dt_ = y_dma(x2_dram[i * 128:(i + 1) * 128, :], STG[:, SB_:SB_ + 1024])
        if "x2" in tap_req:
            dt_ = y_dma(tap_d["x2"][i * 128:(i + 1) * 128, :], STG[:, SB_:SB_ + 1024])
        rt = v_op(lambda e, c_=c8: e.reduce_sum(sm[:, c_:c_ + 1], STG[:, SB_:SB_ + 1024],
                                                axis=AX), "pv", fence=True)
        W("v", rt)
        vt = v_op(lambda e, c_=c8: e.tensor_scalar_mul(sm[:, c_ + 1:c_ + 2],
                                                       sm[:, c_:c_ + 1], -1.0 / D), "pv")
        need("v", ("stg", SC_))
        st_ = v_op(lambda e, c_=c8: e.tensor_scalar(
            STG[:, SC_:SC_ + 1024], STG[:, SB_:SB_ + 1024],
            sm[:, c_ + 1:c_ + 2], None, op0=ALU.add), "pv")
        done(("stg", SB_), (("dy", dt_[1])))
        need("v", ("stg", SD))
        v_op(lambda e: e.tensor_mul(STG[:, SD:SD + 1024], STG[:, SC_:SC_ + 1024],
                                    STG[:, SC_:SC_ + 1024]))
        vt = v_op(lambda e, c_=c8: e.reduce_sum(sm[:, c_ + 2:c_ + 3],
                                                STG[:, SD:SD + 1024], axis=AX), "pv", fence=True)
        done(("stg", SD), vt)
        W("s", vt)
        s_op(lambda e, c_=c8: e.activation(sm[:, c_ + 3:c_ + 4], sm[:, c_ + 2:c_ + 3],
                                           AF.Ln, bias=eps_c[:, 0:1], scale=1.0 / D))
        st2 = s_op(lambda e, c_=c8: e.activation(sm[:, c_ + 4:c_ + 5], sm[:, c_ + 3:c_ + 4],
                                           AF.Exp, bias=0.0, scale=-0.5), "ps")
        W("v", st2)
        st_ = v_op(lambda e, c_=c8: e.tensor_scalar(
            STG[:, SC_:SC_ + 1024], STG[:, SC_:SC_ + 1024],
            sm[:, c_ + 4:c_ + 5], None, op0=ALU.mult), "pv")
        W("t", st_)
        need("t", "psA0")
        for c in range(8):
            tk = t_op(lambda e, c_=c: e.transpose(
                PS_A[:, c_ * 128:(c_ + 1) * 128],
                STG[:, SC_ + c_ * 128:SC_ + (c_ + 1) * 128], idn[:, :]),
                "pt" if c == 7 else None)
        done(("stg", SC_), tk)
        W("v", tk)
        for c in range(8):
            v_op(lambda e, i_=i, c_=c: e.tensor_scalar(
                AR2[:, c_ * 2048 + i_ * 128: c_ * 2048 + (i_ + 1) * 128],
                PS_A[:, c_ * 128:(c_ + 1) * 128],
                ln2g[:, c_:c_ + 1], ln2b[:, c_:c_ + 1], op0=ALU.mult, op1=ALU.add))
        v_op(lambda e: e.tensor_copy(lghi[:, :], PS_A[:, 0:1024]))
        need("v", ("stg", SD))
        v_op(lambda e: e.tensor_copy(STG[:, SD:SD + 1024], lghi[:, :]))
        vt = v_op(lambda e: e.tensor_sub(lglo[:, :], PS_A[:, 0:1024],
                                         STG[:, SD:SD + 1024]), "pv")
        done("psA0", vt)
        done(("stg", SD), vt)
        W("t", vt); W("t", gp_tok)
        need("t", "psC0")
        np2 = 0
        for c in range(8):
            for lhs, rhs in ((lghi, gwhi), (lghi, gwlo), (lglo, gwhi)):
                np2 += 1
                tk = t_op(lambda e, c_=c, l_=lhs, r_=rhs, f_=(np2 == 1),
                          z_=(np2 == 24): e.matmul(
                    PS_C[:, 0:8], l_[:, c_ * 128:(c_ + 1) * 128],
                    r_[:, c_ * 8:(c_ + 1) * 8], start=f_, stop=z_),
                    "pt" if np2 == 24 else None)
        W("v", tk)
        need("v", ("stg", SE))
        v_op(lambda e: e.tensor_add(STG[:, SE:SE + 8], PS_C[:, 0:8], bprbc[:, :]))
        vt = v_op(lambda e, c_=c8: e.reduce_max(sm[:, c_ + 5:c_ + 6],
                                                STG[:, SE:SE + 8], axis=AX, negate=True), "pv", fence=True)
        done("psC0", vt)
        W("v", vt)
        v_op(lambda e, c_=c8: e.tensor_scalar(sm[:, c_ + 6:c_ + 7], sm[:, c_ + 5:c_ + 6],
                                              -1.0, None, op0=ALU.mult))
        v_op(lambda e, c_=c8: e.tensor_scalar(
            STG[:, SE + 8:SE + 16], STG[:, SE:SE + 8],
            sm[:, c_ + 6:c_ + 7], None, op0=ALU.is_equal))
        v_op(lambda e, c_=c8: e.scalar_tensor_tensor(
            STG[:, SE + 16:SE + 24], STG[:, SE + 8:SE + 16], -1e30,
            STG[:, SE:SE + 8], op0=ALU.mult, op1=ALU.add))
        vt = v_op(lambda e, c_=c8: e.reduce_max(sm[:, c_ + 7:c_ + 8],
                                                STG[:, SE + 16:SE + 24], axis=AX), "pv", fence=True)
        W("v", vt)
        v_op(lambda e, c_=c8: e.tensor_scalar(
            STG[:, SE + 24:SE + 32], STG[:, SE + 16:SE + 24],
            sm[:, c_ + 7:c_ + 8], None, op0=ALU.is_equal))
        W("s", kb.tok("pv"))
        st_ = s_op(lambda e, c_=c8: e.activation(
            sm[:, c_ + 2:c_ + 3], sm[:, c_ + 7:c_ + 8], AF.Exp,
            bias=sm[:, c_ + 5:c_ + 6], scale=1.0), "ps")
        W("v", st_)
        v_op(lambda e, c_=c8: e.tensor_scalar(sm[:, c_ + 3:c_ + 4], sm[:, c_ + 2:c_ + 3],
                                              1.0, None, op0=ALU.add))
        rt = v_op(lambda e, c_=c8: e.reciprocal(sm[:, c_ + 0:c_ + 1],
                                                sm[:, c_ + 3:c_ + 4]), "pv", fence=True)
        W("v", rt)
        v_op(lambda e, c_=c8: e.tensor_mul(sm[:, c_ + 1:c_ + 2], sm[:, c_ + 2:c_ + 3],
                                           sm[:, c_ + 0:c_ + 1]))
        v_op(lambda e, c_=c8: e.tensor_scalar(
            STG[:, SE + 24:SE + 32], STG[:, SE + 24:SE + 32],
            sm[:, c_ + 1:c_ + 2], None, op0=ALU.mult))
        vt = v_op(lambda e, i_=i, c_=c8: e.scalar_tensor_tensor(
            ew_all[:, i_ * 8:(i_ + 1) * 8], STG[:, SE + 8:SE + 16],
            sm[:, c_ + 0:c_ + 1], STG[:, SE + 24:SE + 32],
            op0=ALU.mult, op1=ALU.add), "pv")
        done(("stg", SE), vt)
    h2_tok = kb.tok("pv")
    tap_fm("h2", AR2)
    if "ew" in tap_req:
        for i in range(NC):
            W("y", h2_tok)
            y_dma(tap_d["ew"][i * 128:(i + 1) * 128, :], ew_all[:, i * 8:(i + 1) * 8])
    if stages <= 5:
        return _finish(nc, kb, ctx)

    for i in range(NC):
        need("t", "psC0")
        W("t", h2_tok)
        tk = t_op(lambda e, i_=i: e.transpose(
            PS_C[0:8, 0:128], ew_all[:, i_ * 8:(i_ + 1) * 8], idn[:, :]), "pt")
        W("v", tk)
        vt = v_op(lambda e, i_=i: e.tensor_copy(
            ewCb[:, i_ * 128:(i_ + 1) * 128], PS_C[0:8, 0:128]), "pv")
        done("psC0", vt)
    ew_tok = kb.tok("pv")

    W("g", kb.tok("pt"))
    b2t = g_dma(b2bf, dr["b2"][:, :])
    for tile in range(2):
        first_acc = True
        for e_ in range(E):
            W("g", ew_tok)
            need("g", "ewrbf")
            gt = g_dma(ew_rbf[0:1, :], ewCb[e_:e_ + 1, tile * 1024:(tile + 1) * 1024])
            W("t", gt)
            need("t", "psC0", "psC1")
            t_op(lambda e: e.matmul(PS_C[:, 0:512], onesr[0:1, 0:128],
                                    ew_rbf[0:1, 0:512], start=True, stop=True))
            tk = t_op(lambda e: e.matmul(PS_C[:, 512:1024], onesr[0:1, 0:128],
                                         ew_rbf[0:1, 512:1024], start=True, stop=True), "pt")
            W("v", tk)
            need("v", "ewbc")
            vt = v_op(lambda e: e.tensor_copy(ew_bc[:, :], PS_C[:, 0:1024]), "pv")
            done("psC0", vt); done("psC1", vt)
            need("v", "h2s")
            for c in range(8):
                vt = v_op(lambda e, c_=c, t_=tile: e.tensor_mul(
                    h2s[:, c_ * 1024:(c_ + 1) * 1024],
                    AR2[:, c_ * 2048 + t_ * 1024: c_ * 2048 + t_ * 1024 + 1024],
                    ew_bc[:, :]), "pv" if c == 7 else None)
            h2s_tok = vt
            done("ewbc", vt)

            for grp in range(2):
                need("g", "b1row")
                b1t = g_dma(b1row[0:1, :],
                            dr["b1"][e_:e_ + 1, grp * 2048:(grp + 1) * 2048])
                for fi in range(16):
                    slab = fi // 4
                    if fi % 4 == 0:
                        sl1 = slab % 2
                        fbase = grp * 2048 + slab * 512
                        need("g", ("w1s", sl1))
                        w1t = g_dma(
                            w1buf[:, sl1 * 4096:sl1 * 4096 + 4096].rearrange(
                                "p (c f) -> p c f", c=8),
                            dr["W1"][e_, :, fbase:fbase + 512].rearrange(
                                "(c p) f -> p c f", p=128))
                    psk = "psB0" if fi % 2 == 0 else "psA1"
                    ps = PS_B if fi % 2 == 0 else PS_A
                    pso = 0 if fi % 2 == 0 else 1024
                    need("t", psk)
                    if fi % 2 == 0:
                        need("t", "psB1")
                    W("t", w1t)
                    W("t", h2s_tok)
                    W("t", b1t)
                    o1 = (slab % 2) * 4096
                    fr = fi % 4
                    for nb2 in range(2):
                        for c in range(8):
                            t_op(lambda e, c_=c, o_=o1, fr_=fr, nb_=nb2, p_=ps, po_=pso:
                                 e.matmul(
                                p_[:, po_ + nb_ * 512:po_ + nb_ * 512 + 512],
                                w1buf[:, o_ + c_ * 512 + fr_ * 128:
                                      o_ + c_ * 512 + fr_ * 128 + 128],
                                h2s[:, c_ * 1024 + nb_ * 512:c_ * 1024 + nb_ * 512 + 512],
                                start=(c_ == 0), stop=False))
                        tk = t_op(lambda e, fi_=fi, nb_=nb2, p_=ps, po_=pso:
                                  e.matmul(
                            p_[:, po_ + nb_ * 512:po_ + nb_ * 512 + 512],
                            b1row[0:1, fi_ * 128:(fi_ + 1) * 128],
                            ew_rbf[0:1, nb_ * 512:nb_ * 512 + 512],
                            start=False, stop=True), "pt")
                    if fi % 4 == 3:
                        done(("w1s", slab % 2), tk)
                    W("v", tk)
                    need("v", "hid")
                    vt = v_op(lambda e, fi_=fi, p_=ps, po_=pso: e.tensor_single_scalar(
                        AR1[:, fi_ * 1024:(fi_ + 1) * 1024],
                        p_[:, po_:po_ + 1024], 0.0, op=ALU.max), "pv")
                    done(psk, vt)
                    if fi % 2 == 0:
                        done("psB1", vt)
                hid_tok = kb.tok("pv")

                for dk in range(8):
                    sl2 = dk % 2
                    need("g", ("w2s", sl2))
                    fb = grp * 2048
                    w2t1 = g_dma(
                        w2buf[:, sl2 * 2048:sl2 * 2048 + 1024].rearrange(
                            "p (c d) -> p c d", c=8),
                        dr["W2"][e_, fb:fb + 1024, dk * 128:(dk + 1) * 128].rearrange(
                            "(c p) d -> p c d", p=128))
                    w2t = g_dma(
                        w2buf[:, sl2 * 2048 + 1024:sl2 * 2048 + 2048].rearrange(
                            "p (c d) -> p c d", c=8),
                        dr["W2"][e_, fb + 1024:fb + 2048, dk * 128:(dk + 1) * 128].rearrange(
                            "(c p) d -> p c d", p=128))
                    psk = "psA0" if dk % 2 == 0 else "psC0"
                    ps = PS_A if dk % 2 == 0 else PS_C
                    need("t", psk)
                    if dk % 2 == 1:
                        need("t", "psC1")
                    W("t", w2t)
                    W("t", hid_tok)
                    for fi in range(16):
                        for nb2 in range(2):
                            tk = t_op(lambda e, fi_=fi, nb_=nb2, o_=sl2 * 2048, p_=ps:
                                      e.matmul(
                                p_[:, nb_ * 512:nb_ * 512 + 512],
                                w2buf[:, o_ + fi_ * 128:o_ + (fi_ + 1) * 128],
                                AR1[:, fi_ * 1024 + nb_ * 512:fi_ * 1024 + nb_ * 512 + 512],
                                start=(fi_ == 0), stop=(fi_ == 15 and nb_ == 1)),
                                "pt" if (fi == 15 and nb2 == 1) else None)
                    done(("w2s", sl2), tk)
                    W("v", tk)
                    if first_acc:
                        need("v", "moeT")
                        vt = v_op(lambda e, dk_=dk, p_=ps: e.tensor_copy(
                            MOE[:, dk_ * 1024:(dk_ + 1) * 1024], p_[:, 0:1024]), "pv")
                    else:
                        vt = v_op(lambda e, dk_=dk, p_=ps: e.tensor_add(
                            MOE[:, dk_ * 1024:(dk_ + 1) * 1024], p_[:, 0:1024],
                            MOE[:, dk_ * 1024:(dk_ + 1) * 1024]), "pv")
                    done(psk, vt)
                    if dk % 2 == 1:
                        done("psC1", vt)
                done("hid", kb.tok("pt"))
                done("b1row", kb.tok("pt"))
                first_acc = False
            done("h2s", kb.tok("pt"))
            done("ewrbf", kb.tok("pt"))
        moe_tok = kb.tok("pv")

        for j in range(8):
            i = tile * 8 + j
            need("y", ("stg", SA), ("stg", SB_), ("stg", SC_))
            x2t = y_dma(STG[:, SA:SA + 1024], x2_dram[i * 128:(i + 1) * 128, :])
            stt_ = y_dma(STG[:, SB_:SB_ + 1024], s_dram[i * 128:(i + 1) * 128, :])
            xt = y_dma(STG[:, SC_:SC_ + 1024], dr["x"][i * 128:(i + 1) * 128, :])
            need("t", "psC0", "psC1")
            W("t", ew_tok)
            W("t", b2t)
            tk0 = t_op(lambda e, i_=i: e.matmul(
                PS_C[:, 0:512], ewCb[0:8, i_ * 128:(i_ + 1) * 128],
                b2bf[:, 0:512], start=True, stop=True))
            tk0 = t_op(lambda e, i_=i: e.matmul(
                PS_C[:, 512:1024], ewCb[0:8, i_ * 128:(i_ + 1) * 128],
                b2bf[:, 512:1024], start=True, stop=True), "pt")
            need("t", "psB0", "psB1")
            W("t", moe_tok)
            for dk in range(8):
                tk1 = t_op(lambda e, dk_=dk, j_=j: e.transpose(
                    PS_B[:, dk_ * 128:(dk_ + 1) * 128],
                    MOE[:, dk_ * 1024 + j_ * 128:dk_ * 1024 + (j_ + 1) * 128],
                    idn[:, :]), "pt" if dk == 7 else None)
            W("v", tk0); W("v", x2t)
            need("v", ("stg", SD))
            v_op(lambda e: e.tensor_add(STG[:, SD:SD + 1024], PS_C[:, 0:1024],
                                        STG[:, SA:SA + 1024]))
            W("v", tk1)
            vt = v_op(lambda e: e.tensor_add(STG[:, SD:SD + 1024], PS_B[:, 0:1024],
                                             STG[:, SD:SD + 1024]), "pv")
            done("moeT", kb.tok("pt"))
            done("psB0", vt); done("psB1", vt); done("psC0", vt); done("psC1", vt)
            done(("stg", SA), vt)
            if "gpre" in tap_req:
                W("y", vt)
                y_dma(tap_d["gpre"][i * 128:(i + 1) * 128, :], STG[:, SD:SD + 1024])
            W("s", vt)
            need("s", ("stg", SF))
            st2 = s_op(lambda e: e.activation(STG[:, SF:SF + 1024], STG[:, SD:SD + 1024],
                                              AF.Silu, bias=0.0, scale=1.0), "ps")
            done(("stg", SD), st2)
            W("v", st2); W("v", stt_); W("v", xt)
            v_op(lambda e: e.tensor_mul(STG[:, SF:SF + 1024], STG[:, SF:SF + 1024],
                                        STG[:, SB_:SB_ + 1024]))
            vt = v_op(lambda e: e.tensor_add(STG[:, SF:SF + 1024], STG[:, SF:SF + 1024],
                                             STG[:, SC_:SC_ + 1024]), "pv")
            done(("stg", SB_), vt)
            done(("stg", SC_), vt)
            W("y", vt)
            ot = y_dma(out_d[i * 128:(i + 1) * 128, :], STG[:, SF:SF + 1024], inc="od")
            done(("stg", SF), ot)

    return _finish(nc, kb, ctx)


def _finish(nc, kb, ctx):
    kb.wait("y", kb.tok("od"))
    kb.wait("y", kb.tok("dy"))
    kb.wait("g", kb.tok("dg"))
    with nc.Block() as block:
        @block.tensor
        def _(e):
            kb.replay("t", e)
        @block.vector
        def _(e):
            kb.replay("v", e)
        @block.scalar
        def _(e):
            kb.replay("s", e)
        @block.gpsimd
        def _(e):
            kb.replay("g", e)
        @block.sync
        def _(e):
            kb.replay("y", e)
    ctx.close()
    return nc


def _consts():
    inv = 1.0 / 10000.0 ** (np.arange(0, DR, 2, dtype=np.float32) / DR)
    tt = np.arange(T, dtype=np.float32)
    fr = np.outer(tt, inv)
    cos, sin = np.cos(fr).astype(np.float32), np.sin(fr).astype(np.float32)
    idn = np.eye(128, dtype=np.float32)
    masks = np.zeros((128, 4 * 512), dtype=ml_dtypes.bfloat16)
    for a, r in enumerate((0, 128, 256, 384)):
        ii = np.arange(128)[:, None] + r
        jj = np.arange(512)[None, :]
        masks[:, a * 512:(a + 1) * 512] = (ii <= jj).astype(ml_dtypes.bfloat16)
    return {"c_cos": cos, "c_sin": sin, "c_idn": idn, "c_maskT": masks,
            "c_ones": np.ones((1, T), dtype=ml_dtypes.bfloat16),
            "c_onescol": np.ones((128, 1), dtype=ml_dtypes.bfloat16)}


def _prep_shared(inputs):
    f = np.ascontiguousarray
    sh = {
        "w_kv_down": f(inputs["L0_attn_W_kv_down"]),
        "w_k_up": f(inputs["L0_attn_W_k_up"]),
        "w_v_up": f(inputs["L0_attn_W_v_up"]),
        "w_q": f(inputs["L0_attn_W_q"]),
        "w_q_rope": f(inputs["L0_attn_W_q_rope"]),
        "w_k_rope": f(inputs["L0_attn_W_k_rope"]),
        "w_o1": f(inputs["L0_attn_W_o"]),
        "ln1_g": f(inputs["L0_ffn_ln1_g"].reshape(8, 128).T),
        "ln1_b": f(inputs["L0_ffn_ln1_b"].reshape(8, 128).T),
        "w_qkv": f(inputs["L0_ffn_w_qkv"]),
        "b_qkv_q": f(inputs["L0_ffn_b_qkv"][0:D].reshape(16, 64).T),
        "b_qkv_k": f(inputs["L0_ffn_b_qkv"][D:2 * D].reshape(16, 64).T),
        "b_qkv_v": f(inputs["L0_ffn_b_qkv"][2 * D:3 * D].reshape(1, D)),
        "w_o2": f(inputs["L0_ffn_w_o"]),
        "b_o": f(inputs["L0_ffn_b_o"].reshape(1, D)),
        "ln2_g": f(inputs["L0_ffn_ln2_g"].reshape(8, 128).T),
        "ln2_b": f(inputs["L0_ffn_ln2_b"].reshape(8, 128).T),
        "gate_w": f(inputs["L0_ffn_gate_w"].reshape(8, 128, 8).transpose(1, 0, 2).reshape(128, 64)),
        "gate_b": f(inputs["L0_ffn_gate_b"].reshape(1, 8)),
        "W1": f(inputs["L0_ffn_W1"]),
        "b1": f(inputs["L0_ffn_b1"]),
        "W2": f(inputs["L0_ffn_W2"]),
        "b2": f(inputs["L0_ffn_b2"]),
    }
    sh.update(_consts())
    return sh


def kernel(**inputs):
    x = np.ascontiguousarray(np.asarray(inputs["L0_attn_x"], dtype=np.float32))
    shared = _prep_shared({k: np.asarray(v) for k, v in inputs.items()})
    nc = build_program(taps=TAPS, stages=STAGES)
    in_maps = []
    for c in range(8):
        m = dict(shared)
        m["x"] = x[c]
        in_maps.append(m)
    res = run_bass_kernel_spmd(nc, in_maps, core_ids=list(range(8)))
    out = np.stack([np.asarray(r["out"], dtype=np.float32) for r in res.results], axis=0)
    if TAPS:
        kernel.tap_results = res.results
    return out
